# version 06
# speedup vs baseline: 1.1854x; 1.1840x over previous
import numpy as np
import ml_dtypes
from contextlib import ExitStack

import concourse.bass as bass
import concourse.mybir as mybir
from concourse.bass_utils import run_bass_kernel_spmd

F32 = mybir.dt.float32
BF16 = mybir.dt.bfloat16
AF = mybir.ActivationFunctionType
ALU = mybir.AluOpType
AX = mybir.AxisListType
BF = ml_dtypes.bfloat16

ENGINES = ("sync", "tensor", "vector", "scalar", "gpsimd")
NCORES = 8
LN_EPS = 1e-5
TCORE = 4096
NT = TCORE // 512
SEQ = 2048


class Prog:
    MAX_FCONST = 64

    def __init__(self):
        self._pref = ""
        self.nc = bass.Bass("TRN2", target_bir_lowering=False)
        self.stack = ExitStack()
        self.items = {e: [] for e in ENGINES}
        self.last_w = {}
        self.last_r = {}
        self._sb_names = set()
        self._finalized = False
        self.prologue_toks = []
        self._scopes = []
        self._barriers = []
        self._barrier_applied = {e: 0 for e in ENGINES}
        self._fc_vals = {}
        self._fc_tile = self.sb("c_fvals", [128, self.MAX_FCONST], F32)
        self._fc_ap = self.dram_in("c_fvals_in", [128, self.MAX_FCONST], F32)
        t = self.dma("sync", self._fc_tile[:], self._fc_ap, writes=("c_fvals",))
        self.prologue_toks.append(t)

    def fconst(self, value):
        value = float(value)
        if value in self._fc_vals:
            return value
        col = len(self._fc_vals)
        assert col < self.MAX_FCONST
        self._fc_vals[value] = col
        self.nc.const_aps.register(F32, value, self._fc_tile[:, col:col + 1])
        return value

    def fconst_host_array(self):
        a = np.zeros((128, self.MAX_FCONST), dtype=np.float32)
        for v, c in self._fc_vals.items():
            a[:, c] = v
        return a

    def push_scope(self):
        self._scopes.append((ExitStack(), []))

    def pop_scope(self):
        st, names = self._scopes.pop()
        st.close()
        self._barriers.append(self.barrier_token())

    def _enter(self, cm, name):
        if self._scopes:
            st, names = self._scopes[-1]
            names.append(name)
            return st.enter_context(cm)
        return self.stack.enter_context(cm)

    def _uname(self, name):
        if name in self._sb_names:
            i = 2
            while f"{name}__{i}" in self._sb_names:
                i += 1
            name = f"{name}__{i}"
        self._sb_names.add(name)
        return name

    def sb(self, name, shape, dtype):
        name = self._uname(name)
        return self._enter(self.nc.sbuf_tensor(name, shape, dtype), name)

    def ps(self, name, shape, dtype=F32):
        name = self._uname(name)
        return self._enter(self.nc.psum_tensor(name, shape, dtype), name)

    def dram_in(self, name, shape, dtype=F32):
        if name.startswith("w_"):
            name = self._pref + name
        return self.nc.dram_tensor(name, list(shape), dtype, kind="ExternalInput").ap()

    def dram_out(self, name, shape, dtype=F32):
        return self.nc.dram_tensor(name, list(shape), dtype, kind="ExternalOutput").ap()

    def dram_tmp(self, name, shape, dtype=F32):
        return self.nc.dram_tensor(name, list(shape), dtype).ap()

    def emit(self, eng, builder, reads=(), writes=(), dma=False, cc=False,
             extra_waits=()):
        idx = len(self.items[eng])
        tok = (("k" if cc else "d") if (dma or cc) else "c", eng, idx)
        wset = set()
        for k in reads:
            t = self.last_w.get(k)
            if t is not None:
                wset.add(t)
        for k in writes:
            t = self.last_w.get(k)
            if t is not None:
                wset.add(t)
            for t in self.last_r.get(k, ()):
                wset.add(t)
        for t in extra_waits:
            if t is not None:
                wset.add(t)
        nb = len(self._barriers)
        if self._barrier_applied[eng] < nb:
            for bi in range(self._barrier_applied[eng], nb):
                wset |= set(self._barriers[bi])
            self._barrier_applied[eng] = nb
        if eng == "tensor":
            wset = {t for t in wset if not (t[0] == "c" and t[1] == "tensor")}
        wset.discard(tok)
        self.items[eng].append({"builder": builder, "waits": wset, "tok": tok})
        for k in reads:
            self.last_r.setdefault(k, []).append(tok)
        for k in writes:
            self.last_w[k] = tok
            self.last_r[k] = []
        return tok

    def dma(self, eng, out, in_, reads=(), writes=(), **kw):
        return self.emit(
            eng, lambda e: e.dma_start(out=out, in_=in_, **kw),
            reads=reads, writes=writes, dma=True)

    def barrier_token(self):
        toks = []
        for e in ENGINES:
            if self.items[e]:
                toks.append(self.items[e][-1]["tok"])
        return toks

    def finalize(self):
        assert not self._finalized
        self._finalized = True
        nc = self.nc
        needed = set()
        for e in ENGINES:
            for it in self.items[e]:
                needed |= it["waits"]
        KSLOT = 12
        sems = {}
        dpool = {}
        for e in ENGINES:
            sems[("c", e)] = self.stack.enter_context(nc.semaphore(f"sc_{e}"))
            if any(it["tok"][0] == "d" for it in self.items[e]):
                dpool[e] = [self.stack.enter_context(nc.semaphore(f"sd_{e}_{i}"))
                            for i in range(KSLOT)]
            if any(it["tok"][0] == "k" for it in self.items[e]):
                sems[("k", e)] = self.stack.enter_context(nc.semaphore(f"sk_{e}"))
        ptoks = set(self.prologue_toks)
        for e in ENGINES:
            for it in self.items[e]:
                if it["tok"] in ptoks:
                    continue
                it["waits"] |= {t for t in self.prologue_toks if t != it["tok"]}
                needed |= it["waits"]
                break
        value_of = {}
        final_dma = {}
        final_cc = {}
        for e in ENGINES:
            ccnt = 0
            kcnt = 0
            dseq = 0
            for it in self.items[e]:
                kind = it["tok"][0]
                if kind == "k":
                    it["inc"] = True
                    kcnt += 1
                    value_of[it["tok"]] = (("k", e), kcnt)
                    final_cc[e] = kcnt
                elif kind == "d":
                    slot = dseq % KSLOT
                    val = 16 * (dseq // KSLOT + 1)
                    it["inc"] = True
                    it["dslot"] = slot
                    it["dval"] = val
                    value_of[it["tok"]] = (("d", e, slot), val)
                    final_dma[(e, slot)] = val
                    dseq += 1
                elif it["tok"] in needed:
                    it["inc"] = True
                    ccnt += 1
                    value_of[it["tok"]] = (("c", e), ccnt)
                else:
                    it["inc"] = False
        for e in ENGINES:
            water = {}
            for it in self.items[e]:
                wl = []
                if it["tok"][0] == "d" and it.get("dval", 0) > 16:
                    sk = ("d", e, it["dslot"])
                    v = it["dval"] - 16
                    if water.get(sk, 0) < v:
                        water[sk] = v
                        wl.append((sk, v))
                for t in sorted(it["waits"], key=lambda t: (t[1], t[0], t[2])):
                    sem_key, v = value_of[t]
                    if water.get(sem_key, 0) < v:
                        water[sem_key] = v
                        wl.append((sem_key, v))
                it["wl"] = wl
        items = self.items

        with nc.Block() as block:
            def mk(e):
                def body(engobj):
                    for it in items[e]:
                        for (sem_key, v) in it["wl"]:
                            s = (dpool[sem_key[1]][sem_key[2]]
                                 if sem_key[0] == "d" else sems[sem_key])
                            engobj.wait_ge(s, v)
                        inst = it["builder"](engobj)
                        if it["inc"]:
                            assert inst is not None, (e, it["tok"])
                            if it["tok"][0] == "d":
                                inst.then_inc(dpool[e][it["dslot"]], 16)
                            elif it["tok"][0] == "k":
                                inst.then_inc(sems[("k", e)], 1)
                            else:
                                inst.then_inc(sems[("c", e)], 1)
                    if e == "sync":
                        for (qe, slot), v in sorted(final_dma.items()):
                            engobj.wait_ge(dpool[qe][slot], v)
                        for qe, v in sorted(final_cc.items()):
                            engobj.wait_ge(sems[("k", qe)], v)
                return body
            block.sync(mk("sync"))
            block.tensor(mk("tensor"))
            block.vector(mk("vector"))
            block.scalar(mk("scalar"))
            block.gpsimd(mk("gpsimd"))
        return nc


class Ctx:
    def __init__(self, p: Prog, c):
        self.p = p
        self.iden = p.sb("cx_iden", [128, 128], BF16)
        self.ones_col = p.sb("cx_onescol", [128, 1], BF16)
        self.ones_row = p.sb("cx_onesrow", [1, 128], F32)
        p.dma("sync", self.iden[:], c["iden_bf16"], writes=("cx_iden",))
        p.dma("sync", self.ones_col[:], c["ones_col"], writes=("cx_onescol",))
        p.dma("sync", self.ones_row[:], c["ones_row"], writes=("cx_onesrow",))
        self.ps_st = p.ps("cx_psst", [128, 2, 512], F32)
        self.bfa = p.sb("cx_bfa", [128, 8, 512], BF16)
        self.bfb = p.sb("cx_bfb", [128, 8, 512], BF16)
        self.row = p.sb("cx_row", [1, 4, 512], F32)
        self.ones8 = p.sb("cx_ones8", [128, 8], F32)
        self._sqf = None
        self._sqf_n = 0
        p.emit("vector", lambda e: e.memset(self.ones8[:], 1.0),
               writes=("cx_ones8",))
        p.fconst(LN_EPS)

    @property
    def sqf(self):
        if self._sqf is None:
            self._sqf_n += 1
            self._sqf = self.p.sb(f"cx_sqf{self._sqf_n}", [128, 8, 512], F32)
        return self._sqf

    def drop_sqf(self):
        self._sqf = None

    def part_sum(self, src_bf_chunks, key_src, ps_view, key_ps, nch):
        p = self.p
        for k in range(nch):
            p.emit("tensor", lambda e, k=k: e.matmul(
                ps_view, self.ones_col[:], src_bf_chunks(k),
                start=(k == 0), stop=(k == nch - 1)),
                reads=(key_src, "cx_onescol"), writes=(key_ps,))

    def replicate_row(self, row_ap, key_row, ps_view, key_ps):
        self.p.emit("tensor", lambda e: e.matmul(
            ps_view, self.ones_row[:], row_ap, start=True, stop=True),
            reads=(key_row, "cx_onesrow"), writes=(key_ps,))

    def part_sum_f(self, src_f_chunks, key_src, ps_view, key_ps, nch):
        p = self.p
        for k in range(nch):
            p.emit("tensor", lambda e, k=k: e.matmul(
                ps_view, self.ones8[:, 0:1], src_f_chunks(k),
                start=(k == 0), stop=(k == nch - 1)),
                reads=(key_src, "cx_ones8"), writes=(key_ps,))

    def ln_fm(self, x_ap, key_x, out_ap, key_out, tag, nch=8, Tt=512, nfeat=1024,
              out2_bf=None, key_out2=None, precise=False, sub_inplace=False):
        p = self.p
        p.fconst(1.0 / nfeat)
        if precise:
            sq = self.sqf
            p.emit("scalar", lambda e: e.activation(
                out=sq[:, 0:nch, 0:Tt], in_=x_ap,
                func=AF.Square), reads=(key_x,), writes=("cx_sqf",))
            self.part_sum_f(lambda k: x_ap[:, k, 0:Tt] if nch > 1 else x_ap,
                            key_x, self.ps_st[0:1, 0, 0:Tt], ("cx_psst", 0), nch)
            self.part_sum_f(lambda k: sq[:, k, 0:Tt], "cx_sqf",
                            self.ps_st[0:1, 1, 0:Tt], ("cx_psst", 1), nch)
        else:
            p.emit("vector", lambda e: e.tensor_copy(
                self.bfa[:, 0:nch, 0:Tt], x_ap), reads=(key_x,), writes=("cx_bfa",))
            p.emit("scalar", lambda e: e.activation(
                out=self.bfb[:, 0:nch, 0:Tt], in_=self.bfa[:, 0:nch, 0:Tt],
                func=AF.Square), reads=("cx_bfa",), writes=("cx_bfb",))
            self.part_sum(lambda k: self.bfa[:, k, 0:Tt], "cx_bfa",
                          self.ps_st[0:1, 0, 0:Tt], ("cx_psst", 0), nch)
            self.part_sum(lambda k: self.bfb[:, k, 0:Tt], "cx_bfb",
                          self.ps_st[0:1, 1, 0:Tt], ("cx_psst", 1), nch)
        p.emit("vector", lambda e: e.tensor_scalar(
            self.row[:, 0, 0:Tt], self.ps_st[0:1, 0, 0:Tt], 1.0 / nfeat, None,
            op0=ALU.mult), reads=(("cx_psst", 0),), writes=("cx_row0",))
        p.emit("vector", lambda e: e.tensor_scalar(
            self.row[:, 1, 0:Tt], self.ps_st[0:1, 1, 0:Tt], 1.0 / nfeat, None,
            op0=ALU.mult), reads=(("cx_psst", 1),), writes=("cx_row1",))
        p.emit("vector", lambda e: e.tensor_tensor(
            self.row[:, 2, 0:Tt], self.row[:, 0, 0:Tt], self.row[:, 0, 0:Tt],
            op=ALU.mult), reads=("cx_row0",), writes=("cx_row2",))
        p.emit("vector", lambda e: e.tensor_tensor(
            self.row[:, 1, 0:Tt], self.row[:, 1, 0:Tt], self.row[:, 2, 0:Tt],
            op=ALU.subtract), reads=("cx_row1", "cx_row2"), writes=("cx_row1",))
        p.emit("scalar", lambda e: e.activation(
            out=self.row[:, 3, 0:Tt], in_=self.row[:, 1, 0:Tt], func=AF.Sqrt,
            bias=LN_EPS), reads=("cx_row1",), writes=("cx_row3",))
        p.emit("vector", lambda e: e.reciprocal(
            self.row[:, 2, 0:Tt], self.row[:, 3, 0:Tt]),
            reads=("cx_row3",), writes=("cx_row2",))
        self.replicate_row(self.row[:, 0, 0:Tt], "cx_row0",
                           self.ps_st[:, 0, 0:Tt], ("cx_psst", 0))
        self.replicate_row(self.row[:, 2, 0:Tt], "cx_row2",
                           self.ps_st[:, 1, 0:Tt], ("cx_psst", 1))
        mu = self.ps_st[:, 0, 0:Tt].unsqueeze(1).broadcast_to([128, nch, Tt])
        rstd = self.ps_st[:, 1, 0:Tt].unsqueeze(1).broadcast_to([128, nch, Tt])
        if sub_inplace:
            p.emit("vector", lambda e: e.tensor_tensor(
                x_ap, x_ap, mu, op=ALU.subtract),
                reads=(key_x, ("cx_psst", 0)), writes=(key_x,))
            p.emit("vector", lambda e: e.tensor_tensor(
                out_ap, x_ap, rstd, op=ALU.mult),
                reads=(key_x, ("cx_psst", 1)), writes=(key_out,))
            return
        p.emit("vector", lambda e: e.tensor_tensor(
            out_ap, x_ap, mu, op=ALU.subtract),
            reads=(key_x, ("cx_psst", 0)), writes=(key_out,))
        p.emit("vector", lambda e: e.tensor_tensor(
            out_ap, out_ap, rstd, op=ALU.mult),
            reads=(key_out, ("cx_psst", 1)), writes=(key_out,))
        if out2_bf is not None:
            p.emit("vector", lambda e: e.tensor_copy(out2_bf, out_ap),
                   reads=(key_out,), writes=(key_out2,))


def load_weight_bf16(p, stage, stage_key, dst_view, dst_key, hbm_ap, nk, width,
                     queue="sync"):
    src = hbm_ap.rearrange("(k p) n -> p k n", p=128)
    p.dma("gpsimd", dst_view[:, 0:nk, 0:width], src,
          reads=(dst_key,), writes=(dst_key,))


def linear_fm(p, xb_chunks, key_x, key_w, ps, key_ps_fn, out_fn, nk, ndo,
              Tt=512, w_view=None):
    nb = ps.shape[1]
    for oc in range(ndo):
        b = oc % nb
        for k in range(nk):
            p.emit("tensor", lambda e, k=k, oc=oc, b=b: e.matmul(
                ps[:, b, 0:Tt], w_view(k, oc), xb_chunks(k),
                start=(k == 0), stop=(k == nk - 1)),
                reads=(key_w, key_x), writes=(key_ps_fn(b),))
        out_fn(oc, ps[:, b, 0:Tt], key_ps_fn(b))


def emit_l3_ffn(p: Prog, cx: Ctx, xin, xout, W, C):
    p.push_scope()
    m_sb = p.sb("l3f_m", [128, 8, 1024], BF16)
    bias_sb = p.sb("l3f_bias", [128, 8], F32)
    p.push_scope()
    wr_bf = p.sb("l3f_wr", [128, 8, 1024], BF16)
    wi_bf = p.sb("l3f_wi", [128, 8, 1024], BF16)
    wld = p.sb("l3f_wld", [128, 8, 1024], F32)
    cs_c = p.sb("l3f_c", [128, 8, 1024], BF16)
    cs_s = p.sb("l3f_s", [128, 8, 1024], BF16)
    cs_sn = p.sb("l3f_sn", [128, 8, 1024], BF16)
    pr_sb = p.sb("l3f_pr", [128, 8, 1024], BF16)
    pi_sb = p.sb("l3f_pi", [128, 8, 1024], BF16)
    ps_m = p.ps("l3f_psm", [128, 4, 512], F32)

    p.dma("sync", cs_c[:], C["dft_c"].rearrange("(k p) n -> p k n", p=128), writes=("l3f_c",))
    p.dma("sync", cs_s[:], C["dft_s"].rearrange("(k p) n -> p k n", p=128), writes=("l3f_s",))
    p.dma("sync", cs_sn[:], C["dft_sneg"].rearrange("(k p) n -> p k n", p=128), writes=("l3f_sn",))
    load_weight_bf16(p, wld, "l3f_wld", wr_bf[:], "l3f_wr", W["WrT"], 8, 1024)
    load_weight_bf16(p, wld, "l3f_wld", wi_bf[:], "l3f_wi", W["WiT"], 8, 1024)
    p.dma("sync", bias_sb[:], W["bias_pc"], writes=("l3f_bias",))

    p.fconst(1.0 / 1024.0)
    for ft in range(8):
        for ot in range(2):
            b = (ft * 2 + ot) % 2
            for c in range(8):
                p.emit("tensor", lambda e, c=c, ft=ft, ot=ot, b=b: e.matmul(
                    ps_m[:, b, :], wr_bf[:, c, ft * 128:(ft + 1) * 128],
                    cs_c[:, c, ot * 512:(ot + 1) * 512],
                    start=(c == 0), stop=False),
                    reads=("l3f_wr", "l3f_c"), writes=(("l3f_psm", b),))
            for c in range(8):
                p.emit("tensor", lambda e, c=c, ft=ft, ot=ot, b=b: e.matmul(
                    ps_m[:, b, :], wi_bf[:, c, ft * 128:(ft + 1) * 128],
                    cs_sn[:, c, ot * 512:(ot + 1) * 512],
                    start=False, stop=(c == 7)),
                    reads=("l3f_wi", "l3f_sn"), writes=(("l3f_psm", b),))
            p.emit("vector", lambda e, ft=ft, ot=ot, b=b: e.tensor_copy(
                pr_sb[:, ft, ot * 512:(ot + 1) * 512], ps_m[:, b, :]),
                reads=(("l3f_psm", b),), writes=("l3f_pr",))
            for c in range(8):
                p.emit("tensor", lambda e, c=c, ft=ft, ot=ot, b=b: e.matmul(
                    ps_m[:, 2 + b, :], wr_bf[:, c, ft * 128:(ft + 1) * 128],
                    cs_s[:, c, ot * 512:(ot + 1) * 512],
                    start=(c == 0), stop=False),
                    reads=("l3f_wr", "l3f_s"), writes=(("l3f_psm", 2 + b),))
            for c in range(8):
                p.emit("tensor", lambda e, c=c, ft=ft, ot=ot, b=b: e.matmul(
                    ps_m[:, 2 + b, :], wi_bf[:, c, ft * 128:(ft + 1) * 128],
                    cs_c[:, c, ot * 512:(ot + 1) * 512],
                    start=False, stop=(c == 7)),
                    reads=("l3f_wi", "l3f_c"), writes=(("l3f_psm", 2 + b),))
            p.emit("vector", lambda e, ft=ft, ot=ot, b=b: e.tensor_copy(
                pi_sb[:, ft, ot * 512:(ot + 1) * 512], ps_m[:, 2 + b, :]),
                reads=(("l3f_psm", 2 + b),), writes=("l3f_pi",))
    for dt_ in range(8):
        for ot in range(2):
            b = (dt_ * 2 + ot) % 4
            for c in range(8):
                p.emit("tensor", lambda e, c=c, dt_=dt_, ot=ot, b=b: e.matmul(
                    ps_m[:, b, :], cs_c[:, c, dt_ * 128:(dt_ + 1) * 128],
                    pr_sb[:, c, ot * 512:(ot + 1) * 512],
                    start=(c == 0), stop=False),
                    reads=("l3f_c", "l3f_pr"), writes=(("l3f_psm", b),))
            for c in range(8):
                p.emit("tensor", lambda e, c=c, dt_=dt_, ot=ot, b=b: e.matmul(
                    ps_m[:, b, :], cs_s[:, c, dt_ * 128:(dt_ + 1) * 128],
                    pi_sb[:, c, ot * 512:(ot + 1) * 512],
                    start=False, stop=(c == 7)),
                    reads=("l3f_s", "l3f_pi"), writes=(("l3f_psm", b),))
            p.emit("scalar", lambda e, dt_=dt_, ot=ot, b=b: e.activation(
                out=m_sb[:, dt_, ot * 512:(ot + 1) * 512], in_=ps_m[:, b, :],
                func=AF.Copy, scale=1.0 / 1024.0),
                reads=(("l3f_psm", b),), writes=("l3f_m",))

    p.pop_scope()
    xt = p.sb("l3f_xt", [128, 8, 512], F32)
    xn = p.sb("l3f_xn", [128, 8, 512], F32)
    xnb = p.sb("l3f_xnb", [128, 8, 512], BF16)
    yt = p.sb("l3f_yt", [128, 8, 512], F32)
    yn = p.sb("l3f_yn", [128, 8, 512], F32)
    ps_y = p.ps("l3f_psy", [128, 4, 512], F32)
    xin_t = xin.rearrange("(c p) (n t) -> n p c t", p=128, t=512)
    xout_t = xout.rearrange("(c p) (n t) -> n p c t", p=128, t=512)

    for t in range(NT):
        p.dma("sync", xt[:], xin_t[t], reads=("l3f_xt",), writes=("l3f_xt",))
        cx.ln_fm(xt[:], "l3f_xt", xn[:], "l3f_xn", "l3a",
                 out2_bf=xnb[:], key_out2="l3f_xnb")
        linear_fm(p, lambda k: xnb[:, k, :], "l3f_xnb", "l3f_m",
                  ps_y, lambda b: ("l3f_psy", b),
                  lambda oc, pv, kv: p.emit("scalar", lambda e, oc=oc, pv=pv: e.activation(
                      out=yt[:, oc, :], in_=pv, func=AF.Identity,
                      bias=bias_sb[:, oc:oc + 1]),
                      reads=(kv, "l3f_bias"), writes=("l3f_yt",)),
                  nk=8, ndo=8,
                  w_view=lambda k, oc: m_sb[:, k, oc * 128:(oc + 1) * 128])
        cx.ln_fm(yt[:], "l3f_yt", yn[:], "l3f_yn", "l3b")
        p.emit("vector", lambda e: e.tensor_tensor(
            yn[:], yn[:], xt[:], op=ALU.add),
            reads=("l3f_yn", "l3f_xt"), writes=("l3f_yn",))
        p.dma("sync", xout_t[t], yn[:], reads=("l3f_yn",), writes=("l3f_store",))
    p.pop_scope()


def _l2a_selae(swap):
    s = np.zeros((48, 2), np.float32)
    for cr in range(8):
        s[6 * cr + 0, 0 + swap] = 1.0
        s[6 * cr + 1, 0 + swap] = 1.0
        s[6 * cr + 4, (1 - swap) if swap else 1] = 1.0
        s[6 * cr + 5, (1 - swap) if swap else 1] = 1.0
    return s.astype(BF)


def _l2a_selth():
    s = np.zeros((48, 16), np.float32)
    for j in range(16):
        s[6 * (j // 2) + 2 + (j % 2), j] = 1.0
    return s.astype(BF)


def _l1a_rcounts():
    cnt = np.zeros(2047, np.float64)
    starts = np.arange(14) * 128
    for s in starts:
        cnt[s:s + 256] += 1
    cnt = np.maximum(cnt, 1.0)
    r = np.ones(2048, np.float64)
    r[1:2048] = 1.0 / cnt
    return np.broadcast_to(r.astype(np.float32), (128, 2048)).astype(BF).copy()


def _l0f_rotary():
    n, DH, WINDOW = 2048, 64, 512
    half = np.arange(0, DH, 2, dtype=np.float64)
    inv_freq = 1.0 / 10000.0 ** (half / DH)
    t = np.arange(n, dtype=np.float64)
    freqs = t[:, None] * inv_freq[None, :]
    freqs = np.concatenate([freqs, freqs], axis=-1)
    base = (half + 0.4 * DH) / (1.4 * DH)
    power = (t - n // 2) / float(WINDOW // 2)
    scale = base[None, :] ** power[:, None]
    scale = np.concatenate([scale, scale], axis=-1)
    cos, sin = np.cos(freqs), np.sin(freqs)
    def fm64(a):
        return np.tile(np.ascontiguousarray(a.T.astype(np.float32)), (2, 1))
    qc, qs = fm64(cos * scale), fm64(sin * scale)
    kc, ks = fm64(cos / scale), fm64(sin / scale)
    R = np.zeros((64, 64), np.float64)
    for i in range(32):
        R[i, i + 32] = -1.0
        R[i + 32, i] = 1.0
    R2 = np.zeros((128, 128), np.float32)
    R2[0:64, 0:64] = R
    R2[64:128, 64:128] = R
    r2t = np.ascontiguousarray(R2.T)
    o2 = np.zeros((128, 2), np.float32)
    o2[0:64, 0] = 1.0
    o2[64:128, 1] = 1.0
    bo2 = np.zeros((2, 128), np.float32)
    bo2[0, 0:64] = 1.0
    bo2[1, 64:128] = 1.0
    return qc, qs, kc, ks, r2t, o2, bo2


def make_consts():
    _r = _l0f_rotary()
    D = 1024
    k = np.arange(D)
    ang = 2.0 * np.pi * ((np.outer(k, k) % D).astype(np.float64)) / D
    c = np.cos(ang).astype(BF)
    s = np.sin(ang)
    return {
        "iden_bf16": np.eye(128, dtype=np.float32).astype(BF),
        "ones_col": np.ones((128, 1), dtype=np.float32).astype(BF),
        "ones_row": np.ones((1, 128), dtype=np.float32),
        "dft_c": c,
        "dft_s": s.astype(BF),
        "dft_sneg": (-s).astype(BF),
        "sel8": np.repeat(np.eye(8, dtype=np.float32), 128, axis=1).astype(BF),
        "l1a_rcounts": _l1a_rcounts(),
        "sel_ae": _l2a_selae(0), "sel_ea": _l2a_selae(1),
        "sel_th": _l2a_selth(),
        "e64": np.repeat(np.eye(16, dtype=np.float32), 64, axis=1).astype(BF),
        "causal01": np.tril(np.ones((128, 128), np.float32)),
        "causal10": 1.0 - np.tril(np.ones((128, 128), np.float32)),
        "upper01": np.triu(np.ones((128, 128), np.float32)),
        "upper10": 1.0 - np.triu(np.ones((128, 128), np.float32)),
        "rot_qc": _r[0].astype(BF), "rot_qs": _r[1].astype(BF),
        "rot_kc": _r[2].astype(BF), "rot_ks": _r[3].astype(BF),
        "rot_r2t": _r[4].astype(BF),
        "ones2col": _r[5].astype(BF),
        "blockones2": _r[6],
    }


CONST_DTYPES = {"ones_row": F32, "causal01": F32, "causal10": F32,
                "upper01": F32, "upper10": F32, "blockones2": F32}


def consts_aps(p, consts):
    out = {}
    for k, v in consts.items():
        out[k] = p.dram_in("c_" + k, v.shape, CONST_DTYPES.get(k, BF16))
    return out


def consts_in_map(consts):
    return {("c_" + k): v for k, v in consts.items()}


def pc(bias):
    return np.ascontiguousarray(
        np.asarray(bias, np.float32).reshape(8, 128).T)


def fm(x):
    return np.ascontiguousarray(np.asarray(x, np.float32).reshape(-1, 1024).T)


def build_l3_ffn(consts):
    p = Prog()
    xin = p.dram_in("xin", [1024, TCORE])
    xout = p.dram_out("xout", [1024, TCORE])
    c_aps = consts_aps(p, consts)
    cx = Ctx(p, c_aps)
    W = {
        "WrT": p.dram_in("w_WrT", [1024, 1024]),
        "WiT": p.dram_in("w_WiT", [1024, 1024]),
        "bias_pc": p.dram_in("w_bias_pc", [128, 8]),
    }
    emit_l3_ffn(p, cx, xin, xout, W, c_aps)
    nc = p.finalize()
    return nc, p


def l3_ffn_in_map(p, consts, inputs):
    return {
        "c_fvals_in": p.fconst_host_array(),
        "w_WrT": np.ascontiguousarray(inputs["L3_ffn_w_freq_real"].T),
        "w_WiT": np.ascontiguousarray(inputs["L3_ffn_w_freq_imag"].T),
        "w_bias_pc": pc(inputs["L3_ffn_bias"]),
        **consts_in_map(consts),
    }


def top2_masks(p, cx, s, col, key_s, key_c, rw_ap, key_rw):
    p.fconst(-1e30)

    def first_of(eq_lane, scr_lane):
        p.emit("vector", lambda e: e.tensor_tensor_scan(
            s[:, scr_lane, :], cx.ones8[:], s[:, eq_lane, :], 0.0,
            op0=ALU.mult, op1=ALU.add),
            reads=(key_s, "cx_ones8"), writes=(key_s,))
        p.emit("vector", lambda e: e.tensor_tensor(
            s[:, scr_lane, :], s[:, scr_lane, :], s[:, eq_lane, :],
            op=ALU.subtract), reads=(key_s,), writes=(key_s,))
        p.emit("vector", lambda e: e.tensor_scalar(
            s[:, scr_lane, :], s[:, scr_lane, :], 0.0, None,
            op0=ALU.is_equal), reads=(key_s,), writes=(key_s,))
        p.emit("vector", lambda e: e.tensor_tensor(
            s[:, eq_lane, :], s[:, eq_lane, :], s[:, scr_lane, :],
            op=ALU.mult), reads=(key_s,), writes=(key_s,))

    p.emit("vector", lambda e: e.tensor_reduce(
        col[:, 0:1], rw_ap, axis=AX.X, op=ALU.max),
        reads=(key_rw,), writes=(key_c,))
    p.emit("vector", lambda e: e.tensor_scalar(
        s[:, 0, :], rw_ap, col[:, 0:1], None, op0=ALU.is_ge),
        reads=(key_rw, key_c), writes=(key_s,))
    first_of(0, 2)
    p.emit("vector", lambda e: e.scalar_tensor_tensor(
        s[:, 2, :], s[:, 0, :], -1e30, rw_ap, op0=ALU.mult, op1=ALU.add),
        reads=(key_s, key_rw), writes=(key_s,))
    p.emit("vector", lambda e: e.tensor_reduce(
        col[:, 1:2], s[:, 2, :], axis=AX.X, op=ALU.max),
        reads=(key_s,), writes=(key_c,))
    p.emit("vector", lambda e: e.tensor_scalar(
        s[:, 1, :], s[:, 2, :], col[:, 1:2], None, op0=ALU.is_ge),
        reads=(key_s, key_c), writes=(key_s,))
    first_of(1, 2)
    p.emit("vector", lambda e: e.tensor_tensor(
        s[:, 1, :], s[:, 1, :], s[:, 0, :], op=ALU.add),
        reads=(key_s,), writes=(key_s,))


def emit_l2_ffn(p: Prog, cx: Ctx, xin, xout, W, C):
    p.push_scope()
    vg_sb = p.sb("l2f_vg", [128, 8, 512], F32)
    wld = p.sb("l2f_wld", [128, 2, 1024], F32)
    web = p.sb("l2f_web", [128, 8, 1024], BF16)
    wpb = p.sb("l2f_wpb", [128, 8, 1024], BF16)
    be_sb = p.sb("l2f_be", [8, 1024], BF16)
    bp_sb = p.sb("l2f_bp", [128, 8], F32)
    gbd = p.sb("l2f_gbd", [128, 4, 128], F32)
    sel = p.sb("l2f_sel", [8, 1024], BF16)
    rw_all = p.sb("l2f_rwall", [128, 32, 8], F32)
    anyacc = p.sb("l2f_any", [128, 2], F32)
    anycol = p.sb("l2f_anyc", [128, 1], F32)
    anyrow = p.sb("l2f_anyr", [1, 2], F32)
    xt = p.sb("l2f_xt", [128, 8, 512], F32)
    xn = p.sb("l2f_xn", [128, 8, 512], F32)
    xnb = p.sb("l2f_xnb", [128, 8, 512], BF16)
    rhs_e = p.sb("l2f_rhse", [128, 8, 512], BF16)
    acc = p.sb("l2f_acc", [128, 8, 512], F32)
    xe_fm = p.sb("l2f_xefm", [128, 4, 512], F32)
    xe_tm = p.sb("l2f_xetm", [128, 4, 512], F32)
    z_tm = p.sb("l2f_ztm", [128, 4, 512], F32)
    sq2 = p.sb("l2f_sq2", [128, 512], F32)
    g8 = p.sb("l2f_g8", [128, 4, 8], F32)
    gcol = p.sb("l2f_gcol", [128, 4], F32)
    n2 = p.sb("l2f_n2", [128, 4], F32)
    xz8 = p.sb("l2f_xz8", [128, 4, 8], F32)
    xe28 = p.sb("l2f_xe28", [128, 4, 8], F32)
    wtile = p.sb("l2f_wtile", [128, 4, 8], F32)
    wbf = p.sb("l2f_wbf", [128, 8], BF16)
    wT = p.sb("l2f_wT", [8, 4, 128], BF16)
    ps_a = p.ps("l2f_psa", [128, 4, 512], F32)
    ps_b = p.ps("l2f_psb", [128, 1, 512], F32)
    ps_t = p.ps("l2f_pst", [128, 1024], BF16)
    cc_in = p.dram_tmp("l2f_ccin", [128, 2])
    cc_out = p.dram_tmp("l2f_ccout", [128, 2])

    p.dma("sync", vg_sb[:], W["VgT"].rearrange("(k p) n -> p k n", p=128),
          writes=("l2f_vg",))
    p.dma("sync", sel[:], C["sel8"], writes=("l2f_sel",))
    load_weight_bf16(p, wld, "l2f_wld", wpb[:], "l2f_wpb", W["WpT"], 8, 1024)
    p.dma("sync", wld[0:8, 0, 0:1024], W["be"], reads=("l2f_wld",),
          writes=("l2f_wld",))
    p.emit("vector", lambda e: e.tensor_copy(be_sb[:], wld[0:8, 0, 0:1024]),
           reads=("l2f_wld",), writes=("l2f_be",))
    p.dma("sync", bp_sb[:], W["bp_pc"], writes=("l2f_bp",))
    p.emit("vector", lambda e: e.memset(gbd[:], 0.0), writes=("l2f_gbd",))
    for e8 in range(8):
        c, half = e8 // 2, (e8 % 2) * 64
        for k in range(8):
            p.emit("tensor", lambda e, k=k, e8=e8: e.matmul(
                ps_b[0:64, 0, 0:64], vg_sb[:, k, e8 * 64:(e8 + 1) * 64],
                vg_sb[:, k, e8 * 64:(e8 + 1) * 64],
                start=(k == 0), stop=(k == 7)),
                reads=("l2f_vg",), writes=(("l2f_psb", 0),))
        p.emit("vector", lambda e, c=c, half=half: e.tensor_copy(
            gbd[half:half + 64, c, half:half + 64], ps_b[0:64, 0, 0:64]),
            reads=(("l2f_psb", 0),), writes=("l2f_gbd",))

    xin_t = xin.rearrange("(c p) (n t) -> n p c t", p=128, t=512)
    xout_t = xout.rearrange("(c p) (n t) -> n p c t", p=128, t=512)
    p.fconst(0.125)

    p.emit("vector", lambda e: e.memset(anyacc[:], 0.0), writes=("l2f_any",))
    for t in range(NT):
        p.dma("sync", xt[:], xin_t[t], reads=("l2f_xt",), writes=("l2f_xt",))
        cx.ln_fm(xt[:], "l2f_xt", xn[:], "l2f_xn", "l2fa",
                 out2_bf=xnb[:], key_out2="l2f_xnb", precise=True)
        sqn = cx.sqf
        p.emit("scalar", lambda e, sqn=sqn: e.activation(
            out=sqn[:], in_=xn[:], func=AF.Square),
            reads=("l2f_xn",), writes=("cx_sqf",))
        for ec in range(4):
            for k in range(8):
                p.emit("tensor", lambda e, ec=ec, k=k: e.matmul(
                    ps_a[:, ec, :], vg_sb[:, k, ec * 128:(ec + 1) * 128],
                    xn[:, k, :], start=(k == 0), stop=(k == 7)),
                    reads=("l2f_vg", "l2f_xn"), writes=(("l2f_psa", ec),))
            p.emit("vector", lambda e, ec=ec: e.tensor_copy(
                xe_fm[:, ec, :], ps_a[:, ec, :]),
                reads=(("l2f_psa", ec),), writes=("l2f_xefm",))
        for sub in range(4):
            for k in range(8):
                p.emit("tensor", lambda e, sub=sub, k=k: e.matmul(
                    ps_a[:, 0, :], xn[:, k, sub * 128:(sub + 1) * 128],
                    vg_sb[:, k, :], start=(k == 0), stop=(k == 7)),
                    reads=("l2f_xn", "l2f_vg"), writes=(("l2f_psa", 0),))
            p.emit("vector", lambda e, sub=sub: e.tensor_copy(
                xe_tm[:, sub, :], ps_a[:, 0, :]),
                reads=(("l2f_psa", 0),), writes=("l2f_xetm",))
            for c in range(4):
                p.emit("tensor", lambda e, sub=sub, c=c: e.matmul(
                    ps_a[:, 1, c * 128:(c + 1) * 128],
                    xe_fm[:, c, sub * 128:(sub + 1) * 128],
                    gbd[:, c, :], start=True, stop=True),
                    reads=("l2f_xefm", "l2f_gbd"), writes=(("l2f_psa", 1),))
            p.emit("vector", lambda e, sub=sub: e.tensor_copy(
                z_tm[:, sub, :], ps_a[:, 1, :]),
                reads=(("l2f_psa", 1),), writes=("l2f_ztm",))
            for k in range(8):
                p.emit("tensor", lambda e, sub=sub, k=k, sqn=sqn: e.matmul(
                    ps_b[:, 0, 0:1], sqn[:, k, sub * 128:(sub + 1) * 128],
                    cx.ones8[:, 0:1], start=(k == 0), stop=(k == 7)),
                    reads=("cx_sqf", "cx_ones8"), writes=(("l2f_psb", 0),))
            p.emit("vector", lambda e, sub=sub: e.tensor_copy(
                n2[:, sub:sub + 1], ps_b[:, 0, 0:1]),
                reads=(("l2f_psb", 0),), writes=("l2f_n2",))
        for sub in range(4):
            p.emit("scalar", lambda e, sub=sub: e.activation(
                out=sq2[:], in_=xe_tm[:, sub, :], func=AF.Square),
                reads=("l2f_xetm",), writes=("l2f_sq2",))
            p.emit("vector", lambda e, sub=sub: e.tensor_reduce(
                xe28[:, sub, :].unsqueeze(2),
                sq2[:].rearrange("p (e r) -> p e r", e=8),
                axis=AX.X, op=ALU.add), reads=("l2f_sq2",), writes=("l2f_xe28",))
            p.emit("vector", lambda e, sub=sub: e.tensor_tensor(
                sq2[:], xe_tm[:, sub, :], z_tm[:, sub, :], op=ALU.mult),
                reads=("l2f_xetm", "l2f_ztm"), writes=("l2f_sq2",))
            p.emit("vector", lambda e, sub=sub: e.tensor_reduce(
                xz8[:, sub, :].unsqueeze(2),
                sq2[:].rearrange("p (e r) -> p e r", e=8),
                axis=AX.X, op=ALU.add), reads=("l2f_sq2",), writes=("l2f_xz8",))
            p.emit("vector", lambda e, sub=sub: e.scalar_tensor_tensor(
                g8[:, sub, :], xe28[:, sub, :], -2.0, xz8[:, sub, :],
                op0=ALU.mult, op1=ALU.add),
                reads=("l2f_xe28", "l2f_xz8"), writes=("l2f_g8",))
            p.emit("vector", lambda e, sub=sub: e.tensor_tensor(
                g8[:, sub, :], g8[:, sub, :],
                n2[:, sub:sub + 1].broadcast_to([128, 8]), op=ALU.add),
                reads=("l2f_g8", "l2f_n2"), writes=("l2f_g8",))
            p.emit("vector", lambda e, sub=sub: e.tensor_scalar(
                g8[:, sub, :], g8[:, sub, :], 0.0, None, op0=ALU.max),
                reads=("l2f_g8",), writes=("l2f_g8",))
            p.emit("scalar", lambda e, sub=sub: e.activation(
                out=g8[:, sub, :], in_=g8[:, sub, :], func=AF.Sqrt),
                reads=("l2f_g8",), writes=("l2f_g8",))
            p.emit("vector", lambda e, sub=sub: e.tensor_reduce(
                gcol[:, sub:sub + 1], g8[:, sub, :], axis=AX.X, op=ALU.min),
                reads=("l2f_g8",), writes=("l2f_gcol",))
            p.emit("scalar", lambda e, sub=sub: e.activation(
                out=g8[:, sub, :], in_=g8[:, sub, :], func=AF.Exp,
                scale=-1.0, bias=gcol[:, sub:sub + 1]),
                reads=("l2f_g8", "l2f_gcol"), writes=("l2f_g8",))
            p.emit("vector", lambda e, sub=sub: e.tensor_reduce(
                gcol[:, sub:sub + 1], g8[:, sub, :], axis=AX.X, op=ALU.add),
                reads=("l2f_g8",), writes=("l2f_gcol",))
            p.emit("vector", lambda e, sub=sub: e.reciprocal(
                gcol[:, sub:sub + 1], gcol[:, sub:sub + 1]),
                reads=("l2f_gcol",), writes=("l2f_gcol",))
            p.emit("vector", lambda e, sub=sub, t=t: e.tensor_scalar(
                rw_all[:, t * 4 + sub, :], g8[:, sub, :],
                gcol[:, sub:sub + 1], None, op0=ALU.mult),
                reads=("l2f_g8", "l2f_gcol"), writes=("l2f_rwall",))
            p.emit("vector", lambda e, sub=sub, t=t: e.tensor_scalar(
                g8[:, sub, :], rw_all[:, t * 4 + sub, :], 0.125, None,
                op0=ALU.is_gt), reads=("l2f_rwall",), writes=("l2f_g8",))
            p.emit("vector", lambda e, sub=sub: e.tensor_reduce(
                gcol[:, sub:sub + 1], g8[:, sub, :], axis=AX.X, op=ALU.max),
                reads=("l2f_g8",), writes=("l2f_gcol",))
            p.emit("vector", lambda e, sub=sub: e.tensor_tensor(
                anyacc[:, 0:1], anyacc[:, 0:1], gcol[:, sub:sub + 1],
                op=ALU.max), reads=("l2f_any", "l2f_gcol"), writes=("l2f_any",))

    p.dma("gpsimd", cc_in, anyacc[:], reads=("l2f_any",), writes=("l2f_ccin",))
    p.emit("gpsimd", lambda e: e.collective_compute(
        "AllReduce", ALU.max, replica_groups=[list(range(NCORES))],
        ins=[cc_in.opt()], outs=[cc_out.opt()]),
        reads=("l2f_ccin",), writes=("l2f_ccout",), cc=True)
    p.dma("gpsimd", anyacc[:], cc_out, reads=("l2f_ccout",),
          writes=("l2f_any",))
    p.emit("vector", lambda e: e.tensor_copy(wbf[:, 0:2], anyacc[:]),
           reads=("l2f_any",), writes=("l2f_wbf",))
    p.emit("tensor", lambda e: e.matmul(
        ps_b[0:1, 0, 0:2], cx.ones_col[:], wbf[:, 0:2], start=True, stop=True),
        reads=("l2f_wbf", "cx_onescol"), writes=(("l2f_psb", 0),))
    p.emit("vector", lambda e: e.tensor_scalar(
        anyrow[:, 0:1], ps_b[0:1, 0, 0:1], 0.0, None, op0=ALU.is_gt),
        reads=(("l2f_psb", 0),), writes=("l2f_anyr",))
    cx.replicate_row(anyrow[:, 0:1], "l2f_anyr", ps_b[:, 0, 0:1],
                     ("l2f_psb", 0))
    p.emit("vector", lambda e: e.tensor_copy(anycol[:], ps_b[:, 0, 0:1]),
           reads=(("l2f_psb", 0),), writes=("l2f_anyc",))

    for t in range(NT):
        p.dma("sync", xt[:], xin_t[t], reads=("l2f_xt",), writes=("l2f_xt",))
        cx.ln_fm(xt[:], "l2f_xt", xn[:], "l2f_xn", "l2fb",
                 out2_bf=xnb[:], key_out2="l2f_xnb")
        for sub in range(4):
            rw = rw_all[:, t * 4 + sub, :]
            top2_masks(p, cx, g8, gcol, "l2f_g8", "l2f_gcol", rw, "l2f_rwall")
            p.emit("vector", lambda e, rw=rw: e.tensor_scalar(
                g8[:, 2, :], rw, 0.125, None, op0=ALU.is_gt),
                reads=("l2f_rwall",), writes=("l2f_g8",))
            p.emit("vector", lambda e: e.tensor_tensor(
                g8[:, 2, :], g8[:, 2, :], g8[:, 1, :], op=ALU.mult),
                reads=("l2f_g8", "l2f_g8"), writes=("l2f_g8",))
            p.emit("vector", lambda e, rw=rw: e.tensor_tensor(
                g8[:, 2, :], g8[:, 2, :], rw, op=ALU.mult),
                reads=("l2f_g8", "l2f_rwall"), writes=("l2f_g8",))
            p.emit("vector", lambda e: e.tensor_reduce(
                gcol[:, 2:3], g8[:, 2, :], axis=AX.X, op=ALU.add),
                reads=("l2f_g8",), writes=("l2f_gcol",))
            p.emit("vector", lambda e: e.tensor_scalar(
                gcol[:, 3:4], gcol[:, 2:3], 0.0, None, op0=ALU.is_equal),
                reads=("l2f_gcol",), writes=("l2f_gcol",))
            p.emit("vector", lambda e: e.tensor_tensor(
                gcol[:, 2:3], gcol[:, 2:3], gcol[:, 3:4], op=ALU.add),
                reads=("l2f_gcol", "l2f_gcol"), writes=("l2f_gcol",))
            p.emit("vector", lambda e: e.reciprocal(
                gcol[:, 2:3], gcol[:, 2:3]), reads=("l2f_gcol",),
                writes=("l2f_gcol",))
            p.emit("vector", lambda e: e.tensor_scalar(
                g8[:, 2, :], g8[:, 2, :], gcol[:, 2:3], None, op0=ALU.mult),
                reads=("l2f_g8", "l2f_gcol"), writes=("l2f_g8",))
            p.emit("vector", lambda e: e.tensor_tensor(
                g8[:, 2, :], g8[:, 2, :], g8[:, 0, :], op=ALU.subtract),
                reads=("l2f_g8", "l2f_g8"), writes=("l2f_g8",))
            p.emit("vector", lambda e: e.tensor_scalar(
                g8[:, 2, :], g8[:, 2, :], anycol[:], None, op0=ALU.mult),
                reads=("l2f_g8", "l2f_anyc"), writes=("l2f_g8",))
            p.emit("vector", lambda e, sub=sub: e.tensor_tensor(
                wtile[:, sub, :], g8[:, 2, :], g8[:, 0, :], op=ALU.add),
                reads=("l2f_g8", "l2f_g8"), writes=("l2f_wtile",))
        for sub in range(4):
            p.emit("vector", lambda e, sub=sub: e.tensor_copy(
                wbf[:, 0:8], wtile[:, sub, :]),
                reads=("l2f_wtile",), writes=("l2f_wbf",))
            p.emit("tensor", lambda e, sub=sub: e.transpose(
                ps_t[0:8, 0:128], wbf[:, 0:8], cx.iden[:]),
                reads=("l2f_wbf", "cx_iden"), writes=("l2f_pst",))
            p.emit("vector", lambda e, sub=sub: e.tensor_copy(
                wT[0:8, sub, :], ps_t[0:8, 0:128]),
                reads=("l2f_pst",), writes=("l2f_wT",))
        for oc in range(8):
            b = oc % 4
            for k in range(8):
                p.emit("tensor", lambda e, oc=oc, k=k, b=b: e.matmul(
                    ps_a[:, b, :], wpb[:, k, oc * 128:(oc + 1) * 128],
                    xnb[:, k, :], start=(k == 0), stop=False),
                    reads=("l2f_wpb", "l2f_xnb"), writes=(("l2f_psa", b),))
            p.emit("tensor", lambda e, oc=oc, b=b: e.matmul(
                ps_a[:, b, :], be_sb[:, oc * 128:(oc + 1) * 128],
                wT[:].rearrange("p a b -> p (a b)"),
                start=False, stop=True),
                reads=("l2f_be", "l2f_wT"), writes=(("l2f_psa", b),))
            p.emit("scalar", lambda e, oc=oc, b=b: e.activation(
                out=acc[:, oc, :], in_=ps_a[:, b, :], func=AF.Identity,
                bias=bp_sb[:, oc:oc + 1]),
                reads=(("l2f_psa", b), "l2f_bp"), writes=("l2f_acc",))
        for e8 in range(8):
            load_weight_bf16(p, wld, "l2f_wld", web[:], "l2f_web",
                             W["WeT"][e8], 8, 1024)
            p.emit("tensor", lambda e, e8=e8: e.matmul(
                ps_b[:, 0, :], sel[:, e8 * 128:(e8 + 1) * 128],
                wT[:].rearrange("p a b -> p (a b)"), start=True, stop=True),
                reads=("l2f_sel", "l2f_wT"), writes=(("l2f_psb", 0),))
            p.emit("vector", lambda e: e.tensor_tensor(
                rhs_e[:], xnb[:],
                ps_b[:, 0, :].unsqueeze(1).broadcast_to([128, 8, 512]),
                op=ALU.mult),
                reads=("l2f_xnb", ("l2f_psb", 0)), writes=("l2f_rhse",))
            for oc in range(8):
                b = oc % 4
                for k in range(8):
                    p.emit("tensor", lambda e, oc=oc, k=k, b=b: e.matmul(
                        ps_a[:, b, :], web[:, k, oc * 128:(oc + 1) * 128],
                        rhs_e[:, k, :], start=(k == 0), stop=(k == 7)),
                        reads=("l2f_web", "l2f_rhse"), writes=(("l2f_psa", b),))
                p.emit("vector", lambda e, oc=oc, b=b: e.tensor_tensor(
                    acc[:, oc, :], acc[:, oc, :], ps_a[:, b, :], op=ALU.add),
                    reads=("l2f_acc", ("l2f_psa", b)), writes=("l2f_acc",))
        import os
        if os.environ.get("L2DBG") and t == 0:
            dbg = p.dram_out("dbg", [128, 64, 8])
            p.dma("sync", dbg[:, 0:32, :], rw_all[:], reads=("l2f_rwall",), writes=("l2f_dbg",))
            p.dma("sync", dbg[:, 32:36, :], wtile[:], reads=("l2f_wtile",), writes=("l2f_dbg",))
            p.emit("vector", lambda e: e.tensor_scalar(
                g8[:, 3, :], cx.ones8[:], anycol[:], None, op0=ALU.mult),
                reads=("cx_ones8", "l2f_anyc"), writes=("l2f_g8",))
            p.dma("sync", dbg[:, 36, :], g8[:, 3, :], reads=("l2f_g8",), writes=("l2f_dbg",))
        cx.ln_fm(acc[:], "l2f_acc", xn[:], "l2f_xn2", "l2fc")
        p.emit("vector", lambda e: e.tensor_tensor(
            xn[:], xn[:], xt[:], op=ALU.add),
            reads=("l2f_xn2", "l2f_xt"), writes=("l2f_xn2",))
        p.dma("sync", xout_t[t], xn[:], reads=("l2f_xn2",), writes=("l2f_st",))
    p.pop_scope()
    cx.drop_sqf()


def build_l2_ffn(consts):
    p = Prog()
    xin = p.dram_in("xin", [1024, TCORE])
    xout = p.dram_out("xout", [1024, TCORE])
    c_aps = consts_aps(p, consts)
    cx = Ctx(p, c_aps)
    W = {
        "WpT": p.dram_in("w_WpT", [1024, 1024]),
        "bp_pc": p.dram_in("w_bp_pc", [128, 8]),
        "WeT": p.dram_in("w_WeT", [8, 1024, 1024]),
        "be": p.dram_in("w_be", [8, 1024]),
        "VgT": p.dram_in("w_VgT", [1024, 512]),
    }
    emit_l2_ffn(p, cx, xin, xout, W, c_aps)
    nc = p.finalize()
    return nc, p


def l2_ffn_in_map(p, consts, inputs):
    We = np.asarray(inputs["L2_ffn_We"], np.float32)
    return {
        "c_fvals_in": p.fconst_host_array(),
        "w_WpT": np.ascontiguousarray(np.asarray(inputs["L2_ffn_Wp"], np.float32).T),
        "w_bp_pc": pc(inputs["L2_ffn_bp"]),
        "w_WeT": np.ascontiguousarray(We.transpose(0, 2, 1)),
        "w_be": np.ascontiguousarray(np.asarray(inputs["L2_ffn_be"], np.float32)),
        "w_VgT": np.ascontiguousarray(
            np.asarray(inputs["L2_ffn_Vg"], np.float32).transpose(1, 0, 2)
            .reshape(1024, 512)),
        **consts_in_map(consts),
    }


def emit_l1_ffn(p: Prog, cx: Ctx, xin, xout, W, C):
    TSUP = 1024
    NSUP = TCORE // TSUP
    p.push_scope()
    wg_sb = p.sb("l1f_wg", [128, 8, 8], F32)
    sel = p.sb("l1f_sel", [8, 1024], BF16)
    w1h = p.sb("l1f_w1h", [128, 8, 1024], BF16)
    w3h = p.sb("l1f_w3h", [128, 8, 1024], BF16)
    w2h = p.sb("l1f_w2h", [128, 8, 1024], BF16)
    wld = p.sb("l1f_wld", [128, 4, 1024], F32)
    xnb = p.sb("l1f_xnb", [128, 8, 1024], BF16)
    acc = p.sb("l1f_acc", [128, 8, 1024], F32)
    hsb = p.sb("l1f_h", [128, 8, 512], BF16)
    wrep = p.sb("l1f_wrep", [128, 512], BF16)
    wT = p.sb("l1f_wT", [8, 8, 128], BF16)
    wbf = p.sb("l1f_wbf", [128, 8], BF16)
    g8 = p.sb("l1f_g8", [128, 4, 8], F32)
    gcol = p.sb("l1f_gcol", [128, 4], F32)
    xt = p.sb("l1f_xt", [128, 8, 512], F32)
    xn = p.sb("l1f_xn", [128, 8, 512], F32)
    yn = xn
    ps_a = p.ps("l1f_psa", [128, 2, 512], F32)
    ps_g = p.ps("l1f_psg", [128, 2, 512], F32)
    ps_y = p.ps("l1f_psy", [128, 1, 512], F32)
    ps_t = p.ps("l1f_pst", [128, 1024], BF16)

    p.dma("sync", sel[:], C["sel8"], writes=("l1f_sel",))
    p.dma("sync", wg_sb[:], W["Wg"].rearrange("(k p) n -> p k n", p=128),
          writes=("l1f_wg",))
    p.fconst(1.0 / 3.0)

    xin_t = xin.rearrange("(c p) (n t) -> n p c t", p=128, t=512)
    xout_t = xout.rearrange("(c p) (n t) -> n p c t", p=128, t=512)

    for sup in range(NSUP):
        for ts in range(2):
            t = sup * 2 + ts
            p.dma("sync", xt[:], xin_t[t], reads=("l1f_xt",), writes=("l1f_xt",))
            cx.ln_fm(xt[:], "l1f_xt", xn[:], "l1f_xn", "l1fg", precise=True)
            p.emit("vector", lambda e, ts=ts: e.tensor_copy(
                xnb[:, :, ts * 512:(ts + 1) * 512], xn[:]),
                reads=("l1f_xn",), writes=("l1f_xnb",))
            for sub in range(4):
                for k in range(8):
                    p.emit("tensor", lambda e, sub=sub, k=k: e.matmul(
                        ps_g[0:128, 0, 0:8],
                        xn[:, k, sub * 128:(sub + 1) * 128],
                        wg_sb[:, k, :], start=(k == 0), stop=(k == 7)),
                        reads=("l1f_xn", "l1f_wg"), writes=(("l1f_psg", 0),))
                p.emit("scalar", lambda e, sub=sub: e.activation(
                    out=g8[:, 3, :], in_=ps_g[:, 0, 0:8], func=AF.Sigmoid),
                    reads=(("l1f_psg", 0),), writes=("l1f_g8",))
                top2_masks(p, cx, g8, gcol, "l1f_g8", "l1f_gcol",
                           g8[:, 3, :], "l1f_g8")
                p.emit("vector", lambda e: e.tensor_tensor(
                    g8[:, 1, :], g8[:, 1, :], g8[:, 3, :], op=ALU.mult),
                    reads=("l1f_g8",), writes=("l1f_g8",))
                p.emit("vector", lambda e: e.tensor_reduce(
                    gcol[:, 0:1], g8[:, 1, :], axis=AX.X, op=ALU.add),
                    reads=("l1f_g8",), writes=("l1f_gcol",))
                p.emit("vector", lambda e: e.reciprocal(
                    gcol[:, 0:1], gcol[:, 0:1]),
                    reads=("l1f_gcol",), writes=("l1f_gcol",))
                p.emit("vector", lambda e: e.tensor_scalar(
                    g8[:, 1, :], g8[:, 1, :], gcol[:, 0:1], None,
                    op0=ALU.mult), reads=("l1f_g8", "l1f_gcol"),
                    writes=("l1f_g8",))
                p.emit("vector", lambda e: e.tensor_copy(
                    wbf[:, 0:8], g8[:, 1, :]),
                    reads=("l1f_g8",), writes=("l1f_wbf",))
                p.emit("tensor", lambda e: e.transpose(
                    ps_t[0:8, 0:128], wbf[:, 0:8], cx.iden[:]),
                    reads=("l1f_wbf", "cx_iden"), writes=(("l1f_pst",),))
                p.emit("vector", lambda e, ts=ts, sub=sub: e.tensor_copy(
                    wT[0:8, ts * 4 + sub, :], ps_t[0:8, 0:128]),
                    reads=(("l1f_pst",),), writes=("l1f_wT",))
        p.emit("vector", lambda e: e.memset(acc[:], 0.0), writes=("l1f_acc",))

        for ex in range(9):
            for ffh in range(2):
                if ex == 0:
                    w1ap, w3ap, w2ap = W["Ws1"], W["Ws3"], W["Ws2"]
                else:
                    w1ap, w3ap, w2ap = (W["We1"][ex - 1], W["We3"][ex - 1],
                                        W["We2"][ex - 1])
                load_weight_bf16(p, wld, "l1f_wld", w1h[:], "l1f_w1h",
                                 w1ap[:, ffh * 1024:(ffh + 1) * 1024], 8, 1024)
                load_weight_bf16(p, wld, "l1f_wld", w3h[:], "l1f_w3h",
                                 w3ap[:, ffh * 1024:(ffh + 1) * 1024], 8, 1024)
                load_weight_bf16(p, wld, "l1f_wld", w2h[:], "l1f_w2h",
                                 w2ap.rearrange("(h x) n -> h x n", h=2)[ffh],
                                 8, 1024)
                for tsl in range(2):
                    if ex > 0:
                        p.emit("tensor", lambda e, ex=ex, tsl=tsl: e.matmul(
                            ps_y[:, 0, :], sel[:, (ex - 1) * 128:ex * 128],
                            wT[:, tsl * 4:(tsl + 1) * 4, :]
                            .rearrange("p a b -> p (a b)"),
                            start=True, stop=True),
                            reads=("l1f_sel", "l1f_wT"), writes=(("l1f_psy", 0),))
                        p.emit("vector", lambda e: e.tensor_copy(
                            wrep[:], ps_y[:, 0, :]),
                            reads=(("l1f_psy", 0),), writes=("l1f_wrep",))
                    for oc in range(8):
                        b = oc % 2
                        for k in range(8):
                            p.emit("tensor", lambda e, oc=oc, k=k, b=b, tsl=tsl: e.matmul(
                                ps_a[:, b, :], w1h[:, k, oc * 128:(oc + 1) * 128],
                                xnb[:, k, tsl * 512:(tsl + 1) * 512],
                                start=(k == 0), stop=(k == 7)),
                                reads=("l1f_w1h", "l1f_xnb"), writes=(("l1f_psa", b),))
                        for k in range(8):
                            p.emit("tensor", lambda e, oc=oc, k=k, b=b, tsl=tsl: e.matmul(
                                ps_g[:, b, :], w3h[:, k, oc * 128:(oc + 1) * 128],
                                xnb[:, k, tsl * 512:(tsl + 1) * 512],
                                start=(k == 0), stop=(k == 7)),
                                reads=("l1f_w3h", "l1f_xnb"), writes=(("l1f_psg", b),))
                        p.emit("scalar", lambda e, oc=oc, b=b: e.activation(
                            out=hsb[:, oc, :], in_=ps_a[:, b, :], func=AF.Silu),
                            reads=(("l1f_psa", b),), writes=("l1f_h",))
                        p.emit("vector", lambda e, oc=oc, b=b: e.tensor_tensor(
                            hsb[:, oc, :], hsb[:, oc, :], ps_g[:, b, :],
                            op=ALU.mult),
                            reads=("l1f_h", ("l1f_psg", b)), writes=("l1f_h",))
                        if ex > 0:
                            p.emit("vector", lambda e, oc=oc: e.tensor_tensor(
                                hsb[:, oc, :], hsb[:, oc, :], wrep[:],
                                op=ALU.mult),
                                reads=("l1f_h", "l1f_wrep"), writes=("l1f_h",))
                    for oc in range(8):
                        for k in range(8):
                            p.emit("tensor", lambda e, oc=oc, k=k: e.matmul(
                                ps_y[:, 0, :], w2h[:, k, oc * 128:(oc + 1) * 128],
                                hsb[:, k, :], start=(k == 0), stop=(k == 7)),
                                reads=("l1f_w2h", "l1f_h"), writes=(("l1f_psy", 0),))
                        p.emit("vector", lambda e, oc=oc, tsl=tsl: e.scalar_tensor_tensor(
                            acc[:, oc, tsl * 512:(tsl + 1) * 512], ps_y[:, 0, :],
                            1.0 / 3.0, acc[:, oc, tsl * 512:(tsl + 1) * 512],
                            op0=ALU.mult, op1=ALU.add),
                            reads=(("l1f_psy", 0), "l1f_acc"), writes=("l1f_acc",))

        for ts in range(2):
            t = sup * 2 + ts
            p.dma("sync", xt[:], xin_t[t], reads=("l1f_xt",), writes=("l1f_xt",))
            av = acc[:, :, ts * 512:(ts + 1) * 512]
            cx.ln_fm(av, "l1f_acc", yn[:], "l1f_yn", "l1fe")
            p.emit("vector", lambda e: e.tensor_tensor(
                yn[:], yn[:], xt[:], op=ALU.add),
                reads=("l1f_yn", "l1f_xt"), writes=("l1f_yn",))
            p.dma("sync", xout_t[t], yn[:], reads=("l1f_yn",), writes=("l1f_st",))
    p.pop_scope()
    cx.drop_sqf()


def build_l1_ffn(consts):
    p = Prog()
    xin = p.dram_in("xin", [1024, TCORE])
    xout = p.dram_out("xout", [1024, TCORE])
    c_aps = consts_aps(p, consts)
    cx = Ctx(p, c_aps)
    W = {
        "Wg": p.dram_in("w_Wg", [1024, 8]),
        "Ws1": p.dram_in("w_Ws1", [1024, 2048]),
        "Ws3": p.dram_in("w_Ws3", [1024, 2048]),
        "Ws2": p.dram_in("w_Ws2", [2048, 1024]),
        "We1": p.dram_in("w_We1", [8, 1024, 2048]),
        "We3": p.dram_in("w_We3", [8, 1024, 2048]),
        "We2": p.dram_in("w_We2", [8, 2048, 1024]),
    }
    emit_l1_ffn(p, cx, xin, xout, W, c_aps)
    nc = p.finalize()
    return nc, p


def l1_ffn_in_map(p, consts, inputs):
    g = lambda n: np.ascontiguousarray(np.asarray(inputs[n], np.float32))
    return {
        "c_fvals_in": p.fconst_host_array(),
        "w_Wg": g("L1_ffn_Wg"), "w_Ws1": g("L1_ffn_Ws1"),
        "w_Ws3": g("L1_ffn_Ws3"), "w_Ws2": g("L1_ffn_Ws2"),
        "w_We1": g("L1_ffn_We1"), "w_We3": g("L1_ffn_We3"),
        "w_We2": g("L1_ffn_We2"),
        **consts_in_map(consts),
    }


def emit_l3_attn(p: Prog, cx: Ctx, xin, xout, W, C):
    p.push_scope()
    mask01 = p.sb("l3a_m01", [128, 128], F32)
    mask10 = p.sb("l3a_m10", [128, 128], F32)
    hg_dram = p.dram_tmp("l3a_hgd", [2, 16])
    xin_t = xin.rearrange("(c p) (n t) -> n p c t", p=128, t=512)
    xout_t = xout.rearrange("(c p) (n t) -> n p c t", p=128, t=512)
    p.dma("sync", mask01[:], C["causal01"], writes=("l3a_m01",))
    p.dma("sync", mask10[:], C["causal10"], writes=("l3a_m10",))
    p.fconst(1.0 / 2048.0)
    p.fconst(0.125)
    p.fconst(-1e9)

    for b in range(2):
        p.push_scope()
        oacc = p.sb("l3a_o", [128, 8, 2048], BF16)
        xm = p.sb("l3a_xm", [128, 8, 1], F32)
        hgrep8 = p.sb("l3a_hg8", [128, 16], F32)
        hgneg = p.sb("l3a_hgn", [128, 16], F32)
        p.push_scope()
        kf = p.sb("l3a_k", [128, 8, 2048], BF16)
        vf = p.sb("l3a_v", [128, 8, 2048], BF16)
        xnb_s = p.sb("l3a_xnb", [128, 8, 2048], BF16)

        p.push_scope()
        wkb = p.sb("l3a_wk", [128, 8, 1024], BF16)
        wst = p.sb("l3a_wst", [128, 2, 1024], F32)
        bkv = p.sb("l3a_bkv", [128, 16], F32)
        xt = p.sb("l3a_xt", [128, 8, 512], F32)
        ps_k = p.ps("l3a_psk", [128, 4, 512], F32)
        p.dma("sync", bkv[:], W["bkv_pc"], writes=("l3a_bkv",))
        for wi, (wap, dst, dkey, bc) in enumerate((
                (W["wkT"], kf, "l3a_kf", 0), (W["wvT"], vf, "l3a_vf", 8))):
            load_weight_bf16(p, wst, "l3a_wst", wkb[:], "l3a_wk", wap, 8, 1024)
            for ts in range(4):
                t = b * 4 + ts
                if wi == 0:
                    p.dma("sync", xt[:], xin_t[t], reads=("l3a_xt",),
                          writes=("l3a_xt",))
                    cx.ln_fm(xt[:], "l3a_xt",
                             xnb_s[:, :, ts * 512:(ts + 1) * 512], "l3a_xnbs",
                             "l3at", sub_inplace=True)
                for oc in range(8):
                    bnk = oc % 4
                    for k in range(8):
                        p.emit("tensor", lambda e, oc=oc, k=k, bnk=bnk, ts=ts: e.matmul(
                            ps_k[:, bnk, :], wkb[:, k, oc * 128:(oc + 1) * 128],
                            xnb_s[:, k, ts * 512:(ts + 1) * 512],
                            start=(k == 0), stop=(k == 7)),
                            reads=("l3a_wk", "l3a_xnbs"), writes=(("l3a_psk", bnk),))
                    p.emit("scalar", lambda e, oc=oc, bnk=bnk, dst=dst, ts=ts, bc=bc, dkey=dkey: e.activation(
                        out=dst[:, oc, ts * 512:(ts + 1) * 512],
                        in_=ps_k[:, bnk, :], func=AF.Identity,
                        bias=bkv[:, bc + oc:bc + oc + 1]),
                        reads=(("l3a_psk", bnk), "l3a_bkv"), writes=(dkey,))
        p.pop_scope()

        p.push_scope()
        sw1b = p.sb("l3a_sw1", [128, 8, 512], F32)
        sw2b = p.sb("l3a_sw2", [128, 4, 3], F32)
        pw1b = p.sb("l3a_pw1", [128, 8, 512], F32)
        pw2b = p.sb("l3a_pw2", [128, 4, 16], F32)
        sbias = p.sb("l3a_sb", [128, 32], F32)
        h1 = p.sb("l3a_h1", [128, 4, 1], F32)
        rows = p.sb("l3a_rows", [16, 24], F32)
        rbf = p.sb("l3a_rbf", [16, 24], BF16)
        hgrow = p.sb("l3a_hgrow", [1, 32], F32)
        ps_g = p.ps("l3a_psg", [128, 1, 512], F32)
        ps_tr = p.ps("l3a_pstr", [128, 1024], BF16)
        p.emit("vector", lambda e: e.tensor_reduce(
            xm[:], xnb_s[:], axis=AX.X, op=ALU.add),
            reads=("l3a_xnbs",), writes=("l3a_xm",))
        p.emit("vector", lambda e: e.tensor_scalar(
            xm[:], xm[:], 1.0 / 2048.0, None, op0=ALU.mult),
            reads=("l3a_xm",), writes=("l3a_xm",))
        p.dma("sync", sw1b[:], W["sw1T"].rearrange("(k p) n -> p k n", p=128),
              writes=("l3a_sw1",))
        p.dma("sync", sw2b[:], W["sw2T"].rearrange("(k p) n -> p k n", p=128),
              writes=("l3a_sw2",))
        p.dma("sync", sbias[:], W["sb_pc"], writes=("l3a_sbias",))
        for oc in range(4):
            for k in range(8):
                p.emit("tensor", lambda e, oc=oc, k=k: e.matmul(
                    ps_g[0:128, 0, 0:1], sw1b[:, k, oc * 128:(oc + 1) * 128],
                    xm[:, k, :], start=(k == 0), stop=(k == 7)),
                    reads=("l3a_sw1", "l3a_xm"), writes=(("l3a_psg", 0),))
            p.emit("scalar", lambda e, oc=oc: e.activation(
                out=h1[:, oc, :], in_=ps_g[:, 0, 0:1], func=AF.Gelu,
                bias=sbias[:, oc:oc + 1]),
                reads=(("l3a_psg", 0), "l3a_sbias"), writes=("l3a_h1",))
        for k in range(4):
            p.emit("tensor", lambda e, k=k: e.matmul(
                ps_g[0:3, 0, 0:1], sw2b[:, k, :], h1[:, k, :],
                start=(k == 0), stop=(k == 3)),
                reads=("l3a_sw2", "l3a_h1"), writes=(("l3a_psg", 0),))
        p.emit("vector", lambda e: e.tensor_tensor(
            rows[0:3, 0:1], ps_g[0:3, 0, 0:1], sbias[0:3, 4:5],
            op=ALU.add), reads=(("l3a_psg", 0), "l3a_sbias"),
            writes=("l3a_rows",))
        for pi in range(3):
            p.dma("sync", pw1b[:], W["pw1T"][pi].rearrange("(k p) n -> p k n", p=128),
                  reads=("l3a_pw1",), writes=("l3a_pw1",))
            p.dma("sync", pw2b[:], W["pw2T"][pi].rearrange("(k p) n -> p k n", p=128),
                  reads=("l3a_pw2",), writes=("l3a_pw2",))
            for oc in range(4):
                for k in range(8):
                    p.emit("tensor", lambda e, oc=oc, k=k: e.matmul(
                        ps_g[0:128, 0, 0:1], pw1b[:, k, oc * 128:(oc + 1) * 128],
                        xm[:, k, :], start=(k == 0), stop=(k == 7)),
                        reads=("l3a_pw1", "l3a_xm"), writes=(("l3a_psg", 0),))
                p.emit("scalar", lambda e, oc=oc, pi=pi: e.activation(
                    out=h1[:, oc, :], in_=ps_g[:, 0, 0:1], func=AF.Gelu,
                    bias=sbias[:, 5 + pi * 4 + oc:6 + pi * 4 + oc]),
                    reads=(("l3a_psg", 0), "l3a_sbias"), writes=("l3a_h1",))
            for k in range(4):
                p.emit("tensor", lambda e, k=k, pi=pi: e.matmul(
                    ps_g[0:16, 0, 0:1], pw2b[:, k, :], h1[:, k, :],
                    start=(k == 0), stop=(k == 3)),
                    reads=("l3a_pw2", "l3a_h1"), writes=(("l3a_psg", 0),))
            p.emit("vector", lambda e, pi=pi: e.tensor_tensor(
                rows[0:16, 1 + pi:2 + pi], ps_g[0:16, 0, 0:1],
                sbias[0:16, 17 + pi:18 + pi], op=ALU.add),
                reads=(("l3a_psg", 0), "l3a_sbias"), writes=("l3a_rows",))
        p.emit("vector", lambda e: e.tensor_copy(rbf[:, 0:4], rows[:, 0:4]),
               reads=("l3a_rows",), writes=("l3a_rbf",))
        p.emit("tensor", lambda e: e.transpose(
            ps_tr[0:4, 0:16], rbf[:, 0:4], cx.iden[0:16, 0:16]),
            reads=("l3a_rbf", "cx_iden"), writes=("l3a_pstr",))
        smx = p.sb("l3a_smx", [4, 16], F32)
        scol = p.sb("l3a_scol", [4, 2], F32)
        p.emit("vector", lambda e: e.tensor_copy(smx[0:4, :], ps_tr[0:4, 0:16]),
               reads=("l3a_pstr",), writes=("l3a_smx",))
        p.emit("vector", lambda e: e.tensor_reduce(
            scol[0:1, 0:1], smx[0:1, 0:3], axis=AX.X, op=ALU.max),
            reads=("l3a_smx",), writes=("l3a_scol",))
        p.emit("vector", lambda e: e.tensor_scalar(
            scol[0:1, 1:2], scol[0:1, 0:1], -1.0, None, op0=ALU.mult),
            reads=("l3a_scol",), writes=("l3a_scol",))
        p.emit("scalar", lambda e: e.activation(
            out=smx[0:1, 0:3], in_=smx[0:1, 0:3], func=AF.Exp,
            bias=scol[0:1, 1:2]), reads=("l3a_smx", "l3a_scol"),
            writes=("l3a_smx",))
        p.emit("vector", lambda e: e.tensor_reduce(
            scol[0:1, 0:1], smx[0:1, 0:3], axis=AX.X, op=ALU.add),
            reads=("l3a_smx",), writes=("l3a_scol",))
        p.emit("vector", lambda e: e.reciprocal(
            scol[0:1, 0:1], scol[0:1, 0:1]),
            reads=("l3a_scol",), writes=("l3a_scol",))
        p.emit("vector", lambda e: e.tensor_scalar(
            smx[0:1, 0:3], smx[0:1, 0:3], scol[0:1, 0:1], None, op0=ALU.mult),
            reads=("l3a_smx", "l3a_scol"), writes=("l3a_smx",))
        gsm = p.sb("l3a_gsm", [4, 16], F32)
        p.emit("tensor", lambda e: e.transpose(
            ps_tr[0:3, 0:16], rbf[:, 1:4], cx.iden[0:16, 0:16]),
            reads=("l3a_rbf", "cx_iden"), writes=("l3a_pstr",))
        p.emit("vector", lambda e: e.tensor_copy(gsm[0:3, :], ps_tr[0:3, 0:16]),
               reads=("l3a_pstr",), writes=("l3a_gsm",))
        p.emit("vector", lambda e: e.tensor_reduce(
            scol[0:3, 0:1], gsm[0:3, :], axis=AX.X, op=ALU.max),
            reads=("l3a_gsm",), writes=("l3a_scol",))
        p.emit("vector", lambda e: e.tensor_scalar(
            scol[0:3, 1:2], scol[0:3, 0:1], -1.0, None, op0=ALU.mult),
            reads=("l3a_scol",), writes=("l3a_scol",))
        p.emit("scalar", lambda e: e.activation(
            out=gsm[0:3, :], in_=gsm[0:3, :], func=AF.Exp,
            bias=scol[0:3, 1:2]), reads=("l3a_gsm", "l3a_scol"),
            writes=("l3a_gsm",))
        p.emit("vector", lambda e: e.tensor_reduce(
            scol[0:3, 0:1], gsm[0:3, :], axis=AX.X, op=ALU.add),
            reads=("l3a_gsm",), writes=("l3a_scol",))
        p.emit("vector", lambda e: e.reciprocal(
            scol[0:3, 0:1], scol[0:3, 0:1]),
            reads=("l3a_scol",), writes=("l3a_scol",))
        p.emit("vector", lambda e: e.tensor_scalar(
            gsm[0:3, :], gsm[0:3, :], scol[0:3, 0:1], None, op0=ALU.mult),
            reads=("l3a_gsm", "l3a_scol"), writes=("l3a_gsm",))
        p.emit("vector", lambda e: e.tensor_copy(rbf[0:1, 8:11], smx[0:1, 0:3]),
               reads=("l3a_smx",), writes=("l3a_rbf",))
        p.emit("tensor", lambda e: e.transpose(
            ps_tr[0:3, 0:1], rbf[0:1, 8:11], cx.iden[0:1, 0:1]),
            reads=("l3a_rbf", "cx_iden"), writes=("l3a_pstr",))
        p.emit("vector", lambda e: e.tensor_copy(scol[0:3, 1:2], ps_tr[0:3, 0:1]),
               reads=("l3a_pstr",), writes=("l3a_scol",))
        p.emit("vector", lambda e: e.tensor_scalar(
            gsm[0:3, :], gsm[0:3, :], scol[0:3, 1:2], None, op0=ALU.mult),
            reads=("l3a_gsm", "l3a_scol"), writes=("l3a_gsm",))
        p.emit("tensor", lambda e: e.matmul(
            ps_g[0:1, 0, 0:16], cx.ones8[0:3, 0:1], gsm[0:3, :],
            start=True, stop=True),
            reads=("l3a_gsm", "cx_ones8"), writes=(("l3a_psg", 0),))
        p.emit("vector", lambda e: e.tensor_copy(hgrow[0:1, 0:16], ps_g[0:1, 0, 0:16]),
               reads=(("l3a_psg", 0),), writes=("l3a_hgrow",))
        cx.replicate_row(hgrow[0:1, 0:16], "l3a_hgrow", ps_g[:, 0, 0:16],
                         ("l3a_psg", 0))
        p.emit("scalar", lambda e: e.activation(
            out=hgrep8[:], in_=ps_g[:, 0, 0:16], func=AF.Identity, scale=0.125),
            reads=(("l3a_psg", 0),), writes=("l3a_hg8",))
        p.emit("scalar", lambda e: e.activation(
            out=hgneg[:], in_=ps_g[:, 0, 0:16], func=AF.Identity, scale=-1e9),
            reads=(("l3a_psg", 0),), writes=("l3a_hgn",))
        p.pop_scope()

        p.push_scope()
        wqb = p.sb("l3a_wq2", [128, 8, 1024], BF16)
        wst2 = p.sb("l3a_wst2", [128, 2, 1024], F32)
        bqv = p.sb("l3a_bq2", [128, 8], F32)
        kh_sb = p.sb("l3a_kh", [64, 2048], BF16)
        vh_sb = p.sb("l3a_vh", [64, 2048], BF16)
        vt_sb = p.sb("l3a_vt", [128, 16, 64], BF16)
        s_sb = p.sb("l3a_s", [128, 2048], F32)
        pbf = p.sb("l3a_pbf", [128, 2048], BF16)
        ptev = p.sb("l3a_ptev", [128, 128], BF16)
        qcol = p.sb("l3a_qcol", [128, 4], F32)
        qtile = p.sb("l3a_qt", [64, 128], BF16)
        ps_s = p.ps("l3a_pss", [128, 4, 512], F32)
        ps_o = p.ps("l3a_pso", [128, 1, 512], F32)
        ps_t2 = p.ps("l3a_pst2", [128, 1024], BF16)
        load_weight_bf16(p, wst2, "l3a_wst2", wqb[:], "l3a_wq2", W["wqT"], 8, 1024)
        p.dma("sync", bqv[:], W["bq_pc"], writes=("l3a_bq2",))
        for h in range(16):
            hb = (h % 2) * 64
            hc = h // 2
            p.emit("vector", lambda e, hb=hb, hc=hc: e.tensor_copy(
                kh_sb[:], kf[hb:hb + 64, hc, :]),
                reads=("l3a_kf",), writes=("l3a_kh",))
            p.emit("vector", lambda e, hb=hb, hc=hc: e.tensor_copy(
                vh_sb[:], vf[hb:hb + 64, hc, :]),
                reads=("l3a_vf",), writes=("l3a_vh",))
            for kc in range(16):
                p.emit("tensor", lambda e, kc=kc: e.transpose(
                    ps_t2[0:128, 0:64], vh_sb[:, kc * 128:(kc + 1) * 128],
                    cx.iden[0:64, 0:64]), reads=("l3a_vh", "cx_iden"),
                    writes=("l3a_pst2",))
                p.emit("vector", lambda e, kc=kc: e.tensor_copy(
                    vt_sb[:, kc, :], ps_t2[0:128, 0:64]),
                    reads=("l3a_pst2",), writes=("l3a_vt",))
            for qt in range(16):
                nk = qt + 1
                for k in range(8):
                    p.emit("tensor", lambda e, k=k, h=h, qt=qt: e.matmul(
                        ps_o[0:64, 0, 0:128],
                        wqb[:, k, h * 64:(h + 1) * 64],
                        xnb_s[:, k, qt * 128:(qt + 1) * 128],
                        start=(k == 0), stop=(k == 7)),
                        reads=("l3a_wq2", "l3a_xnbs"), writes=(("l3a_pso", 0),))
                p.emit("scalar", lambda e, hb=hb, hc=hc: e.activation(
                    out=qtile[:], in_=ps_o[0:64, 0, 0:128], func=AF.Identity,
                    bias=bqv[hb:hb + 64, hc:hc + 1]),
                    reads=(("l3a_pso", 0), "l3a_bq2"), writes=("l3a_qt2",))
                for kc in range(nk):
                    p.emit("tensor", lambda e, kc=kc, qt=qt: e.matmul(
                        ps_s[:, kc // 4, (kc % 4) * 128:(kc % 4) * 128 + 128],
                        qtile[:], kh_sb[:, kc * 128:(kc + 1) * 128],
                        start=True, stop=True),
                        reads=("l3a_qt2", "l3a_kh"), writes=(("l3a_pss", kc // 4),))
                for g4 in range((nk + 3) // 4):
                    w_ = min(512, nk * 128 - g4 * 512)
                    p.emit("scalar", lambda e, g4=g4, w_=w_, h=h: e.activation(
                        out=s_sb[:, g4 * 512:g4 * 512 + w_],
                        in_=ps_s[:, g4, 0:w_], func=AF.Identity,
                        scale=hgrep8[:, h:h + 1]),
                        reads=(("l3a_pss", g4), "l3a_hg8"), writes=("l3a_s",))
                p.emit("vector", lambda e, qt=qt: e.tensor_tensor(
                    s_sb[:, qt * 128:(qt + 1) * 128],
                    s_sb[:, qt * 128:(qt + 1) * 128], mask01[:], op=ALU.mult),
                    reads=("l3a_s", "l3a_m01"), writes=("l3a_s",))
                p.emit("vector", lambda e, qt=qt, h=h: e.scalar_tensor_tensor(
                    s_sb[:, qt * 128:(qt + 1) * 128], mask10[:],
                    hgneg[:, h:h + 1], s_sb[:, qt * 128:(qt + 1) * 128],
                    op0=ALU.mult, op1=ALU.add),
                    reads=("l3a_s", "l3a_m10", "l3a_hgn"), writes=("l3a_s",))
                p.emit("vector", lambda e, qt=qt: e.tensor_reduce(
                    qcol[:, 0:1], s_sb[:, 0:(qt + 1) * 128], axis=AX.X,
                    op=ALU.max), reads=("l3a_s",), writes=("l3a_qcol",))
                p.emit("vector", lambda e: e.tensor_scalar(
                    qcol[:, 1:2], qcol[:, 0:1], -1.0, None, op0=ALU.mult),
                    reads=("l3a_qcol",), writes=("l3a_qcol",))
                p.emit("scalar", lambda e, qt=qt: e.activation(
                    out=s_sb[:, 0:(qt + 1) * 128], in_=s_sb[:, 0:(qt + 1) * 128],
                    func=AF.Exp, bias=qcol[:, 1:2]),
                    reads=("l3a_s", "l3a_qcol"), writes=("l3a_s",))
                p.emit("vector", lambda e, qt=qt: e.tensor_reduce(
                    qcol[:, 2:3], s_sb[:, 0:(qt + 1) * 128], axis=AX.X,
                    op=ALU.add), reads=("l3a_s",), writes=("l3a_qcol",))
                p.emit("vector", lambda e: e.reciprocal(
                    qcol[:, 3:4], qcol[:, 2:3]),
                    reads=("l3a_qcol",), writes=("l3a_qcol",))
                p.emit("vector", lambda e, qt=qt: e.tensor_scalar(
                    pbf[:, 0:(qt + 1) * 128], s_sb[:, 0:(qt + 1) * 128],
                    qcol[:, 3:4], None, op0=ALU.mult),
                    reads=("l3a_s", "l3a_qcol"), writes=("l3a_pbf",))
                for kc in range(nk):
                    p.emit("tensor", lambda e, kc=kc: e.transpose(
                        ps_t2[0:128, 0:128], pbf[:, kc * 128:(kc + 1) * 128],
                        cx.iden[:]), reads=("l3a_pbf", "cx_iden"),
                        writes=("l3a_pst2",))
                    p.emit("vector", lambda e, kc=kc: e.tensor_copy(
                        ptev[:], ps_t2[0:128, 0:128]),
                        reads=("l3a_pst2",), writes=("l3a_ptev",))
                    p.emit("tensor", lambda e, kc=kc, qt=qt: e.matmul(
                        ps_o[0:64, 0, 0:128], vt_sb[:, kc, :], ptev[:],
                        start=(kc == 0), stop=(kc == qt)),
                        reads=("l3a_vt", "l3a_ptev"), writes=(("l3a_pso", 0),))
                p.emit("vector", lambda e, qt=qt, hb=hb, hc=hc: e.tensor_copy(
                    oacc[hb:hb + 64, hc, qt * 128:(qt + 1) * 128],
                    ps_o[0:64, 0, 0:128]),
                    reads=(("l3a_pso", 0),), writes=("l3a_oacc",))
        p.pop_scope()
        p.pop_scope()

        p.push_scope()
        wob = p.sb("l3a_wo", [128, 8, 1024], BF16)
        wst3 = p.sb("l3a_wst3", [128, 2, 1024], F32)
        bo = p.sb("l3a_bo", [128, 8], F32)
        xt2 = p.sb("l3a_xt2", [128, 8, 512], F32)
        yt2 = p.sb("l3a_yt2", [128, 8, 512], F32)
        yn2 = p.sb("l3a_yn2", [128, 8, 512], F32)
        ps_y2 = p.ps("l3a_psy2", [128, 4, 512], F32)
        load_weight_bf16(p, wst3, "l3a_wst3", wob[:], "l3a_wo", W["woT"], 8, 1024)
        p.dma("sync", bo[:], W["bo_pc"], writes=("l3a_bo",))
        for ts in range(4):
            t = b * 4 + ts
            p.dma("sync", xt2[:], xin_t[t], reads=("l3a_xt2",), writes=("l3a_xt2",))
            for oc in range(8):
                bnk = oc % 4
                for k in range(8):
                    p.emit("tensor", lambda e, oc=oc, k=k, bnk=bnk, ts=ts: e.matmul(
                        ps_y2[:, bnk, :], wob[:, k, oc * 128:(oc + 1) * 128],
                        oacc[:, k, ts * 512:(ts + 1) * 512],
                        start=(k == 0), stop=(k == 7)),
                        reads=("l3a_wo", "l3a_oacc"), writes=(("l3a_psy2", bnk),))
                p.emit("scalar", lambda e, oc=oc, bnk=bnk: e.activation(
                    out=yt2[:, oc, :], in_=ps_y2[:, bnk, :], func=AF.Identity,
                    bias=bo[:, oc:oc + 1]),
                    reads=(("l3a_psy2", bnk), "l3a_bo"), writes=("l3a_yt2",))
            cx.ln_fm(yt2[:], "l3a_yt2", yn2[:], "l3a_yn2", "l3ae")
            p.emit("vector", lambda e: e.tensor_tensor(
                yn2[:], yn2[:], xt2[:], op=ALU.add),
                reads=("l3a_yn2", "l3a_xt2"), writes=("l3a_yn2",))
            p.dma("sync", xout_t[t], yn2[:], reads=("l3a_yn2",), writes=("l3a_st",))
        p.pop_scope()
        p.pop_scope()
    p.pop_scope()


def build_l3_attn(consts):
    p = Prog()
    xin = p.dram_in("xin", [1024, TCORE])
    xout = p.dram_out("xout", [1024, TCORE])
    c_aps = consts_aps(p, consts)
    cx = Ctx(p, c_aps)
    W = {
        "wqT": p.dram_in("w_wqT", [1024, 1024]),
        "wkT": p.dram_in("w_wkT", [1024, 1024]),
        "wvT": p.dram_in("w_wvT", [1024, 1024]),
        "woT": p.dram_in("w_woT", [1024, 1024]),
        "bq_pc": p.dram_in("w_bq_pc", [128, 8]),
        "bkv_pc": p.dram_in("w_bkv_pc", [128, 16]),
        "bo_pc": p.dram_in("w_bo_pc", [128, 8]),
        "sw1T": p.dram_in("w_sw1T", [1024, 512]),
        "sw2T": p.dram_in("w_sw2T", [512, 3]),
        "pw1T": p.dram_in("w_pw1T", [3, 1024, 512]),
        "pw2T": p.dram_in("w_pw2T", [3, 512, 16]),
        "sb_pc": p.dram_in("w_sb_pc", [128, 32]),
    }
    emit_l3_attn(p, cx, xin, xout, W, c_aps)
    nc = p.finalize()
    return nc, p


def l3_attn_in_map(p, consts, inputs):
    g = lambda n: np.asarray(inputs[n], np.float32)
    sbv = np.zeros((128, 32), np.float32)
    sbv[:, 0:4] = pc_n(g("L3_attn_sb1"), 4)
    sbv[0:3, 4] = g("L3_attn_sb2")
    pb1 = g("L3_attn_pb1")
    for pi in range(3):
        sbv[:, 5 + pi * 4:9 + pi * 4] = pc_n(pb1[pi], 4)
    pb2 = g("L3_attn_pb2")
    for pi in range(3):
        sbv[0:16, 17 + pi] = pb2[pi]
    return {
        "c_fvals_in": p.fconst_host_array(),
        "w_wqT": np.ascontiguousarray(g("L3_attn_wq").T),
        "w_wkT": np.ascontiguousarray(g("L3_attn_wk").T),
        "w_wvT": np.ascontiguousarray(g("L3_attn_wv").T),
        "w_woT": np.ascontiguousarray(g("L3_attn_wo").T),
        "w_bq_pc": pc(g("L3_attn_bq")),
        "w_bkv_pc": np.concatenate([pc(g("L3_attn_bk")), pc(g("L3_attn_bv"))], 1),
        "w_bo_pc": pc(g("L3_attn_bo")),
        "w_sw1T": np.ascontiguousarray(g("L3_attn_sw1").T),
        "w_sw2T": np.ascontiguousarray(g("L3_attn_sw2").T),
        "w_pw1T": np.ascontiguousarray(g("L3_attn_pw1").transpose(0, 2, 1)),
        "w_pw2T": np.ascontiguousarray(g("L3_attn_pw2").transpose(0, 2, 1)),
        "w_sb_pc": sbv,
        **consts_in_map(consts),
    }


def pc_n(bias, nch):
    return np.ascontiguousarray(
        np.asarray(bias, np.float32).reshape(nch, 128).T)


def emit_l1_attn(p: Prog, cx: Ctx, xin, xout, W, C):
    p.push_scope()
    rcnt = p.sb("l1a_rc", [128, 2048], BF16)
    p.dma("sync", rcnt[:], C["l1a_rcounts"], writes=("l1a_rc",))
    xin_t = xin.rearrange("(c p) (n t) -> n p c t", p=128, t=512)
    xout_t = xout.rearrange("(c p) (n t) -> n p c t", p=128, t=512)
    p.fconst(0.125)

    for b in range(2):
        p.push_scope()
        oacc = p.sb("l1a_o", [128, 8, 2048], BF16)
        p.push_scope()
        kf = p.sb("l1a_k", [128, 8, 2048], BF16)
        vf = p.sb("l1a_v", [128, 8, 2048], BF16)
        xnb_s = p.sb("l1a_xnb", [128, 8, 2048], BF16)
        wb = p.sb("l1a_w", [128, 8, 1024], BF16)
        wst = p.sb("l1a_wst", [128, 2, 1024], F32)
        bqkv = p.sb("l1a_b", [128, 24], F32)
        p.push_scope()
        xt = p.sb("l1a_xt", [128, 8, 512], F32)
        ps_k = p.ps("l1a_psk", [128, 4, 512], F32)
        p.dma("sync", bqkv[:], W["bqkv_pc"], writes=("l1a_b",))
        for wi, (wap, dst, dkey, bc) in enumerate((
                (W["WkT"], kf, "l1a_kf", 8), (W["WvT"], vf, "l1a_vf", 16))):
            load_weight_bf16(p, wst, "l1a_wst", wb[:], "l1a_w", wap, 8, 1024)
            for ts in range(4):
                t = b * 4 + ts
                if wi == 0:
                    p.dma("sync", xt[:], xin_t[t], reads=("l1a_xt",),
                          writes=("l1a_xt",))
                    cx.ln_fm(xt[:], "l1a_xt",
                             xnb_s[:, :, ts * 512:(ts + 1) * 512], "l1a_xnbs",
                             "l1at", sub_inplace=True)
                for oc in range(8):
                    bnk = oc % 4
                    for k in range(8):
                        p.emit("tensor", lambda e, oc=oc, k=k, bnk=bnk, ts=ts: e.matmul(
                            ps_k[:, bnk, :], wb[:, k, oc * 128:(oc + 1) * 128],
                            xnb_s[:, k, ts * 512:(ts + 1) * 512],
                            start=(k == 0), stop=(k == 7)),
                            reads=("l1a_w", "l1a_xnbs"), writes=(("l1a_psk", bnk),))
                    p.emit("scalar", lambda e, oc=oc, bnk=bnk, dst=dst, ts=ts, bc=bc, dkey=dkey: e.activation(
                        out=dst[:, oc, ts * 512:(ts + 1) * 512],
                        in_=ps_k[:, bnk, :], func=AF.Identity,
                        bias=bqkv[:, bc + oc:bc + oc + 1]),
                        reads=(("l1a_psk", bnk), "l1a_b"), writes=(dkey,))
        p.pop_scope()
        p.push_scope()
        kh = p.sb("l1a_kh", [64, 2048], BF16)
        vh = p.sb("l1a_vh", [64, 2048], BF16)
        vt_sb = p.sb("l1a_vt", [128, 16, 64], BF16)
        qh = p.sb("l1a_qh", [64, 2048], BF16)
        s_sb = p.sb("l1a_s", [128, 256], F32)
        p0 = p.sb("l1a_p0", [1, 2048], F32)
        p0col = p.sb("l1a_p0c", [1, 4], F32)
        gprod = p.sb("l1a_gp", [64, 512], F32)
        gtmp = p.sb("l1a_gt", [64, 2], F32)
        pbf = p.sb("l1a_pbf", [128, 256], BF16)
        ptev = p.sb("l1a_ptev", [128, 128], BF16)
        qcol = p.sb("l1a_qcol", [128, 4], F32)
        ps_s = p.ps("l1a_pss", [128, 4, 512], F32)
        ps_o = p.ps("l1a_pso", [128, 1, 512], F32)
        ps_t2 = p.ps("l1a_pst2", [128, 1024], BF16)
        load_weight_bf16(p, wst, "l1a_wst", wb[:], "l1a_w", W["WqT"], 8, 1024)
        p.emit("vector", lambda e: e.memset(oacc[:], 0.0), writes=("l1a_oacc",))
        for h in range(16):
            hb, hc = (h % 2) * 64, h // 2
            p.emit("vector", lambda e, hb=hb, hc=hc: e.tensor_copy(
                kh[:], kf[hb:hb + 64, hc, :]), reads=("l1a_kf",),
                writes=("l1a_kh",))
            p.emit("vector", lambda e, hb=hb, hc=hc: e.tensor_copy(
                vh[:], vf[hb:hb + 64, hc, :]), reads=("l1a_vf",),
                writes=("l1a_vh",))
            for sl in range(4):
                for k in range(8):
                    p.emit("tensor", lambda e, sl=sl, k=k, h=h: e.matmul(
                        ps_s[0:64, sl, :], wb[:, k, h * 64:(h + 1) * 64],
                        xnb_s[:, k, sl * 512:(sl + 1) * 512],
                        start=(k == 0), stop=(k == 7)),
                        reads=("l1a_w", "l1a_xnbs"), writes=(("l1a_pss", sl),))
                p.emit("scalar", lambda e, sl=sl, hb=hb, hc=hc: e.activation(
                    out=qh[:, sl * 512:(sl + 1) * 512], in_=ps_s[0:64, sl, :],
                    func=AF.Identity, bias=bqkv[hb:hb + 64, hc:hc + 1]),
                    reads=(("l1a_pss", sl), "l1a_b"), writes=("l1a_qh",))
            for c in range(16):
                w_ = 128 if c < 15 else 127
                p.emit("tensor", lambda e, c=c, w_=w_: e.transpose(
                    ps_t2[0:w_, 0:64], vh[:, 1 + c * 128:1 + c * 128 + w_],
                    cx.iden[0:64, 0:64]), reads=("l1a_vh", "cx_iden"),
                    writes=("l1a_pst2",))
                p.emit("vector", lambda e, c=c, w_=w_: e.tensor_copy(
                    vt_sb[0:w_, c, :], ps_t2[0:w_, 0:64]),
                    reads=("l1a_pst2",), writes=("l1a_vt",))
            for sl in range(4):
                p.emit("tensor", lambda e, sl=sl: e.matmul(
                    ps_s[0:1, sl, :], qh[:, 0:1], kh[:, sl * 512:(sl + 1) * 512],
                    start=True, stop=True),
                    reads=("l1a_qh", "l1a_kh"), writes=(("l1a_pss", sl),))
                p.emit("scalar", lambda e, sl=sl: e.activation(
                    out=p0[0:1, sl * 512:(sl + 1) * 512], in_=ps_s[0:1, sl, :],
                    func=AF.Identity, scale=0.125),
                    reads=(("l1a_pss", sl),), writes=("l1a_p0",))
            p.emit("vector", lambda e: e.tensor_reduce(
                p0col[0:1, 0:1], p0[0:1, :], axis=AX.X, op=ALU.max),
                reads=("l1a_p0",), writes=("l1a_p0c",))
            p.emit("vector", lambda e: e.tensor_scalar(
                p0col[0:1, 1:2], p0col[0:1, 0:1], -1.0, None, op0=ALU.mult),
                reads=("l1a_p0c",), writes=("l1a_p0c",))
            p.emit("scalar", lambda e: e.activation(
                out=p0[0:1, :], in_=p0[0:1, :], func=AF.Exp,
                bias=p0col[0:1, 1:2]), reads=("l1a_p0", "l1a_p0c"),
                writes=("l1a_p0",))
            p.emit("vector", lambda e: e.tensor_reduce(
                p0col[0:1, 2:3], p0[0:1, :], axis=AX.X, op=ALU.add),
                reads=("l1a_p0",), writes=("l1a_p0c",))
            p.emit("vector", lambda e: e.reciprocal(
                p0col[0:1, 3:4], p0col[0:1, 2:3]),
                reads=("l1a_p0c",), writes=("l1a_p0c",))
            p.emit("vector", lambda e: e.tensor_scalar(
                p0[0:1, :], p0[0:1, :], p0col[0:1, 3:4], None, op0=ALU.mult),
                reads=("l1a_p0", "l1a_p0c"), writes=("l1a_p0",))
            p.emit("vector", lambda e: e.memset(gtmp[:, 0:1], 0.0),
                   writes=("l1a_gt",))
            for sl in range(4):
                p.emit("tensor", lambda e, sl=sl: e.matmul(
                    ps_s[0:64, sl, :],
                    cx.ones_row[0:1, 0:64], p0[0:1, sl * 512:(sl + 1) * 512],
                    start=True, stop=True),
                    reads=("l1a_p0", "cx_onesrow"), writes=(("l1a_pss", sl),))
                p.emit("vector", lambda e, sl=sl: e.tensor_tensor(
                    gprod[:, :], vh[:, sl * 512:(sl + 1) * 512],
                    ps_s[0:64, sl, :], op=ALU.mult),
                    reads=("l1a_vh", ("l1a_pss", sl)), writes=("l1a_gp",))
                p.emit("vector", lambda e: e.tensor_reduce(
                    gtmp[:, 1:2], gprod[:, :], axis=AX.X, op=ALU.add),
                    reads=("l1a_gp",), writes=("l1a_gt",))
                p.emit("vector", lambda e: e.tensor_tensor(
                    gtmp[:, 0:1], gtmp[:, 0:1], gtmp[:, 1:2], op=ALU.add),
                    reads=("l1a_gt",), writes=("l1a_gt",))
            p.emit("vector", lambda e, hb=hb, hc=hc: e.tensor_copy(
                oacc[hb:hb + 64, hc, 0:1], gtmp[:, 0:1]),
                reads=("l1a_gt",), writes=("l1a_oacc",))
            for w in range(14):
                base = 1 + w * 128
                for qt in range(2):
                    p.emit("tensor", lambda e, w=w, qt=qt, base=base: e.matmul(
                        ps_s[:, 0, 0:256], qh[:, base + qt * 128:base + qt * 128 + 128],
                        kh[:, base:base + 256], start=True, stop=True),
                        reads=("l1a_qh", "l1a_kh"), writes=(("l1a_pss", 0),))
                    p.emit("scalar", lambda e: e.activation(
                        out=s_sb[:, 0:256], in_=ps_s[:, 0, 0:256],
                        func=AF.Identity, scale=0.125),
                        reads=(("l1a_pss", 0),), writes=("l1a_s",))
                    p.emit("vector", lambda e: e.tensor_reduce(
                        qcol[:, 0:1], s_sb[:, 0:256], axis=AX.X, op=ALU.max),
                        reads=("l1a_s",), writes=("l1a_qcol",))
                    p.emit("vector", lambda e: e.tensor_scalar(
                        qcol[:, 1:2], qcol[:, 0:1], -1.0, None, op0=ALU.mult),
                        reads=("l1a_qcol",), writes=("l1a_qcol",))
                    p.emit("scalar", lambda e: e.activation(
                        out=s_sb[:, 0:256], in_=s_sb[:, 0:256], func=AF.Exp,
                        bias=qcol[:, 1:2]), reads=("l1a_s", "l1a_qcol"),
                        writes=("l1a_s",))
                    p.emit("vector", lambda e: e.tensor_reduce(
                        qcol[:, 2:3], s_sb[:, 0:256], axis=AX.X, op=ALU.add),
                        reads=("l1a_s",), writes=("l1a_qcol",))
                    p.emit("vector", lambda e: e.reciprocal(
                        qcol[:, 3:4], qcol[:, 2:3]),
                        reads=("l1a_qcol",), writes=("l1a_qcol",))
                    p.emit("vector", lambda e: e.tensor_scalar(
                        pbf[:, 0:256], s_sb[:, 0:256], qcol[:, 3:4], None,
                        op0=ALU.mult), reads=("l1a_s", "l1a_qcol"),
                        writes=("l1a_pbf",))
                    for kc in range(2):
                        p.emit("tensor", lambda e, kc=kc: e.transpose(
                            ps_t2[0:128, 0:128], pbf[:, kc * 128:(kc + 1) * 128],
                            cx.iden[:]), reads=("l1a_pbf", "cx_iden"),
                            writes=("l1a_pst2",))
                        p.emit("vector", lambda e: e.tensor_copy(
                            ptev[:], ps_t2[0:128, 0:128]),
                            reads=("l1a_pst2",), writes=("l1a_ptev",))
                        p.emit("tensor", lambda e, kc=kc, w=w: e.matmul(
                            ps_o[0:64, 0, 0:128], vt_sb[:, w + kc, :], ptev[:],
                            start=(kc == 0), stop=(kc == 1)),
                            reads=("l1a_vt", "l1a_ptev"), writes=(("l1a_pso", 0),))
                    p.emit("vector", lambda e, w=w, qt=qt, hb=hb, hc=hc: e.tensor_tensor(
                        oacc[hb:hb + 64, hc, 1 + (w + qt) * 128:1 + (w + qt) * 128 + 128],
                        oacc[hb:hb + 64, hc, 1 + (w + qt) * 128:1 + (w + qt) * 128 + 128],
                        ps_o[0:64, 0, 0:128], op=ALU.add),
                        reads=("l1a_oacc", ("l1a_pso", 0)), writes=("l1a_oacc",))
        p.emit("vector", lambda e: e.tensor_tensor(
            oacc[:], oacc[:], rcnt[:].unsqueeze(1).broadcast_to([128, 8, 2048]),
            op=ALU.mult), reads=("l1a_oacc", "l1a_rc"), writes=("l1a_oacc",))
        p.pop_scope()
        p.pop_scope()
        p.push_scope()
        wob = p.sb("l1a_wo", [128, 8, 1024], BF16)
        wst3 = p.sb("l1a_wst3", [128, 2, 1024], F32)
        bo = p.sb("l1a_bo", [128, 8], F32)
        xt2 = p.sb("l1a_xt2", [128, 8, 512], F32)
        yt2 = p.sb("l1a_yt2", [128, 8, 512], F32)
        yn2 = p.sb("l1a_yn2", [128, 8, 512], F32)
        ps_y2 = p.ps("l1a_psy2", [128, 4, 512], F32)
        load_weight_bf16(p, wst3, "l1a_wst3", wob[:], "l1a_wo", W["WoT"], 8, 1024)
        p.dma("sync", bo[:], W["bo_pc"], writes=("l1a_bo",))
        for ts in range(4):
            t = b * 4 + ts
            p.dma("sync", xt2[:], xin_t[t], reads=("l1a_xt2",), writes=("l1a_xt2",))
            for oc in range(8):
                bnk = oc % 4
                for k in range(8):
                    p.emit("tensor", lambda e, oc=oc, k=k, bnk=bnk, ts=ts: e.matmul(
                        ps_y2[:, bnk, :], wob[:, k, oc * 128:(oc + 1) * 128],
                        oacc[:, k, ts * 512:(ts + 1) * 512],
                        start=(k == 0), stop=(k == 7)),
                        reads=("l1a_wo", "l1a_oacc"), writes=(("l1a_psy2", bnk),))
                p.emit("scalar", lambda e, oc=oc, bnk=bnk: e.activation(
                    out=yt2[:, oc, :], in_=ps_y2[:, bnk, :], func=AF.Identity,
                    bias=bo[:, oc:oc + 1]),
                    reads=(("l1a_psy2", bnk), "l1a_bo"), writes=("l1a_yt2",))
            cx.ln_fm(yt2[:], "l1a_yt2", yn2[:], "l1a_yn2", "l1ae")
            p.emit("vector", lambda e: e.tensor_tensor(
                yn2[:], yn2[:], xt2[:], op=ALU.add),
                reads=("l1a_yn2", "l1a_xt2"), writes=("l1a_yn2",))
            p.dma("sync", xout_t[t], yn2[:], reads=("l1a_yn2",), writes=("l1a_st",))
        p.pop_scope()
        p.pop_scope()
    p.pop_scope()


def build_l1_attn(consts):
    p = Prog()
    xin = p.dram_in("xin", [1024, TCORE])
    xout = p.dram_out("xout", [1024, TCORE])
    c_aps = consts_aps(p, consts)
    cx = Ctx(p, c_aps)
    W = {
        "WqT": p.dram_in("w_WqT", [1024, 1024]),
        "WkT": p.dram_in("w_WkT", [1024, 1024]),
        "WvT": p.dram_in("w_WvT", [1024, 1024]),
        "WoT": p.dram_in("w_WoT", [1024, 1024]),
        "bqkv_pc": p.dram_in("w_bqkv_pc", [128, 24]),
        "bo_pc": p.dram_in("w_bo_pc", [128, 8]),
    }
    emit_l1_attn(p, cx, xin, xout, W, c_aps)
    nc = p.finalize()
    return nc, p


def l1_attn_in_map(p, consts, inputs):
    g = lambda n: np.asarray(inputs[n], np.float32)
    return {
        "c_fvals_in": p.fconst_host_array(),
        "w_WqT": np.ascontiguousarray(g("L1_attn_Wq")),
        "w_WkT": np.ascontiguousarray(g("L1_attn_Wk")),
        "w_WvT": np.ascontiguousarray(g("L1_attn_Wv")),
        "w_WoT": np.ascontiguousarray(g("L1_attn_Wo")),
        "w_bqkv_pc": np.concatenate(
            [pc(g("L1_attn_bq")), pc(g("L1_attn_bk")), pc(g("L1_attn_bv"))], 1),
        "w_bo_pc": pc(g("L1_attn_bo")),
        **consts_in_map(consts),
    }


def emit_l0_ffn(p: Prog, cx: Ctx, xin, xout, W, C):
    p.push_scope()
    qc_t = p.sb("l0f_qc", [128, 2048], BF16)
    qs_t = p.sb("l0f_qs", [128, 2048], BF16)
    kc_t = p.sb("l0f_kc", [128, 2048], BF16)
    ks_t = p.sb("l0f_ks", [128, 2048], BF16)
    r2t = p.sb("l0f_r2t", [128, 128], BF16)
    ones2 = p.sb("l0f_o2", [128, 2], BF16)
    bo2 = p.sb("l0f_bo2", [2, 128], F32)
    cu01 = p.sb("l0f_cu01", [128, 128], F32)
    cu10 = p.sb("l0f_cu10", [128, 128], F32)
    mask01 = p.sb("l0f_m01", [128, 128], F32)
    mask10 = p.sb("l0f_m10", [128, 128], F32)
    p.dma("sync", qc_t[:], C["rot_qc"], writes=("l0f_qc",))
    p.dma("sync", qs_t[:], C["rot_qs"], writes=("l0f_qs",))
    p.dma("sync", kc_t[:], C["rot_kc"], writes=("l0f_kc",))
    p.dma("sync", ks_t[:], C["rot_ks"], writes=("l0f_ks",))
    p.dma("sync", r2t[:], C["rot_r2t"], writes=("l0f_r2t",))
    p.dma("sync", ones2[:], C["ones2col"], writes=("l0f_o2",))
    p.dma("sync", bo2[:], C["blockones2"], writes=("l0f_bo2",))
    p.dma("sync", cu01[:], C["upper01"], writes=("l0f_cu01",))
    p.dma("sync", cu10[:], C["upper10"], writes=("l0f_cu10",))
    p.dma("sync", mask01[:], C["causal01"], writes=("l0f_m01",))
    p.dma("sync", mask10[:], C["causal10"], writes=("l0f_m10",))
    p.fconst(1e-12)
    p.fconst(8.0)
    p.fconst(-1e30)
    xin_t = xin.rearrange("(c p) (n t) -> n p c t", p=128, t=512)
    xout_t = xout.rearrange("(c p) (n t) -> n p c t", p=128, t=512)

    for b in range(2):
        p.push_scope()
        xr = p.sb("l0f_x", [128, 8, 2048], F32)
        p.push_scope()
        xt0 = p.sb("l0f_xt0", [128, 8, 512], F32)
        for ts in range(4):
            t = b * 4 + ts
            p.dma("sync", xt0[:], xin_t[t], reads=("l0f_xt0",), writes=("l0f_xt0",))
            cx.ln_fm(xt0[:], "l0f_xt0", xr[:, :, ts * 512:(ts + 1) * 512],
                     "l0f_x", "l0fi")
        p.pop_scope()

        for li in range(2):
            p.push_scope()
            qf = p.sb("l0f_qf", [128, 4, 2048], BF16)
            kfb = p.sb("l0f_kfb", [128, 4, 2048], BF16)
            vfb = p.sb("l0f_vfb", [128, 4, 2048], BF16)
            oatt = p.sb("l0f_oatt", [128, 4, 2048], BF16)
            p.push_scope()
            hb = p.sb("l0f_hb", [128, 8, 512], BF16)
            wqkv = p.sb("l0f_wqkv", [128, 8, 512], BF16)
            wst = p.sb("l0f_wst", [128, 2, 1024], F32)
            qsc = p.sb("l0f_qsc", [128, 2], F32)
            rawq = p.sb("l0f_rawq", [128, 512], F32)
            sqq = p.sb("l0f_sqq", [128, 512], BF16)
            nrm = p.sb("l0f_nrm", [2, 512], F32)
            t1b = p.sb("l0f_t1", [128, 512], F32)
            ps_q = p.ps("l0f_psq", [128, 2, 512], F32)
            ps_r = p.ps("l0f_psr", [128, 2, 512], F32)
            p.dma("sync", qsc[:], W["qksc"][li], writes=("l0f_qsc",))
            def build_hb(sl):
                lo = sl * 512
                p.emit("vector", lambda e, lo=lo: e.tensor_copy(
                    hb[:, 0:4, :], xr[:, 0:4, lo:lo + 512]), reads=("l0f_x",),
                    writes=("l0f_hb",))
                if lo == 0:
                    p.emit("vector", lambda e: e.memset(
                        hb[:, 4:8, 0:1], 0.0), writes=("l0f_hb",))
                    p.emit("vector", lambda e: e.tensor_copy(
                        hb[:, 4:8, 1:512], xr[:, 4:8, 0:511]), reads=("l0f_x",),
                        writes=("l0f_hb",))
                else:
                    p.emit("vector", lambda e, lo=lo: e.tensor_copy(
                        hb[:, 4:8, :], xr[:, 4:8, lo - 1:lo + 511]),
                        reads=("l0f_x",), writes=("l0f_hb",))
            for part, dst, dkey in ((0, qf, "l0f_qff"), (1, kfb, "l0f_kff"),
                                    (2, vfb, "l0f_vff")):
                load_weight_bf16(p, wst, "l0f_wst", wqkv[:], "l0f_wqkv",
                                 W["wqkv"][li][:, part * 512:(part + 1) * 512],
                                 8, 512)
                for sl in range(4):
                    build_hb(sl)
                    for oc in range(4):
                        bnk = oc % 2
                        for k in range(8):
                            p.emit("tensor", lambda e, oc=oc, k=k, bnk=bnk: e.matmul(
                                ps_q[:, bnk, :], wqkv[:, k, oc * 128:(oc + 1) * 128],
                                hb[:, k, :],
                                start=(k == 0), stop=(k == 7)),
                                reads=("l0f_wqkv", "l0f_hb"), writes=(("l0f_psq", bnk),))
                        if part == 2:
                            p.emit("vector", lambda e, oc=oc, sl=sl, bnk=bnk, dst=dst, dkey=dkey: e.tensor_copy(
                                dst[:, oc, sl * 512:(sl + 1) * 512], ps_q[:, bnk, :]),
                                reads=(("l0f_psq", bnk),), writes=(dkey,))
                            continue
                        p.emit("vector", lambda e, bnk=bnk: e.tensor_copy(
                            rawq[:], ps_q[:, bnk, :]),
                            reads=(("l0f_psq", bnk),), writes=("l0f_rawq",))
                        p.emit("scalar", lambda e: e.activation(
                            out=sqq[:], in_=rawq[:], func=AF.Square),
                            reads=("l0f_rawq",), writes=("l0f_sqq",))
                        p.emit("tensor", lambda e: e.matmul(
                            ps_r[0:2, 0, :], ones2[:], sqq[:],
                            start=True, stop=True),
                            reads=("l0f_sqq", "l0f_o2"), writes=(("l0f_psr", 0),))
                        p.emit("scalar", lambda e: e.activation(
                            out=nrm[:, :], in_=ps_r[0:2, 0, :], func=AF.Sqrt),
                            reads=(("l0f_psr", 0),), writes=("l0f_nrm",))
                        p.emit("vector", lambda e: e.tensor_scalar(
                            nrm[:, :], nrm[:, :], 1e-12, None, op0=ALU.max),
                            reads=("l0f_nrm",), writes=("l0f_nrm",))
                        p.emit("vector", lambda e: e.reciprocal(
                            nrm[:, :], nrm[:, :]), reads=("l0f_nrm",),
                            writes=("l0f_nrm",))
                        p.emit("tensor", lambda e: e.matmul(
                            ps_r[:, 0, :], bo2[:], nrm[:, :],
                            start=True, stop=True),
                            reads=("l0f_nrm", "l0f_bo2"), writes=(("l0f_psr", 0),))
                        p.emit("vector", lambda e: e.tensor_tensor(
                            rawq[:], rawq[:], ps_r[:, 0, :], op=ALU.mult),
                            reads=("l0f_rawq", ("l0f_psr", 0)), writes=("l0f_rawq",))
                        p.emit("scalar", lambda e, part=part: e.activation(
                            out=sqq[:], in_=rawq[:], func=AF.Identity,
                            scale=qsc[:, part:part + 1]),
                            reads=("l0f_rawq", "l0f_qsc"), writes=("l0f_sqq",))
                        p.emit("tensor", lambda e: e.matmul(
                            ps_r[:, 1, :], r2t[:], sqq[:], start=True, stop=True),
                            reads=("l0f_sqq", "l0f_r2t"), writes=(("l0f_psr", 1),))
                        ctab, ckey = (qc_t, "l0f_qc") if part == 0 else (kc_t, "l0f_kc")
                        stab, skey = (qs_t, "l0f_qs") if part == 0 else (ks_t, "l0f_ks")
                        p.emit("vector", lambda e, sl=sl, ctab=ctab, ckey=ckey: e.tensor_tensor(
                            rawq[:], sqq[:], ctab[:, sl * 512:(sl + 1) * 512],
                            op=ALU.mult), reads=("l0f_sqq", ckey),
                            writes=("l0f_rawq",))
                        p.emit("vector", lambda e, sl=sl, stab=stab, skey=skey: e.tensor_tensor(
                            t1b[:], ps_r[:, 1, :],
                            stab[:, sl * 512:(sl + 1) * 512], op=ALU.mult),
                            reads=(("l0f_psr", 1), skey), writes=("l0f_t1",))
                        p.emit("vector", lambda e, oc=oc, sl=sl, dst=dst, dkey=dkey: e.tensor_tensor(
                            dst[:, oc, sl * 512:(sl + 1) * 512], rawq[:], t1b[:],
                            op=ALU.add), reads=("l0f_rawq", "l0f_t1"),
                            writes=(dkey,))
            p.pop_scope()
            p.push_scope()
            vt_sb = p.sb("l0f_vt", [128, 16, 64], BF16)
            kh = p.sb("l0f_kh", [64, 2048], BF16)
            vh = p.sb("l0f_vh", [64, 2048], BF16)
            qh = p.sb("l0f_qh", [64, 2048], BF16)
            s_sb = [p.sb(f"l0f_s{i}", [128, 640], F32) for i in range(2)]
            pbf = [p.sb(f"l0f_pbf{i}", [128, 640], BF16) for i in range(2)]
            ptev = [p.sb(f"l0f_ptev{i}", [128, 128], BF16) for i in range(2)]
            qcol = [p.sb(f"l0f_qcol{i}", [128, 4], F32) for i in range(2)]
            ps_s = p.ps("l0f_pss", [128, 2, 512], F32)
            ps_o = p.ps("l0f_pso", [128, 2, 512], F32)
            ps_t2 = p.ps("l0f_pst2", [128, 2, 1024], BF16)
            for h in range(8):
                hbp, hcc = (h % 2) * 64, h // 2
                p.emit("vector", lambda e, hbp=hbp, hcc=hcc: e.tensor_copy(
                    qh[:], qf[hbp:hbp + 64, hcc, :]), reads=("l0f_qff",),
                    writes=("l0f_qh",))
                p.emit("vector", lambda e, hbp=hbp, hcc=hcc: e.tensor_copy(
                    kh[:], kfb[hbp:hbp + 64, hcc, :]), reads=("l0f_kff",),
                    writes=("l0f_kh",))
                p.emit("vector", lambda e, hbp=hbp, hcc=hcc: e.tensor_copy(
                    vh[:], vfb[hbp:hbp + 64, hcc, :]), reads=("l0f_vff",),
                    writes=("l0f_vh",))
                for c in range(16):
                    cp = c % 2
                    p.emit("tensor", lambda e, c=c, cp=cp: e.transpose(
                        ps_t2[:, cp, 0:64], vh[:, c * 128:(c + 1) * 128],
                        cx.iden[0:64, 0:64]), reads=("l0f_vh", "cx_iden"),
                        writes=(("l0f_pst2", cp),))
                    p.emit("vector", lambda e, c=c, cp=cp: e.tensor_copy(
                        vt_sb[:, c, :], ps_t2[:, cp, 0:64]),
                        reads=(("l0f_pst2", cp),), writes=("l0f_vt",))
                for w in range(4):
                    for qt in range(4):
                        qa = w * 4 + qt
                        pr = qa % 2
                        sK, pK, qK = f"l0f_s{pr}", f"l0f_pbf{pr}", f"l0f_qcol{pr}"
                        sb_, pb_, qc_ = s_sb[pr], pbf[pr], qcol[pr]
                        kc_lo = max(0, qa - 4)
                        nkc = qa - kc_lo + 1
                        for kc in range(kc_lo, qa + 1):
                            j = kc - kc_lo
                            p.emit("tensor", lambda e, qa=qa, kc=kc, j=j: e.matmul(
                                ps_s[:, j // 4, (j % 4) * 128:(j % 4) * 128 + 128],
                                qh[:, qa * 128:(qa + 1) * 128],
                                kh[:, kc * 128:(kc + 1) * 128],
                                start=True, stop=True),
                                reads=("l0f_qh", "l0f_kh"),
                                writes=(("l0f_pss", j // 4),))
                        for g4 in range((nkc + 3) // 4):
                            w_ = min(512, nkc * 128 - g4 * 512)
                            p.emit("scalar", lambda e, g4=g4, w_=w_, sb_=sb_: e.activation(
                                out=sb_[:, g4 * 512:g4 * 512 + w_],
                                in_=ps_s[:, g4, 0:w_], func=AF.Identity,
                                scale=8.0),
                                reads=(("l0f_pss", g4),), writes=(sK,))
                        if qa - kc_lo == 4:
                            p.emit("vector", lambda e, sb_=sb_: e.tensor_tensor(
                                sb_[:, 0:128], sb_[:, 0:128], cu01[:],
                                op=ALU.mult), reads=(sK, "l0f_cu01"),
                                writes=(sK,))
                            p.emit("vector", lambda e, sb_=sb_: e.scalar_tensor_tensor(
                                sb_[:, 0:128], cu10[:], -1e30, sb_[:, 0:128],
                                op0=ALU.mult, op1=ALU.add),
                                reads=(sK, "l0f_cu10"), writes=(sK,))
                        off = (qa - kc_lo) * 128
                        p.emit("vector", lambda e, off=off, sb_=sb_: e.tensor_tensor(
                            sb_[:, off:off + 128], sb_[:, off:off + 128],
                            mask01[:], op=ALU.mult),
                            reads=(sK, "l0f_m01"), writes=(sK,))
                        p.emit("vector", lambda e, off=off, sb_=sb_: e.scalar_tensor_tensor(
                            sb_[:, off:off + 128], mask10[:], -1e30,
                            sb_[:, off:off + 128], op0=ALU.mult, op1=ALU.add),
                            reads=(sK, "l0f_m10"), writes=(sK,))
                        L = nkc * 128
                        p.emit("vector", lambda e, L=L, sb_=sb_, qc_=qc_: e.tensor_reduce(
                            qc_[:, 0:1], sb_[:, 0:L], axis=AX.X, op=ALU.max),
                            reads=(sK,), writes=(qK,))
                        p.emit("vector", lambda e, qc_=qc_: e.tensor_scalar(
                            qc_[:, 1:2], qc_[:, 0:1], -1.0, None, op0=ALU.mult),
                            reads=(qK,), writes=(qK,))
                        p.emit("scalar", lambda e, L=L, sb_=sb_, qc_=qc_: e.activation(
                            out=sb_[:, 0:L], in_=sb_[:, 0:L], func=AF.Exp,
                            bias=qc_[:, 1:2]), reads=(sK, qK),
                            writes=(sK,))
                        p.emit("vector", lambda e, L=L, sb_=sb_, qc_=qc_: e.tensor_reduce(
                            qc_[:, 2:3], sb_[:, 0:L], axis=AX.X, op=ALU.add),
                            reads=(sK,), writes=(qK,))
                        p.emit("vector", lambda e, qc_=qc_: e.reciprocal(
                            qc_[:, 3:4], qc_[:, 2:3]),
                            reads=(qK,), writes=(qK,))
                        p.emit("vector", lambda e, L=L, sb_=sb_, pb_=pb_, qc_=qc_: e.tensor_scalar(
                            pb_[:, 0:L], sb_[:, 0:L], qc_[:, 3:4], None,
                            op0=ALU.mult), reads=(sK, qK),
                            writes=(pK,))
                        for kc in range(kc_lo, qa + 1):
                            j = kc - kc_lo
                            jp = j % 2
                            p.emit("tensor", lambda e, j=j, jp=jp, pb_=pb_: e.transpose(
                                ps_t2[:, jp, 0:128], pb_[:, j * 128:(j + 1) * 128],
                                cx.iden[:]), reads=(pK, "cx_iden"),
                                writes=(("l0f_pst2", jp),))
                            p.emit("vector", lambda e, jp=jp: e.tensor_copy(
                                ptev[jp][:], ps_t2[:, jp, 0:128]),
                                reads=(("l0f_pst2", jp),), writes=(f"l0f_ptev{jp}",))
                            p.emit("tensor", lambda e, kc=kc, j=j, jp=jp, nkc=nkc, pr=pr: e.matmul(
                                ps_o[0:64, pr, 0:128], vt_sb[:, kc, :], ptev[jp][:],
                                start=(j == 0), stop=(j == nkc - 1)),
                                reads=("l0f_vt", f"l0f_ptev{jp}"), writes=(("l0f_pso", pr),))
                        p.emit("vector", lambda e, qa=qa, hbp=hbp, hcc=hcc, pr=pr: e.tensor_copy(
                            oatt[hbp:hbp + 64, hcc, qa * 128:(qa + 1) * 128],
                            ps_o[0:64, pr, 0:128]),
                            reads=(("l0f_pso", pr),), writes=("l0f_oatt",))
            p.pop_scope()
            p.push_scope()
            wob = p.sb("l0f_wo", [128, 4, 1024], BF16)
            wst2 = p.sb("l0f_wst2", [128, 2, 1024], F32)
            tmp = p.sb("l0f_tmp", [128, 512], F32)
            ps_y = p.ps("l0f_psy", [128, 2, 512], F32)
            load_weight_bf16(p, wst2, "l0f_wst2", wob[:], "l0f_wo",
                             W["wo"][li], 4, 1024)
            for sl in range(4):
                for oc in range(8):
                    bnk = oc % 2
                    for k in range(4):
                        p.emit("tensor", lambda e, sl=sl, oc=oc, k=k, bnk=bnk: e.matmul(
                            ps_y[:, bnk, :], wob[:, k, oc * 128:(oc + 1) * 128],
                            oatt[:, k, sl * 512:(sl + 1) * 512],
                            start=(k == 0), stop=(k == 3)),
                            reads=("l0f_wo", "l0f_oatt"), writes=(("l0f_psy", bnk),))
                    p.emit("vector", lambda e, sl=sl, oc=oc, bnk=bnk: e.tensor_tensor(
                        xr[:, oc, sl * 512:(sl + 1) * 512],
                        xr[:, oc, sl * 512:(sl + 1) * 512],
                        ps_y[:, bnk, :], op=ALU.add),
                        reads=("l0f_x", ("l0f_psy", bnk)), writes=("l0f_x",))
            p.pop_scope()
            p.pop_scope()

            p.push_scope()
            hsh = p.sb("l0f_hsh", [128, 8, 512], F32)
            hnb = p.sb("l0f_hnb", [128, 8, 512], BF16)
            h3 = p.sb("l0f_h3", [128, 22, 512], BF16)
            lgb = p.sb("l0f_lgb", [128, 16], F32)
            b1c = p.sb("l0f_b1c", [128, 44], F32)
            b2c = p.sb("l0f_b2c", [128, 8], F32)
            w1g = [p.sb(f"l0f_w1g{i}", [128, 8, 512], BF16) for i in range(2)]
            gel = [p.sb(f"l0f_gel{i}", [128, 512], BF16) for i in range(2)]
            w2g = [p.sb(f"l0f_w2g{i}", [128, 22, 128], BF16) for i in range(2)]
            ps_f = p.ps("l0f_psf", [128, 4, 512], F32)
            ps_f2 = p.ps("l0f_psf2", [128, 2, 512], F32)
            p.dma("sync", lgb[:], W["lnb"][li], writes=("l0f_lgb",))
            p.dma("sync", b1c[:], W["b1p"][li], writes=("l0f_b1c",))
            p.dma("sync", b2c[:], W["b2p"][li], writes=("l0f_b2c",))
            for sl in range(4):
                lo = sl * 512
                p.emit("vector", lambda e, lo=lo: e.tensor_copy(
                    hsh[:, 0:4, :], xr[:, 0:4, lo:lo + 512]),
                    reads=("l0f_x",), writes=("l0f_hsh",))
                if lo == 0:
                    p.emit("vector", lambda e: e.memset(
                        hsh[:, 4:8, 0:1], 0.0), writes=("l0f_hsh",))
                    p.emit("vector", lambda e: e.tensor_copy(
                        hsh[:, 4:8, 1:512], xr[:, 4:8, 0:511]),
                        reads=("l0f_x",), writes=("l0f_hsh",))
                else:
                    p.emit("vector", lambda e, lo=lo: e.tensor_copy(
                        hsh[:, 4:8, 0:512], xr[:, 4:8, lo - 1:lo + 511]),
                        reads=("l0f_x",), writes=("l0f_hsh",))
                cx.ln_fm(hsh[:], "l0f_hsh", hsh[:], "l0f_hsh", "l0ff")
                for oc in range(8):
                    p.emit("scalar", lambda e, oc=oc: e.activation(
                        out=hnb[:, oc, :], in_=hsh[:, oc, :], func=AF.Identity,
                        scale=lgb[:, oc:oc + 1], bias=lgb[:, 8 + oc:9 + oc]),
                        reads=("l0f_hsh", "l0f_lgb"), writes=("l0f_hnb",))
                for g in range(11):
                    bi = g % 2
                    kw1 = f"l0f_w1g{bi}"
                    for half, off in ((0, 0), (1, 2816)):
                        p.dma("gpsimd", w1g[bi][:, :, half * 256:(half + 1) * 256],
                              W["w1pad"][li][:, off + g * 256:off + (g + 1) * 256]
                              .rearrange("(k p) n -> p k n", p=128),
                              reads=(kw1,), writes=(kw1,))
                    for jj in range(2):
                        j = g * 2 + jj
                        ba, bg = jj * 2, jj * 2 + 1
                        for k in range(8):
                            p.emit("tensor", lambda e, k=k, bi=bi, jj=jj, ba=ba: e.matmul(
                                ps_f[:, ba, :], w1g[bi][:, k, jj * 128:(jj + 1) * 128],
                                hnb[:, k, :], start=(k == 0), stop=(k == 7)),
                                reads=(kw1, "l0f_hnb"), writes=(("l0f_psf", ba),))
                        for k in range(8):
                            p.emit("tensor", lambda e, k=k, bi=bi, jj=jj, bg=bg: e.matmul(
                                ps_f[:, bg, :], w1g[bi][:, k, 256 + jj * 128:256 + (jj + 1) * 128],
                                hnb[:, k, :], start=(k == 0), stop=(k == 7)),
                                reads=(kw1, "l0f_hnb"), writes=(("l0f_psf", bg),))
                        p.emit("scalar", lambda e, j=j, jj=jj, bg=bg: e.activation(
                            out=gel[jj][:], in_=ps_f[:, bg, :], func=AF.Gelu,
                            bias=b1c[:, 22 + j:23 + j]),
                            reads=(("l0f_psf", bg), "l0f_b1c"), writes=(f"l0f_gel{jj}",))
                        p.emit("vector", lambda e, j=j, jj=jj, ba=ba: e.scalar_tensor_tensor(
                            h3[:, j, :], ps_f[:, ba, :], b1c[:, j:j + 1],
                            gel[jj][:], op0=ALU.add, op1=ALU.mult),
                            reads=(("l0f_psf", ba), "l0f_b1c", f"l0f_gel{jj}"),
                            writes=("l0f_h3",))
                for oc in range(8):
                    bi = oc % 2
                    kw2 = f"l0f_w2g{bi}"
                    p.dma("gpsimd", w2g[bi][:],
                          W["w2pad"][li][:, oc * 128:(oc + 1) * 128]
                          .rearrange("(k p) n -> p k n", p=128),
                          reads=(kw2,), writes=(kw2,))
                    for j in range(22):
                        p.emit("tensor", lambda e, j=j, bi=bi: e.matmul(
                            ps_f2[:, bi, :], w2g[bi][:, j, :], h3[:, j, :],
                            start=(j == 0), stop=(j == 21)),
                            reads=(kw2, "l0f_h3"), writes=(("l0f_psf2", bi),))
                    p.emit("vector", lambda e, oc=oc, lo=lo, bi=bi: e.scalar_tensor_tensor(
                        xr[:, oc, lo:lo + 512], ps_f2[:, bi, :], b2c[:, oc:oc + 1],
                        xr[:, oc, lo:lo + 512], op0=ALU.add, op1=ALU.add),
                        reads=(("l0f_psf2", bi), "l0f_b2c", "l0f_x"),
                        writes=("l0f_x",))
            p.pop_scope()

        p.push_scope()
        xt9 = p.sb("l0f_xt9", [128, 8, 512], F32)
        yn9 = p.sb("l0f_yn9", [128, 8, 512], F32)
        for ts in range(4):
            t = b * 4 + ts
            p.dma("sync", xt9[:], xin_t[t], reads=("l0f_xt9",), writes=("l0f_xt9",))
            cx.ln_fm(xr[:, :, ts * 512:(ts + 1) * 512], "l0f_x", yn9[:],
                     "l0f_yn9", "l0fo")
            p.emit("vector", lambda e: e.tensor_tensor(
                yn9[:], yn9[:], xt9[:], op=ALU.add),
                reads=("l0f_yn9", "l0f_xt9"), writes=("l0f_yn9",))
            p.dma("sync", xout_t[t], yn9[:], reads=("l0f_yn9",), writes=("l0f_st",))
        p.pop_scope()
        p.pop_scope()
    p.pop_scope()


def build_l0_ffn(consts):
    p = Prog()
    xin = p.dram_in("xin", [1024, TCORE])
    xout = p.dram_out("xout", [1024, TCORE])
    c_aps = consts_aps(p, consts)
    cx = Ctx(p, c_aps)
    W = {
        "wqkv": p.dram_in("w_wqkv", [2, 1024, 1536]),
        "qksc": p.dram_in("w_qksc", [2, 128, 2]),
        "wo": p.dram_in("w_wo", [2, 512, 1024]),
        "lnb": p.dram_in("w_lnb", [2, 128, 16]),
        "w1pad": p.dram_in("w_w1pad", [2, 1024, 5632]),
        "b1p": p.dram_in("w_b1p", [2, 128, 44]),
        "w2pad": p.dram_in("w_w2pad", [2, 2816, 1024]),
        "b2p": p.dram_in("w_b2p", [2, 128, 8]),
    }
    emit_l0_ffn(p, cx, xin, xout, W, c_aps)
    nc = p.finalize()
    return nc, p


def l0_ffn_in_map(p, consts, inputs):
    g = lambda n: np.asarray(inputs[n], np.float32)
    w1 = g("L0_ffn_w1")
    w1pad = np.zeros((2, 1024, 5632), np.float32)
    w1pad[:, :, 0:2730] = w1[:, :, 0:2730]
    w1pad[:, :, 2816:2816 + 2730] = w1[:, :, 2730:5460]
    b1 = g("L0_ffn_b1")
    b1p = np.zeros((2, 44, 128), np.float32)
    b1p[:, 0:22, :] = np.pad(b1[:, 0:2730], ((0, 0), (0, 86))).reshape(2, 22, 128)
    b1p[:, 22:44, :] = np.pad(b1[:, 2730:5460], ((0, 0), (0, 86))).reshape(2, 22, 128)
    b1p = b1p.transpose(0, 2, 1)
    b1pc = np.zeros((2, 128, 44), np.float32)
    for li in range(2):
        fa = np.pad(b1[li, 0:2730], (0, 86)).reshape(22, 128)
        fg = np.pad(b1[li, 2730:5460], (0, 86)).reshape(22, 128)
        b1pc[li, :, 0:22] = fa.T
        b1pc[li, :, 22:44] = fg.T
    w2 = g("L0_ffn_w2")
    w2pad = np.zeros((2, 2816, 1024), np.float32)
    w2pad[:, 0:2730, :] = w2
    qs = g("L0_ffn_q_scale")
    ks = g("L0_ffn_k_scale")
    qksc = np.zeros((2, 128, 2), np.float32)
    for li in range(2):
        qksc[li, :, 0] = np.tile(qs[li], 2)
        qksc[li, :, 1] = np.tile(ks[li], 2)
    lg = g("L0_ffn_ln_g"); lb = g("L0_ffn_ln_b")
    lnb = np.zeros((2, 128, 16), np.float32)
    for li in range(2):
        lnb[li, :, 0:8] = lg[li].reshape(8, 128).T
        lnb[li, :, 8:16] = lb[li].reshape(8, 128).T
    b2p = np.stack([pc(g("L0_ffn_b2")[li]) for li in range(2)])
    return {
        "c_fvals_in": p.fconst_host_array(),
        "w_wqkv": np.ascontiguousarray(g("L0_ffn_wqkv")),
        "w_qksc": qksc,
        "w_wo": np.ascontiguousarray(g("L0_ffn_wo")),
        "w_lnb": lnb,
        "w_w1pad": w1pad,
        "w_b1p": b1pc,
        "w_w2pad": w2pad,
        "w_b2p": b2p,
        **consts_in_map(consts),
    }


def emit_l0_attn(p: Prog, cx: Ctx, xin, xout, W, C):
    TC = 256
    NTC = SEQ // TC
    p.push_scope()
    lnwb = p.sb("l0a_lnwb", [128, 16], F32)
    ae = p.sb("l0a_ae", [128, 16, 16], F32)
    dcv = p.sb("l0a_dcv", [128, 112], F32)
    xpT = p.sb("l0a_xpT", [128, 16, 128], BF16)
    dtpT = p.sb("l0a_dtpT", [64, 2048], BF16)
    stage = p.sb("l0a_stage", [128, 1, 1024], F32)
    opT = p.sb("l0a_opT", [128, 16, 1024], BF16)
    bdram = p.dram_tmp("l0a_bd", [32, TC], BF16)
    p.dma("sync", lnwb[:], W["lnwb_pc"], writes=("l0a_lnwb",))
    p.dma("sync", ae[:].rearrange("p a b -> p (a b)"), W["aexp"], writes=("l0a_ae",))
    p.dma("sync", dcv[:], W["dcv_pc"], writes=("l0a_dcv",))
    load_weight_bf16(p, stage, "l0a_stage", xpT[:], "l0a_xpT", W["xpT"], 16, 128)
    p.dma("sync", stage[0:64, 0, 0:1024], W["dtpT"][:, 0:1024],
          reads=("l0a_stage",), writes=("l0a_stage",))
    p.emit("vector", lambda e: e.tensor_copy(dtpT[:, 0:1024], stage[0:64, 0, 0:1024]),
           reads=("l0a_stage",), writes=("l0a_dtpT",))
    p.dma("sync", stage[0:64, 0, 0:1024], W["dtpT"][:, 1024:2048],
          reads=("l0a_dtpT",), writes=("l0a_stage",))
    p.emit("vector", lambda e: e.tensor_copy(dtpT[:, 1024:2048], stage[0:64, 0, 0:1024]),
           reads=("l0a_stage",), writes=("l0a_dtpT",))
    load_weight_bf16(p, stage, "l0a_stage", opT[:], "l0a_opT", W["opT"], 16, 1024)
    p.emit("scalar", lambda e: e.activation(
        out=ae[:].rearrange("p a b -> p (a b)"),
        in_=ae[:].rearrange("p a b -> p (a b)"), func=AF.Exp),
        reads=("l0a_ae",), writes=("l0a_ae",))
    xin_t = xin.rearrange("(c p) (n t) -> n p c t", p=128, t=TC)
    xout_t = xout.rearrange("(c p) (n t) -> n p c t", p=128, t=TC)
    p.fconst(1.0)

    for b in range(2):
        p.push_scope()
        hstate = p.sb("l0a_hst", [128, 16, 16], F32)
        halo = p.sb("l0a_halo", [128, 16, 4], F32)
        p.emit("vector", lambda e: e.memset(hstate[:].rearrange("p a b -> p (a b)"), 0.0),
               writes=("l0a_hst",))
        p.emit("vector", lambda e: e.memset(halo[:].rearrange("p a b -> p (a b)"), 0.0),
               writes=("l0a_halo",))
        p.push_scope()
        xt = p.sb("l0a_xt", [128, 8, TC], F32)
        xno = p.sb("l0a_xno", [128, 8, TC], F32)
        xnib = p.sb("l0a_xnib", [128, 8, TC], BF16)
        wio = p.sb("l0a_wio", [128, 8, 256], BF16)
        zsl = p.sb("l0a_z", [128, 16, TC], F32)
        xc_f = p.sb("l0a_xcf", [128, 16, TC], F32)
        dtr = p.sb("l0a_dtr", [64, TC], BF16)
        bc_sb = p.sb("l0a_bcsb", [128, TC], BF16)
        bcrow = p.sb("l0a_bcrow", [1, 16 * TC], BF16)
        onesrb = p.sb("l0a_or", [1, 128], BF16)
        brep = p.sb("l0a_brep", [128, 16, TC], BF16)
        crep = p.sb("l0a_crep", [128, 16, TC], BF16)
        da_t = p.sb("l0a_da", [128, 16, TC], F32)
        u_t = p.sb("l0a_u", [128, 16, TC], BF16)
        h_t = p.sb("l0a_h", [128, 16, TC], F32)
        dtv = p.sb("l0a_dtv", [128, TC], F32)
        dtx = p.sb("l0a_dtx", [128, TC], BF16)
        ysl = p.sb("l0a_y", [128, TC], F32)
        yb = p.sb("l0a_yb", [128, 16, TC], BF16)
        tmp = p.sb("l0a_tmp", [128, TC], F32)
        ps_a = p.ps("l0a_psa", [128, 4, 512], F32)
        ps_b = p.ps("l0a_psb", [128, 2, 512], F32)

        p.emit("vector", lambda e: e.tensor_copy(onesrb[:], cx.ones_row[:]),
               reads=("cx_onesrow",), writes=("l0a_or",))
        for tc in range(NTC):
            t = b * NTC + tc
            p.dma("sync", xt[:], xin_t[t], reads=("l0a_xt",), writes=("l0a_xt",))
            cx.ln_fm(xt[:], "l0a_xt", xno[:], "l0a_xno", "l0ao", Tt=TC)
            lnscr = h_t[:, 0:8, :]
            cx.ln_fm(xno[:], "l0a_xno", lnscr, "l0a_h", "l0ai", Tt=TC)
            for oc in range(8):
                p.emit("scalar", lambda e, oc=oc, lnscr=lnscr: e.activation(
                    out=xnib[:, oc, :], in_=lnscr[:, oc, :], func=AF.Identity,
                    scale=lnwb[:, oc:oc + 1], bias=lnwb[:, 8 + oc:9 + oc]),
                    reads=("l0a_h", "l0a_lnwb"), writes=("l0a_xnib",))
            for half, dst, dkey, dt_ in ((0, None, "l0a_xin_ps", None),
                                         (1, zsl, "l0a_z", None)):
                for og in range(8):
                    load_weight_bf16(p, stage, "l0a_stage", wio[:], "l0a_wio",
                                     W["ipT"][:, half * 2048 + og * 256:
                                              half * 2048 + (og + 1) * 256],
                                     8, 256)
                    for oc4 in range(2):
                        oc = og * 2 + oc4
                        bnk = oc4 % 4
                        for k in range(8):
                            p.emit("tensor", lambda e, oc4=oc4, k=k, bnk=bnk: e.matmul(
                                ps_a[:, bnk, 0:TC], wio[:, k, oc4 * 128:(oc4 + 1) * 128],
                                xnib[:, k, :], start=(k == 0), stop=(k == 7)),
                                reads=("l0a_wio", "l0a_xnib"), writes=(("l0a_psa", bnk),))
                        if half == 1:
                            p.emit("vector", lambda e, oc=oc, bnk=bnk: e.tensor_copy(
                                zsl[:, oc, :], ps_a[:, bnk, 0:TC]),
                                reads=(("l0a_psa", bnk),), writes=("l0a_z",))
                        else:
                            p.emit("vector", lambda e, oc=oc, bnk=bnk: e.tensor_copy(
                                xc_f[:, oc, :], ps_a[:, bnk, 0:TC]),
                                reads=(("l0a_psa", bnk),), writes=("l0a_xcf0",))
                for oc in range(16):
                    if half == 1:
                        continue
                    p.emit("scalar", lambda e, oc=oc: e.activation(
                        out=tmp[:], in_=xc_f[:, oc, :], func=AF.Identity,
                        scale=dcv[:, 48 + oc:49 + oc]),
                        reads=("l0a_xcf0", "l0a_dcv"), writes=("l0a_tmp",))
                    for kk in range(3):
                        sh = 3 - kk
                        p.emit("vector", lambda e, oc=oc, kk=kk, sh=sh: e.scalar_tensor_tensor(
                            tmp[:, 0:sh], halo[:, oc, 4 - sh:4],
                            dcv[:, kk * 16 + oc:kk * 16 + oc + 1], tmp[:, 0:sh],
                            op0=ALU.mult, op1=ALU.add),
                            reads=("l0a_halo", "l0a_dcv", "l0a_tmp"),
                            writes=("l0a_tmp",))
                        p.emit("vector", lambda e, oc=oc, kk=kk, sh=sh: e.scalar_tensor_tensor(
                            tmp[:, sh:TC], xc_f[:, oc, 0:TC - sh],
                            dcv[:, kk * 16 + oc:kk * 16 + oc + 1], tmp[:, sh:TC],
                            op0=ALU.mult, op1=ALU.add),
                            reads=("l0a_xcf0", "l0a_dcv", "l0a_tmp"),
                            writes=("l0a_tmp",))
                    p.emit("vector", lambda e, oc=oc: e.tensor_copy(
                        halo[:, oc, 1:4], xc_f[:, oc, TC - 3:TC]),
                        reads=("l0a_xcf0",), writes=("l0a_halo",))
                    p.emit("scalar", lambda e, oc=oc: e.activation(
                        out=xc_f[:, oc, :], in_=tmp[:], func=AF.Silu,
                        bias=dcv[:, 64 + oc:65 + oc]),
                        reads=("l0a_tmp", "l0a_dcv"), writes=("l0a_xcf",))
                    p.emit("vector", lambda e, oc=oc: e.tensor_copy(
                        yb[:, oc, :], xc_f[:, oc, :]),
                        reads=("l0a_xcf",), writes=("l0a_yb",))
            for k in range(16):
                p.emit("tensor", lambda e, k=k: e.matmul(
                    ps_b[:, 0, 0:TC], xpT[:, k, :], yb[:, k, :],
                    start=(k == 0), stop=(k == 15)),
                    reads=("l0a_xpT", "l0a_yb"), writes=(("l0a_psb", 0),))
            p.emit("vector", lambda e: e.tensor_copy(
                dtr[:], ps_b[0:64, 0, 0:TC]),
                reads=(("l0a_psb", 0),), writes=("l0a_dtr",))
            p.emit("vector", lambda e: e.tensor_copy(
                bc_sb[64:96, 0:TC], ps_b[64:96, 0, 0:TC]),
                reads=(("l0a_psb", 0),), writes=("l0a_bcsb",))
            p.dma("sync", bdram[:], bc_sb[64:96, 0:TC],
                  reads=("l0a_bcsb", "l0a_bd"), writes=("l0a_bd",))
            for part, dstr, dkeyr in ((0, brep, "l0a_brep"), (1, crep, "l0a_crep")):
                p.dma("sync", bcrow[:],
                      bdram[part * 16:(part + 1) * 16, :]
                      .rearrange("a c -> (a c)").unsqueeze(0),
                      reads=("l0a_bd", "l0a_bcrow"), writes=("l0a_bcrow",))
                for g in range((16 * TC) // 512):
                    p.emit("tensor", lambda e, g=g: e.matmul(
                        ps_a[:, g % 2, :], onesrb[:],
                        bcrow[:, g * 512:(g + 1) * 512],
                        start=True, stop=True),
                        reads=("l0a_bcrow", "l0a_or"), writes=(("l0a_psa", g % 2),))
                    p.emit("vector", lambda e, g=g, dstr=dstr, dkeyr=dkeyr: e.tensor_copy(
                        dstr[:].rearrange("p a c -> p (a c)")[:, g * 512:(g + 1) * 512],
                        ps_a[:, g % 2, :]),
                        reads=(("l0a_psa", g % 2),), writes=(dkeyr,))
            for dc in range(16):
                p.emit("tensor", lambda e, dc=dc: e.matmul(
                    ps_b[:, 1, 0:TC], dtpT[:, dc * 128:(dc + 1) * 128], dtr[:],
                    start=True, stop=True),
                    reads=("l0a_dtpT", "l0a_dtr"), writes=(("l0a_psb", 1),))
                p.emit("scalar", lambda e, dc=dc: e.activation(
                    out=dtv[:], in_=ps_b[:, 1, 0:TC], func=AF.Identity,
                    bias=dcv[:, 96 + dc:97 + dc]),
                    reads=(("l0a_psb", 1), "l0a_dcv"), writes=("l0a_dtv",))
                p.emit("scalar", lambda e: e.activation(
                    out=tmp[:], in_=dtv[:], func=AF.Abs),
                    reads=("l0a_dtv",), writes=("l0a_tmp",))
                p.emit("scalar", lambda e: e.activation(
                    out=tmp[:], in_=tmp[:], func=AF.Exp, scale=-1.0),
                    reads=("l0a_tmp",), writes=("l0a_tmp",))
                p.emit("scalar", lambda e: e.activation(
                    out=tmp[:], in_=tmp[:], func=AF.Ln, bias=1.0),
                    reads=("l0a_tmp",), writes=("l0a_tmp",))
                p.emit("vector", lambda e: e.tensor_scalar(
                    dtv[:], dtv[:], 0.0, None, op0=ALU.max),
                    reads=("l0a_dtv",), writes=("l0a_dtv",))
                p.emit("vector", lambda e: e.tensor_tensor(
                    dtv[:], dtv[:], tmp[:], op=ALU.add),
                    reads=("l0a_dtv", "l0a_tmp"), writes=("l0a_dtv",))
                p.emit("vector", lambda e, dc=dc: e.tensor_tensor(
                    da_t[:], dtv[:].unsqueeze(1).broadcast_to([128, 16, TC]),
                    ae[:, dc, :].unsqueeze(2).broadcast_to([128, 16, TC]),
                    op=ALU.mult), reads=("l0a_dtv", "l0a_ae"), writes=("l0a_da",))
                p.emit("scalar", lambda e: e.activation(
                    out=da_t[:].rearrange("p a c -> p (a c)"),
                    in_=da_t[:].rearrange("p a c -> p (a c)"), func=AF.Exp,
                    scale=-1.0), reads=("l0a_da",), writes=("l0a_da",))
                p.emit("vector", lambda e, dc=dc: e.tensor_tensor(
                    dtx[:], dtv[:], xc_f[:, dc, :], op=ALU.mult),
                    reads=("l0a_dtv", "l0a_xcf"), writes=("l0a_dtx",))
                p.emit("vector", lambda e: e.tensor_tensor(
                    u_t[:], dtx[:].unsqueeze(1).broadcast_to([128, 16, TC]),
                    brep[:], op=ALU.mult),
                    reads=("l0a_dtx", "l0a_brep"), writes=("l0a_u",))
                for s in range(16):
                    p.emit("vector", lambda e, s=s, dc=dc: e.tensor_tensor_scan(
                        h_t[:, s, :], da_t[:, s, :], u_t[:, s, :],
                        hstate[:, dc, s:s + 1], op0=ALU.mult, op1=ALU.add),
                        reads=("l0a_da", "l0a_u", "l0a_hst"), writes=("l0a_h",))
                    p.emit("vector", lambda e, s=s, dc=dc: e.tensor_copy(
                        hstate[:, dc, s:s + 1], h_t[:, s, TC - 1:TC]),
                        reads=("l0a_h",), writes=("l0a_hst",))
                p.emit("vector", lambda e: e.tensor_tensor(
                    h_t[:], h_t[:], crep[:], op=ALU.mult),
                    reads=("l0a_h", "l0a_crep"), writes=("l0a_h",))
                p.emit("vector", lambda e: e.tensor_reduce(
                    ysl[:].unsqueeze(2),
                    h_t[:].rearrange("p s c -> p c s"), axis=AX.X, op=ALU.add),
                    reads=("l0a_h",), writes=("l0a_y",))
                p.emit("vector", lambda e, dc=dc: e.scalar_tensor_tensor(
                    ysl[:], xc_f[:, dc, :], dcv[:, 64 + 16 + dc:64 + 17 + dc],
                    ysl[:], op0=ALU.mult, op1=ALU.add),
                    reads=("l0a_xcf", "l0a_dcv", "l0a_y"), writes=("l0a_y",))
                p.emit("scalar", lambda e, dc=dc: e.activation(
                    out=tmp[:], in_=zsl[:, dc, :], func=AF.Silu),
                    reads=("l0a_z",), writes=("l0a_tmp",))
                p.emit("vector", lambda e, dc=dc: e.tensor_tensor(
                    yb[:, dc, :], ysl[:], tmp[:], op=ALU.mult),
                    reads=("l0a_y", "l0a_tmp"), writes=("l0a_yb",))
            for oc in range(8):
                bnk = oc % 4
                for k in range(16):
                    p.emit("tensor", lambda e, oc=oc, k=k, bnk=bnk: e.matmul(
                        ps_a[:, bnk, 0:TC], opT[:, k, oc * 128:(oc + 1) * 128],
                        yb[:, k, :], start=(k == 0), stop=(k == 15)),
                        reads=("l0a_opT", "l0a_yb"), writes=(("l0a_psa", bnk),))
                p.emit("vector", lambda e, oc=oc, bnk=bnk: e.tensor_tensor(
                    xno[:, oc, :], xno[:, oc, :], ps_a[:, bnk, 0:TC],
                    op=ALU.add), reads=("l0a_xno", ("l0a_psa", bnk)),
                    writes=("l0a_xno",))
            cx.ln_fm(xno[:], "l0a_xno", xno[:], "l0a_xno", "l0az", Tt=TC)
            p.emit("vector", lambda e: e.tensor_tensor(
                xno[:], xno[:], xt[:], op=ALU.add),
                reads=("l0a_xno", "l0a_xt"), writes=("l0a_xno",))
            p.dma("sync", xout_t[t], xno[:], reads=("l0a_xno",), writes=("l0a_st",))
        p.pop_scope()
        p.pop_scope()
    p.pop_scope()


def build_l0_attn(consts):
    p = Prog()
    xin = p.dram_in("xin", [1024, TCORE])
    xout = p.dram_out("xout", [1024, TCORE])
    c_aps = consts_aps(p, consts)
    cx = Ctx(p, c_aps)
    W = {
        "lnwb_pc": p.dram_in("w_lnwb", [128, 16]),
        "aexp": p.dram_in("w_aexp", [128, 256]),
        "dcv_pc": p.dram_in("w_dcv", [128, 112]),
        "xpT": p.dram_in("w_xpT", [2048, 128]),
        "dtpT": p.dram_in("w_dtpT", [64, 2048]),
        "ipT": p.dram_in("w_ipT", [1024, 4096]),
        "opT": p.dram_in("w_opT", [2048, 1024]),
    }
    emit_l0_attn(p, cx, xin, xout, W, c_aps)
    nc = p.finalize()
    return nc, p


def l0_attn_in_map(p, consts, inputs):
    g = lambda n: np.asarray(inputs[n], np.float32)
    lnwb = np.concatenate([pc(g("L0_attn_ln_w")), pc(g("L0_attn_ln_b"))], 1)
    alog = g("L0_attn_A_log")
    aexp = np.zeros((128, 256), np.float32)
    for dc in range(16):
        aexp[:, dc * 16:(dc + 1) * 16] = alog[dc * 128:(dc + 1) * 128, :]
    convw = g("L0_attn_conv_w")
    dcv2 = np.zeros((128, 112), np.float32)
    for kk in range(4):
        dcv2[:, kk * 16:(kk + 1) * 16] = convw[:, kk].reshape(16, 128).T
    dcv2[:, 64:80] = g("L0_attn_conv_b").reshape(16, 128).T
    dcv2[:, 80:96] = g("L0_attn_D_param").reshape(16, 128).T
    dcv2[:, 96:112] = g("L0_attn_dt_proj_b").reshape(16, 128).T
    xp = g("L0_attn_x_proj_w")
    xpT = np.zeros((2048, 128), np.float32)
    xpT[:, 0:96] = xp.T
    return {
        "c_fvals_in": p.fconst_host_array(),
        "w_lnwb": lnwb,
        "w_aexp": aexp,
        "w_dcv": dcv2,
        "w_xpT": xpT,
        "w_dtpT": np.ascontiguousarray(g("L0_attn_dt_proj_w").T),
        "w_ipT": np.ascontiguousarray(g("L0_attn_in_proj_w").T),
        "w_opT": np.ascontiguousarray(g("L0_attn_out_proj_w").T),
        **consts_in_map(consts),
    }


def emit_l2_attn(p: Prog, cx: Ctx, xin, xout, W, C):
    NCH = 32
    ROWS = 1024
    p.push_scope()
    k_loc = p.dram_tmp("l2a_kloc", [1024, TCORE], BF16)
    v_loc = p.dram_tmp("l2a_vloc", [1024, TCORE], BF16)
    q_loc = p.dram_tmp("l2a_qloc", [1024, TCORE], BF16)
    kv_in = p.dram_tmp("l2a_kvin", [2048, TCORE], BF16)
    kv_out = p.dram_tmp("l2a_kvout", [8 * 2048, TCORE], BF16)
    th_in = p.dram_tmp("l2a_thin", [6, 32])
    th_out = p.dram_tmp("l2a_thout", [48, 32])
    thr_d = p.dram_tmp("l2a_thrd", [32, 1024])
    y_in = p.dram_tmp("l2a_yin", [1024, 1024], BF16)
    y_out = p.dram_tmp("l2a_yout2", [1024, 1024], BF16)
    sc_in = p.dram_tmp("l2a_scin", [1, 2])
    sc_out = p.dram_tmp("l2a_scout", [1, 2])
    f_in = p.dram_tmp("l2a_fin", [1024, TCORE], BF16)
    f_out = p.dram_tmp("l2a_fout", [1024, TCORE], BF16)

    p.push_scope()
    wqb = p.sb("l2a_wq", [128, 8, 1024], BF16)
    wkb = p.sb("l2a_wk", [128, 8, 1024], BF16)
    wvb = p.sb("l2a_wv", [128, 8, 1024], BF16)
    wst = p.sb("l2a_wst", [128, 2, 1024], F32)
    nrmw = p.sb("l2a_nrmw", [128, 16], F32)
    agw = p.sb("l2a_agw", [128, 8, 192], BF16)
    xt = p.sb("l2a_xt", [128, 8, 512], F32)
    xnb = p.sb("l2a_xnb", [128, 8, 512], BF16)
    sil = p.sb("l2a_sil", [128, 8, 512], BF16)
    sq = p.sb("l2a_sq", [128, 8, 512], BF16)
    row = p.sb("l2a_row", [1, 4, 512], F32)
    prod = p.sb("l2a_prod", [128, 8, 512], BF16)
    tht = p.sb("l2a_tht", [1, 6, 32], F32)
    out_b = p.sb("l2a_outb", [128, 8, 512], BF16)
    ps_m = p.ps("l2a_psm", [128, 4, 512], F32)
    load_weight_bf16(p, wst, "l2a_wst", wqb[:], "l2a_wq", W["wqT"], 8, 1024)
    load_weight_bf16(p, wst, "l2a_wst", wkb[:], "l2a_wk", W["wkT"], 8, 1024)
    load_weight_bf16(p, wst, "l2a_wst", wvb[:], "l2a_wv", W["wvT"], 8, 1024)
    p.dma("sync", nrmw[:], W["nrm_pc"], writes=("l2a_nrmw",))
    load_weight_bf16(p, wst, "l2a_wst", agw[:], "l2a_agw", W["agwT"], 8, 192)
    p.emit("vector", lambda e: e.memset(tht[:].rearrange("z a b -> z (a b)"), 0.0),
           writes=("l2a_tht",))
    p.fconst(1e-6)
    p.fconst(0.01)
    xin_t = xin.rearrange("(c p) (n t) -> n p c t", p=128, t=512)
    xout_t = xout.rearrange("(c p) (n t) -> n p c t", p=128, t=512)

    def proj_norm(wb, wkey, nrm_col, dst_dram, t):
        for oc in range(8):
            bnk = oc % 4
            for k in range(8):
                p.emit("tensor", lambda e, oc=oc, k=k, bnk=bnk: e.matmul(
                    ps_m[:, bnk, :], wb[:, k, oc * 128:(oc + 1) * 128],
                    xnb[:, k, :], start=(k == 0), stop=(k == 7)),
                    reads=(wkey, "l2a_xnb"), writes=(("l2a_psm", bnk),))
            p.emit("scalar", lambda e, oc=oc, bnk=bnk: e.activation(
                out=sil[:, oc, :], in_=ps_m[:, bnk, :], func=AF.Silu),
                reads=(("l2a_psm", bnk),), writes=("l2a_sil",))
        if nrm_col is None:
            p.dma("sync", dst_dram.rearrange("(c p) (n t) -> n p c t", p=128, t=512)[t],
                  sil[:], reads=("l2a_sil",), writes=("l2a_dst",))
            return
        p.emit("scalar", lambda e: e.activation(
            out=sq[:], in_=sil[:], func=AF.Square),
            reads=("l2a_sil",), writes=("l2a_sq",))
        cx.part_sum(lambda k: sq[:, k, :], "l2a_sq",
                    cx.ps_st[0:1, 0, :], ("cx_psst", 0), 8)
        p.emit("vector", lambda e: e.tensor_scalar(
            row[:, 0, :], cx.ps_st[0:1, 0, :], 1.0 / 1024.0, None, op0=ALU.mult),
            reads=(("cx_psst", 0),), writes=("l2a_row",))
        p.emit("scalar", lambda e: e.activation(
            out=row[:, 1, :], in_=row[:, 0, :], func=AF.Sqrt, bias=1e-6),
            reads=("l2a_row",), writes=("l2a_row",))
        p.emit("vector", lambda e: e.reciprocal(
            row[:, 2, :], row[:, 1, :]), reads=("l2a_row",), writes=("l2a_row",))
        cx.replicate_row(row[:, 2, :], "l2a_row", cx.ps_st[:, 1, :], ("cx_psst", 1))
        p.emit("vector", lambda e: e.tensor_tensor(
            sil[:], sil[:],
            cx.ps_st[:, 1, :].unsqueeze(1).broadcast_to([128, 8, 512]),
            op=ALU.mult), reads=("l2a_sil", ("cx_psst", 1)), writes=("l2a_sil",))
        for oc in range(8):
            p.emit("scalar", lambda e, oc=oc, nrm_col=nrm_col: e.activation(
                out=out_b[:, oc, :], in_=sil[:, oc, :], func=AF.Identity,
                scale=nrmw[:, nrm_col + oc:nrm_col + oc + 1]),
                reads=("l2a_sil", "l2a_nrmw"), writes=("l2a_outb",))
        p.dma("sync", dst_dram.rearrange("(c p) (n t) -> n p c t", p=128, t=512)[t],
              out_b[:], reads=("l2a_outb",), writes=("l2a_dst",))

    for t in range(NT):
        p.dma("sync", xt[:], xin_t[t], reads=("l2a_xt",), writes=("l2a_xt",))
        cx.ln_fm(xt[:], "l2a_xt", xnb[:], "l2a_xnb", "l2at", sub_inplace=True)
        proj_norm(wkb, "l2a_wk", 8, k_loc, t)
        proj_norm(wvb, "l2a_wv", None, v_loc, t)
        proj_norm(wqb, "l2a_wq", 0, q_loc, t)
        for gi in range(3):
            p.emit("vector", lambda e, gi=gi: e.tensor_tensor(
                prod[:], xnb[:].rearrange("p c (n u) -> p c n u", u=64),
                agw[:, :, gi * 64:(gi + 1) * 64].unsqueeze(2)
                .broadcast_to([128, 8, 8, 64]), op=ALU.mult),
                reads=("l2a_xnb", "l2a_agw"), writes=("l2a_prod",))
            cx.part_sum(lambda k: prod[:, k, :], "l2a_prod",
                        cx.ps_st[0:1, 0, :], ("cx_psst", 0), 8)
            p.emit("vector", lambda e: e.tensor_copy(
                row[:, 0, :], cx.ps_st[0:1, 0, :]),
                reads=(("cx_psst", 0),), writes=("l2a_row",))
            p.emit("vector", lambda e, gi=gi, t=t: e.tensor_reduce(
                tht[0:1, gi * 2 + (t // 4), (t % 4) * 8:(t % 4) * 8 + 8]
                .unsqueeze(2),
                row[:, 0, :].rearrange("z (n u) -> z n u", u=64),
                axis=AX.X, op=ALU.add),
                reads=("l2a_row",), writes=("l2a_tht",))
    p.dma("sync", th_in.rearrange("(z a) b -> z a b", z=1), tht[0:1, :, :],
          reads=("l2a_tht",), writes=("l2a_thin",))
    p.dma("sync", kv_in[0:1024, :], k_loc, reads=("l2a_dst",), writes=("l2a_kvin",))
    p.dma("sync", kv_in[1024:2048, :], v_loc, reads=("l2a_kvin",), writes=("l2a_kvin",))
    p.emit("gpsimd", lambda e: e.collective_compute(
        "AllGather", ALU.bypass, replica_groups=[list(range(NCORES))],
        ins=[kv_in.opt()], outs=[kv_out.opt()]),
        reads=("l2a_kvin",), writes=("l2a_kvout",), cc=True)
    p.emit("gpsimd", lambda e: e.collective_compute(
        "AllGather", ALU.bypass, replica_groups=[list(range(NCORES))],
        ins=[th_in.opt()], outs=[th_out.opt()]),
        reads=("l2a_thin",), writes=("l2a_thout",), cc=True)
    p.pop_scope()

    becol = p.sb("l2a_becol", [128, 64], F32)
    p.push_scope()
    thsb = p.sb("l2a_thsb", [48, 32], F32)
    e64 = p.sb("l2a_e64", [16, 1024], BF16)
    thbf = p.sb("l2a_thbf", [16, 32], BF16)
    gtab = p.sb("l2a_gtab", [1, 3, 32], F32)
    brow = p.sb("l2a_brow", [1, 64], F32)
    ps_g = p.ps("l2a_psg", [128, 2, 512], F32)
    p.dma("sync", e64[:], C["e64"], writes=("l2a_e64",))
    p.dma("sync", thsb[:], th_out, reads=("l2a_thout",), writes=("l2a_thsb",))
    sel2 = p.sb("l2a_sel2", [48, 2], BF16)
    p.dma("sync", sel2[:], C["sel_ae"], writes=("l2a_sel2",))
    thbf48 = p.sb("l2a_thbf48", [48, 32], BF16)
    p.emit("vector", lambda e: e.tensor_copy(thbf48[:], thsb[:]),
           reads=("l2a_thsb",), writes=("l2a_thbf48",))
    p.emit("tensor", lambda e: e.matmul(
        ps_g[0:2, 0, 0:32], sel2[:], thbf48[:], start=True, stop=True),
        reads=("l2a_thbf48", "l2a_sel2"), writes=(("l2a_psg", 0),))
    p.emit("scalar", lambda e: e.activation(
        out=gtab[0:1, 0, :], in_=ps_g[0:1, 0, 0:32], func=AF.Sigmoid,
        scale=1.0 / 16.0), reads=(("l2a_psg", 0),), writes=("l2a_gtab",))
    p.emit("vector", lambda e: e.tensor_scalar(
        gtab[0:1, 0, :], gtab[0:1, 0, :], -1.0, 1.0, op0=ALU.mult, op1=ALU.add),
        reads=("l2a_gtab",), writes=("l2a_gtab",))
    p.emit("vector", lambda e: e.tensor_copy(brow[0:1, 0:32], gtab[0:1, 0, :]),
           reads=("l2a_gtab",), writes=("l2a_brow",))
    sel2b = p.sb("l2a_sel2b", [48, 2], BF16)
    p.dma("sync", sel2b[:], C["sel_ea"], writes=("l2a_sel2b",))
    p.emit("tensor", lambda e: e.matmul(
        ps_g[0:2, 1, 0:32], sel2b[:], thbf48[:], start=True, stop=True),
        reads=("l2a_thbf48", "l2a_sel2b"), writes=(("l2a_psg", 1),))
    p.emit("scalar", lambda e: e.activation(
        out=brow[0:1, 32:64], in_=ps_g[0:1, 1, 0:32], func=AF.Sigmoid,
        scale=1.0 / 16.0), reads=(("l2a_psg", 1),), writes=("l2a_brow",))
    cx.replicate_row(brow[0:1, 0:64], "l2a_brow", ps_g[:, 0, 0:64], ("l2a_psg", 0))
    p.emit("vector", lambda e: e.tensor_copy(becol[:], ps_g[:, 0, 0:64]),
           reads=(("l2a_psg", 0),), writes=("l2a_becol",))
    thsg = p.sb("l2a_thsg", [16, 32], BF16)
    selth = p.sb("l2a_selth", [48, 16], BF16)
    p.dma("sync", selth[:], C["sel_th"], writes=("l2a_selth",))
    p.emit("tensor", lambda e: e.matmul(
        ps_g[0:16, 0, 0:32], selth[:], thbf48[:], start=True, stop=True),
        reads=("l2a_thbf48", "l2a_selth"), writes=(("l2a_psg", 0),))
    p.emit("scalar", lambda e: e.activation(
        out=thsg[:], in_=ps_g[0:16, 0, 0:32], func=AF.Sigmoid),
        reads=(("l2a_psg", 0),), writes=("l2a_thsg",))
    for g2 in range(2):
        p.emit("tensor", lambda e, g2=g2: e.matmul(
            ps_g[0:32, g2, 0:512], thsg[:], e64[:, g2 * 512:(g2 + 1) * 512],
            start=True, stop=True),
            reads=("l2a_thsg", "l2a_e64"), writes=(("l2a_psg", g2),))
    thr_sb = p.sb("l2a_thrsb", [32, 1024], F32)
    for g2 in range(2):
        p.emit("scalar", lambda e, g2=g2: e.activation(
            out=thr_sb[:, g2 * 512:(g2 + 1) * 512], in_=ps_g[0:32, g2, 0:512],
            func=AF.Identity, scale=0.01),
            reads=(("l2a_psg", g2),), writes=("l2a_thrsb",))
    p.dma("sync", thr_d, thr_sb[:], reads=("l2a_thrsb",), writes=("l2a_thrd",))
    p.pop_scope()

    w0m = p.sb("l2a_w0m", [128, 2, 1024], F32)
    w1m = p.sb("l2a_w1m", [128, 8, 256], F32)
    s0m = p.sb("l2a_s0m", [128, 2, 1024], F32)
    s1m = p.sb("l2a_s1m", [128, 8, 256], F32)
    lnm = p.sb("l2a_lnm", [128, 8], F32)
    slnm = p.sb("l2a_slnm", [128, 8], F32)
    dlnp = p.sb("l2a_dlnp", [128, 8], F32)
    w0T = p.sb("l2a_w0T", [128, 8, 256], BF16)
    w1T = p.sb("l2a_w1T", [128, 2, 1024], BF16)
    w1N = p.sb("l2a_w1N", [128, 8, 256], BF16)
    mscr = p.sb("l2a_mscr", [128, 2, 1024], BF16)
    ps_m2 = p.ps("l2a_psm2", [128, 4, 512], F32)
    ps_t3 = p.ps("l2a_pst3", [128, 1024], BF16)
    p.push_scope()
    kc_sb = p.sb("l2a_kc", [128, 8, 1024], BF16)
    vc_sb = p.sb("l2a_vc", [128, 8, 512], BF16)
    trbuf = p.sb("l2a_trbuf", [128, 512], BF16)
    hT_sb = p.sb("l2a_hT", [128, 8, 256], BF16)
    dpT_sb = p.sb("l2a_dpT", [128, 8, 256], BF16)
    h_b = p.sb("l2a_hb", [128, 2, 1024], BF16)
    dsil = p.sb("l2a_dsil", [128, 2, 1024], BF16)
    dy_b = p.sb("l2a_dyb", [128, 8, 1024], BF16)
    y_sb = p.sb("l2a_ysb", [128, 8, 1024], BF16)
    scr = p.sb("l2a_scr", [128, 8, 1024], BF16)
    yhat = scr
    dpf = p.sb("l2a_dpf", [128, 8, 1024], BF16)
    dW0 = p.sb("l2a_dW0", [128, 2, 1024], F32)
    dW1 = p.sb("l2a_dW1", [128, 8, 256], F32)
    rrep = p.sb("l2a_rrep", [128, 1024], BF16)
    srep2 = p.sb("l2a_srep", [128, 1024], BF16)
    rw2 = p.sb("l2a_rw2", [1, 2, 1024], F32)
    ccol = p.sb("l2a_ccol", [128, 2], F32)
    ccbf = p.sb("l2a_ccbf", [128, 2], BF16)
    scs = p.sb("l2a_scs", [1, 2], F32)
    p.fconst(2.0 / (16 * 64 * 1024.0))
    p.fconst(1.0 / 1024.0)
    p.dma("sync", w0m[:], W["w0sh"].rearrange("(c p) n -> p c n", p=128),
          writes=("l2a_w0m",))
    p.dma("sync", w1m[:], W["w1shT"].rearrange("(c p) n -> p c n", p=128),
          writes=("l2a_w1m",))
    p.dma("sync", lnm[:], W["memln_pc"], writes=("l2a_lnm",))
    p.emit("vector", lambda e: e.memset(s0m[:].rearrange("p a b -> p (a b)"), 0.0),
           writes=("l2a_s0m",))
    p.emit("vector", lambda e: e.memset(s1m[:].rearrange("p a b -> p (a b)"), 0.0),
           writes=("l2a_s1m",))
    p.emit("vector", lambda e: e.memset(slnm[:], 0.0), writes=("l2a_slnm",))

    def transpose_into(src_ap, key_src, dst_ap, key_dst, pr, pc_):
        p.emit("tensor", lambda e: e.transpose(
            ps_t3[0:pc_, 0:pr], src_ap, cx.iden[0:pr, 0:pr]),
            reads=(key_src, "cx_iden"), writes=("l2a_pst3",))
        p.emit("vector", lambda e: e.tensor_copy(dst_ap, ps_t3[0:pc_, 0:pr]),
            reads=("l2a_pst3",), writes=(key_dst,))

    def refresh_mirrors():
        p.emit("vector", lambda e: e.tensor_copy(mscr[:], w0m[:]),
               reads=("l2a_w0m",), writes=("l2a_mscr",))
        for ic in range(2):
            for dkc in range(8):
                transpose_into(mscr[:, ic, dkc * 128:(dkc + 1) * 128], "l2a_mscr",
                               w0T[:, dkc, ic * 128:(ic + 1) * 128], "l2a_w0T",
                               128, 128)
        p.emit("vector", lambda e: e.tensor_copy(
            mscr[:].rearrange("p a b -> p (a b)")[:, 0:2048],
            w1m[:].rearrange("p a b -> p (a b)")),
            reads=("l2a_w1m",), writes=("l2a_mscr",))
        p.emit("vector", lambda e: e.tensor_copy(
            w1N[:].rearrange("p a b -> p (a b)"),
            mscr[:].rearrange("p a b -> p (a b)")[:, 0:2048]),
            reads=("l2a_mscr",), writes=("l2a_w1N",))
        mv = mscr[:].rearrange("p a b -> p (a b)").rearrange(
            "p (a b) -> p a b", a=8)
        for oc in range(8):
            for ic in range(2):
                transpose_into(mv[:, oc, ic * 128:(ic + 1) * 128], "l2a_mscr",
                               w1T[:, ic, oc * 128:(oc + 1) * 128], "l2a_w1T",
                               128, 128)

    kv_v = kv_out.rearrange("(r x p) (bl n) -> p x r bl n", p=128, x=16, bl=2)
    for c in range(NCH):
        refresh_mirrors()
        for r8 in range(8):
            for b2 in range(2):
                p.dma("sync",
                      kc_sb[:, :, r8 * 128 + b2 * 64:r8 * 128 + b2 * 64 + 64],
                      kv_v[:, 0:8, r8, b2, c * 64:(c + 1) * 64],
                      reads=("l2a_kc",), writes=("l2a_kc",))

        p.dma("sync", rw2[0:1, 0, 0:1024], thr_d[c:c + 1, :],
              reads=("l2a_rw2",), writes=("l2a_rw2",))
        for ic in range(2):
            for rc2 in range(2):
                bnk = ic * 2 + rc2
                for dk in range(8):
                    p.emit("tensor", lambda e, ic=ic, rc2=rc2, dk=dk, bnk=bnk: e.matmul(
                        ps_m2[:, bnk, :], w0T[:, dk, ic * 128:(ic + 1) * 128],
                        kc_sb[:, dk, rc2 * 512:(rc2 + 1) * 512],
                        start=(dk == 0), stop=(dk == 7)),
                        reads=("l2a_w0T", "l2a_kc"), writes=(("l2a_psm2", bnk),))
                p.emit("scalar", lambda e, ic=ic, rc2=rc2, bnk=bnk: e.activation(
                    out=h_b[:, ic, rc2 * 512:(rc2 + 1) * 512],
                    in_=ps_m2[:, bnk, :], func=AF.Silu),
                    reads=(("l2a_psm2", bnk),), writes=("l2a_hb",))
                p.emit("scalar", lambda e, ic=ic, rc2=rc2, bnk=bnk: e.activation(
                    out=dsil[:, ic, rc2 * 512:(rc2 + 1) * 512],
                    in_=ps_m2[:, bnk, :], func=AF.Derivative_silu),
                    reads=(("l2a_psm2", bnk),), writes=("l2a_dsil",))
        for oc in range(8):
            for rc2 in range(2):
                bnk = (oc * 2 + rc2) % 4
                for ic in range(2):
                    p.emit("tensor", lambda e, oc=oc, rc2=rc2, ic=ic, bnk=bnk: e.matmul(
                        ps_m2[:, bnk, :], w1T[:, ic, oc * 128:(oc + 1) * 128],
                        h_b[:, ic, rc2 * 512:(rc2 + 1) * 512],
                        start=(ic == 0), stop=(ic == 1)),
                        reads=("l2a_w1T", "l2a_hb"), writes=(("l2a_psm2", bnk),))
                p.emit("vector", lambda e, oc=oc, rc2=rc2, bnk=bnk: e.tensor_copy(
                    y_sb[:, oc, rc2 * 512:(rc2 + 1) * 512], ps_m2[:, bnk, :]),
                    reads=(("l2a_psm2", bnk),), writes=("l2a_ysb",))
        p.dma("sync", y_in.rearrange("(a p) n -> p a n", p=128), y_sb[:],
              reads=("l2a_ysb",), writes=("l2a_yin",))
        p.emit("gpsimd", lambda e: e.collective_compute(
            "AllReduce", ALU.add, replica_groups=[list(range(NCORES))],
            ins=[y_in.opt()], outs=[y_out.opt()]),
            reads=("l2a_yin",), writes=("l2a_yout2",), cc=True)
        p.dma("sync", y_sb[:], y_out.rearrange("(a p) n -> p a n", p=128),
              reads=("l2a_ysb", "l2a_yout2"), writes=("l2a_ysb",))
        p.emit("scalar", lambda e: e.activation(
            out=scr[:], in_=y_sb[:], func=AF.Square),
            reads=("l2a_ysb",), writes=("l2a_scr",))
        for g2 in range(2):
            cx.part_sum(lambda k, g2=g2: scr[:, k, g2 * 512:(g2 + 1) * 512],
                        "l2a_scr", cx.ps_st[0:1, 0, :], ("cx_psst", 0), 8)
            p.emit("vector", lambda e, g2=g2: e.tensor_scalar(
                rw2[:, 0, g2 * 512:(g2 + 1) * 512], cx.ps_st[0:1, 0, :],
                1.0 / 1024.0, None, op0=ALU.mult),
                reads=(("cx_psst", 0),), writes=("l2a_rw2",))
        p.emit("scalar", lambda e: e.activation(
            out=rw2[:, 1, :], in_=rw2[:, 0, :], func=AF.Sqrt, bias=1e-6),
            reads=("l2a_rw2",), writes=("l2a_rw2",))
        p.emit("vector", lambda e: e.reciprocal(
            rw2[:, 1, :], rw2[:, 1, :]), reads=("l2a_rw2",), writes=("l2a_rw2",))
        for g2 in range(2):
            cx.replicate_row(rw2[0:1, 1, g2 * 512:(g2 + 1) * 512], "l2a_rw2",
                             ps_m2[:, g2, :], ("l2a_psm2", g2))
            p.emit("vector", lambda e, g2=g2: e.tensor_copy(
                rrep[:, g2 * 512:(g2 + 1) * 512], ps_m2[:, g2, :]),
                reads=(("l2a_psm2", g2),), writes=("l2a_rrep",))
        p.emit("vector", lambda e: e.tensor_tensor(
            scr[:], y_sb[:], rrep[:].unsqueeze(1).broadcast_to([128, 8, 1024]),
            op=ALU.mult), reads=("l2a_ysb", "l2a_rrep"), writes=("l2a_scr",))
        for oc in range(8):
            p.emit("scalar", lambda e, oc=oc: e.activation(
                out=dpf[:, oc, :], in_=scr[:, oc, :], func=AF.Identity,
                scale=lnm[:, oc:oc + 1]),
                reads=("l2a_scr", "l2a_lnm"), writes=("l2a_dpf",))
        p.emit("vector", lambda e: e.tensor_tensor(
            dpf[:], dpf[:], kc_sb[:], op=ALU.add),
            reads=("l2a_dpf", "l2a_kc"), writes=("l2a_dpf",))
        for vh2 in range(2):
            for r4 in range(4):
                for b2 in range(2):
                    p.dma("sync",
                          vc_sb[:, :, r4 * 128 + b2 * 64:r4 * 128 + b2 * 64 + 64],
                          kv_v[:, 8:16, vh2 * 4 + r4, b2, c * 64:(c + 1) * 64],
                          reads=("l2a_vc",), writes=("l2a_vc",))
            p.emit("vector", lambda e, vh2=vh2: e.tensor_tensor(
                dpf[:, :, vh2 * 512:(vh2 + 1) * 512],
                dpf[:, :, vh2 * 512:(vh2 + 1) * 512], vc_sb[:],
                op=ALU.subtract),
                reads=("l2a_dpf", "l2a_vc"), writes=("l2a_dpf",))
        for g2 in range(2):
            cx.replicate_row(rw2[0:1, 0, g2 * 512:(g2 + 1) * 512], "l2a_rw2",
                             ps_m2[:, g2, :], ("l2a_psm2", g2))
            p.emit("vector", lambda e, g2=g2: e.tensor_tensor(
                dpf[:, :, g2 * 512:(g2 + 1) * 512],
                dpf[:, :, g2 * 512:(g2 + 1) * 512],
                ps_m2[:, g2, :].unsqueeze(1).broadcast_to([128, 8, 512]),
                op=ALU.mult),
                reads=("l2a_dpf", ("l2a_psm2", g2)), writes=("l2a_dpf",))
        p.emit("vector", lambda e: e.tensor_scalar(
            dpf[:], dpf[:], 2.0 / (16 * 64 * 1024.0), None, op0=ALU.mult),
            reads=("l2a_dpf",), writes=("l2a_dpf",))
        p.emit("vector", lambda e: e.tensor_tensor(
            scr[:], dpf[:], scr[:], op=ALU.mult),
            reads=("l2a_dpf", "l2a_scr"), writes=("l2a_scr",))
        p.emit("vector", lambda e: e.tensor_reduce(
            dlnp[:].unsqueeze(2), scr[:], axis=AX.X, op=ALU.add),
            reads=("l2a_scr",), writes=("l2a_dlnp",))
        for oc in range(8):
            p.emit("scalar", lambda e, oc=oc: e.activation(
                out=scr[:, oc, :], in_=dpf[:, oc, :], func=AF.Identity,
                scale=lnm[:, oc:oc + 1]),
                reads=("l2a_dpf", "l2a_lnm"), writes=("l2a_scr",))
        p.emit("vector", lambda e: e.tensor_tensor(
            scr[:], scr[:], y_sb[:], op=ALU.mult),
            reads=("l2a_scr", "l2a_ysb"), writes=("l2a_scr",))
        for g2 in range(2):
            cx.part_sum(lambda k, g2=g2: scr[:, k, g2 * 512:(g2 + 1) * 512],
                        "l2a_scr", cx.ps_st[0:1, 0, :], ("cx_psst", 0), 8)
            p.emit("vector", lambda e, g2=g2: e.tensor_scalar(
                rw2[:, 0, g2 * 512:(g2 + 1) * 512], cx.ps_st[0:1, 0, :],
                1.0 / 1024.0, None, op0=ALU.mult),
                reads=(("cx_psst", 0),), writes=("l2a_rw2",))
        for _ in range(3):
            p.emit("vector", lambda e: e.tensor_tensor(
                rw2[:, 0, :], rw2[:, 0, :], rw2[:, 1, :], op=ALU.mult),
                reads=("l2a_rw2",), writes=("l2a_rw2",))
        for g2 in range(2):
            cx.replicate_row(rw2[0:1, 0, g2 * 512:(g2 + 1) * 512], "l2a_rw2",
                             ps_m2[:, g2, :], ("l2a_psm2", g2))
            p.emit("vector", lambda e, g2=g2: e.tensor_copy(
                srep2[:, g2 * 512:(g2 + 1) * 512], ps_m2[:, g2, :]),
                reads=(("l2a_psm2", g2),), writes=("l2a_srep",))
        for oc in range(8):
            p.emit("scalar", lambda e, oc=oc: e.activation(
                out=scr[:, oc, :], in_=dpf[:, oc, :], func=AF.Identity,
                scale=lnm[:, oc:oc + 1]),
                reads=("l2a_dpf", "l2a_lnm"), writes=("l2a_scr",))
        p.emit("vector", lambda e: e.tensor_tensor(
            scr[:], scr[:], rrep[:].unsqueeze(1).broadcast_to([128, 8, 1024]),
            op=ALU.mult), reads=("l2a_scr", "l2a_rrep"), writes=("l2a_scr",))
        p.emit("vector", lambda e: e.tensor_tensor(
            dy_b[:], y_sb[:], srep2[:].unsqueeze(1).broadcast_to([128, 8, 1024]),
            op=ALU.mult), reads=("l2a_ysb", "l2a_srep"), writes=("l2a_dyb",))
        p.emit("vector", lambda e: e.tensor_tensor(
            dy_b[:], scr[:], dy_b[:], op=ALU.subtract),
            reads=("l2a_scr", "l2a_dyb"), writes=("l2a_dyb",))
        for ic in range(2):
            for rc2 in range(2):
                bnk = ic * 2 + rc2
                for oc in range(8):
                    p.emit("tensor", lambda e, ic=ic, rc2=rc2, oc=oc, bnk=bnk: e.matmul(
                        ps_m2[:, bnk, :], w1N[:, oc, ic * 128:(ic + 1) * 128],
                        dy_b[:, oc, rc2 * 512:(rc2 + 1) * 512],
                        start=(oc == 0), stop=(oc == 7)),
                        reads=("l2a_w1N", "l2a_dyb"), writes=(("l2a_psm2", bnk),))
                p.emit("vector", lambda e, ic=ic, rc2=rc2, bnk=bnk: e.tensor_tensor(
                    h_b[:, ic, rc2 * 512:(rc2 + 1) * 512] if False else
                    dsil[:, ic, rc2 * 512:(rc2 + 1) * 512],
                    dsil[:, ic, rc2 * 512:(rc2 + 1) * 512], ps_m2[:, bnk, :],
                    op=ALU.mult),
                    reads=("l2a_dsil", ("l2a_psm2", bnk)), writes=("l2a_dsil",))
        for ic in range(2):
            for rc in range(8):
                transpose_into(h_b[:, ic, rc * 128:(rc + 1) * 128], "l2a_hb",
                               hT_sb[:, rc, ic * 128:(ic + 1) * 128], "l2a_hT",
                               128, 128)
                transpose_into(dsil[:, ic, rc * 128:(rc + 1) * 128], "l2a_dsil",
                               dpT_sb[:, rc, ic * 128:(ic + 1) * 128], "l2a_dpT",
                               128, 128)
        for oc in range(8):
            bnk = oc % 2
            for rc in range(8):
                transpose_into(dy_b[:, oc, rc * 128:(rc + 1) * 128], "l2a_dyb",
                               trbuf[:, 0:128], "l2a_trbuf", 128, 128)
                p.emit("tensor", lambda e, oc=oc, rc=rc, bnk=bnk: e.matmul(
                    ps_m2[:, bnk, 0:256], trbuf[:, 0:128],
                    hT_sb[:, rc, :], start=(rc == 0), stop=(rc == 7)),
                    reads=("l2a_trbuf", "l2a_hT"), writes=(("l2a_psm2", bnk),))
            p.emit("vector", lambda e, oc=oc, bnk=bnk: e.tensor_copy(
                dW1[:, oc, :], ps_m2[:, bnk, 0:256]),
                reads=(("l2a_psm2", bnk),), writes=("l2a_dW1",))
        for d2 in range(2):
            for rc in range(8):
                for q4 in range(4):
                    transpose_into(
                        kc_sb[:, d2 * 4 + q4, rc * 128:(rc + 1) * 128], "l2a_kc",
                        trbuf[:, q4 * 128:(q4 + 1) * 128], "l2a_trbuf", 128, 128)
                for ic in range(2):
                    bnk = 2 + ic
                    p.emit("tensor", lambda e, ic=ic, d2=d2, rc=rc, bnk=bnk: e.matmul(
                        ps_m2[:, bnk, :], dpT_sb[:, rc, ic * 128:(ic + 1) * 128],
                        trbuf[:, 0:512], start=(rc == 0), stop=(rc == 7)),
                        reads=("l2a_dpT", "l2a_trbuf"), writes=(("l2a_psm2", 2 + ic),))
            for ic in range(2):
                p.emit("vector", lambda e, ic=ic, d2=d2: e.tensor_copy(
                    dW0[:, ic, d2 * 512:(d2 + 1) * 512], ps_m2[:, 2 + ic, :]),
                    reads=(("l2a_psm2", 2 + ic),), writes=("l2a_dW0",))
        p.emit("scalar", lambda e: e.activation(
            out=scr[:, 0:2, 0:1024], in_=dW0[:], func=AF.Square),
            reads=("l2a_dW0",), writes=("l2a_scr",))
        p.emit("scalar", lambda e: e.activation(
            out=scr[:, 2, 0:1024],
            in_=dW1[:].rearrange("p a b -> p (a b)")[:, 0:1024],
            func=AF.Square), reads=("l2a_dW1",), writes=("l2a_scr",))
        p.emit("scalar", lambda e: e.activation(
            out=scr[:, 3, 0:1024],
            in_=dW1[:].rearrange("p a b -> p (a b)")[:, 1024:2048],
            func=AF.Square), reads=("l2a_dW1",), writes=("l2a_scr",))
        p.emit("scalar", lambda e: e.activation(
            out=scr[:, 4, 0:8], in_=dlnp[:], func=AF.Square, scale=1.0),
            reads=("l2a_dlnp",), writes=("l2a_scr",))
        p.emit("vector", lambda e: e.tensor_scalar(
            scr[:, 4, 0:8], scr[:, 4, 0:8], 0.125, None, op0=ALU.mult),
            reads=("l2a_scr",), writes=("l2a_scr",))
        p.emit("vector", lambda e: e.memset(scr[:, 4, 8:1024], 0.0),
               writes=("l2a_scr",))
        p.emit("vector", lambda e: e.tensor_reduce(
            ccol[:, 0:1],
            scr[:, 0:5, :].rearrange("p a b -> p (a b)"), axis=AX.X,
            op=ALU.add),
            reads=("l2a_scr",), writes=("l2a_ccol",))
        p.emit("vector", lambda e: e.tensor_copy(ccbf[:, 0:1], ccol[:, 0:1]),
               reads=("l2a_ccol",), writes=("l2a_ccbf",))
        p.emit("tensor", lambda e: e.matmul(
            cx.ps_st[0:1, 0, 0:1], cx.ones_col[:], ccbf[:, 0:1],
            start=True, stop=True),
            reads=("l2a_ccbf", "cx_onescol"), writes=(("cx_psst", 0),))
        p.emit("vector", lambda e: e.tensor_copy(scs[0:1, 0:1], cx.ps_st[0:1, 0, 0:1]),
               reads=(("cx_psst", 0),), writes=("l2a_scs",))
        p.emit("vector", lambda e: e.memset(scs[0:1, 1:2], 0.0),
               writes=("l2a_scs",))
        p.dma("gpsimd", sc_in, scs[:], reads=("l2a_scs",), writes=("l2a_scin",))
        p.emit("gpsimd", lambda e: e.collective_compute(
            "AllReduce", ALU.add, replica_groups=[list(range(NCORES))],
            ins=[sc_in.opt()], outs=[sc_out.opt()]),
            reads=("l2a_scin",), writes=("l2a_scout",), cc=True)
        p.dma("gpsimd", scs[:], sc_out, reads=("l2a_scout",), writes=("l2a_scs",))
        p.emit("scalar", lambda e: e.activation(
            out=scs[0:1, 0:1], in_=scs[0:1, 0:1], func=AF.Sqrt),
            reads=("l2a_scs",), writes=("l2a_scs",))
        p.emit("vector", lambda e: e.tensor_scalar(
            scs[0:1, 0:1], scs[0:1, 0:1], 1e-6, None, op0=ALU.add),
            reads=("l2a_scs",), writes=("l2a_scs",))
        p.emit("vector", lambda e: e.reciprocal(
            scs[0:1, 0:1], scs[0:1, 0:1]), reads=("l2a_scs",), writes=("l2a_scs",))
        p.emit("vector", lambda e: e.tensor_scalar(
            scs[0:1, 0:1], scs[0:1, 0:1], 1.0, None, op0=ALU.min),
            reads=("l2a_scs",), writes=("l2a_scs",))
        p.emit("vector", lambda e: e.tensor_scalar(
            scs[0:1, 1:2], scs[0:1, 0:1], -1.0, None, op0=ALU.mult),
            reads=("l2a_scs",), writes=("l2a_scs",))
        cx.replicate_row(scs[0:1, 0:2], "l2a_scs", cx.ps_st[:, 0, 0:2],
                         ("cx_psst", 0))
        p.emit("vector", lambda e: e.tensor_copy(ccol[:, 0:2], cx.ps_st[:, 0, 0:2]),
               reads=(("cx_psst", 0),), writes=("l2a_ccol",))
        for (sm, smk, gm, gmk, wm, wmk) in (
                (s0m, "l2a_s0m", dW0, "l2a_dW0", w0m, "l2a_w0m"),
                (s1m, "l2a_s1m", dW1, "l2a_dW1", w1m, "l2a_w1m"),
                (slnm, "l2a_slnm", dlnp, "l2a_dlnp", lnm, "l2a_lnm")):
            sv = sm[:].rearrange("p a b -> p (a b)") if sm is not slnm else sm[:]
            gv = gm[:].rearrange("p a b -> p (a b)") if gm is not dlnp else gm[:]
            wv = wm[:].rearrange("p a b -> p (a b)") if wm is not lnm else wm[:]
            p.emit("vector", lambda e, sv=sv, c=c: e.tensor_scalar(
                sv, sv, becol[:, 32 + c:33 + c], None, op0=ALU.mult),
                reads=(smk, "l2a_becol"), writes=(smk,))
            p.emit("vector", lambda e, sv=sv, gv=gv: e.scalar_tensor_tensor(
                sv, gv, ccol[:, 1:2], sv, op0=ALU.mult, op1=ALU.add),
                reads=(smk, gmk, "l2a_ccol"), writes=(smk,))
            p.emit("vector", lambda e, wv=wv, c=c: e.tensor_scalar(
                wv, wv, becol[:, c:c + 1], None, op0=ALU.mult),
                reads=(wmk, "l2a_becol"), writes=(wmk,))
            p.emit("vector", lambda e, wv=wv, sv=sv: e.tensor_tensor(
                wv, wv, sv, op=ALU.add),
                reads=(wmk, smk), writes=(wmk,))

    p.pop_scope()
    p.push_scope()
    kc4 = p.sb("l2a_kc4b", [128, 8, 512], BF16)
    hb4 = p.sb("l2a_hb4b", [128, 2, 512], BF16)
    ysb4 = p.sb("l2a_ysb4", [128, 8, 512], BF16)
    scr4 = p.sb("l2a_scr4", [128, 8, 512], BF16)
    rw24 = p.sb("l2a_rw24", [1, 2, 512], F32)
    refresh_mirrors()
    qv = q_loc.rearrange("(c p) (n t) -> n p c t", p=128, t=512)
    fv_in = f_in.rearrange("(c p) (n t) -> n p c t", p=128, t=512)
    fv_out = f_out.rearrange("(c p) (n t) -> n p c t", p=128, t=512)
    for t in range(NT):
        p.dma("sync", kc4[:], qv[t], reads=("l2a_kc",),
              writes=("l2a_kc",))
        for ic in range(2):
            bnk = ic
            for dk in range(8):
                p.emit("tensor", lambda e, ic=ic, dk=dk, bnk=bnk: e.matmul(
                    ps_m2[:, bnk, :], w0T[:, dk, ic * 128:(ic + 1) * 128],
                    kc4[:, dk, 0:512], start=(dk == 0), stop=(dk == 7)),
                    reads=("l2a_w0T", "l2a_kc"), writes=(("l2a_psm2", bnk),))
            p.emit("scalar", lambda e, ic=ic, bnk=bnk: e.activation(
                out=hb4[:, ic, 0:512], in_=ps_m2[:, bnk, :], func=AF.Silu),
                reads=(("l2a_psm2", bnk),), writes=("l2a_hb",))
        for oc in range(8):
            bnk = oc % 4
            for ic in range(2):
                p.emit("tensor", lambda e, oc=oc, ic=ic, bnk=bnk: e.matmul(
                    ps_m2[:, bnk, :], w1T[:, ic, oc * 128:(oc + 1) * 128],
                    hb4[:, ic, 0:512], start=(ic == 0), stop=(ic == 1)),
                    reads=("l2a_w1T", "l2a_hb"), writes=(("l2a_psm2", bnk),))
            p.emit("vector", lambda e, oc=oc, bnk=bnk: e.tensor_copy(
                ysb4[:, oc, 0:512], ps_m2[:, bnk, :]),
                reads=(("l2a_psm2", bnk),), writes=("l2a_ysb",))
        p.dma("sync", fv_in[t], ysb4[:, :, 0:512], reads=("l2a_ysb",),
              writes=("l2a_fin",))
    p.emit("gpsimd", lambda e: e.collective_compute(
        "AllReduce", ALU.add, replica_groups=[list(range(NCORES))],
        ins=[f_in.opt()], outs=[f_out.opt()]),
        reads=("l2a_fin",), writes=("l2a_fout",), cc=True)
    for t in range(NT):
        p.dma("sync", ysb4[:, :, 0:512], fv_out[t], reads=("l2a_ysb", "l2a_fout"),
              writes=("l2a_ysb",))
        p.dma("sync", kc4[:, :, 0:512], qv[t], reads=("l2a_kc",),
              writes=("l2a_kc",))

        p.emit("scalar", lambda e: e.activation(
            out=scr4[:, :, 0:512], in_=ysb4[:, :, 0:512], func=AF.Square),
            reads=("l2a_ysb",), writes=("l2a_scr",))
        cx.part_sum(lambda k: scr4[:, k, 0:512], "l2a_scr",
                    cx.ps_st[0:1, 0, :], ("cx_psst", 0), 8)
        p.emit("vector", lambda e: e.tensor_scalar(
            rw24[:, 0, 0:512], cx.ps_st[0:1, 0, :], 1.0 / 1024.0, None,
            op0=ALU.mult), reads=(("cx_psst", 0),), writes=("l2a_rw2",))
        p.emit("scalar", lambda e: e.activation(
            out=rw24[:, 1, 0:512], in_=rw24[:, 0, 0:512], func=AF.Sqrt,
            bias=1e-6), reads=("l2a_rw2",), writes=("l2a_rw2",))
        p.emit("vector", lambda e: e.reciprocal(
            rw24[:, 1, 0:512], rw24[:, 1, 0:512]),
            reads=("l2a_rw2",), writes=("l2a_rw2",))
        cx.replicate_row(rw24[0:1, 1, 0:512], "l2a_rw2", ps_m2[:, 0, :],
                         ("l2a_psm2", 0))
        p.emit("vector", lambda e: e.tensor_tensor(
            scr4[:, :, 0:512], ysb4[:, :, 0:512],
            ps_m2[:, 0, :].unsqueeze(1).broadcast_to([128, 8, 512]),
            op=ALU.mult), reads=("l2a_ysb", ("l2a_psm2", 0)),
            writes=("l2a_scr",))
        for oc in range(8):
            p.emit("scalar", lambda e, oc=oc: e.activation(
                out=scr4[:, oc, 0:512], in_=scr4[:, oc, 0:512],
                func=AF.Identity, scale=lnm[:, oc:oc + 1]),
                reads=("l2a_scr", "l2a_lnm"), writes=("l2a_scr",))
        p.emit("vector", lambda e: e.tensor_tensor(
            scr4[:, :, 0:512], scr4[:, :, 0:512], kc4[:, :, 0:512],
            op=ALU.add), reads=("l2a_scr", "l2a_kc"), writes=("l2a_scr",))
        if t == 0:
            xtile = p.sb("l2a_xt3", [128, 8, 512], F32)
            lno = p.sb("l2a_lno", [128, 8, 512], F32)
        p.dma("sync", xtile[:], xin_t[t], reads=("l2a_xt3",), writes=("l2a_xt3",))
        p.emit("vector", lambda e: e.tensor_copy(lno[:], scr4[:, :, 0:512]),
               reads=("l2a_scr",), writes=("l2a_lno",))
        cx.ln_fm(lno[:], "l2a_lno", lno[:], "l2a_lno", "l2az")
        p.emit("vector", lambda e: e.tensor_tensor(
            lno[:], lno[:], xtile[:], op=ALU.add),
            reads=("l2a_lno", "l2a_xt3"), writes=("l2a_lno",))
        p.dma("sync", xout_t[t], lno[:], reads=("l2a_lno",), writes=("l2a_st2",))
    p.pop_scope()
    p.pop_scope()


def build_l2_attn(consts):
    p = Prog()
    xin = p.dram_in("xin", [1024, TCORE])
    xout = p.dram_out("xout", [1024, TCORE])
    c_aps = consts_aps(p, consts)
    cx = Ctx(p, c_aps)
    W = {
        "wqT": p.dram_in("w_wqT", [1024, 1024]),
        "wkT": p.dram_in("w_wkT", [1024, 1024]),
        "wvT": p.dram_in("w_wvT", [1024, 1024]),
        "nrm_pc": p.dram_in("w_nrm_pc", [128, 16]),
        "agwT": p.dram_in("w_agwT", [1024, 192]),
        "w0sh": p.dram_in("w_w0sh", [256, 1024]),
        "w1shT": p.dram_in("w_w1sh", [1024, 256]),
        "memln_pc": p.dram_in("w_memln", [128, 8]),
    }
    emit_l2_attn(p, cx, xin, xout, W, c_aps)
    nc = p.finalize()
    return nc, p


def l2_attn_in_map(p, consts, inputs, core=0):
    g = lambda n: np.asarray(inputs[n], np.float32)
    agw = np.zeros((1024, 192), np.float32)
    for gi, nm in enumerate(("L2_attn_alpha_w", "L2_attn_theta_w",
                             "L2_attn_eta_w")):
        w = g(nm).reshape(64, 1024)
        agw[:, gi * 64:(gi + 1) * 64] = w.T
    sl = slice(core * 256, (core + 1) * 256)
    return {
        "c_fvals_in": p.fconst_host_array(),
        "w_wqT": np.ascontiguousarray(g("L2_attn_wq_w").T),
        "w_wkT": np.ascontiguousarray(g("L2_attn_wk_w").T),
        "w_wvT": np.ascontiguousarray(g("L2_attn_wv_w").T),
        "w_nrm_pc": np.concatenate([pc(g("L2_attn_q_norm_w")),
                                    pc(g("L2_attn_k_norm_w"))], 1),
        "w_agwT": agw,
        "w_w0sh": np.ascontiguousarray(g("L2_attn_mem_w0")[sl, :]),
        "w_w1sh": np.ascontiguousarray(g("L2_attn_mem_w1")[:, sl]),
        "w_memln": pc(g("L2_attn_mem_ln_w")),
        **consts_in_map(consts),
    }


_CHAIN = (
    ("l0_attn", "build_l0_attn", "l0_attn_in_map"),
    ("l0_ffn", "build_l0_ffn", "l0_ffn_in_map"),
    ("l1_attn", "build_l1_attn", "l1_attn_in_map"),
    ("l1_ffn", "build_l1_ffn", "l1_ffn_in_map"),
    ("l2_attn", "build_l2_attn", "l2_attn_in_map"),
    ("l2_ffn", "build_l2_ffn", "l2_ffn_in_map"),
    ("l3_attn", "build_l3_attn", "l3_attn_in_map"),
    ("l3_ffn", "build_l3_ffn", "l3_ffn_in_map"),
)


def _wd_l0_attn(p):
    return {
        "lnwb_pc": p.dram_in("w_lnwb", [128, 16]),
        "aexp": p.dram_in("w_aexp", [128, 256]),
        "dcv_pc": p.dram_in("w_dcv", [128, 112]),
        "xpT": p.dram_in("w_xpT", [2048, 128]),
        "dtpT": p.dram_in("w_dtpT", [64, 2048]),
        "ipT": p.dram_in("w_ipT", [1024, 4096]),
        "opT": p.dram_in("w_opT", [2048, 1024]),
    }


def _wd_l0_ffn(p):
    return {
        "wqkv": p.dram_in("w_wqkv", [2, 1024, 1536]),
        "qksc": p.dram_in("w_qksc", [2, 128, 2]),
        "wo": p.dram_in("w_wo", [2, 512, 1024]),
        "lnb": p.dram_in("w_lnb", [2, 128, 16]),
        "w1pad": p.dram_in("w_w1pad", [2, 1024, 5632]),
        "b1p": p.dram_in("w_b1p", [2, 128, 44]),
        "w2pad": p.dram_in("w_w2pad", [2, 2816, 1024]),
        "b2p": p.dram_in("w_b2p", [2, 128, 8]),
    }


def _wd_l1_attn(p):
    return {
        "WqT": p.dram_in("w_WqT", [1024, 1024]),
        "WkT": p.dram_in("w_WkT", [1024, 1024]),
        "WvT": p.dram_in("w_WvT", [1024, 1024]),
        "WoT": p.dram_in("w_WoT", [1024, 1024]),
        "bqkv_pc": p.dram_in("w_bqkv_pc", [128, 24]),
        "bo_pc": p.dram_in("w_bo_pc", [128, 8]),
    }


def _wd_l1_ffn(p):
    return {
        "Wg": p.dram_in("w_Wg", [1024, 8]),
        "Ws1": p.dram_in("w_Ws1", [1024, 2048]),
        "Ws3": p.dram_in("w_Ws3", [1024, 2048]),
        "Ws2": p.dram_in("w_Ws2", [2048, 1024]),
        "We1": p.dram_in("w_We1", [8, 1024, 2048]),
        "We3": p.dram_in("w_We3", [8, 1024, 2048]),
        "We2": p.dram_in("w_We2", [8, 2048, 1024]),
    }


def _wd_l2_attn(p):
    return {
        "wqT": p.dram_in("w_wqT", [1024, 1024]),
        "wkT": p.dram_in("w_wkT", [1024, 1024]),
        "wvT": p.dram_in("w_wvT", [1024, 1024]),
        "nrm_pc": p.dram_in("w_nrm_pc", [128, 16]),
        "agwT": p.dram_in("w_agwT", [1024, 192]),
        "w0sh": p.dram_in("w_w0sh", [256, 1024]),
        "w1shT": p.dram_in("w_w1sh", [1024, 256]),
        "memln_pc": p.dram_in("w_memln", [128, 8]),
    }


def _wd_l2_ffn(p):
    return {
        "WpT": p.dram_in("w_WpT", [1024, 1024]),
        "bp_pc": p.dram_in("w_bp_pc", [128, 8]),
        "WeT": p.dram_in("w_WeT", [8, 1024, 1024]),
        "be": p.dram_in("w_be", [8, 1024]),
        "VgT": p.dram_in("w_VgT", [1024, 512]),
    }


def _wd_l3_attn(p):
    return {
        "wqT": p.dram_in("w_wqT", [1024, 1024]),
        "wkT": p.dram_in("w_wkT", [1024, 1024]),
        "wvT": p.dram_in("w_wvT", [1024, 1024]),
        "woT": p.dram_in("w_woT", [1024, 1024]),
        "bq_pc": p.dram_in("w_bq_pc", [128, 8]),
        "bkv_pc": p.dram_in("w_bkv_pc", [128, 16]),
        "bo_pc": p.dram_in("w_bo_pc", [128, 8]),
        "sw1T": p.dram_in("w_sw1T", [1024, 512]),
        "sw2T": p.dram_in("w_sw2T", [512, 3]),
        "pw1T": p.dram_in("w_pw1T", [3, 1024, 512]),
        "pw2T": p.dram_in("w_pw2T", [3, 512, 16]),
        "sb_pc": p.dram_in("w_sb_pc", [128, 32]),
    }


def _wd_l3_ffn(p):
    return {
        "WrT": p.dram_in("w_WrT", [1024, 1024]),
        "WiT": p.dram_in("w_WiT", [1024, 1024]),
        "bias_pc": p.dram_in("w_bias_pc", [128, 8]),
    }


_FUSED = (
    ("l0a", _wd_l0_attn, emit_l0_attn, "l0_attn_in_map", False),
    ("l0f", _wd_l0_ffn, emit_l0_ffn, "l0_ffn_in_map", False),
    ("l1a", _wd_l1_attn, emit_l1_attn, "l1_attn_in_map", False),
    ("l1f", _wd_l1_ffn, emit_l1_ffn, "l1_ffn_in_map", False),
    ("l2a", _wd_l2_attn, emit_l2_attn, "l2_attn_in_map", True),
    ("l2f", _wd_l2_ffn, emit_l2_ffn, "l2_ffn_in_map", False),
    ("l3a", _wd_l3_attn, emit_l3_attn, "l3_attn_in_map", False),
    ("l3f", _wd_l3_ffn, emit_l3_ffn, "l3_ffn_in_map", False),
)


def build_fused(consts):
    p = Prog()
    xin = p.dram_in("xin", [1024, TCORE])
    xout = p.dram_out("xout", [1024, TCORE])
    c_aps = consts_aps(p, consts)
    cx = Ctx(p, c_aps)
    acts = [xin] + [p.dram_tmp(f"act{i}", [1024, TCORE]) for i in range(7)] \
        + [xout]
    for i, (pref, wd, emitter, _imf, _percore) in enumerate(_FUSED):
        p._pref = pref + "_"
        W = wd(p)
        p._pref = ""
        emitter(p, cx, acts[i], acts[i + 1], W, c_aps)
    nc = p.finalize()
    return nc, p


def fused_in_maps(p, consts, inputs):
    g = globals()
    base = {"c_fvals_in": p.fconst_host_array(), **consts_in_map(consts)}
    maps = [dict(base) for _ in range(NCORES)]
    for pref, _wd, _em, imf, percore in _FUSED:
        if percore:
            for i in range(NCORES):
                for k, v in g[imf](p, consts, inputs, core=i).items():
                    if k.startswith("w_"):
                        maps[i][pref + "_" + k] = v
        else:
            m = g[imf](p, consts, inputs)
            for k, v in m.items():
                if k.startswith("w_"):
                    for i in range(NCORES):
                        maps[i][pref + "_" + k] = v
    return maps


def kernel(**inputs):
    consts = make_consts()
    x = np.asarray(inputs["L0_attn_x"], dtype=np.float32)
    shards = [fm(x[2 * i:2 * i + 2]) for i in range(NCORES)]
    nc, p = build_fused(consts)
    maps = fused_in_maps(p, consts, inputs)
    in_maps = [dict(maps[i], xin=shards[i]) for i in range(NCORES)]
    res = run_bass_kernel_spmd(nc, in_maps, core_ids=list(range(NCORES)))
    out = np.concatenate([np.ascontiguousarray(r["xout"]).T.reshape(2, 2048, 1024)
                          for r in res.results])
    return out


# version 07 (final)
# speedup vs baseline: 1.2142x; 1.0243x over previous
import numpy as np
import ml_dtypes
from contextlib import ExitStack

import concourse.bass as bass
import concourse.mybir as mybir
from concourse.bass_utils import run_bass_kernel_spmd

F32 = mybir.dt.float32
BF16 = mybir.dt.bfloat16
AF = mybir.ActivationFunctionType
ALU = mybir.AluOpType
AX = mybir.AxisListType
BF = ml_dtypes.bfloat16

ENGINES = ("sync", "tensor", "vector", "scalar", "gpsimd")
NCORES = 8
LN_EPS = 1e-5
TCORE = 4096
NT = TCORE // 512
SEQ = 2048


class Prog:
    MAX_FCONST = 64

    def __init__(self):
        self._pref = ""
        self.nc = bass.Bass("TRN2", target_bir_lowering=False)
        self.stack = ExitStack()
        self.items = {e: [] for e in ENGINES}
        self.last_w = {}
        self.last_r = {}
        self._sb_names = set()
        self._finalized = False
        self.prologue_toks = []
        self._scopes = []
        self._barriers = []
        self._barrier_applied = {e: 0 for e in ENGINES}
        self._fc_vals = {}
        self._fc_tile = self.sb("c_fvals", [128, self.MAX_FCONST], F32)
        self._fc_ap = self.dram_in("c_fvals_in", [128, self.MAX_FCONST], F32)
        t = self.dma("sync", self._fc_tile[:], self._fc_ap, writes=("c_fvals",))
        self.prologue_toks.append(t)

    def fconst(self, value):
        value = float(value)
        if value in self._fc_vals:
            return value
        col = len(self._fc_vals)
        assert col < self.MAX_FCONST
        self._fc_vals[value] = col
        self.nc.const_aps.register(F32, value, self._fc_tile[:, col:col + 1])
        return value

    def fconst_host_array(self):
        a = np.zeros((128, self.MAX_FCONST), dtype=np.float32)
        for v, c in self._fc_vals.items():
            a[:, c] = v
        return a

    def push_scope(self):
        self._scopes.append((ExitStack(), []))

    def pop_scope(self):
        st, names = self._scopes.pop()
        st.close()
        self._barriers.append(self.barrier_token())

    def _enter(self, cm, name):
        if self._scopes:
            st, names = self._scopes[-1]
            names.append(name)
            return st.enter_context(cm)
        return self.stack.enter_context(cm)

    def _uname(self, name):
        if name in self._sb_names:
            i = 2
            while f"{name}__{i}" in self._sb_names:
                i += 1
            name = f"{name}__{i}"
        self._sb_names.add(name)
        return name

    def sb(self, name, shape, dtype):
        name = self._uname(name)
        return self._enter(self.nc.sbuf_tensor(name, shape, dtype), name)

    def ps(self, name, shape, dtype=F32):
        name = self._uname(name)
        return self._enter(self.nc.psum_tensor(name, shape, dtype), name)

    def dram_in(self, name, shape, dtype=F32):
        if name.startswith("w_"):
            name = self._pref + name
        return self.nc.dram_tensor(name, list(shape), dtype, kind="ExternalInput").ap()

    def dram_out(self, name, shape, dtype=F32):
        return self.nc.dram_tensor(name, list(shape), dtype, kind="ExternalOutput").ap()

    def dram_tmp(self, name, shape, dtype=F32):
        return self.nc.dram_tensor(name, list(shape), dtype).ap()

    def dram_shared(self, name, shape, dtype=F32):
        return self.nc.dram_tensor(name, list(shape), dtype,
                                   addr_space="Shared").ap()

    def dma_t(self, out, in_, reads=(), writes=()):
        return self.emit(
            "sync", lambda e: e.dma_start_transpose(out=out, in_=in_),
            reads=reads, writes=writes, dma=True)

    def emit(self, eng, builder, reads=(), writes=(), dma=False, cc=False,
             extra_waits=()):
        idx = len(self.items[eng])
        tok = (("k" if cc else "d") if (dma or cc) else "c", eng, idx)
        wset = set()
        for k in reads:
            t = self.last_w.get(k)
            if t is not None:
                wset.add(t)
        for k in writes:
            t = self.last_w.get(k)
            if t is not None:
                wset.add(t)
            for t in self.last_r.get(k, ()):
                wset.add(t)
        for t in extra_waits:
            if t is not None:
                wset.add(t)
        nb = len(self._barriers)
        if self._barrier_applied[eng] < nb:
            for bi in range(self._barrier_applied[eng], nb):
                wset |= set(self._barriers[bi])
            self._barrier_applied[eng] = nb
        if eng == "tensor":
            wset = {t for t in wset if not (t[0] == "c" and t[1] == "tensor")}
        wset.discard(tok)
        self.items[eng].append({"builder": builder, "waits": wset, "tok": tok})
        for k in reads:
            self.last_r.setdefault(k, []).append(tok)
        for k in writes:
            self.last_w[k] = tok
            self.last_r[k] = []
        return tok

    def dma(self, eng, out, in_, reads=(), writes=(), **kw):
        return self.emit(
            eng, lambda e: e.dma_start(out=out, in_=in_, **kw),
            reads=reads, writes=writes, dma=True)

    def barrier_token(self):
        toks = []
        for e in ENGINES:
            if self.items[e]:
                toks.append(self.items[e][-1]["tok"])
        return toks

    def finalize(self):
        assert not self._finalized
        self._finalized = True
        nc = self.nc
        needed = set()
        for e in ENGINES:
            for it in self.items[e]:
                needed |= it["waits"]
        KSLOT = 12
        sems = {}
        dpool = {}
        for e in ENGINES:
            sems[("c", e)] = self.stack.enter_context(nc.semaphore(f"sc_{e}"))
            if any(it["tok"][0] == "d" for it in self.items[e]):
                dpool[e] = [self.stack.enter_context(nc.semaphore(f"sd_{e}_{i}"))
                            for i in range(KSLOT)]
            if any(it["tok"][0] == "k" for it in self.items[e]):
                sems[("k", e)] = self.stack.enter_context(nc.semaphore(f"sk_{e}"))
        ptoks = set(self.prologue_toks)
        for e in ENGINES:
            for it in self.items[e]:
                if it["tok"] in ptoks:
                    continue
                it["waits"] |= {t for t in self.prologue_toks if t != it["tok"]}
                needed |= it["waits"]
                break
        value_of = {}
        final_dma = {}
        final_cc = {}
        for e in ENGINES:
            ccnt = 0
            kcnt = 0
            dseq = 0
            for it in self.items[e]:
                kind = it["tok"][0]
                if kind == "k":
                    it["inc"] = True
                    kcnt += 1
                    value_of[it["tok"]] = (("k", e), kcnt)
                    final_cc[e] = kcnt
                elif kind == "d":
                    slot = dseq % KSLOT
                    val = 16 * (dseq // KSLOT + 1)
                    it["inc"] = True
                    it["dslot"] = slot
                    it["dval"] = val
                    value_of[it["tok"]] = (("d", e, slot), val)
                    final_dma[(e, slot)] = val
                    dseq += 1
                elif it["tok"] in needed:
                    it["inc"] = True
                    ccnt += 1
                    value_of[it["tok"]] = (("c", e), ccnt)
                else:
                    it["inc"] = False
        for e in ENGINES:
            water = {}
            for it in self.items[e]:
                wl = []
                if it["tok"][0] == "d" and it.get("dval", 0) > 16:
                    sk = ("d", e, it["dslot"])
                    v = it["dval"] - 16
                    if water.get(sk, 0) < v:
                        water[sk] = v
                        wl.append((sk, v))
                for t in sorted(it["waits"], key=lambda t: (t[1], t[0], t[2])):
                    sem_key, v = value_of[t]
                    if water.get(sem_key, 0) < v:
                        water[sem_key] = v
                        wl.append((sem_key, v))
                it["wl"] = wl
        items = self.items

        with nc.Block() as block:
            def mk(e):
                def body(engobj):
                    for it in items[e]:
                        for (sem_key, v) in it["wl"]:
                            s = (dpool[sem_key[1]][sem_key[2]]
                                 if sem_key[0] == "d" else sems[sem_key])
                            engobj.wait_ge(s, v)
                        inst = it["builder"](engobj)
                        if it["inc"]:
                            assert inst is not None, (e, it["tok"])
                            if it["tok"][0] == "d":
                                inst.then_inc(dpool[e][it["dslot"]], 16)
                            elif it["tok"][0] == "k":
                                inst.then_inc(sems[("k", e)], 1)
                            else:
                                inst.then_inc(sems[("c", e)], 1)
                    if e == "sync":
                        for (qe, slot), v in sorted(final_dma.items()):
                            engobj.wait_ge(dpool[qe][slot], v)
                        for qe, v in sorted(final_cc.items()):
                            engobj.wait_ge(sems[("k", qe)], v)
                return body
            block.sync(mk("sync"))
            block.tensor(mk("tensor"))
            block.vector(mk("vector"))
            block.scalar(mk("scalar"))
            block.gpsimd(mk("gpsimd"))
        return nc


class Ctx:
    def __init__(self, p: Prog, c):
        self.p = p
        self.iden = p.sb("cx_iden", [128, 128], BF16)
        self.ones_col = p.sb("cx_onescol", [128, 1], BF16)
        self.ones_row = p.sb("cx_onesrow", [1, 128], F32)
        p.dma("sync", self.iden[:], c["iden_bf16"], writes=("cx_iden",))
        p.dma("sync", self.ones_col[:], c["ones_col"], writes=("cx_onescol",))
        p.dma("sync", self.ones_row[:], c["ones_row"], writes=("cx_onesrow",))
        self.ps_st = p.ps("cx_psst", [128, 2, 512], F32)
        self.bfa = p.sb("cx_bfa", [128, 8, 512], BF16)
        self.bfb = p.sb("cx_bfb", [128, 8, 512], BF16)
        self.row = p.sb("cx_row", [1, 4, 512], F32)
        self.ones8 = p.sb("cx_ones8", [128, 8], F32)
        self._sqf = None
        self._sqf_n = 0
        p.emit("vector", lambda e: e.memset(self.ones8[:], 1.0),
               writes=("cx_ones8",))
        p.fconst(LN_EPS)

    @property
    def sqf(self):
        if self._sqf is None:
            self._sqf_n += 1
            self._sqf = self.p.sb(f"cx_sqf{self._sqf_n}", [128, 8, 512], F32)
        return self._sqf

    def drop_sqf(self):
        self._sqf = None

    def part_sum(self, src_bf_chunks, key_src, ps_view, key_ps, nch):
        p = self.p
        for k in range(nch):
            p.emit("tensor", lambda e, k=k: e.matmul(
                ps_view, self.ones_col[:], src_bf_chunks(k),
                start=(k == 0), stop=(k == nch - 1)),
                reads=(key_src, "cx_onescol"), writes=(key_ps,))

    def replicate_row(self, row_ap, key_row, ps_view, key_ps):
        self.p.emit("tensor", lambda e: e.matmul(
            ps_view, self.ones_row[:], row_ap, start=True, stop=True),
            reads=(key_row, "cx_onesrow"), writes=(key_ps,))

    def part_sum_f(self, src_f_chunks, key_src, ps_view, key_ps, nch):
        p = self.p
        for k in range(nch):
            p.emit("tensor", lambda e, k=k: e.matmul(
                ps_view, self.ones8[:, 0:1], src_f_chunks(k),
                start=(k == 0), stop=(k == nch - 1)),
                reads=(key_src, "cx_ones8"), writes=(key_ps,))

    def ln_fm(self, x_ap, key_x, out_ap, key_out, tag, nch=8, Tt=512, nfeat=1024,
              out2_bf=None, key_out2=None, precise=False, sub_inplace=False):
        p = self.p
        p.fconst(1.0 / nfeat)
        if precise:
            sq = self.sqf
            p.emit("scalar", lambda e: e.activation(
                out=sq[:, 0:nch, 0:Tt], in_=x_ap,
                func=AF.Square), reads=(key_x,), writes=("cx_sqf",))
            self.part_sum_f(lambda k: x_ap[:, k, 0:Tt] if nch > 1 else x_ap,
                            key_x, self.ps_st[0:1, 0, 0:Tt], ("cx_psst", 0), nch)
            self.part_sum_f(lambda k: sq[:, k, 0:Tt], "cx_sqf",
                            self.ps_st[0:1, 1, 0:Tt], ("cx_psst", 1), nch)
        else:
            p.emit("vector", lambda e: e.tensor_copy(
                self.bfa[:, 0:nch, 0:Tt], x_ap), reads=(key_x,), writes=("cx_bfa",))
            p.emit("scalar", lambda e: e.activation(
                out=self.bfb[:, 0:nch, 0:Tt], in_=self.bfa[:, 0:nch, 0:Tt],
                func=AF.Square), reads=("cx_bfa",), writes=("cx_bfb",))
            self.part_sum(lambda k: self.bfa[:, k, 0:Tt], "cx_bfa",
                          self.ps_st[0:1, 0, 0:Tt], ("cx_psst", 0), nch)
            self.part_sum(lambda k: self.bfb[:, k, 0:Tt], "cx_bfb",
                          self.ps_st[0:1, 1, 0:Tt], ("cx_psst", 1), nch)
        p.emit("vector", lambda e: e.tensor_scalar(
            self.row[:, 0, 0:Tt], self.ps_st[0:1, 0, 0:Tt], 1.0 / nfeat, None,
            op0=ALU.mult), reads=(("cx_psst", 0),), writes=("cx_row0",))
        p.emit("vector", lambda e: e.tensor_scalar(
            self.row[:, 1, 0:Tt], self.ps_st[0:1, 1, 0:Tt], 1.0 / nfeat, None,
            op0=ALU.mult), reads=(("cx_psst", 1),), writes=("cx_row1",))
        p.emit("vector", lambda e: e.tensor_tensor(
            self.row[:, 2, 0:Tt], self.row[:, 0, 0:Tt], self.row[:, 0, 0:Tt],
            op=ALU.mult), reads=("cx_row0",), writes=("cx_row2",))
        p.emit("vector", lambda e: e.tensor_tensor(
            self.row[:, 1, 0:Tt], self.row[:, 1, 0:Tt], self.row[:, 2, 0:Tt],
            op=ALU.subtract), reads=("cx_row1", "cx_row2"), writes=("cx_row1",))
        p.emit("scalar", lambda e: e.activation(
            out=self.row[:, 3, 0:Tt], in_=self.row[:, 1, 0:Tt], func=AF.Sqrt,
            bias=LN_EPS), reads=("cx_row1",), writes=("cx_row3",))
        p.emit("vector", lambda e: e.reciprocal(
            self.row[:, 2, 0:Tt], self.row[:, 3, 0:Tt]),
            reads=("cx_row3",), writes=("cx_row2",))
        self.replicate_row(self.row[:, 0, 0:Tt], "cx_row0",
                           self.ps_st[:, 0, 0:Tt], ("cx_psst", 0))
        self.replicate_row(self.row[:, 2, 0:Tt], "cx_row2",
                           self.ps_st[:, 1, 0:Tt], ("cx_psst", 1))
        mu = self.ps_st[:, 0, 0:Tt].unsqueeze(1).broadcast_to([128, nch, Tt])
        rstd = self.ps_st[:, 1, 0:Tt].unsqueeze(1).broadcast_to([128, nch, Tt])
        if sub_inplace:
            p.emit("vector", lambda e: e.tensor_tensor(
                x_ap, x_ap, mu, op=ALU.subtract),
                reads=(key_x, ("cx_psst", 0)), writes=(key_x,))
            p.emit("vector", lambda e: e.tensor_tensor(
                out_ap, x_ap, rstd, op=ALU.mult),
                reads=(key_x, ("cx_psst", 1)), writes=(key_out,))
            return
        p.emit("vector", lambda e: e.tensor_tensor(
            out_ap, x_ap, mu, op=ALU.subtract),
            reads=(key_x, ("cx_psst", 0)), writes=(key_out,))
        p.emit("vector", lambda e: e.tensor_tensor(
            out_ap, out_ap, rstd, op=ALU.mult),
            reads=(key_out, ("cx_psst", 1)), writes=(key_out,))
        if out2_bf is not None:
            p.emit("vector", lambda e: e.tensor_copy(out2_bf, out_ap),
                   reads=(key_out,), writes=(key_out2,))


def load_weight_bf16(p, stage, stage_key, dst_view, dst_key, hbm_ap, nk, width,
                     queue="sync"):
    src = hbm_ap.rearrange("(k p) n -> p k n", p=128)
    p.dma("gpsimd", dst_view[:, 0:nk, 0:width], src,
          reads=(dst_key,), writes=(dst_key,))


def linear_fm(p, xb_chunks, key_x, key_w, ps, key_ps_fn, out_fn, nk, ndo,
              Tt=512, w_view=None):
    nb = ps.shape[1]
    for oc in range(ndo):
        b = oc % nb
        for k in range(nk):
            p.emit("tensor", lambda e, k=k, oc=oc, b=b: e.matmul(
                ps[:, b, 0:Tt], w_view(k, oc), xb_chunks(k),
                start=(k == 0), stop=(k == nk - 1)),
                reads=(key_w, key_x), writes=(key_ps_fn(b),))
        out_fn(oc, ps[:, b, 0:Tt], key_ps_fn(b))


def emit_l3_ffn(p: Prog, cx: Ctx, xin, xout, W, C):
    p.push_scope()
    m_sb = p.sb("l3f_m", [128, 8, 1024], BF16)
    bias_sb = p.sb("l3f_bias", [128, 8], F32)
    p.push_scope()
    wr_bf = p.sb("l3f_wr", [128, 8, 1024], BF16)
    wi_bf = p.sb("l3f_wi", [128, 8, 1024], BF16)
    wld = p.sb("l3f_wld", [128, 8, 1024], F32)
    cs_c = p.sb("l3f_c", [128, 8, 1024], BF16)
    cs_s = p.sb("l3f_s", [128, 8, 1024], BF16)
    cs_sn = p.sb("l3f_sn", [128, 8, 1024], BF16)
    pr_sb = p.sb("l3f_pr", [128, 8, 1024], BF16)
    pi_sb = p.sb("l3f_pi", [128, 8, 1024], BF16)
    ps_m = p.ps("l3f_psm", [128, 4, 512], F32)

    p.dma("sync", cs_c[:], C["dft_c"].rearrange("(k p) n -> p k n", p=128), writes=("l3f_c",))
    p.dma("sync", cs_s[:], C["dft_s"].rearrange("(k p) n -> p k n", p=128), writes=("l3f_s",))
    p.dma("sync", cs_sn[:], C["dft_sneg"].rearrange("(k p) n -> p k n", p=128), writes=("l3f_sn",))
    load_weight_bf16(p, wld, "l3f_wld", wr_bf[:], "l3f_wr", W["WrT"], 8, 1024)
    load_weight_bf16(p, wld, "l3f_wld", wi_bf[:], "l3f_wi", W["WiT"], 8, 1024)
    p.dma("sync", bias_sb[:], W["bias_pc"], writes=("l3f_bias",))

    p.fconst(1.0 / 1024.0)
    for ft in range(8):
        for ot in range(2):
            b = (ft * 2 + ot) % 2
            for c in range(8):
                p.emit("tensor", lambda e, c=c, ft=ft, ot=ot, b=b: e.matmul(
                    ps_m[:, b, :], wr_bf[:, c, ft * 128:(ft + 1) * 128],
                    cs_c[:, c, ot * 512:(ot + 1) * 512],
                    start=(c == 0), stop=False),
                    reads=("l3f_wr", "l3f_c"), writes=(("l3f_psm", b),))
            for c in range(8):
                p.emit("tensor", lambda e, c=c, ft=ft, ot=ot, b=b: e.matmul(
                    ps_m[:, b, :], wi_bf[:, c, ft * 128:(ft + 1) * 128],
                    cs_sn[:, c, ot * 512:(ot + 1) * 512],
                    start=False, stop=(c == 7)),
                    reads=("l3f_wi", "l3f_sn"), writes=(("l3f_psm", b),))
            p.emit("vector", lambda e, ft=ft, ot=ot, b=b: e.tensor_copy(
                pr_sb[:, ft, ot * 512:(ot + 1) * 512], ps_m[:, b, :]),
                reads=(("l3f_psm", b),), writes=("l3f_pr",))
            for c in range(8):
                p.emit("tensor", lambda e, c=c, ft=ft, ot=ot, b=b: e.matmul(
                    ps_m[:, 2 + b, :], wr_bf[:, c, ft * 128:(ft + 1) * 128],
                    cs_s[:, c, ot * 512:(ot + 1) * 512],
                    start=(c == 0), stop=False),
                    reads=("l3f_wr", "l3f_s"), writes=(("l3f_psm", 2 + b),))
            for c in range(8):
                p.emit("tensor", lambda e, c=c, ft=ft, ot=ot, b=b: e.matmul(
                    ps_m[:, 2 + b, :], wi_bf[:, c, ft * 128:(ft + 1) * 128],
                    cs_c[:, c, ot * 512:(ot + 1) * 512],
                    start=False, stop=(c == 7)),
                    reads=("l3f_wi", "l3f_c"), writes=(("l3f_psm", 2 + b),))
            p.emit("vector", lambda e, ft=ft, ot=ot, b=b: e.tensor_copy(
                pi_sb[:, ft, ot * 512:(ot + 1) * 512], ps_m[:, 2 + b, :]),
                reads=(("l3f_psm", 2 + b),), writes=("l3f_pi",))
    for dt_ in range(8):
        for ot in range(2):
            b = (dt_ * 2 + ot) % 4
            for c in range(8):
                p.emit("tensor", lambda e, c=c, dt_=dt_, ot=ot, b=b: e.matmul(
                    ps_m[:, b, :], cs_c[:, c, dt_ * 128:(dt_ + 1) * 128],
                    pr_sb[:, c, ot * 512:(ot + 1) * 512],
                    start=(c == 0), stop=False),
                    reads=("l3f_c", "l3f_pr"), writes=(("l3f_psm", b),))
            for c in range(8):
                p.emit("tensor", lambda e, c=c, dt_=dt_, ot=ot, b=b: e.matmul(
                    ps_m[:, b, :], cs_s[:, c, dt_ * 128:(dt_ + 1) * 128],
                    pi_sb[:, c, ot * 512:(ot + 1) * 512],
                    start=False, stop=(c == 7)),
                    reads=("l3f_s", "l3f_pi"), writes=(("l3f_psm", b),))
            p.emit("scalar", lambda e, dt_=dt_, ot=ot, b=b: e.activation(
                out=m_sb[:, dt_, ot * 512:(ot + 1) * 512], in_=ps_m[:, b, :],
                func=AF.Copy, scale=1.0 / 1024.0),
                reads=(("l3f_psm", b),), writes=("l3f_m",))

    p.pop_scope()
    xt = p.sb("l3f_xt", [128, 8, 512], F32)
    xn = p.sb("l3f_xn", [128, 8, 512], F32)
    xnb = p.sb("l3f_xnb", [128, 8, 512], BF16)
    yt = p.sb("l3f_yt", [128, 8, 512], F32)
    yn = p.sb("l3f_yn", [128, 8, 512], F32)
    ps_y = p.ps("l3f_psy", [128, 4, 512], F32)
    xin_t = xin.rearrange("(c p) (n t) -> n p c t", p=128, t=512)
    xout_t = xout.rearrange("(c p) (n t) -> n p c t", p=128, t=512)

    for t in range(NT):
        p.dma("sync", xt[:], xin_t[t], reads=("l3f_xt",), writes=("l3f_xt",))
        cx.ln_fm(xt[:], "l3f_xt", xn[:], "l3f_xn", "l3a",
                 out2_bf=xnb[:], key_out2="l3f_xnb")
        linear_fm(p, lambda k: xnb[:, k, :], "l3f_xnb", "l3f_m",
                  ps_y, lambda b: ("l3f_psy", b),
                  lambda oc, pv, kv: p.emit("scalar", lambda e, oc=oc, pv=pv: e.activation(
                      out=yt[:, oc, :], in_=pv, func=AF.Identity,
                      bias=bias_sb[:, oc:oc + 1]),
                      reads=(kv, "l3f_bias"), writes=("l3f_yt",)),
                  nk=8, ndo=8,
                  w_view=lambda k, oc: m_sb[:, k, oc * 128:(oc + 1) * 128])
        cx.ln_fm(yt[:], "l3f_yt", yn[:], "l3f_yn", "l3b")
        p.emit("vector", lambda e: e.tensor_tensor(
            yn[:], yn[:], xt[:], op=ALU.add),
            reads=("l3f_yn", "l3f_xt"), writes=("l3f_yn",))
        p.dma("sync", xout_t[t], yn[:], reads=("l3f_yn",), writes=("l3f_store",))
    p.pop_scope()


def _l2a_selae(swap):
    s = np.zeros((48, 2), np.float32)
    for cr in range(8):
        s[6 * cr + 0, 0 + swap] = 1.0
        s[6 * cr + 1, 0 + swap] = 1.0
        s[6 * cr + 4, (1 - swap) if swap else 1] = 1.0
        s[6 * cr + 5, (1 - swap) if swap else 1] = 1.0
    return s.astype(BF)


def _l2a_selth():
    s = np.zeros((48, 16), np.float32)
    for j in range(16):
        s[6 * (j // 2) + 2 + (j % 2), j] = 1.0
    return s.astype(BF)


def _l1a_rcounts():
    cnt = np.zeros(2047, np.float64)
    starts = np.arange(14) * 128
    for s in starts:
        cnt[s:s + 256] += 1
    cnt = np.maximum(cnt, 1.0)
    r = np.ones(2048, np.float64)
    r[1:2048] = 1.0 / cnt
    return np.broadcast_to(r.astype(np.float32), (128, 2048)).astype(BF).copy()


def _l0f_rotary():
    n, DH, WINDOW = 2048, 64, 512
    half = np.arange(0, DH, 2, dtype=np.float64)
    inv_freq = 1.0 / 10000.0 ** (half / DH)
    t = np.arange(n, dtype=np.float64)
    freqs = t[:, None] * inv_freq[None, :]
    freqs = np.concatenate([freqs, freqs], axis=-1)
    base = (half + 0.4 * DH) / (1.4 * DH)
    power = (t - n // 2) / float(WINDOW // 2)
    scale = base[None, :] ** power[:, None]
    scale = np.concatenate([scale, scale], axis=-1)
    cos, sin = np.cos(freqs), np.sin(freqs)
    def fm64(a):
        return np.tile(np.ascontiguousarray(a.T.astype(np.float32)), (2, 1))
    qc, qs = fm64(cos * scale), fm64(sin * scale)
    kc, ks = fm64(cos / scale), fm64(sin / scale)
    R = np.zeros((64, 64), np.float64)
    for i in range(32):
        R[i, i + 32] = -1.0
        R[i + 32, i] = 1.0
    R2 = np.zeros((128, 128), np.float32)
    R2[0:64, 0:64] = R
    R2[64:128, 64:128] = R
    r2t = np.ascontiguousarray(R2.T)
    o2 = np.zeros((128, 2), np.float32)
    o2[0:64, 0] = 1.0
    o2[64:128, 1] = 1.0
    bo2 = np.zeros((2, 128), np.float32)
    bo2[0, 0:64] = 1.0
    bo2[1, 64:128] = 1.0
    return qc, qs, kc, ks, r2t, o2, bo2


def make_consts():
    _r = _l0f_rotary()
    D = 1024
    k = np.arange(D)
    ang = 2.0 * np.pi * ((np.outer(k, k) % D).astype(np.float64)) / D
    c = np.cos(ang).astype(BF)
    s = np.sin(ang)
    return {
        "iden_bf16": np.eye(128, dtype=np.float32).astype(BF),
        "ones_col": np.ones((128, 1), dtype=np.float32).astype(BF),
        "ones_row": np.ones((1, 128), dtype=np.float32),
        "dft_c": c,
        "dft_s": s.astype(BF),
        "dft_sneg": (-s).astype(BF),
        "sel8": np.repeat(np.eye(8, dtype=np.float32), 128, axis=1).astype(BF),
        "l1a_rcounts": _l1a_rcounts(),
        "sel_ae": _l2a_selae(0), "sel_ea": _l2a_selae(1),
        "sel_th": _l2a_selth(),
        "e64": np.repeat(np.eye(16, dtype=np.float32), 64, axis=1).astype(BF),
        "causal01": np.tril(np.ones((128, 128), np.float32)),
        "causal10": 1.0 - np.tril(np.ones((128, 128), np.float32)),
        "upper01": np.triu(np.ones((128, 128), np.float32)),
        "upper10": 1.0 - np.triu(np.ones((128, 128), np.float32)),
        "rot_qc": _r[0].astype(BF), "rot_qs": _r[1].astype(BF),
        "rot_kc": _r[2].astype(BF), "rot_ks": _r[3].astype(BF),
        "rot_r2t": _r[4].astype(BF),
        "ones2col": _r[5].astype(BF),
        "blockones2": _r[6],
    }


CONST_DTYPES = {"ones_row": F32, "causal01": F32, "causal10": F32,
                "upper01": F32, "upper10": F32, "blockones2": F32}


def consts_aps(p, consts):
    out = {}
    for k, v in consts.items():
        out[k] = p.dram_in("c_" + k, v.shape, CONST_DTYPES.get(k, BF16))
    return out


def consts_in_map(consts):
    return {("c_" + k): v for k, v in consts.items()}


def pc(bias):
    return np.ascontiguousarray(
        np.asarray(bias, np.float32).reshape(8, 128).T)


def fm(x):
    return np.ascontiguousarray(np.asarray(x, np.float32).reshape(-1, 1024).T)


def build_l3_ffn(consts):
    p = Prog()
    xin = p.dram_in("xin", [1024, TCORE])
    xout = p.dram_out("xout", [1024, TCORE])
    c_aps = consts_aps(p, consts)
    cx = Ctx(p, c_aps)
    W = {
        "WrT": p.dram_in("w_WrT", [1024, 1024]),
        "WiT": p.dram_in("w_WiT", [1024, 1024]),
        "bias_pc": p.dram_in("w_bias_pc", [128, 8]),
    }
    emit_l3_ffn(p, cx, xin, xout, W, c_aps)
    nc = p.finalize()
    return nc, p


def l3_ffn_in_map(p, consts, inputs):
    return {
        "c_fvals_in": p.fconst_host_array(),
        "w_WrT": np.ascontiguousarray(inputs["L3_ffn_w_freq_real"].T),
        "w_WiT": np.ascontiguousarray(inputs["L3_ffn_w_freq_imag"].T),
        "w_bias_pc": pc(inputs["L3_ffn_bias"]),
        **consts_in_map(consts),
    }


def top2_masks(p, cx, s, col, key_s, key_c, rw_ap, key_rw):
    p.fconst(-1e30)

    def first_of(eq_lane, scr_lane):
        p.emit("vector", lambda e: e.tensor_tensor_scan(
            s[:, scr_lane, :], cx.ones8[:], s[:, eq_lane, :], 0.0,
            op0=ALU.mult, op1=ALU.add),
            reads=(key_s, "cx_ones8"), writes=(key_s,))
        p.emit("vector", lambda e: e.tensor_tensor(
            s[:, scr_lane, :], s[:, scr_lane, :], s[:, eq_lane, :],
            op=ALU.subtract), reads=(key_s,), writes=(key_s,))
        p.emit("vector", lambda e: e.tensor_scalar(
            s[:, scr_lane, :], s[:, scr_lane, :], 0.0, None,
            op0=ALU.is_equal), reads=(key_s,), writes=(key_s,))
        p.emit("vector", lambda e: e.tensor_tensor(
            s[:, eq_lane, :], s[:, eq_lane, :], s[:, scr_lane, :],
            op=ALU.mult), reads=(key_s,), writes=(key_s,))

    p.emit("vector", lambda e: e.tensor_reduce(
        col[:, 0:1], rw_ap, axis=AX.X, op=ALU.max),
        reads=(key_rw,), writes=(key_c,))
    p.emit("vector", lambda e: e.tensor_scalar(
        s[:, 0, :], rw_ap, col[:, 0:1], None, op0=ALU.is_ge),
        reads=(key_rw, key_c), writes=(key_s,))
    first_of(0, 2)
    p.emit("vector", lambda e: e.scalar_tensor_tensor(
        s[:, 2, :], s[:, 0, :], -1e30, rw_ap, op0=ALU.mult, op1=ALU.add),
        reads=(key_s, key_rw), writes=(key_s,))
    p.emit("vector", lambda e: e.tensor_reduce(
        col[:, 1:2], s[:, 2, :], axis=AX.X, op=ALU.max),
        reads=(key_s,), writes=(key_c,))
    p.emit("vector", lambda e: e.tensor_scalar(
        s[:, 1, :], s[:, 2, :], col[:, 1:2], None, op0=ALU.is_ge),
        reads=(key_s, key_c), writes=(key_s,))
    first_of(1, 2)
    p.emit("vector", lambda e: e.tensor_tensor(
        s[:, 1, :], s[:, 1, :], s[:, 0, :], op=ALU.add),
        reads=(key_s,), writes=(key_s,))


def emit_l2_ffn(p: Prog, cx: Ctx, xin, xout, W, C):
    p.push_scope()
    vg_sb = p.sb("l2f_vg", [128, 8, 512], F32)
    wld = p.sb("l2f_wld", [128, 2, 1024], F32)
    web = p.sb("l2f_web", [128, 8, 1024], BF16)
    wpb = p.sb("l2f_wpb", [128, 8, 1024], BF16)
    be_sb = p.sb("l2f_be", [8, 1024], BF16)
    bp_sb = p.sb("l2f_bp", [128, 8], F32)
    gbd = p.sb("l2f_gbd", [128, 4, 128], F32)
    sel = p.sb("l2f_sel", [8, 1024], BF16)
    rw_all = p.sb("l2f_rwall", [128, 32, 8], F32)
    anyacc = p.sb("l2f_any", [128, 2], F32)
    anycol = p.sb("l2f_anyc", [128, 1], F32)
    anyrow = p.sb("l2f_anyr", [1, 2], F32)
    xt = p.sb("l2f_xt", [128, 8, 512], F32)
    xn = p.sb("l2f_xn", [128, 8, 512], F32)
    xnb = p.sb("l2f_xnb", [128, 8, 512], BF16)
    rhs_e = p.sb("l2f_rhse", [128, 8, 512], BF16)
    acc = p.sb("l2f_acc", [128, 8, 512], F32)
    xe_fm = p.sb("l2f_xefm", [128, 4, 512], F32)
    xe_tm = p.sb("l2f_xetm", [128, 4, 512], F32)
    z_tm = p.sb("l2f_ztm", [128, 4, 512], F32)
    sq2 = p.sb("l2f_sq2", [128, 512], F32)
    g8 = p.sb("l2f_g8", [128, 4, 8], F32)
    gcol = p.sb("l2f_gcol", [128, 4], F32)
    n2 = p.sb("l2f_n2", [128, 4], F32)
    xz8 = p.sb("l2f_xz8", [128, 4, 8], F32)
    xe28 = p.sb("l2f_xe28", [128, 4, 8], F32)
    wtile = p.sb("l2f_wtile", [128, 4, 8], F32)
    wbf = p.sb("l2f_wbf", [128, 8], BF16)
    wT = p.sb("l2f_wT", [8, 4, 128], BF16)
    ps_a = p.ps("l2f_psa", [128, 4, 512], F32)
    ps_b = p.ps("l2f_psb", [128, 1, 512], F32)
    ps_t = p.ps("l2f_pst", [128, 1024], BF16)
    cc_in = p.dram_tmp("l2f_ccin", [128, 2])
    cc_out = p.dram_shared("l2f_ccout", [128, 2])

    p.dma("sync", vg_sb[:], W["VgT"].rearrange("(k p) n -> p k n", p=128),
          writes=("l2f_vg",))
    p.dma("sync", sel[:], C["sel8"], writes=("l2f_sel",))
    load_weight_bf16(p, wld, "l2f_wld", wpb[:], "l2f_wpb", W["WpT"], 8, 1024)
    p.dma("sync", wld[0:8, 0, 0:1024], W["be"], reads=("l2f_wld",),
          writes=("l2f_wld",))
    p.emit("vector", lambda e: e.tensor_copy(be_sb[:], wld[0:8, 0, 0:1024]),
           reads=("l2f_wld",), writes=("l2f_be",))
    p.dma("sync", bp_sb[:], W["bp_pc"], writes=("l2f_bp",))
    p.emit("vector", lambda e: e.memset(gbd[:], 0.0), writes=("l2f_gbd",))
    for e8 in range(8):
        c, half = e8 // 2, (e8 % 2) * 64
        for k in range(8):
            p.emit("tensor", lambda e, k=k, e8=e8: e.matmul(
                ps_b[0:64, 0, 0:64], vg_sb[:, k, e8 * 64:(e8 + 1) * 64],
                vg_sb[:, k, e8 * 64:(e8 + 1) * 64],
                start=(k == 0), stop=(k == 7)),
                reads=("l2f_vg",), writes=(("l2f_psb", 0),))
        p.emit("vector", lambda e, c=c, half=half: e.tensor_copy(
            gbd[half:half + 64, c, half:half + 64], ps_b[0:64, 0, 0:64]),
            reads=(("l2f_psb", 0),), writes=("l2f_gbd",))

    xin_t = xin.rearrange("(c p) (n t) -> n p c t", p=128, t=512)
    xout_t = xout.rearrange("(c p) (n t) -> n p c t", p=128, t=512)
    p.fconst(0.125)

    p.emit("vector", lambda e: e.memset(anyacc[:], 0.0), writes=("l2f_any",))
    for t in range(NT):
        p.dma("sync", xt[:], xin_t[t], reads=("l2f_xt",), writes=("l2f_xt",))
        cx.ln_fm(xt[:], "l2f_xt", xn[:], "l2f_xn", "l2fa",
                 out2_bf=xnb[:], key_out2="l2f_xnb", precise=True)
        sqn = cx.sqf
        p.emit("scalar", lambda e, sqn=sqn: e.activation(
            out=sqn[:], in_=xn[:], func=AF.Square),
            reads=("l2f_xn",), writes=("cx_sqf",))
        for ec in range(4):
            for k in range(8):
                p.emit("tensor", lambda e, ec=ec, k=k: e.matmul(
                    ps_a[:, ec, :], vg_sb[:, k, ec * 128:(ec + 1) * 128],
                    xn[:, k, :], start=(k == 0), stop=(k == 7)),
                    reads=("l2f_vg", "l2f_xn"), writes=(("l2f_psa", ec),))
            p.emit("vector", lambda e, ec=ec: e.tensor_copy(
                xe_fm[:, ec, :], ps_a[:, ec, :]),
                reads=(("l2f_psa", ec),), writes=("l2f_xefm",))
        for sub in range(4):
            for k in range(8):
                p.emit("tensor", lambda e, sub=sub, k=k: e.matmul(
                    ps_a[:, 0, :], xn[:, k, sub * 128:(sub + 1) * 128],
                    vg_sb[:, k, :], start=(k == 0), stop=(k == 7)),
                    reads=("l2f_xn", "l2f_vg"), writes=(("l2f_psa", 0),))
            p.emit("vector", lambda e, sub=sub: e.tensor_copy(
                xe_tm[:, sub, :], ps_a[:, 0, :]),
                reads=(("l2f_psa", 0),), writes=("l2f_xetm",))
            for c in range(4):
                p.emit("tensor", lambda e, sub=sub, c=c: e.matmul(
                    ps_a[:, 1, c * 128:(c + 1) * 128],
                    xe_fm[:, c, sub * 128:(sub + 1) * 128],
                    gbd[:, c, :], start=True, stop=True),
                    reads=("l2f_xefm", "l2f_gbd"), writes=(("l2f_psa", 1),))
            p.emit("vector", lambda e, sub=sub: e.tensor_copy(
                z_tm[:, sub, :], ps_a[:, 1, :]),
                reads=(("l2f_psa", 1),), writes=("l2f_ztm",))
            for k in range(8):
                p.emit("tensor", lambda e, sub=sub, k=k, sqn=sqn: e.matmul(
                    ps_b[:, 0, 0:1], sqn[:, k, sub * 128:(sub + 1) * 128],
                    cx.ones8[:, 0:1], start=(k == 0), stop=(k == 7)),
                    reads=("cx_sqf", "cx_ones8"), writes=(("l2f_psb", 0),))
            p.emit("vector", lambda e, sub=sub: e.tensor_copy(
                n2[:, sub:sub + 1], ps_b[:, 0, 0:1]),
                reads=(("l2f_psb", 0),), writes=("l2f_n2",))
        for sub in range(4):
            p.emit("scalar", lambda e, sub=sub: e.activation(
                out=sq2[:], in_=xe_tm[:, sub, :], func=AF.Square),
                reads=("l2f_xetm",), writes=("l2f_sq2",))
            p.emit("vector", lambda e, sub=sub: e.tensor_reduce(
                xe28[:, sub, :].unsqueeze(2),
                sq2[:].rearrange("p (e r) -> p e r", e=8),
                axis=AX.X, op=ALU.add), reads=("l2f_sq2",), writes=("l2f_xe28",))
            p.emit("vector", lambda e, sub=sub: e.tensor_tensor(
                sq2[:], xe_tm[:, sub, :], z_tm[:, sub, :], op=ALU.mult),
                reads=("l2f_xetm", "l2f_ztm"), writes=("l2f_sq2",))
            p.emit("vector", lambda e, sub=sub: e.tensor_reduce(
                xz8[:, sub, :].unsqueeze(2),
                sq2[:].rearrange("p (e r) -> p e r", e=8),
                axis=AX.X, op=ALU.add), reads=("l2f_sq2",), writes=("l2f_xz8",))
            p.emit("vector", lambda e, sub=sub: e.scalar_tensor_tensor(
                g8[:, sub, :], xe28[:, sub, :], -2.0, xz8[:, sub, :],
                op0=ALU.mult, op1=ALU.add),
                reads=("l2f_xe28", "l2f_xz8"), writes=("l2f_g8",))
            p.emit("vector", lambda e, sub=sub: e.tensor_tensor(
                g8[:, sub, :], g8[:, sub, :],
                n2[:, sub:sub + 1].broadcast_to([128, 8]), op=ALU.add),
                reads=("l2f_g8", "l2f_n2"), writes=("l2f_g8",))
            p.emit("vector", lambda e, sub=sub: e.tensor_scalar(
                g8[:, sub, :], g8[:, sub, :], 0.0, None, op0=ALU.max),
                reads=("l2f_g8",), writes=("l2f_g8",))
            p.emit("scalar", lambda e, sub=sub: e.activation(
                out=g8[:, sub, :], in_=g8[:, sub, :], func=AF.Sqrt),
                reads=("l2f_g8",), writes=("l2f_g8",))
            p.emit("vector", lambda e, sub=sub: e.tensor_reduce(
                gcol[:, sub:sub + 1], g8[:, sub, :], axis=AX.X, op=ALU.min),
                reads=("l2f_g8",), writes=("l2f_gcol",))
            p.emit("scalar", lambda e, sub=sub: e.activation(
                out=g8[:, sub, :], in_=g8[:, sub, :], func=AF.Exp,
                scale=-1.0, bias=gcol[:, sub:sub + 1]),
                reads=("l2f_g8", "l2f_gcol"), writes=("l2f_g8",))
            p.emit("vector", lambda e, sub=sub: e.tensor_reduce(
                gcol[:, sub:sub + 1], g8[:, sub, :], axis=AX.X, op=ALU.add),
                reads=("l2f_g8",), writes=("l2f_gcol",))
            p.emit("vector", lambda e, sub=sub: e.reciprocal(
                gcol[:, sub:sub + 1], gcol[:, sub:sub + 1]),
                reads=("l2f_gcol",), writes=("l2f_gcol",))
            p.emit("vector", lambda e, sub=sub, t=t: e.tensor_scalar(
                rw_all[:, t * 4 + sub, :], g8[:, sub, :],
                gcol[:, sub:sub + 1], None, op0=ALU.mult),
                reads=("l2f_g8", "l2f_gcol"), writes=("l2f_rwall",))
            p.emit("vector", lambda e, sub=sub, t=t: e.tensor_scalar(
                g8[:, sub, :], rw_all[:, t * 4 + sub, :], 0.125, None,
                op0=ALU.is_gt), reads=("l2f_rwall",), writes=("l2f_g8",))
            p.emit("vector", lambda e, sub=sub: e.tensor_reduce(
                gcol[:, sub:sub + 1], g8[:, sub, :], axis=AX.X, op=ALU.max),
                reads=("l2f_g8",), writes=("l2f_gcol",))
            p.emit("vector", lambda e, sub=sub: e.tensor_tensor(
                anyacc[:, 0:1], anyacc[:, 0:1], gcol[:, sub:sub + 1],
                op=ALU.max), reads=("l2f_any", "l2f_gcol"), writes=("l2f_any",))

    p.dma("gpsimd", cc_in, anyacc[:], reads=("l2f_any",), writes=("l2f_ccin",))
    p.emit("gpsimd", lambda e: e.collective_compute(
        "AllReduce", ALU.max, replica_groups=[list(range(NCORES))],
        ins=[cc_in.opt()], outs=[cc_out.opt()]),
        reads=("l2f_ccin",), writes=("l2f_ccout",), cc=True)
    p.dma("gpsimd", anyacc[:], cc_out, reads=("l2f_ccout",),
          writes=("l2f_any",))
    p.emit("vector", lambda e: e.tensor_copy(wbf[:, 0:2], anyacc[:]),
           reads=("l2f_any",), writes=("l2f_wbf",))
    p.emit("tensor", lambda e: e.matmul(
        ps_b[0:1, 0, 0:2], cx.ones_col[:], wbf[:, 0:2], start=True, stop=True),
        reads=("l2f_wbf", "cx_onescol"), writes=(("l2f_psb", 0),))
    p.emit("vector", lambda e: e.tensor_scalar(
        anyrow[:, 0:1], ps_b[0:1, 0, 0:1], 0.0, None, op0=ALU.is_gt),
        reads=(("l2f_psb", 0),), writes=("l2f_anyr",))
    cx.replicate_row(anyrow[:, 0:1], "l2f_anyr", ps_b[:, 0, 0:1],
                     ("l2f_psb", 0))
    p.emit("vector", lambda e: e.tensor_copy(anycol[:], ps_b[:, 0, 0:1]),
           reads=(("l2f_psb", 0),), writes=("l2f_anyc",))

    for t in range(NT):
        p.dma("sync", xt[:], xin_t[t], reads=("l2f_xt",), writes=("l2f_xt",))
        cx.ln_fm(xt[:], "l2f_xt", xn[:], "l2f_xn", "l2fb",
                 out2_bf=xnb[:], key_out2="l2f_xnb")
        for sub in range(4):
            rw = rw_all[:, t * 4 + sub, :]
            top2_masks(p, cx, g8, gcol, "l2f_g8", "l2f_gcol", rw, "l2f_rwall")
            p.emit("vector", lambda e, rw=rw: e.tensor_scalar(
                g8[:, 2, :], rw, 0.125, None, op0=ALU.is_gt),
                reads=("l2f_rwall",), writes=("l2f_g8",))
            p.emit("vector", lambda e: e.tensor_tensor(
                g8[:, 2, :], g8[:, 2, :], g8[:, 1, :], op=ALU.mult),
                reads=("l2f_g8", "l2f_g8"), writes=("l2f_g8",))
            p.emit("vector", lambda e, rw=rw: e.tensor_tensor(
                g8[:, 2, :], g8[:, 2, :], rw, op=ALU.mult),
                reads=("l2f_g8", "l2f_rwall"), writes=("l2f_g8",))
            p.emit("vector", lambda e: e.tensor_reduce(
                gcol[:, 2:3], g8[:, 2, :], axis=AX.X, op=ALU.add),
                reads=("l2f_g8",), writes=("l2f_gcol",))
            p.emit("vector", lambda e: e.tensor_scalar(
                gcol[:, 3:4], gcol[:, 2:3], 0.0, None, op0=ALU.is_equal),
                reads=("l2f_gcol",), writes=("l2f_gcol",))
            p.emit("vector", lambda e: e.tensor_tensor(
                gcol[:, 2:3], gcol[:, 2:3], gcol[:, 3:4], op=ALU.add),
                reads=("l2f_gcol", "l2f_gcol"), writes=("l2f_gcol",))
            p.emit("vector", lambda e: e.reciprocal(
                gcol[:, 2:3], gcol[:, 2:3]), reads=("l2f_gcol",),
                writes=("l2f_gcol",))
            p.emit("vector", lambda e: e.tensor_scalar(
                g8[:, 2, :], g8[:, 2, :], gcol[:, 2:3], None, op0=ALU.mult),
                reads=("l2f_g8", "l2f_gcol"), writes=("l2f_g8",))
            p.emit("vector", lambda e: e.tensor_tensor(
                g8[:, 2, :], g8[:, 2, :], g8[:, 0, :], op=ALU.subtract),
                reads=("l2f_g8", "l2f_g8"), writes=("l2f_g8",))
            p.emit("vector", lambda e: e.tensor_scalar(
                g8[:, 2, :], g8[:, 2, :], anycol[:], None, op0=ALU.mult),
                reads=("l2f_g8", "l2f_anyc"), writes=("l2f_g8",))
            p.emit("vector", lambda e, sub=sub: e.tensor_tensor(
                wtile[:, sub, :], g8[:, 2, :], g8[:, 0, :], op=ALU.add),
                reads=("l2f_g8", "l2f_g8"), writes=("l2f_wtile",))
        for sub in range(4):
            p.emit("vector", lambda e, sub=sub: e.tensor_copy(
                wbf[:, 0:8], wtile[:, sub, :]),
                reads=("l2f_wtile",), writes=("l2f_wbf",))
            p.emit("tensor", lambda e, sub=sub: e.transpose(
                ps_t[0:8, 0:128], wbf[:, 0:8], cx.iden[:]),
                reads=("l2f_wbf", "cx_iden"), writes=("l2f_pst",))
            p.emit("vector", lambda e, sub=sub: e.tensor_copy(
                wT[0:8, sub, :], ps_t[0:8, 0:128]),
                reads=("l2f_pst",), writes=("l2f_wT",))
        for oc in range(8):
            b = oc % 4
            for k in range(8):
                p.emit("tensor", lambda e, oc=oc, k=k, b=b: e.matmul(
                    ps_a[:, b, :], wpb[:, k, oc * 128:(oc + 1) * 128],
                    xnb[:, k, :], start=(k == 0), stop=False),
                    reads=("l2f_wpb", "l2f_xnb"), writes=(("l2f_psa", b),))
            p.emit("tensor", lambda e, oc=oc, b=b: e.matmul(
                ps_a[:, b, :], be_sb[:, oc * 128:(oc + 1) * 128],
                wT[:].rearrange("p a b -> p (a b)"),
                start=False, stop=True),
                reads=("l2f_be", "l2f_wT"), writes=(("l2f_psa", b),))
            p.emit("scalar", lambda e, oc=oc, b=b: e.activation(
                out=acc[:, oc, :], in_=ps_a[:, b, :], func=AF.Identity,
                bias=bp_sb[:, oc:oc + 1]),
                reads=(("l2f_psa", b), "l2f_bp"), writes=("l2f_acc",))
        for e8 in range(8):
            load_weight_bf16(p, wld, "l2f_wld", web[:], "l2f_web",
                             W["WeT"][e8], 8, 1024)
            p.emit("tensor", lambda e, e8=e8: e.matmul(
                ps_b[:, 0, :], sel[:, e8 * 128:(e8 + 1) * 128],
                wT[:].rearrange("p a b -> p (a b)"), start=True, stop=True),
                reads=("l2f_sel", "l2f_wT"), writes=(("l2f_psb", 0),))
            p.emit("vector", lambda e: e.tensor_tensor(
                rhs_e[:], xnb[:],
                ps_b[:, 0, :].unsqueeze(1).broadcast_to([128, 8, 512]),
                op=ALU.mult),
                reads=("l2f_xnb", ("l2f_psb", 0)), writes=("l2f_rhse",))
            for oc in range(8):
                b = oc % 4
                for k in range(8):
                    p.emit("tensor", lambda e, oc=oc, k=k, b=b: e.matmul(
                        ps_a[:, b, :], web[:, k, oc * 128:(oc + 1) * 128],
                        rhs_e[:, k, :], start=(k == 0), stop=(k == 7)),
                        reads=("l2f_web", "l2f_rhse"), writes=(("l2f_psa", b),))
                p.emit("vector", lambda e, oc=oc, b=b: e.tensor_tensor(
                    acc[:, oc, :], acc[:, oc, :], ps_a[:, b, :], op=ALU.add),
                    reads=("l2f_acc", ("l2f_psa", b)), writes=("l2f_acc",))
        import os
        if os.environ.get("L2DBG") and t == 0:
            dbg = p.dram_out("dbg", [128, 64, 8])
            p.dma("sync", dbg[:, 0:32, :], rw_all[:], reads=("l2f_rwall",), writes=("l2f_dbg",))
            p.dma("sync", dbg[:, 32:36, :], wtile[:], reads=("l2f_wtile",), writes=("l2f_dbg",))
            p.emit("vector", lambda e: e.tensor_scalar(
                g8[:, 3, :], cx.ones8[:], anycol[:], None, op0=ALU.mult),
                reads=("cx_ones8", "l2f_anyc"), writes=("l2f_g8",))
            p.dma("sync", dbg[:, 36, :], g8[:, 3, :], reads=("l2f_g8",), writes=("l2f_dbg",))
        cx.ln_fm(acc[:], "l2f_acc", xn[:], "l2f_xn2", "l2fc")
        p.emit("vector", lambda e: e.tensor_tensor(
            xn[:], xn[:], xt[:], op=ALU.add),
            reads=("l2f_xn2", "l2f_xt"), writes=("l2f_xn2",))
        p.dma("sync", xout_t[t], xn[:], reads=("l2f_xn2",), writes=("l2f_st",))
    p.pop_scope()
    cx.drop_sqf()


def build_l2_ffn(consts):
    p = Prog()
    xin = p.dram_in("xin", [1024, TCORE])
    xout = p.dram_out("xout", [1024, TCORE])
    c_aps = consts_aps(p, consts)
    cx = Ctx(p, c_aps)
    W = {
        "WpT": p.dram_in("w_WpT", [1024, 1024]),
        "bp_pc": p.dram_in("w_bp_pc", [128, 8]),
        "WeT": p.dram_in("w_WeT", [8, 1024, 1024]),
        "be": p.dram_in("w_be", [8, 1024]),
        "VgT": p.dram_in("w_VgT", [1024, 512]),
    }
    emit_l2_ffn(p, cx, xin, xout, W, c_aps)
    nc = p.finalize()
    return nc, p


def l2_ffn_in_map(p, consts, inputs):
    We = np.asarray(inputs["L2_ffn_We"], np.float32)
    return {
        "c_fvals_in": p.fconst_host_array(),
        "w_WpT": np.ascontiguousarray(np.asarray(inputs["L2_ffn_Wp"], np.float32).T),
        "w_bp_pc": pc(inputs["L2_ffn_bp"]),
        "w_WeT": np.ascontiguousarray(We.transpose(0, 2, 1)),
        "w_be": np.ascontiguousarray(np.asarray(inputs["L2_ffn_be"], np.float32)),
        "w_VgT": np.ascontiguousarray(
            np.asarray(inputs["L2_ffn_Vg"], np.float32).transpose(1, 0, 2)
            .reshape(1024, 512)),
        **consts_in_map(consts),
    }


def emit_l1_ffn(p: Prog, cx: Ctx, xin, xout, W, C):
    TSUP = 1024
    NSUP = TCORE // TSUP
    p.push_scope()
    wg_sb = p.sb("l1f_wg", [128, 8, 8], F32)
    sel = p.sb("l1f_sel", [8, 1024], BF16)
    w1h = p.sb("l1f_w1h", [128, 8, 1024], BF16)
    w3h = p.sb("l1f_w3h", [128, 8, 1024], BF16)
    w2h = p.sb("l1f_w2h", [128, 8, 1024], BF16)
    wld = p.sb("l1f_wld", [128, 4, 1024], F32)
    xnb = p.sb("l1f_xnb", [128, 8, 1024], BF16)
    acc = p.sb("l1f_acc", [128, 8, 1024], F32)
    hsb = p.sb("l1f_h", [128, 8, 512], BF16)
    wrep = p.sb("l1f_wrep", [128, 512], BF16)
    wT = p.sb("l1f_wT", [8, 8, 128], BF16)
    wbf = p.sb("l1f_wbf", [128, 8], BF16)
    g8 = p.sb("l1f_g8", [128, 4, 8], F32)
    gcol = p.sb("l1f_gcol", [128, 4], F32)
    xt = p.sb("l1f_xt", [128, 8, 512], F32)
    xn = p.sb("l1f_xn", [128, 8, 512], F32)
    yn = xn
    ps_a = p.ps("l1f_psa", [128, 2, 512], F32)
    ps_g = p.ps("l1f_psg", [128, 2, 512], F32)
    ps_y = p.ps("l1f_psy", [128, 1, 512], F32)
    ps_t = p.ps("l1f_pst", [128, 1024], BF16)

    p.dma("sync", sel[:], C["sel8"], writes=("l1f_sel",))
    p.dma("sync", wg_sb[:], W["Wg"].rearrange("(k p) n -> p k n", p=128),
          writes=("l1f_wg",))
    p.fconst(1.0 / 3.0)

    xin_t = xin.rearrange("(c p) (n t) -> n p c t", p=128, t=512)
    xout_t = xout.rearrange("(c p) (n t) -> n p c t", p=128, t=512)

    for sup in range(NSUP):
        for ts in range(2):
            t = sup * 2 + ts
            p.dma("sync", xt[:], xin_t[t], reads=("l1f_xt",), writes=("l1f_xt",))
            cx.ln_fm(xt[:], "l1f_xt", xn[:], "l1f_xn", "l1fg", precise=True)
            p.emit("vector", lambda e, ts=ts: e.tensor_copy(
                xnb[:, :, ts * 512:(ts + 1) * 512], xn[:]),
                reads=("l1f_xn",), writes=("l1f_xnb",))
            for sub in range(4):
                for k in range(8):
                    p.emit("tensor", lambda e, sub=sub, k=k: e.matmul(
                        ps_g[0:128, 0, 0:8],
                        xn[:, k, sub * 128:(sub + 1) * 128],
                        wg_sb[:, k, :], start=(k == 0), stop=(k == 7)),
                        reads=("l1f_xn", "l1f_wg"), writes=(("l1f_psg", 0),))
                p.emit("scalar", lambda e, sub=sub: e.activation(
                    out=g8[:, 3, :], in_=ps_g[:, 0, 0:8], func=AF.Sigmoid),
                    reads=(("l1f_psg", 0),), writes=("l1f_g8",))
                top2_masks(p, cx, g8, gcol, "l1f_g8", "l1f_gcol",
                           g8[:, 3, :], "l1f_g8")
                p.emit("vector", lambda e: e.tensor_tensor(
                    g8[:, 1, :], g8[:, 1, :], g8[:, 3, :], op=ALU.mult),
                    reads=("l1f_g8",), writes=("l1f_g8",))
                p.emit("vector", lambda e: e.tensor_reduce(
                    gcol[:, 0:1], g8[:, 1, :], axis=AX.X, op=ALU.add),
                    reads=("l1f_g8",), writes=("l1f_gcol",))
                p.emit("vector", lambda e: e.reciprocal(
                    gcol[:, 0:1], gcol[:, 0:1]),
                    reads=("l1f_gcol",), writes=("l1f_gcol",))
                p.emit("vector", lambda e: e.tensor_scalar(
                    g8[:, 1, :], g8[:, 1, :], gcol[:, 0:1], None,
                    op0=ALU.mult), reads=("l1f_g8", "l1f_gcol"),
                    writes=("l1f_g8",))
                p.emit("vector", lambda e: e.tensor_copy(
                    wbf[:, 0:8], g8[:, 1, :]),
                    reads=("l1f_g8",), writes=("l1f_wbf",))
                p.emit("tensor", lambda e: e.transpose(
                    ps_t[0:8, 0:128], wbf[:, 0:8], cx.iden[:]),
                    reads=("l1f_wbf", "cx_iden"), writes=(("l1f_pst",),))
                p.emit("vector", lambda e, ts=ts, sub=sub: e.tensor_copy(
                    wT[0:8, ts * 4 + sub, :], ps_t[0:8, 0:128]),
                    reads=(("l1f_pst",),), writes=("l1f_wT",))
        p.emit("vector", lambda e: e.memset(acc[:], 0.0), writes=("l1f_acc",))

        for ex in range(9):
            for ffh in range(2):
                if ex == 0:
                    w1ap, w3ap, w2ap = W["Ws1"], W["Ws3"], W["Ws2"]
                else:
                    w1ap, w3ap, w2ap = (W["We1"][ex - 1], W["We3"][ex - 1],
                                        W["We2"][ex - 1])
                load_weight_bf16(p, wld, "l1f_wld", w1h[:], "l1f_w1h",
                                 w1ap[:, ffh * 1024:(ffh + 1) * 1024], 8, 1024)
                load_weight_bf16(p, wld, "l1f_wld", w3h[:], "l1f_w3h",
                                 w3ap[:, ffh * 1024:(ffh + 1) * 1024], 8, 1024)
                load_weight_bf16(p, wld, "l1f_wld", w2h[:], "l1f_w2h",
                                 w2ap.rearrange("(h x) n -> h x n", h=2)[ffh],
                                 8, 1024)
                for tsl in range(2):
                    if ex > 0:
                        p.emit("tensor", lambda e, ex=ex, tsl=tsl: e.matmul(
                            ps_y[:, 0, :], sel[:, (ex - 1) * 128:ex * 128],
                            wT[:, tsl * 4:(tsl + 1) * 4, :]
                            .rearrange("p a b -> p (a b)"),
                            start=True, stop=True),
                            reads=("l1f_sel", "l1f_wT"), writes=(("l1f_psy", 0),))
                        p.emit("vector", lambda e: e.tensor_copy(
                            wrep[:], ps_y[:, 0, :]),
                            reads=(("l1f_psy", 0),), writes=("l1f_wrep",))
                    for oc in range(8):
                        b = oc % 2
                        for k in range(8):
                            p.emit("tensor", lambda e, oc=oc, k=k, b=b, tsl=tsl: e.matmul(
                                ps_a[:, b, :], w1h[:, k, oc * 128:(oc + 1) * 128],
                                xnb[:, k, tsl * 512:(tsl + 1) * 512],
                                start=(k == 0), stop=(k == 7)),
                                reads=("l1f_w1h", "l1f_xnb"), writes=(("l1f_psa", b),))
                        for k in range(8):
                            p.emit("tensor", lambda e, oc=oc, k=k, b=b, tsl=tsl: e.matmul(
                                ps_g[:, b, :], w3h[:, k, oc * 128:(oc + 1) * 128],
                                xnb[:, k, tsl * 512:(tsl + 1) * 512],
                                start=(k == 0), stop=(k == 7)),
                                reads=("l1f_w3h", "l1f_xnb"), writes=(("l1f_psg", b),))
                        p.emit("scalar", lambda e, oc=oc, b=b: e.activation(
                            out=hsb[:, oc, :], in_=ps_a[:, b, :], func=AF.Silu),
                            reads=(("l1f_psa", b),), writes=("l1f_h",))
                        p.emit("vector", lambda e, oc=oc, b=b: e.tensor_tensor(
                            hsb[:, oc, :], hsb[:, oc, :], ps_g[:, b, :],
                            op=ALU.mult),
                            reads=("l1f_h", ("l1f_psg", b)), writes=("l1f_h",))
                        if ex > 0:
                            p.emit("vector", lambda e, oc=oc: e.tensor_tensor(
                                hsb[:, oc, :], hsb[:, oc, :], wrep[:],
                                op=ALU.mult),
                                reads=("l1f_h", "l1f_wrep"), writes=("l1f_h",))
                    for oc in range(8):
                        for k in range(8):
                            p.emit("tensor", lambda e, oc=oc, k=k: e.matmul(
                                ps_y[:, 0, :], w2h[:, k, oc * 128:(oc + 1) * 128],
                                hsb[:, k, :], start=(k == 0), stop=(k == 7)),
                                reads=("l1f_w2h", "l1f_h"), writes=(("l1f_psy", 0),))
                        p.emit("vector", lambda e, oc=oc, tsl=tsl: e.scalar_tensor_tensor(
                            acc[:, oc, tsl * 512:(tsl + 1) * 512], ps_y[:, 0, :],
                            1.0 / 3.0, acc[:, oc, tsl * 512:(tsl + 1) * 512],
                            op0=ALU.mult, op1=ALU.add),
                            reads=(("l1f_psy", 0), "l1f_acc"), writes=("l1f_acc",))

        for ts in range(2):
            t = sup * 2 + ts
            p.dma("sync", xt[:], xin_t[t], reads=("l1f_xt",), writes=("l1f_xt",))
            av = acc[:, :, ts * 512:(ts + 1) * 512]
            cx.ln_fm(av, "l1f_acc", yn[:], "l1f_yn", "l1fe")
            p.emit("vector", lambda e: e.tensor_tensor(
                yn[:], yn[:], xt[:], op=ALU.add),
                reads=("l1f_yn", "l1f_xt"), writes=("l1f_yn",))
            p.dma("sync", xout_t[t], yn[:], reads=("l1f_yn",), writes=("l1f_st",))
    p.pop_scope()
    cx.drop_sqf()


def build_l1_ffn(consts):
    p = Prog()
    xin = p.dram_in("xin", [1024, TCORE])
    xout = p.dram_out("xout", [1024, TCORE])
    c_aps = consts_aps(p, consts)
    cx = Ctx(p, c_aps)
    W = {
        "Wg": p.dram_in("w_Wg", [1024, 8]),
        "Ws1": p.dram_in("w_Ws1", [1024, 2048]),
        "Ws3": p.dram_in("w_Ws3", [1024, 2048]),
        "Ws2": p.dram_in("w_Ws2", [2048, 1024]),
        "We1": p.dram_in("w_We1", [8, 1024, 2048]),
        "We3": p.dram_in("w_We3", [8, 1024, 2048]),
        "We2": p.dram_in("w_We2", [8, 2048, 1024]),
    }
    emit_l1_ffn(p, cx, xin, xout, W, c_aps)
    nc = p.finalize()
    return nc, p


def l1_ffn_in_map(p, consts, inputs):
    g = lambda n: np.ascontiguousarray(np.asarray(inputs[n], np.float32))
    return {
        "c_fvals_in": p.fconst_host_array(),
        "w_Wg": g("L1_ffn_Wg"), "w_Ws1": g("L1_ffn_Ws1"),
        "w_Ws3": g("L1_ffn_Ws3"), "w_Ws2": g("L1_ffn_Ws2"),
        "w_We1": g("L1_ffn_We1"), "w_We3": g("L1_ffn_We3"),
        "w_We2": g("L1_ffn_We2"),
        **consts_in_map(consts),
    }


def emit_l3_attn(p: Prog, cx: Ctx, xin, xout, W, C):
    p.push_scope()
    mask01 = p.sb("l3a_m01", [128, 128], F32)
    mask10 = p.sb("l3a_m10", [128, 128], F32)
    hg_dram = p.dram_tmp("l3a_hgd", [2, 16])
    xin_t = xin.rearrange("(c p) (n t) -> n p c t", p=128, t=512)
    xout_t = xout.rearrange("(c p) (n t) -> n p c t", p=128, t=512)
    p.dma("sync", mask01[:], C["causal01"], writes=("l3a_m01",))
    p.dma("sync", mask10[:], C["causal10"], writes=("l3a_m10",))
    p.fconst(1.0 / 2048.0)
    p.fconst(0.125)
    p.fconst(-1e9)

    for b in range(2):
        p.push_scope()
        oacc = p.sb("l3a_o", [128, 8, 2048], BF16)
        xm = p.sb("l3a_xm", [128, 8, 1], F32)
        hgrep8 = p.sb("l3a_hg8", [128, 16], F32)
        hgneg = p.sb("l3a_hgn", [128, 16], F32)
        p.push_scope()
        kf = p.sb("l3a_k", [128, 8, 2048], BF16)
        vf = p.sb("l3a_v", [128, 8, 2048], BF16)
        xnb_s = p.sb("l3a_xnb", [128, 8, 2048], BF16)

        p.push_scope()
        wkb = p.sb("l3a_wk", [128, 8, 1024], BF16)
        wst = p.sb("l3a_wst", [128, 2, 1024], F32)
        bkv = p.sb("l3a_bkv", [128, 16], F32)
        xt = p.sb("l3a_xt", [128, 8, 512], F32)
        ps_k = p.ps("l3a_psk", [128, 4, 512], F32)
        p.dma("sync", bkv[:], W["bkv_pc"], writes=("l3a_bkv",))
        for wi, (wap, dst, dkey, bc) in enumerate((
                (W["wkT"], kf, "l3a_kf", 0), (W["wvT"], vf, "l3a_vf", 8))):
            load_weight_bf16(p, wst, "l3a_wst", wkb[:], "l3a_wk", wap, 8, 1024)
            for ts in range(4):
                t = b * 4 + ts
                if wi == 0:
                    p.dma("sync", xt[:], xin_t[t], reads=("l3a_xt",),
                          writes=("l3a_xt",))
                    cx.ln_fm(xt[:], "l3a_xt",
                             xnb_s[:, :, ts * 512:(ts + 1) * 512], "l3a_xnbs",
                             "l3at", sub_inplace=True)
                for oc in range(8):
                    bnk = oc % 4
                    for k in range(8):
                        p.emit("tensor", lambda e, oc=oc, k=k, bnk=bnk, ts=ts: e.matmul(
                            ps_k[:, bnk, :], wkb[:, k, oc * 128:(oc + 1) * 128],
                            xnb_s[:, k, ts * 512:(ts + 1) * 512],
                            start=(k == 0), stop=(k == 7)),
                            reads=("l3a_wk", "l3a_xnbs"), writes=(("l3a_psk", bnk),))
                    p.emit("scalar", lambda e, oc=oc, bnk=bnk, dst=dst, ts=ts, bc=bc, dkey=dkey: e.activation(
                        out=dst[:, oc, ts * 512:(ts + 1) * 512],
                        in_=ps_k[:, bnk, :], func=AF.Identity,
                        bias=bkv[:, bc + oc:bc + oc + 1]),
                        reads=(("l3a_psk", bnk), "l3a_bkv"), writes=(dkey,))
        p.pop_scope()

        p.push_scope()
        sw1b = p.sb("l3a_sw1", [128, 8, 512], F32)
        sw2b = p.sb("l3a_sw2", [128, 4, 3], F32)
        pw1b = p.sb("l3a_pw1", [128, 8, 512], F32)
        pw2b = p.sb("l3a_pw2", [128, 4, 16], F32)
        sbias = p.sb("l3a_sb", [128, 32], F32)
        h1 = p.sb("l3a_h1", [128, 4, 1], F32)
        rows = p.sb("l3a_rows", [16, 24], F32)
        rbf = p.sb("l3a_rbf", [16, 24], BF16)
        hgrow = p.sb("l3a_hgrow", [1, 32], F32)
        ps_g = p.ps("l3a_psg", [128, 1, 512], F32)
        ps_tr = p.ps("l3a_pstr", [128, 1024], BF16)
        p.emit("vector", lambda e: e.tensor_reduce(
            xm[:], xnb_s[:], axis=AX.X, op=ALU.add),
            reads=("l3a_xnbs",), writes=("l3a_xm",))
        p.emit("vector", lambda e: e.tensor_scalar(
            xm[:], xm[:], 1.0 / 2048.0, None, op0=ALU.mult),
            reads=("l3a_xm",), writes=("l3a_xm",))
        p.dma("sync", sw1b[:], W["sw1T"].rearrange("(k p) n -> p k n", p=128),
              writes=("l3a_sw1",))
        p.dma("sync", sw2b[:], W["sw2T"].rearrange("(k p) n -> p k n", p=128),
              writes=("l3a_sw2",))
        p.dma("sync", sbias[:], W["sb_pc"], writes=("l3a_sbias",))
        for oc in range(4):
            for k in range(8):
                p.emit("tensor", lambda e, oc=oc, k=k: e.matmul(
                    ps_g[0:128, 0, 0:1], sw1b[:, k, oc * 128:(oc + 1) * 128],
                    xm[:, k, :], start=(k == 0), stop=(k == 7)),
                    reads=("l3a_sw1", "l3a_xm"), writes=(("l3a_psg", 0),))
            p.emit("scalar", lambda e, oc=oc: e.activation(
                out=h1[:, oc, :], in_=ps_g[:, 0, 0:1], func=AF.Gelu,
                bias=sbias[:, oc:oc + 1]),
                reads=(("l3a_psg", 0), "l3a_sbias"), writes=("l3a_h1",))
        for k in range(4):
            p.emit("tensor", lambda e, k=k: e.matmul(
                ps_g[0:3, 0, 0:1], sw2b[:, k, :], h1[:, k, :],
                start=(k == 0), stop=(k == 3)),
                reads=("l3a_sw2", "l3a_h1"), writes=(("l3a_psg", 0),))
        p.emit("vector", lambda e: e.tensor_tensor(
            rows[0:3, 0:1], ps_g[0:3, 0, 0:1], sbias[0:3, 4:5],
            op=ALU.add), reads=(("l3a_psg", 0), "l3a_sbias"),
            writes=("l3a_rows",))
        for pi in range(3):
            p.dma("sync", pw1b[:], W["pw1T"][pi].rearrange("(k p) n -> p k n", p=128),
                  reads=("l3a_pw1",), writes=("l3a_pw1",))
            p.dma("sync", pw2b[:], W["pw2T"][pi].rearrange("(k p) n -> p k n", p=128),
                  reads=("l3a_pw2",), writes=("l3a_pw2",))
            for oc in range(4):
                for k in range(8):
                    p.emit("tensor", lambda e, oc=oc, k=k: e.matmul(
                        ps_g[0:128, 0, 0:1], pw1b[:, k, oc * 128:(oc + 1) * 128],
                        xm[:, k, :], start=(k == 0), stop=(k == 7)),
                        reads=("l3a_pw1", "l3a_xm"), writes=(("l3a_psg", 0),))
                p.emit("scalar", lambda e, oc=oc, pi=pi: e.activation(
                    out=h1[:, oc, :], in_=ps_g[:, 0, 0:1], func=AF.Gelu,
                    bias=sbias[:, 5 + pi * 4 + oc:6 + pi * 4 + oc]),
                    reads=(("l3a_psg", 0), "l3a_sbias"), writes=("l3a_h1",))
            for k in range(4):
                p.emit("tensor", lambda e, k=k, pi=pi: e.matmul(
                    ps_g[0:16, 0, 0:1], pw2b[:, k, :], h1[:, k, :],
                    start=(k == 0), stop=(k == 3)),
                    reads=("l3a_pw2", "l3a_h1"), writes=(("l3a_psg", 0),))
            p.emit("vector", lambda e, pi=pi: e.tensor_tensor(
                rows[0:16, 1 + pi:2 + pi], ps_g[0:16, 0, 0:1],
                sbias[0:16, 17 + pi:18 + pi], op=ALU.add),
                reads=(("l3a_psg", 0), "l3a_sbias"), writes=("l3a_rows",))
        p.emit("vector", lambda e: e.tensor_copy(rbf[:, 0:4], rows[:, 0:4]),
               reads=("l3a_rows",), writes=("l3a_rbf",))
        p.emit("tensor", lambda e: e.transpose(
            ps_tr[0:4, 0:16], rbf[:, 0:4], cx.iden[0:16, 0:16]),
            reads=("l3a_rbf", "cx_iden"), writes=("l3a_pstr",))
        smx = p.sb("l3a_smx", [4, 16], F32)
        scol = p.sb("l3a_scol", [4, 2], F32)
        p.emit("vector", lambda e: e.tensor_copy(smx[0:4, :], ps_tr[0:4, 0:16]),
               reads=("l3a_pstr",), writes=("l3a_smx",))
        p.emit("vector", lambda e: e.tensor_reduce(
            scol[0:1, 0:1], smx[0:1, 0:3], axis=AX.X, op=ALU.max),
            reads=("l3a_smx",), writes=("l3a_scol",))
        p.emit("vector", lambda e: e.tensor_scalar(
            scol[0:1, 1:2], scol[0:1, 0:1], -1.0, None, op0=ALU.mult),
            reads=("l3a_scol",), writes=("l3a_scol",))
        p.emit("scalar", lambda e: e.activation(
            out=smx[0:1, 0:3], in_=smx[0:1, 0:3], func=AF.Exp,
            bias=scol[0:1, 1:2]), reads=("l3a_smx", "l3a_scol"),
            writes=("l3a_smx",))
        p.emit("vector", lambda e: e.tensor_reduce(
            scol[0:1, 0:1], smx[0:1, 0:3], axis=AX.X, op=ALU.add),
            reads=("l3a_smx",), writes=("l3a_scol",))
        p.emit("vector", lambda e: e.reciprocal(
            scol[0:1, 0:1], scol[0:1, 0:1]),
            reads=("l3a_scol",), writes=("l3a_scol",))
        p.emit("vector", lambda e: e.tensor_scalar(
            smx[0:1, 0:3], smx[0:1, 0:3], scol[0:1, 0:1], None, op0=ALU.mult),
            reads=("l3a_smx", "l3a_scol"), writes=("l3a_smx",))
        gsm = p.sb("l3a_gsm", [4, 16], F32)
        p.emit("tensor", lambda e: e.transpose(
            ps_tr[0:3, 0:16], rbf[:, 1:4], cx.iden[0:16, 0:16]),
            reads=("l3a_rbf", "cx_iden"), writes=("l3a_pstr",))
        p.emit("vector", lambda e: e.tensor_copy(gsm[0:3, :], ps_tr[0:3, 0:16]),
               reads=("l3a_pstr",), writes=("l3a_gsm",))
        p.emit("vector", lambda e: e.tensor_reduce(
            scol[0:3, 0:1], gsm[0:3, :], axis=AX.X, op=ALU.max),
            reads=("l3a_gsm",), writes=("l3a_scol",))
        p.emit("vector", lambda e: e.tensor_scalar(
            scol[0:3, 1:2], scol[0:3, 0:1], -1.0, None, op0=ALU.mult),
            reads=("l3a_scol",), writes=("l3a_scol",))
        p.emit("scalar", lambda e: e.activation(
            out=gsm[0:3, :], in_=gsm[0:3, :], func=AF.Exp,
            bias=scol[0:3, 1:2]), reads=("l3a_gsm", "l3a_scol"),
            writes=("l3a_gsm",))
        p.emit("vector", lambda e: e.tensor_reduce(
            scol[0:3, 0:1], gsm[0:3, :], axis=AX.X, op=ALU.add),
            reads=("l3a_gsm",), writes=("l3a_scol",))
        p.emit("vector", lambda e: e.reciprocal(
            scol[0:3, 0:1], scol[0:3, 0:1]),
            reads=("l3a_scol",), writes=("l3a_scol",))
        p.emit("vector", lambda e: e.tensor_scalar(
            gsm[0:3, :], gsm[0:3, :], scol[0:3, 0:1], None, op0=ALU.mult),
            reads=("l3a_gsm", "l3a_scol"), writes=("l3a_gsm",))
        p.emit("vector", lambda e: e.tensor_copy(rbf[0:1, 8:11], smx[0:1, 0:3]),
               reads=("l3a_smx",), writes=("l3a_rbf",))
        p.emit("tensor", lambda e: e.transpose(
            ps_tr[0:3, 0:1], rbf[0:1, 8:11], cx.iden[0:1, 0:1]),
            reads=("l3a_rbf", "cx_iden"), writes=("l3a_pstr",))
        p.emit("vector", lambda e: e.tensor_copy(scol[0:3, 1:2], ps_tr[0:3, 0:1]),
               reads=("l3a_pstr",), writes=("l3a_scol",))
        p.emit("vector", lambda e: e.tensor_scalar(
            gsm[0:3, :], gsm[0:3, :], scol[0:3, 1:2], None, op0=ALU.mult),
            reads=("l3a_gsm", "l3a_scol"), writes=("l3a_gsm",))
        p.emit("tensor", lambda e: e.matmul(
            ps_g[0:1, 0, 0:16], cx.ones8[0:3, 0:1], gsm[0:3, :],
            start=True, stop=True),
            reads=("l3a_gsm", "cx_ones8"), writes=(("l3a_psg", 0),))
        p.emit("vector", lambda e: e.tensor_copy(hgrow[0:1, 0:16], ps_g[0:1, 0, 0:16]),
               reads=(("l3a_psg", 0),), writes=("l3a_hgrow",))
        cx.replicate_row(hgrow[0:1, 0:16], "l3a_hgrow", ps_g[:, 0, 0:16],
                         ("l3a_psg", 0))
        p.emit("scalar", lambda e: e.activation(
            out=hgrep8[:], in_=ps_g[:, 0, 0:16], func=AF.Identity, scale=0.125),
            reads=(("l3a_psg", 0),), writes=("l3a_hg8",))
        p.emit("scalar", lambda e: e.activation(
            out=hgneg[:], in_=ps_g[:, 0, 0:16], func=AF.Identity, scale=-1e9),
            reads=(("l3a_psg", 0),), writes=("l3a_hgn",))
        p.pop_scope()

        p.push_scope()
        wqb = p.sb("l3a_wq2", [128, 8, 1024], BF16)
        wst2 = p.sb("l3a_wst2", [128, 2, 1024], F32)
        bqv = p.sb("l3a_bq2", [128, 8], F32)
        kh_sb = p.sb("l3a_kh", [64, 2048], BF16)
        vh_sb = p.sb("l3a_vh", [64, 2048], BF16)
        vt_sb = p.sb("l3a_vt", [128, 16, 64], BF16)
        s_sb = p.sb("l3a_s", [128, 2048], F32)
        pbf = p.sb("l3a_pbf", [128, 2048], BF16)
        ptev = p.sb("l3a_ptev", [128, 128], BF16)
        qcol = p.sb("l3a_qcol", [128, 4], F32)
        qtile = p.sb("l3a_qt", [64, 128], BF16)
        ps_s = p.ps("l3a_pss", [128, 4, 512], F32)
        ps_o = p.ps("l3a_pso", [128, 1, 512], F32)
        ps_t2 = p.ps("l3a_pst2", [128, 1024], BF16)
        load_weight_bf16(p, wst2, "l3a_wst2", wqb[:], "l3a_wq2", W["wqT"], 8, 1024)
        p.dma("sync", bqv[:], W["bq_pc"], writes=("l3a_bq2",))
        for h in range(16):
            hb = (h % 2) * 64
            hc = h // 2
            p.emit("vector", lambda e, hb=hb, hc=hc: e.tensor_copy(
                kh_sb[:], kf[hb:hb + 64, hc, :]),
                reads=("l3a_kf",), writes=("l3a_kh",))
            p.emit("vector", lambda e, hb=hb, hc=hc: e.tensor_copy(
                vh_sb[:], vf[hb:hb + 64, hc, :]),
                reads=("l3a_vf",), writes=("l3a_vh",))
            for kc in range(16):
                p.emit("tensor", lambda e, kc=kc: e.transpose(
                    ps_t2[0:128, 0:64], vh_sb[:, kc * 128:(kc + 1) * 128],
                    cx.iden[0:64, 0:64]), reads=("l3a_vh", "cx_iden"),
                    writes=("l3a_pst2",))
                p.emit("vector", lambda e, kc=kc: e.tensor_copy(
                    vt_sb[:, kc, :], ps_t2[0:128, 0:64]),
                    reads=("l3a_pst2",), writes=("l3a_vt",))
            for qt in range(16):
                nk = qt + 1
                for k in range(8):
                    p.emit("tensor", lambda e, k=k, h=h, qt=qt: e.matmul(
                        ps_o[0:64, 0, 0:128],
                        wqb[:, k, h * 64:(h + 1) * 64],
                        xnb_s[:, k, qt * 128:(qt + 1) * 128],
                        start=(k == 0), stop=(k == 7)),
                        reads=("l3a_wq2", "l3a_xnbs"), writes=(("l3a_pso", 0),))
                p.emit("scalar", lambda e, hb=hb, hc=hc: e.activation(
                    out=qtile[:], in_=ps_o[0:64, 0, 0:128], func=AF.Identity,
                    bias=bqv[hb:hb + 64, hc:hc + 1]),
                    reads=(("l3a_pso", 0), "l3a_bq2"), writes=("l3a_qt2",))
                for kc in range(nk):
                    p.emit("tensor", lambda e, kc=kc, qt=qt: e.matmul(
                        ps_s[:, kc // 4, (kc % 4) * 128:(kc % 4) * 128 + 128],
                        qtile[:], kh_sb[:, kc * 128:(kc + 1) * 128],
                        start=True, stop=True),
                        reads=("l3a_qt2", "l3a_kh"), writes=(("l3a_pss", kc // 4),))
                for g4 in range((nk + 3) // 4):
                    w_ = min(512, nk * 128 - g4 * 512)
                    p.emit("scalar", lambda e, g4=g4, w_=w_, h=h: e.activation(
                        out=s_sb[:, g4 * 512:g4 * 512 + w_],
                        in_=ps_s[:, g4, 0:w_], func=AF.Identity,
                        scale=hgrep8[:, h:h + 1]),
                        reads=(("l3a_pss", g4), "l3a_hg8"), writes=("l3a_s",))
                p.emit("vector", lambda e, qt=qt: e.tensor_tensor(
                    s_sb[:, qt * 128:(qt + 1) * 128],
                    s_sb[:, qt * 128:(qt + 1) * 128], mask01[:], op=ALU.mult),
                    reads=("l3a_s", "l3a_m01"), writes=("l3a_s",))
                p.emit("vector", lambda e, qt=qt, h=h: e.scalar_tensor_tensor(
                    s_sb[:, qt * 128:(qt + 1) * 128], mask10[:],
                    hgneg[:, h:h + 1], s_sb[:, qt * 128:(qt + 1) * 128],
                    op0=ALU.mult, op1=ALU.add),
                    reads=("l3a_s", "l3a_m10", "l3a_hgn"), writes=("l3a_s",))
                p.emit("vector", lambda e, qt=qt: e.tensor_reduce(
                    qcol[:, 0:1], s_sb[:, 0:(qt + 1) * 128], axis=AX.X,
                    op=ALU.max), reads=("l3a_s",), writes=("l3a_qcol",))
                p.emit("vector", lambda e: e.tensor_scalar(
                    qcol[:, 1:2], qcol[:, 0:1], -1.0, None, op0=ALU.mult),
                    reads=("l3a_qcol",), writes=("l3a_qcol",))
                p.emit("scalar", lambda e, qt=qt: e.activation(
                    out=s_sb[:, 0:(qt + 1) * 128], in_=s_sb[:, 0:(qt + 1) * 128],
                    func=AF.Exp, bias=qcol[:, 1:2]),
                    reads=("l3a_s", "l3a_qcol"), writes=("l3a_s",))
                p.emit("vector", lambda e, qt=qt: e.tensor_reduce(
                    qcol[:, 2:3], s_sb[:, 0:(qt + 1) * 128], axis=AX.X,
                    op=ALU.add), reads=("l3a_s",), writes=("l3a_qcol",))
                p.emit("vector", lambda e: e.reciprocal(
                    qcol[:, 3:4], qcol[:, 2:3]),
                    reads=("l3a_qcol",), writes=("l3a_qcol",))
                p.emit("vector", lambda e, qt=qt: e.tensor_scalar(
                    pbf[:, 0:(qt + 1) * 128], s_sb[:, 0:(qt + 1) * 128],
                    qcol[:, 3:4], None, op0=ALU.mult),
                    reads=("l3a_s", "l3a_qcol"), writes=("l3a_pbf",))
                for kc in range(nk):
                    p.emit("tensor", lambda e, kc=kc: e.transpose(
                        ps_t2[0:128, 0:128], pbf[:, kc * 128:(kc + 1) * 128],
                        cx.iden[:]), reads=("l3a_pbf", "cx_iden"),
                        writes=("l3a_pst2",))
                    p.emit("vector", lambda e, kc=kc: e.tensor_copy(
                        ptev[:], ps_t2[0:128, 0:128]),
                        reads=("l3a_pst2",), writes=("l3a_ptev",))
                    p.emit("tensor", lambda e, kc=kc, qt=qt: e.matmul(
                        ps_o[0:64, 0, 0:128], vt_sb[:, kc, :], ptev[:],
                        start=(kc == 0), stop=(kc == qt)),
                        reads=("l3a_vt", "l3a_ptev"), writes=(("l3a_pso", 0),))
                p.emit("vector", lambda e, qt=qt, hb=hb, hc=hc: e.tensor_copy(
                    oacc[hb:hb + 64, hc, qt * 128:(qt + 1) * 128],
                    ps_o[0:64, 0, 0:128]),
                    reads=(("l3a_pso", 0),), writes=("l3a_oacc",))
        p.pop_scope()
        p.pop_scope()

        p.push_scope()
        wob = p.sb("l3a_wo", [128, 8, 1024], BF16)
        wst3 = p.sb("l3a_wst3", [128, 2, 1024], F32)
        bo = p.sb("l3a_bo", [128, 8], F32)
        xt2 = p.sb("l3a_xt2", [128, 8, 512], F32)
        yt2 = p.sb("l3a_yt2", [128, 8, 512], F32)
        yn2 = p.sb("l3a_yn2", [128, 8, 512], F32)
        ps_y2 = p.ps("l3a_psy2", [128, 4, 512], F32)
        load_weight_bf16(p, wst3, "l3a_wst3", wob[:], "l3a_wo", W["woT"], 8, 1024)
        p.dma("sync", bo[:], W["bo_pc"], writes=("l3a_bo",))
        for ts in range(4):
            t = b * 4 + ts
            p.dma("sync", xt2[:], xin_t[t], reads=("l3a_xt2",), writes=("l3a_xt2",))
            for oc in range(8):
                bnk = oc % 4
                for k in range(8):
                    p.emit("tensor", lambda e, oc=oc, k=k, bnk=bnk, ts=ts: e.matmul(
                        ps_y2[:, bnk, :], wob[:, k, oc * 128:(oc + 1) * 128],
                        oacc[:, k, ts * 512:(ts + 1) * 512],
                        start=(k == 0), stop=(k == 7)),
                        reads=("l3a_wo", "l3a_oacc"), writes=(("l3a_psy2", bnk),))
                p.emit("scalar", lambda e, oc=oc, bnk=bnk: e.activation(
                    out=yt2[:, oc, :], in_=ps_y2[:, bnk, :], func=AF.Identity,
                    bias=bo[:, oc:oc + 1]),
                    reads=(("l3a_psy2", bnk), "l3a_bo"), writes=("l3a_yt2",))
            cx.ln_fm(yt2[:], "l3a_yt2", yn2[:], "l3a_yn2", "l3ae")
            p.emit("vector", lambda e: e.tensor_tensor(
                yn2[:], yn2[:], xt2[:], op=ALU.add),
                reads=("l3a_yn2", "l3a_xt2"), writes=("l3a_yn2",))
            p.dma("sync", xout_t[t], yn2[:], reads=("l3a_yn2",), writes=("l3a_st",))
        p.pop_scope()
        p.pop_scope()
    p.pop_scope()


def build_l3_attn(consts):
    p = Prog()
    xin = p.dram_in("xin", [1024, TCORE])
    xout = p.dram_out("xout", [1024, TCORE])
    c_aps = consts_aps(p, consts)
    cx = Ctx(p, c_aps)
    W = {
        "wqT": p.dram_in("w_wqT", [1024, 1024]),
        "wkT": p.dram_in("w_wkT", [1024, 1024]),
        "wvT": p.dram_in("w_wvT", [1024, 1024]),
        "woT": p.dram_in("w_woT", [1024, 1024]),
        "bq_pc": p.dram_in("w_bq_pc", [128, 8]),
        "bkv_pc": p.dram_in("w_bkv_pc", [128, 16]),
        "bo_pc": p.dram_in("w_bo_pc", [128, 8]),
        "sw1T": p.dram_in("w_sw1T", [1024, 512]),
        "sw2T": p.dram_in("w_sw2T", [512, 3]),
        "pw1T": p.dram_in("w_pw1T", [3, 1024, 512]),
        "pw2T": p.dram_in("w_pw2T", [3, 512, 16]),
        "sb_pc": p.dram_in("w_sb_pc", [128, 32]),
    }
    emit_l3_attn(p, cx, xin, xout, W, c_aps)
    nc = p.finalize()
    return nc, p


def l3_attn_in_map(p, consts, inputs):
    g = lambda n: np.asarray(inputs[n], np.float32)
    sbv = np.zeros((128, 32), np.float32)
    sbv[:, 0:4] = pc_n(g("L3_attn_sb1"), 4)
    sbv[0:3, 4] = g("L3_attn_sb2")
    pb1 = g("L3_attn_pb1")
    for pi in range(3):
        sbv[:, 5 + pi * 4:9 + pi * 4] = pc_n(pb1[pi], 4)
    pb2 = g("L3_attn_pb2")
    for pi in range(3):
        sbv[0:16, 17 + pi] = pb2[pi]
    return {
        "c_fvals_in": p.fconst_host_array(),
        "w_wqT": np.ascontiguousarray(g("L3_attn_wq").T),
        "w_wkT": np.ascontiguousarray(g("L3_attn_wk").T),
        "w_wvT": np.ascontiguousarray(g("L3_attn_wv").T),
        "w_woT": np.ascontiguousarray(g("L3_attn_wo").T),
        "w_bq_pc": pc(g("L3_attn_bq")),
        "w_bkv_pc": np.concatenate([pc(g("L3_attn_bk")), pc(g("L3_attn_bv"))], 1),
        "w_bo_pc": pc(g("L3_attn_bo")),
        "w_sw1T": np.ascontiguousarray(g("L3_attn_sw1").T),
        "w_sw2T": np.ascontiguousarray(g("L3_attn_sw2").T),
        "w_pw1T": np.ascontiguousarray(g("L3_attn_pw1").transpose(0, 2, 1)),
        "w_pw2T": np.ascontiguousarray(g("L3_attn_pw2").transpose(0, 2, 1)),
        "w_sb_pc": sbv,
        **consts_in_map(consts),
    }


def pc_n(bias, nch):
    return np.ascontiguousarray(
        np.asarray(bias, np.float32).reshape(nch, 128).T)


def emit_l1_attn(p: Prog, cx: Ctx, xin, xout, W, C):
    p.push_scope()
    rcnt = p.sb("l1a_rc", [128, 2048], BF16)
    p.dma("sync", rcnt[:], C["l1a_rcounts"], writes=("l1a_rc",))
    xin_t = xin.rearrange("(c p) (n t) -> n p c t", p=128, t=512)
    xout_t = xout.rearrange("(c p) (n t) -> n p c t", p=128, t=512)
    p.fconst(0.125)

    for b in range(2):
        p.push_scope()
        oacc = p.sb("l1a_o", [128, 8, 2048], BF16)
        p.push_scope()
        kf = p.sb("l1a_k", [128, 8, 2048], BF16)
        vf = p.sb("l1a_v", [128, 8, 2048], BF16)
        xnb_s = p.sb("l1a_xnb", [128, 8, 2048], BF16)
        wb = p.sb("l1a_w", [128, 8, 1024], BF16)
        wst = p.sb("l1a_wst", [128, 2, 1024], F32)
        bqkv = p.sb("l1a_b", [128, 24], F32)
        p.push_scope()
        xt = p.sb("l1a_xt", [128, 8, 512], F32)
        ps_k = p.ps("l1a_psk", [128, 4, 512], F32)
        p.dma("sync", bqkv[:], W["bqkv_pc"], writes=("l1a_b",))
        for wi, (wap, dst, dkey, bc) in enumerate((
                (W["WkT"], kf, "l1a_kf", 8), (W["WvT"], vf, "l1a_vf", 16))):
            load_weight_bf16(p, wst, "l1a_wst", wb[:], "l1a_w", wap, 8, 1024)
            for ts in range(4):
                t = b * 4 + ts
                if wi == 0:
                    p.dma("sync", xt[:], xin_t[t], reads=("l1a_xt",),
                          writes=("l1a_xt",))
                    cx.ln_fm(xt[:], "l1a_xt",
                             xnb_s[:, :, ts * 512:(ts + 1) * 512], "l1a_xnbs",
                             "l1at", sub_inplace=True)
                for oc in range(8):
                    bnk = oc % 4
                    for k in range(8):
                        p.emit("tensor", lambda e, oc=oc, k=k, bnk=bnk, ts=ts: e.matmul(
                            ps_k[:, bnk, :], wb[:, k, oc * 128:(oc + 1) * 128],
                            xnb_s[:, k, ts * 512:(ts + 1) * 512],
                            start=(k == 0), stop=(k == 7)),
                            reads=("l1a_w", "l1a_xnbs"), writes=(("l1a_psk", bnk),))
                    p.emit("scalar", lambda e, oc=oc, bnk=bnk, dst=dst, ts=ts, bc=bc, dkey=dkey: e.activation(
                        out=dst[:, oc, ts * 512:(ts + 1) * 512],
                        in_=ps_k[:, bnk, :], func=AF.Identity,
                        bias=bqkv[:, bc + oc:bc + oc + 1]),
                        reads=(("l1a_psk", bnk), "l1a_b"), writes=(dkey,))
        p.pop_scope()
        p.push_scope()
        kh = p.sb("l1a_kh", [64, 2048], BF16)
        vh = p.sb("l1a_vh", [64, 2048], BF16)
        vt_sb = p.sb("l1a_vt", [128, 16, 64], BF16)
        qh = p.sb("l1a_qh", [64, 2048], BF16)
        s_sb = p.sb("l1a_s", [128, 256], F32)
        p0 = p.sb("l1a_p0", [1, 2048], F32)
        p0col = p.sb("l1a_p0c", [1, 4], F32)
        gprod = p.sb("l1a_gp", [64, 512], F32)
        gtmp = p.sb("l1a_gt", [64, 2], F32)
        pbf = p.sb("l1a_pbf", [128, 256], BF16)
        ptev = p.sb("l1a_ptev", [128, 128], BF16)
        qcol = p.sb("l1a_qcol", [128, 4], F32)
        ps_s = p.ps("l1a_pss", [128, 4, 512], F32)
        ps_o = p.ps("l1a_pso", [128, 1, 512], F32)
        ps_t2 = p.ps("l1a_pst2", [128, 1024], BF16)
        load_weight_bf16(p, wst, "l1a_wst", wb[:], "l1a_w", W["WqT"], 8, 1024)
        p.emit("vector", lambda e: e.memset(oacc[:], 0.0), writes=("l1a_oacc",))
        for h in range(16):
            hb, hc = (h % 2) * 64, h // 2
            p.emit("vector", lambda e, hb=hb, hc=hc: e.tensor_copy(
                kh[:], kf[hb:hb + 64, hc, :]), reads=("l1a_kf",),
                writes=("l1a_kh",))
            p.emit("vector", lambda e, hb=hb, hc=hc: e.tensor_copy(
                vh[:], vf[hb:hb + 64, hc, :]), reads=("l1a_vf",),
                writes=("l1a_vh",))
            for sl in range(4):
                for k in range(8):
                    p.emit("tensor", lambda e, sl=sl, k=k, h=h: e.matmul(
                        ps_s[0:64, sl, :], wb[:, k, h * 64:(h + 1) * 64],
                        xnb_s[:, k, sl * 512:(sl + 1) * 512],
                        start=(k == 0), stop=(k == 7)),
                        reads=("l1a_w", "l1a_xnbs"), writes=(("l1a_pss", sl),))
                p.emit("scalar", lambda e, sl=sl, hb=hb, hc=hc: e.activation(
                    out=qh[:, sl * 512:(sl + 1) * 512], in_=ps_s[0:64, sl, :],
                    func=AF.Identity, bias=bqkv[hb:hb + 64, hc:hc + 1]),
                    reads=(("l1a_pss", sl), "l1a_b"), writes=("l1a_qh",))
            for c in range(16):
                w_ = 128 if c < 15 else 127
                p.emit("tensor", lambda e, c=c, w_=w_: e.transpose(
                    ps_t2[0:w_, 0:64], vh[:, 1 + c * 128:1 + c * 128 + w_],
                    cx.iden[0:64, 0:64]), reads=("l1a_vh", "cx_iden"),
                    writes=("l1a_pst2",))
                p.emit("vector", lambda e, c=c, w_=w_: e.tensor_copy(
                    vt_sb[0:w_, c, :], ps_t2[0:w_, 0:64]),
                    reads=("l1a_pst2",), writes=("l1a_vt",))
            for sl in range(4):
                p.emit("tensor", lambda e, sl=sl: e.matmul(
                    ps_s[0:1, sl, :], qh[:, 0:1], kh[:, sl * 512:(sl + 1) * 512],
                    start=True, stop=True),
                    reads=("l1a_qh", "l1a_kh"), writes=(("l1a_pss", sl),))
                p.emit("scalar", lambda e, sl=sl: e.activation(
                    out=p0[0:1, sl * 512:(sl + 1) * 512], in_=ps_s[0:1, sl, :],
                    func=AF.Identity, scale=0.125),
                    reads=(("l1a_pss", sl),), writes=("l1a_p0",))
            p.emit("vector", lambda e: e.tensor_reduce(
                p0col[0:1, 0:1], p0[0:1, :], axis=AX.X, op=ALU.max),
                reads=("l1a_p0",), writes=("l1a_p0c",))
            p.emit("vector", lambda e: e.tensor_scalar(
                p0col[0:1, 1:2], p0col[0:1, 0:1], -1.0, None, op0=ALU.mult),
                reads=("l1a_p0c",), writes=("l1a_p0c",))
            p.emit("scalar", lambda e: e.activation(
                out=p0[0:1, :], in_=p0[0:1, :], func=AF.Exp,
                bias=p0col[0:1, 1:2]), reads=("l1a_p0", "l1a_p0c"),
                writes=("l1a_p0",))
            p.emit("vector", lambda e: e.tensor_reduce(
                p0col[0:1, 2:3], p0[0:1, :], axis=AX.X, op=ALU.add),
                reads=("l1a_p0",), writes=("l1a_p0c",))
            p.emit("vector", lambda e: e.reciprocal(
                p0col[0:1, 3:4], p0col[0:1, 2:3]),
                reads=("l1a_p0c",), writes=("l1a_p0c",))
            p.emit("vector", lambda e: e.tensor_scalar(
                p0[0:1, :], p0[0:1, :], p0col[0:1, 3:4], None, op0=ALU.mult),
                reads=("l1a_p0", "l1a_p0c"), writes=("l1a_p0",))
            p.emit("vector", lambda e: e.memset(gtmp[:, 0:1], 0.0),
                   writes=("l1a_gt",))
            for sl in range(4):
                p.emit("tensor", lambda e, sl=sl: e.matmul(
                    ps_s[0:64, sl, :],
                    cx.ones_row[0:1, 0:64], p0[0:1, sl * 512:(sl + 1) * 512],
                    start=True, stop=True),
                    reads=("l1a_p0", "cx_onesrow"), writes=(("l1a_pss", sl),))
                p.emit("vector", lambda e, sl=sl: e.tensor_tensor(
                    gprod[:, :], vh[:, sl * 512:(sl + 1) * 512],
                    ps_s[0:64, sl, :], op=ALU.mult),
                    reads=("l1a_vh", ("l1a_pss", sl)), writes=("l1a_gp",))
                p.emit("vector", lambda e: e.tensor_reduce(
                    gtmp[:, 1:2], gprod[:, :], axis=AX.X, op=ALU.add),
                    reads=("l1a_gp",), writes=("l1a_gt",))
                p.emit("vector", lambda e: e.tensor_tensor(
                    gtmp[:, 0:1], gtmp[:, 0:1], gtmp[:, 1:2], op=ALU.add),
                    reads=("l1a_gt",), writes=("l1a_gt",))
            p.emit("vector", lambda e, hb=hb, hc=hc: e.tensor_copy(
                oacc[hb:hb + 64, hc, 0:1], gtmp[:, 0:1]),
                reads=("l1a_gt",), writes=("l1a_oacc",))
            for w in range(14):
                base = 1 + w * 128
                for qt in range(2):
                    p.emit("tensor", lambda e, w=w, qt=qt, base=base: e.matmul(
                        ps_s[:, 0, 0:256], qh[:, base + qt * 128:base + qt * 128 + 128],
                        kh[:, base:base + 256], start=True, stop=True),
                        reads=("l1a_qh", "l1a_kh"), writes=(("l1a_pss", 0),))
                    p.emit("scalar", lambda e: e.activation(
                        out=s_sb[:, 0:256], in_=ps_s[:, 0, 0:256],
                        func=AF.Identity, scale=0.125),
                        reads=(("l1a_pss", 0),), writes=("l1a_s",))
                    p.emit("vector", lambda e: e.tensor_reduce(
                        qcol[:, 0:1], s_sb[:, 0:256], axis=AX.X, op=ALU.max),
                        reads=("l1a_s",), writes=("l1a_qcol",))
                    p.emit("vector", lambda e: e.tensor_scalar(
                        qcol[:, 1:2], qcol[:, 0:1], -1.0, None, op0=ALU.mult),
                        reads=("l1a_qcol",), writes=("l1a_qcol",))
                    p.emit("scalar", lambda e: e.activation(
                        out=s_sb[:, 0:256], in_=s_sb[:, 0:256], func=AF.Exp,
                        bias=qcol[:, 1:2]), reads=("l1a_s", "l1a_qcol"),
                        writes=("l1a_s",))
                    p.emit("vector", lambda e: e.tensor_reduce(
                        qcol[:, 2:3], s_sb[:, 0:256], axis=AX.X, op=ALU.add),
                        reads=("l1a_s",), writes=("l1a_qcol",))
                    p.emit("vector", lambda e: e.reciprocal(
                        qcol[:, 3:4], qcol[:, 2:3]),
                        reads=("l1a_qcol",), writes=("l1a_qcol",))
                    p.emit("vector", lambda e: e.tensor_scalar(
                        pbf[:, 0:256], s_sb[:, 0:256], qcol[:, 3:4], None,
                        op0=ALU.mult), reads=("l1a_s", "l1a_qcol"),
                        writes=("l1a_pbf",))
                    for kc in range(2):
                        p.emit("tensor", lambda e, kc=kc: e.transpose(
                            ps_t2[0:128, 0:128], pbf[:, kc * 128:(kc + 1) * 128],
                            cx.iden[:]), reads=("l1a_pbf", "cx_iden"),
                            writes=("l1a_pst2",))
                        p.emit("vector", lambda e: e.tensor_copy(
                            ptev[:], ps_t2[0:128, 0:128]),
                            reads=("l1a_pst2",), writes=("l1a_ptev",))
                        p.emit("tensor", lambda e, kc=kc, w=w: e.matmul(
                            ps_o[0:64, 0, 0:128], vt_sb[:, w + kc, :], ptev[:],
                            start=(kc == 0), stop=(kc == 1)),
                            reads=("l1a_vt", "l1a_ptev"), writes=(("l1a_pso", 0),))
                    p.emit("vector", lambda e, w=w, qt=qt, hb=hb, hc=hc: e.tensor_tensor(
                        oacc[hb:hb + 64, hc, 1 + (w + qt) * 128:1 + (w + qt) * 128 + 128],
                        oacc[hb:hb + 64, hc, 1 + (w + qt) * 128:1 + (w + qt) * 128 + 128],
                        ps_o[0:64, 0, 0:128], op=ALU.add),
                        reads=("l1a_oacc", ("l1a_pso", 0)), writes=("l1a_oacc",))
        p.emit("vector", lambda e: e.tensor_tensor(
            oacc[:], oacc[:], rcnt[:].unsqueeze(1).broadcast_to([128, 8, 2048]),
            op=ALU.mult), reads=("l1a_oacc", "l1a_rc"), writes=("l1a_oacc",))
        p.pop_scope()
        p.pop_scope()
        p.push_scope()
        wob = p.sb("l1a_wo", [128, 8, 1024], BF16)
        wst3 = p.sb("l1a_wst3", [128, 2, 1024], F32)
        bo = p.sb("l1a_bo", [128, 8], F32)
        xt2 = p.sb("l1a_xt2", [128, 8, 512], F32)
        yt2 = p.sb("l1a_yt2", [128, 8, 512], F32)
        yn2 = p.sb("l1a_yn2", [128, 8, 512], F32)
        ps_y2 = p.ps("l1a_psy2", [128, 4, 512], F32)
        load_weight_bf16(p, wst3, "l1a_wst3", wob[:], "l1a_wo", W["WoT"], 8, 1024)
        p.dma("sync", bo[:], W["bo_pc"], writes=("l1a_bo",))
        for ts in range(4):
            t = b * 4 + ts
            p.dma("sync", xt2[:], xin_t[t], reads=("l1a_xt2",), writes=("l1a_xt2",))
            for oc in range(8):
                bnk = oc % 4
                for k in range(8):
                    p.emit("tensor", lambda e, oc=oc, k=k, bnk=bnk, ts=ts: e.matmul(
                        ps_y2[:, bnk, :], wob[:, k, oc * 128:(oc + 1) * 128],
                        oacc[:, k, ts * 512:(ts + 1) * 512],
                        start=(k == 0), stop=(k == 7)),
                        reads=("l1a_wo", "l1a_oacc"), writes=(("l1a_psy2", bnk),))
                p.emit("scalar", lambda e, oc=oc, bnk=bnk: e.activation(
                    out=yt2[:, oc, :], in_=ps_y2[:, bnk, :], func=AF.Identity,
                    bias=bo[:, oc:oc + 1]),
                    reads=(("l1a_psy2", bnk), "l1a_bo"), writes=("l1a_yt2",))
            cx.ln_fm(yt2[:], "l1a_yt2", yn2[:], "l1a_yn2", "l1ae")
            p.emit("vector", lambda e: e.tensor_tensor(
                yn2[:], yn2[:], xt2[:], op=ALU.add),
                reads=("l1a_yn2", "l1a_xt2"), writes=("l1a_yn2",))
            p.dma("sync", xout_t[t], yn2[:], reads=("l1a_yn2",), writes=("l1a_st",))
        p.pop_scope()
        p.pop_scope()
    p.pop_scope()


def build_l1_attn(consts):
    p = Prog()
    xin = p.dram_in("xin", [1024, TCORE])
    xout = p.dram_out("xout", [1024, TCORE])
    c_aps = consts_aps(p, consts)
    cx = Ctx(p, c_aps)
    W = {
        "WqT": p.dram_in("w_WqT", [1024, 1024]),
        "WkT": p.dram_in("w_WkT", [1024, 1024]),
        "WvT": p.dram_in("w_WvT", [1024, 1024]),
        "WoT": p.dram_in("w_WoT", [1024, 1024]),
        "bqkv_pc": p.dram_in("w_bqkv_pc", [128, 24]),
        "bo_pc": p.dram_in("w_bo_pc", [128, 8]),
    }
    emit_l1_attn(p, cx, xin, xout, W, c_aps)
    nc = p.finalize()
    return nc, p


def l1_attn_in_map(p, consts, inputs):
    g = lambda n: np.asarray(inputs[n], np.float32)
    return {
        "c_fvals_in": p.fconst_host_array(),
        "w_WqT": np.ascontiguousarray(g("L1_attn_Wq")),
        "w_WkT": np.ascontiguousarray(g("L1_attn_Wk")),
        "w_WvT": np.ascontiguousarray(g("L1_attn_Wv")),
        "w_WoT": np.ascontiguousarray(g("L1_attn_Wo")),
        "w_bqkv_pc": np.concatenate(
            [pc(g("L1_attn_bq")), pc(g("L1_attn_bk")), pc(g("L1_attn_bv"))], 1),
        "w_bo_pc": pc(g("L1_attn_bo")),
        **consts_in_map(consts),
    }


def emit_l0_ffn(p: Prog, cx: Ctx, xin, xout, W, C):
    p.push_scope()
    qc_t = p.sb("l0f_qc", [128, 2048], BF16)
    qs_t = p.sb("l0f_qs", [128, 2048], BF16)
    kc_t = p.sb("l0f_kc", [128, 2048], BF16)
    ks_t = p.sb("l0f_ks", [128, 2048], BF16)
    r2t = p.sb("l0f_r2t", [128, 128], BF16)
    ones2 = p.sb("l0f_o2", [128, 2], BF16)
    bo2 = p.sb("l0f_bo2", [2, 128], F32)
    cu01 = p.sb("l0f_cu01", [128, 128], F32)
    cu10 = p.sb("l0f_cu10", [128, 128], F32)
    mask01 = p.sb("l0f_m01", [128, 128], F32)
    mask10 = p.sb("l0f_m10", [128, 128], F32)
    p.dma("sync", qc_t[:], C["rot_qc"], writes=("l0f_qc",))
    p.dma("sync", qs_t[:], C["rot_qs"], writes=("l0f_qs",))
    p.dma("sync", kc_t[:], C["rot_kc"], writes=("l0f_kc",))
    p.dma("sync", ks_t[:], C["rot_ks"], writes=("l0f_ks",))
    p.dma("sync", r2t[:], C["rot_r2t"], writes=("l0f_r2t",))
    p.dma("sync", ones2[:], C["ones2col"], writes=("l0f_o2",))
    p.dma("sync", bo2[:], C["blockones2"], writes=("l0f_bo2",))
    p.dma("sync", cu01[:], C["upper01"], writes=("l0f_cu01",))
    p.dma("sync", cu10[:], C["upper10"], writes=("l0f_cu10",))
    p.dma("sync", mask01[:], C["causal01"], writes=("l0f_m01",))
    p.dma("sync", mask10[:], C["causal10"], writes=("l0f_m10",))
    p.fconst(1e-12)
    p.fconst(8.0)
    p.fconst(-1e30)
    xin_t = xin.rearrange("(c p) (n t) -> n p c t", p=128, t=512)
    xout_t = xout.rearrange("(c p) (n t) -> n p c t", p=128, t=512)

    for b in range(2):
        p.push_scope()
        xr = p.sb("l0f_x", [128, 8, 2048], F32)
        p.push_scope()
        xt0 = p.sb("l0f_xt0", [128, 8, 512], F32)
        for ts in range(4):
            t = b * 4 + ts
            p.dma("sync", xt0[:], xin_t[t], reads=("l0f_xt0",), writes=("l0f_xt0",))
            cx.ln_fm(xt0[:], "l0f_xt0", xr[:, :, ts * 512:(ts + 1) * 512],
                     "l0f_x", "l0fi")
        p.pop_scope()

        for li in range(2):
            p.push_scope()
            qf = p.sb("l0f_qf", [128, 4, 2048], BF16)
            kfb = p.sb("l0f_kfb", [128, 4, 2048], BF16)
            vfb = p.sb("l0f_vfb", [128, 4, 2048], BF16)
            oatt = p.sb("l0f_oatt", [128, 4, 2048], BF16)
            p.push_scope()
            hb = p.sb("l0f_hb", [128, 8, 512], BF16)
            wqkv = p.sb("l0f_wqkv", [128, 8, 512], BF16)
            wst = p.sb("l0f_wst", [128, 2, 1024], F32)
            qsc = p.sb("l0f_qsc", [128, 2], F32)
            rawq = p.sb("l0f_rawq", [128, 512], F32)
            sqq = p.sb("l0f_sqq", [128, 512], BF16)
            nrm = p.sb("l0f_nrm", [2, 512], F32)
            t1b = p.sb("l0f_t1", [128, 512], F32)
            ps_q = p.ps("l0f_psq", [128, 2, 512], F32)
            ps_r = p.ps("l0f_psr", [128, 2, 512], F32)
            p.dma("sync", qsc[:], W["qksc"][li], writes=("l0f_qsc",))
            def build_hb(sl):
                lo = sl * 512
                p.emit("vector", lambda e, lo=lo: e.tensor_copy(
                    hb[:, 0:4, :], xr[:, 0:4, lo:lo + 512]), reads=("l0f_x",),
                    writes=("l0f_hb",))
                if lo == 0:
                    p.emit("vector", lambda e: e.memset(
                        hb[:, 4:8, 0:1], 0.0), writes=("l0f_hb",))
                    p.emit("vector", lambda e: e.tensor_copy(
                        hb[:, 4:8, 1:512], xr[:, 4:8, 0:511]), reads=("l0f_x",),
                        writes=("l0f_hb",))
                else:
                    p.emit("vector", lambda e, lo=lo: e.tensor_copy(
                        hb[:, 4:8, :], xr[:, 4:8, lo - 1:lo + 511]),
                        reads=("l0f_x",), writes=("l0f_hb",))
            for part, dst, dkey in ((0, qf, "l0f_qff"), (1, kfb, "l0f_kff"),
                                    (2, vfb, "l0f_vff")):
                load_weight_bf16(p, wst, "l0f_wst", wqkv[:], "l0f_wqkv",
                                 W["wqkv"][li][:, part * 512:(part + 1) * 512],
                                 8, 512)
                for sl in range(4):
                    build_hb(sl)
                    for oc in range(4):
                        bnk = oc % 2
                        for k in range(8):
                            p.emit("tensor", lambda e, oc=oc, k=k, bnk=bnk: e.matmul(
                                ps_q[:, bnk, :], wqkv[:, k, oc * 128:(oc + 1) * 128],
                                hb[:, k, :],
                                start=(k == 0), stop=(k == 7)),
                                reads=("l0f_wqkv", "l0f_hb"), writes=(("l0f_psq", bnk),))
                        if part == 2:
                            p.emit("vector", lambda e, oc=oc, sl=sl, bnk=bnk, dst=dst, dkey=dkey: e.tensor_copy(
                                dst[:, oc, sl * 512:(sl + 1) * 512], ps_q[:, bnk, :]),
                                reads=(("l0f_psq", bnk),), writes=(dkey,))
                            continue
                        p.emit("vector", lambda e, bnk=bnk: e.tensor_copy(
                            rawq[:], ps_q[:, bnk, :]),
                            reads=(("l0f_psq", bnk),), writes=("l0f_rawq",))
                        p.emit("scalar", lambda e: e.activation(
                            out=sqq[:], in_=rawq[:], func=AF.Square),
                            reads=("l0f_rawq",), writes=("l0f_sqq",))
                        p.emit("tensor", lambda e: e.matmul(
                            ps_r[0:2, 0, :], ones2[:], sqq[:],
                            start=True, stop=True),
                            reads=("l0f_sqq", "l0f_o2"), writes=(("l0f_psr", 0),))
                        p.emit("scalar", lambda e: e.activation(
                            out=nrm[:, :], in_=ps_r[0:2, 0, :], func=AF.Sqrt),
                            reads=(("l0f_psr", 0),), writes=("l0f_nrm",))
                        p.emit("vector", lambda e: e.tensor_scalar(
                            nrm[:, :], nrm[:, :], 1e-12, None, op0=ALU.max),
                            reads=("l0f_nrm",), writes=("l0f_nrm",))
                        p.emit("vector", lambda e: e.reciprocal(
                            nrm[:, :], nrm[:, :]), reads=("l0f_nrm",),
                            writes=("l0f_nrm",))
                        p.emit("tensor", lambda e: e.matmul(
                            ps_r[:, 0, :], bo2[:], nrm[:, :],
                            start=True, stop=True),
                            reads=("l0f_nrm", "l0f_bo2"), writes=(("l0f_psr", 0),))
                        p.emit("vector", lambda e: e.tensor_tensor(
                            rawq[:], rawq[:], ps_r[:, 0, :], op=ALU.mult),
                            reads=("l0f_rawq", ("l0f_psr", 0)), writes=("l0f_rawq",))
                        p.emit("scalar", lambda e, part=part: e.activation(
                            out=sqq[:], in_=rawq[:], func=AF.Identity,
                            scale=qsc[:, part:part + 1]),
                            reads=("l0f_rawq", "l0f_qsc"), writes=("l0f_sqq",))
                        p.emit("tensor", lambda e: e.matmul(
                            ps_r[:, 1, :], r2t[:], sqq[:], start=True, stop=True),
                            reads=("l0f_sqq", "l0f_r2t"), writes=(("l0f_psr", 1),))
                        ctab, ckey = (qc_t, "l0f_qc") if part == 0 else (kc_t, "l0f_kc")
                        stab, skey = (qs_t, "l0f_qs") if part == 0 else (ks_t, "l0f_ks")
                        p.emit("vector", lambda e, sl=sl, ctab=ctab, ckey=ckey: e.tensor_tensor(
                            rawq[:], sqq[:], ctab[:, sl * 512:(sl + 1) * 512],
                            op=ALU.mult), reads=("l0f_sqq", ckey),
                            writes=("l0f_rawq",))
                        p.emit("vector", lambda e, sl=sl, stab=stab, skey=skey: e.tensor_tensor(
                            t1b[:], ps_r[:, 1, :],
                            stab[:, sl * 512:(sl + 1) * 512], op=ALU.mult),
                            reads=(("l0f_psr", 1), skey), writes=("l0f_t1",))
                        p.emit("vector", lambda e, oc=oc, sl=sl, dst=dst, dkey=dkey: e.tensor_tensor(
                            dst[:, oc, sl * 512:(sl + 1) * 512], rawq[:], t1b[:],
                            op=ALU.add), reads=("l0f_rawq", "l0f_t1"),
                            writes=(dkey,))
            p.pop_scope()
            p.push_scope()
            vt_sb = p.sb("l0f_vt", [128, 16, 64], BF16)
            kh = p.sb("l0f_kh", [64, 2048], BF16)
            vh = p.sb("l0f_vh", [64, 2048], BF16)
            qh = p.sb("l0f_qh", [64, 2048], BF16)
            s_sb = [p.sb(f"l0f_s{i}", [128, 640], F32) for i in range(2)]
            pbf = [p.sb(f"l0f_pbf{i}", [128, 640], BF16) for i in range(2)]
            ptev = [[p.sb(f"l0f_ptev{i}_{j}", [128, 128], BF16)
                     for j in range(5)] for i in range(2)]
            qcol = [p.sb(f"l0f_qcol{i}", [128, 4], F32) for i in range(2)]
            ps_s = p.ps("l0f_pss", [128, 2, 512], F32)
            ps_o = p.ps("l0f_pso", [128, 2, 512], F32)
            ps_t2 = p.ps("l0f_pst2", [128, 2, 1024], BF16)
            for h in range(8):
                hbp, hcc = (h % 2) * 64, h // 2
                p.emit("vector", lambda e, hbp=hbp, hcc=hcc: e.tensor_copy(
                    qh[:], qf[hbp:hbp + 64, hcc, :]), reads=("l0f_qff",),
                    writes=("l0f_qh",))
                p.emit("vector", lambda e, hbp=hbp, hcc=hcc: e.tensor_copy(
                    kh[:], kfb[hbp:hbp + 64, hcc, :]), reads=("l0f_kff",),
                    writes=("l0f_kh",))
                p.emit("vector", lambda e, hbp=hbp, hcc=hcc: e.tensor_copy(
                    vh[:], vfb[hbp:hbp + 64, hcc, :]), reads=("l0f_vff",),
                    writes=("l0f_vh",))
                for c in range(16):
                    cp = c % 2
                    p.emit("tensor", lambda e, c=c, cp=cp: e.transpose(
                        ps_t2[:, cp, 0:64], vh[:, c * 128:(c + 1) * 128],
                        cx.iden[0:64, 0:64]), reads=("l0f_vh", "cx_iden"),
                        writes=(("l0f_pst2", cp),))
                    p.emit("vector", lambda e, c=c, cp=cp: e.tensor_copy(
                        vt_sb[:, c, :], ps_t2[:, cp, 0:64]),
                        reads=(("l0f_pst2", cp),), writes=("l0f_vt",))
                for w in range(4):
                    for qt in range(4):
                        qa = w * 4 + qt
                        pr = qa % 2
                        sK, pK, qK = f"l0f_s{pr}", f"l0f_pbf{pr}", f"l0f_qcol{pr}"
                        sb_, pb_, qc_ = s_sb[pr], pbf[pr], qcol[pr]
                        kc_lo = max(0, qa - 4)
                        nkc = qa - kc_lo + 1
                        for kc in range(kc_lo, qa + 1):
                            j = kc - kc_lo
                            p.emit("tensor", lambda e, qa=qa, kc=kc, j=j: e.matmul(
                                ps_s[:, j // 4, (j % 4) * 128:(j % 4) * 128 + 128],
                                qh[:, qa * 128:(qa + 1) * 128],
                                kh[:, kc * 128:(kc + 1) * 128],
                                start=True, stop=True),
                                reads=("l0f_qh", "l0f_kh"),
                                writes=(("l0f_pss", j // 4),))
                        for g4 in range((nkc + 3) // 4):
                            w_ = min(512, nkc * 128 - g4 * 512)
                            p.emit("scalar", lambda e, g4=g4, w_=w_, sb_=sb_: e.activation(
                                out=sb_[:, g4 * 512:g4 * 512 + w_],
                                in_=ps_s[:, g4, 0:w_], func=AF.Identity,
                                scale=8.0),
                                reads=(("l0f_pss", g4),), writes=(sK,))
                        if qa - kc_lo == 4:
                            p.emit("vector", lambda e, sb_=sb_: e.tensor_tensor(
                                sb_[:, 0:128], sb_[:, 0:128], cu01[:],
                                op=ALU.mult), reads=(sK, "l0f_cu01"),
                                writes=(sK,))
                            p.emit("vector", lambda e, sb_=sb_: e.scalar_tensor_tensor(
                                sb_[:, 0:128], cu10[:], -1e30, sb_[:, 0:128],
                                op0=ALU.mult, op1=ALU.add),
                                reads=(sK, "l0f_cu10"), writes=(sK,))
                        off = (qa - kc_lo) * 128
                        p.emit("vector", lambda e, off=off, sb_=sb_: e.tensor_tensor(
                            sb_[:, off:off + 128], sb_[:, off:off + 128],
                            mask01[:], op=ALU.mult),
                            reads=(sK, "l0f_m01"), writes=(sK,))
                        p.emit("vector", lambda e, off=off, sb_=sb_: e.scalar_tensor_tensor(
                            sb_[:, off:off + 128], mask10[:], -1e30,
                            sb_[:, off:off + 128], op0=ALU.mult, op1=ALU.add),
                            reads=(sK, "l0f_m10"), writes=(sK,))
                        L = nkc * 128
                        p.emit("vector", lambda e, L=L, sb_=sb_, qc_=qc_: e.tensor_reduce(
                            qc_[:, 0:1], sb_[:, 0:L], axis=AX.X, op=ALU.max),
                            reads=(sK,), writes=(qK,))
                        p.emit("vector", lambda e, qc_=qc_: e.tensor_scalar(
                            qc_[:, 1:2], qc_[:, 0:1], -1.0, None, op0=ALU.mult),
                            reads=(qK,), writes=(qK,))
                        p.emit("scalar", lambda e, L=L, sb_=sb_, qc_=qc_: e.activation(
                            out=sb_[:, 0:L], in_=sb_[:, 0:L], func=AF.Exp,
                            bias=qc_[:, 1:2]), reads=(sK, qK),
                            writes=(sK,))
                        p.emit("vector", lambda e, L=L, sb_=sb_, qc_=qc_: e.tensor_reduce(
                            qc_[:, 2:3], sb_[:, 0:L], axis=AX.X, op=ALU.add),
                            reads=(sK,), writes=(qK,))
                        p.emit("vector", lambda e, qc_=qc_: e.reciprocal(
                            qc_[:, 3:4], qc_[:, 2:3]),
                            reads=(qK,), writes=(qK,))
                        p.emit("vector", lambda e, L=L, sb_=sb_, pb_=pb_, qc_=qc_: e.tensor_scalar(
                            pb_[:, 0:L], sb_[:, 0:L], qc_[:, 3:4], None,
                            op0=ALU.mult), reads=(sK, qK),
                            writes=(pK,))
                        for kc in range(kc_lo, qa + 1):
                            j = kc - kc_lo
                            jp = j % 2
                            p.emit("tensor", lambda e, j=j, jp=jp, pb_=pb_: e.transpose(
                                ps_t2[:, jp, 0:128], pb_[:, j * 128:(j + 1) * 128],
                                cx.iden[:]), reads=(pK, "cx_iden"),
                                writes=(("l0f_pst2", jp),))
                            p.emit("vector", lambda e, jp=jp, pr=pr, j=j: e.tensor_copy(
                                ptev[pr][j][:], ps_t2[:, jp, 0:128]),
                                reads=(("l0f_pst2", jp),), writes=(f"l0f_ptev{pr}_{j}",))
                        for kc in range(kc_lo, qa + 1):
                            j = kc - kc_lo
                            p.emit("tensor", lambda e, kc=kc, j=j, nkc=nkc, pr=pr: e.matmul(
                                ps_o[0:64, pr, 0:128], vt_sb[:, kc, :], ptev[pr][j][:],
                                start=(j == 0), stop=(j == nkc - 1)),
                                reads=("l0f_vt", f"l0f_ptev{pr}_{j}"), writes=(("l0f_pso", pr),))
                        p.emit("vector", lambda e, qa=qa, hbp=hbp, hcc=hcc, pr=pr: e.tensor_copy(
                            oatt[hbp:hbp + 64, hcc, qa * 128:(qa + 1) * 128],
                            ps_o[0:64, pr, 0:128]),
                            reads=(("l0f_pso", pr),), writes=("l0f_oatt",))
            p.pop_scope()
            p.push_scope()
            wob = p.sb("l0f_wo", [128, 4, 1024], BF16)
            wst2 = p.sb("l0f_wst2", [128, 2, 1024], F32)
            tmp = p.sb("l0f_tmp", [128, 512], F32)
            ps_y = p.ps("l0f_psy", [128, 2, 512], F32)
            load_weight_bf16(p, wst2, "l0f_wst2", wob[:], "l0f_wo",
                             W["wo"][li], 4, 1024)
            for sl in range(4):
                for oc in range(8):
                    bnk = oc % 2
                    for k in range(4):
                        p.emit("tensor", lambda e, sl=sl, oc=oc, k=k, bnk=bnk: e.matmul(
                            ps_y[:, bnk, :], wob[:, k, oc * 128:(oc + 1) * 128],
                            oatt[:, k, sl * 512:(sl + 1) * 512],
                            start=(k == 0), stop=(k == 3)),
                            reads=("l0f_wo", "l0f_oatt"), writes=(("l0f_psy", bnk),))
                    p.emit("vector", lambda e, sl=sl, oc=oc, bnk=bnk: e.tensor_tensor(
                        xr[:, oc, sl * 512:(sl + 1) * 512],
                        xr[:, oc, sl * 512:(sl + 1) * 512],
                        ps_y[:, bnk, :], op=ALU.add),
                        reads=("l0f_x", ("l0f_psy", bnk)), writes=("l0f_x",))
            p.pop_scope()
            p.pop_scope()

            p.push_scope()
            hsh = p.sb("l0f_hsh", [128, 8, 512], F32)
            hnb = p.sb("l0f_hnb", [128, 8, 512], BF16)
            h3 = p.sb("l0f_h3", [128, 22, 512], BF16)
            lgb = p.sb("l0f_lgb", [128, 16], F32)
            b1c = p.sb("l0f_b1c", [128, 44], F32)
            b2c = p.sb("l0f_b2c", [128, 8], F32)
            w1g = [p.sb(f"l0f_w1g{i}", [128, 8, 512], BF16) for i in range(2)]
            gel = [p.sb(f"l0f_gel{i}", [128, 512], BF16) for i in range(2)]
            w2g = [p.sb(f"l0f_w2g{i}", [128, 22, 128], BF16) for i in range(2)]
            ps_f = p.ps("l0f_psf", [128, 4, 512], F32)
            ps_f2 = p.ps("l0f_psf2", [128, 2, 512], F32)
            p.dma("sync", lgb[:], W["lnb"][li], writes=("l0f_lgb",))
            p.dma("sync", b1c[:], W["b1p"][li], writes=("l0f_b1c",))
            p.dma("sync", b2c[:], W["b2p"][li], writes=("l0f_b2c",))
            for sl in range(4):
                lo = sl * 512
                p.emit("vector", lambda e, lo=lo: e.tensor_copy(
                    hsh[:, 0:4, :], xr[:, 0:4, lo:lo + 512]),
                    reads=("l0f_x",), writes=("l0f_hsh",))
                if lo == 0:
                    p.emit("vector", lambda e: e.memset(
                        hsh[:, 4:8, 0:1], 0.0), writes=("l0f_hsh",))
                    p.emit("vector", lambda e: e.tensor_copy(
                        hsh[:, 4:8, 1:512], xr[:, 4:8, 0:511]),
                        reads=("l0f_x",), writes=("l0f_hsh",))
                else:
                    p.emit("vector", lambda e, lo=lo: e.tensor_copy(
                        hsh[:, 4:8, 0:512], xr[:, 4:8, lo - 1:lo + 511]),
                        reads=("l0f_x",), writes=("l0f_hsh",))
                cx.ln_fm(hsh[:], "l0f_hsh", hsh[:], "l0f_hsh", "l0ff")
                for oc in range(8):
                    p.emit("scalar", lambda e, oc=oc: e.activation(
                        out=hnb[:, oc, :], in_=hsh[:, oc, :], func=AF.Identity,
                        scale=lgb[:, oc:oc + 1], bias=lgb[:, 8 + oc:9 + oc]),
                        reads=("l0f_hsh", "l0f_lgb"), writes=("l0f_hnb",))
                for g in range(11):
                    bi = g % 2
                    kw1 = f"l0f_w1g{bi}"
                    for half, off in ((0, 0), (1, 2816)):
                        p.dma("gpsimd", w1g[bi][:, :, half * 256:(half + 1) * 256],
                              W["w1pad"][li][:, off + g * 256:off + (g + 1) * 256]
                              .rearrange("(k p) n -> p k n", p=128),
                              reads=(kw1,), writes=(kw1,))
                    for jj in range(2):
                        j = g * 2 + jj
                        ba, bg = jj * 2, jj * 2 + 1
                        for k in range(8):
                            p.emit("tensor", lambda e, k=k, bi=bi, jj=jj, ba=ba: e.matmul(
                                ps_f[:, ba, :], w1g[bi][:, k, jj * 128:(jj + 1) * 128],
                                hnb[:, k, :], start=(k == 0), stop=(k == 7)),
                                reads=(kw1, "l0f_hnb"), writes=(("l0f_psf", ba),))
                        for k in range(8):
                            p.emit("tensor", lambda e, k=k, bi=bi, jj=jj, bg=bg: e.matmul(
                                ps_f[:, bg, :], w1g[bi][:, k, 256 + jj * 128:256 + (jj + 1) * 128],
                                hnb[:, k, :], start=(k == 0), stop=(k == 7)),
                                reads=(kw1, "l0f_hnb"), writes=(("l0f_psf", bg),))
                        p.emit("scalar", lambda e, j=j, jj=jj, bg=bg: e.activation(
                            out=gel[jj][:], in_=ps_f[:, bg, :], func=AF.Gelu,
                            bias=b1c[:, 22 + j:23 + j]),
                            reads=(("l0f_psf", bg), "l0f_b1c"), writes=(f"l0f_gel{jj}",))
                        p.emit("vector", lambda e, j=j, jj=jj, ba=ba: e.scalar_tensor_tensor(
                            h3[:, j, :], ps_f[:, ba, :], b1c[:, j:j + 1],
                            gel[jj][:], op0=ALU.add, op1=ALU.mult),
                            reads=(("l0f_psf", ba), "l0f_b1c", f"l0f_gel{jj}"),
                            writes=("l0f_h3",))
                for oc in range(8):
                    bi = oc % 2
                    kw2 = f"l0f_w2g{bi}"
                    p.dma("gpsimd", w2g[bi][:],
                          W["w2pad"][li][:, oc * 128:(oc + 1) * 128]
                          .rearrange("(k p) n -> p k n", p=128),
                          reads=(kw2,), writes=(kw2,))
                    for j in range(22):
                        p.emit("tensor", lambda e, j=j, bi=bi: e.matmul(
                            ps_f2[:, bi, :], w2g[bi][:, j, :], h3[:, j, :],
                            start=(j == 0), stop=(j == 21)),
                            reads=(kw2, "l0f_h3"), writes=(("l0f_psf2", bi),))
                    p.emit("vector", lambda e, oc=oc, lo=lo, bi=bi: e.scalar_tensor_tensor(
                        xr[:, oc, lo:lo + 512], ps_f2[:, bi, :], b2c[:, oc:oc + 1],
                        xr[:, oc, lo:lo + 512], op0=ALU.add, op1=ALU.add),
                        reads=(("l0f_psf2", bi), "l0f_b2c", "l0f_x"),
                        writes=("l0f_x",))
            p.pop_scope()

        p.push_scope()
        xt9 = p.sb("l0f_xt9", [128, 8, 512], F32)
        yn9 = p.sb("l0f_yn9", [128, 8, 512], F32)
        for ts in range(4):
            t = b * 4 + ts
            p.dma("sync", xt9[:], xin_t[t], reads=("l0f_xt9",), writes=("l0f_xt9",))
            cx.ln_fm(xr[:, :, ts * 512:(ts + 1) * 512], "l0f_x", yn9[:],
                     "l0f_yn9", "l0fo")
            p.emit("vector", lambda e: e.tensor_tensor(
                yn9[:], yn9[:], xt9[:], op=ALU.add),
                reads=("l0f_yn9", "l0f_xt9"), writes=("l0f_yn9",))
            p.dma("sync", xout_t[t], yn9[:], reads=("l0f_yn9",), writes=("l0f_st",))
        p.pop_scope()
        p.pop_scope()
    p.pop_scope()


def build_l0_ffn(consts):
    p = Prog()
    xin = p.dram_in("xin", [1024, TCORE])
    xout = p.dram_out("xout", [1024, TCORE])
    c_aps = consts_aps(p, consts)
    cx = Ctx(p, c_aps)
    W = {
        "wqkv": p.dram_in("w_wqkv", [2, 1024, 1536]),
        "qksc": p.dram_in("w_qksc", [2, 128, 2]),
        "wo": p.dram_in("w_wo", [2, 512, 1024]),
        "lnb": p.dram_in("w_lnb", [2, 128, 16]),
        "w1pad": p.dram_in("w_w1pad", [2, 1024, 5632]),
        "b1p": p.dram_in("w_b1p", [2, 128, 44]),
        "w2pad": p.dram_in("w_w2pad", [2, 2816, 1024]),
        "b2p": p.dram_in("w_b2p", [2, 128, 8]),
    }
    emit_l0_ffn(p, cx, xin, xout, W, c_aps)
    nc = p.finalize()
    return nc, p


def l0_ffn_in_map(p, consts, inputs):
    g = lambda n: np.asarray(inputs[n], np.float32)
    w1 = g("L0_ffn_w1")
    w1pad = np.zeros((2, 1024, 5632), np.float32)
    w1pad[:, :, 0:2730] = w1[:, :, 0:2730]
    w1pad[:, :, 2816:2816 + 2730] = w1[:, :, 2730:5460]
    b1 = g("L0_ffn_b1")
    b1p = np.zeros((2, 44, 128), np.float32)
    b1p[:, 0:22, :] = np.pad(b1[:, 0:2730], ((0, 0), (0, 86))).reshape(2, 22, 128)
    b1p[:, 22:44, :] = np.pad(b1[:, 2730:5460], ((0, 0), (0, 86))).reshape(2, 22, 128)
    b1p = b1p.transpose(0, 2, 1)
    b1pc = np.zeros((2, 128, 44), np.float32)
    for li in range(2):
        fa = np.pad(b1[li, 0:2730], (0, 86)).reshape(22, 128)
        fg = np.pad(b1[li, 2730:5460], (0, 86)).reshape(22, 128)
        b1pc[li, :, 0:22] = fa.T
        b1pc[li, :, 22:44] = fg.T
    w2 = g("L0_ffn_w2")
    w2pad = np.zeros((2, 2816, 1024), np.float32)
    w2pad[:, 0:2730, :] = w2
    qs = g("L0_ffn_q_scale")
    ks = g("L0_ffn_k_scale")
    qksc = np.zeros((2, 128, 2), np.float32)
    for li in range(2):
        qksc[li, :, 0] = np.tile(qs[li], 2)
        qksc[li, :, 1] = np.tile(ks[li], 2)
    lg = g("L0_ffn_ln_g"); lb = g("L0_ffn_ln_b")
    lnb = np.zeros((2, 128, 16), np.float32)
    for li in range(2):
        lnb[li, :, 0:8] = lg[li].reshape(8, 128).T
        lnb[li, :, 8:16] = lb[li].reshape(8, 128).T
    b2p = np.stack([pc(g("L0_ffn_b2")[li]) for li in range(2)])
    return {
        "c_fvals_in": p.fconst_host_array(),
        "w_wqkv": np.ascontiguousarray(g("L0_ffn_wqkv")),
        "w_qksc": qksc,
        "w_wo": np.ascontiguousarray(g("L0_ffn_wo")),
        "w_lnb": lnb,
        "w_w1pad": w1pad,
        "w_b1p": b1pc,
        "w_w2pad": w2pad,
        "w_b2p": b2p,
        **consts_in_map(consts),
    }


def emit_l0_attn(p: Prog, cx: Ctx, xin, xout, W, C):
    TC = 256
    NTC = SEQ // TC
    p.push_scope()
    lnwb = p.sb("l0a_lnwb", [128, 16], F32)
    ae = p.sb("l0a_ae", [128, 16, 16], F32)
    dcv = p.sb("l0a_dcv", [128, 112], F32)
    xpT = p.sb("l0a_xpT", [128, 16, 128], BF16)
    dtpT = p.sb("l0a_dtpT", [64, 2048], BF16)
    opT = p.sb("l0a_opT", [128, 16, 1024], BF16)
    bdram = p.dram_tmp("l0a_bd", [32, TC], BF16)
    p.dma("sync", lnwb[:], W["lnwb_pc"], writes=("l0a_lnwb",))
    p.dma("sync", ae[:].rearrange("p a b -> p (a b)"), W["aexp"], writes=("l0a_ae",))
    p.dma("sync", dcv[:], W["dcv_pc"], writes=("l0a_dcv",))
    load_weight_bf16(p, None, "", xpT[:], "l0a_xpT", W["xpT"], 16, 128)
    p.dma("gpsimd", dtpT[:], W["dtpT"], reads=("l0a_dtpT",), writes=("l0a_dtpT",))
    load_weight_bf16(p, None, "", opT[:], "l0a_opT", W["opT"], 16, 1024)
    p.emit("scalar", lambda e: e.activation(
        out=ae[:].rearrange("p a b -> p (a b)"),
        in_=ae[:].rearrange("p a b -> p (a b)"), func=AF.Exp),
        reads=("l0a_ae",), writes=("l0a_ae",))
    xin_t = xin.rearrange("(c p) (n t) -> n p c t", p=128, t=TC)
    xout_t = xout.rearrange("(c p) (n t) -> n p c t", p=128, t=TC)
    p.fconst(1.0)

    for b in range(2):
        p.push_scope()
        hstate = p.sb("l0a_hst", [128, 16, 16], F32)
        halo = p.sb("l0a_halo", [128, 16, 4], F32)
        p.emit("vector", lambda e: e.memset(hstate[:].rearrange("p a b -> p (a b)"), 0.0),
               writes=("l0a_hstA", "l0a_hstB"))
        p.emit("vector", lambda e: e.memset(halo[:].rearrange("p a b -> p (a b)"), 0.0),
               writes=("l0a_halo",))
        p.push_scope()
        xt = p.sb("l0a_xt", [128, 8, TC], F32)
        xno = p.sb("l0a_xno", [128, 8, TC], F32)
        xnib = p.sb("l0a_xnib", [128, 8, TC], BF16)
        wio = p.sb("l0a_wio", [128, 8, 256], BF16)
        zsl = p.sb("l0a_z", [128, 16, TC], BF16)
        xc_f = p.sb("l0a_xcf", [128, 16, TC], BF16)
        dtr = p.sb("l0a_dtr", [64, TC], BF16)
        bc_sb = p.sb("l0a_bcsb", [128, TC], BF16)
        bcrow = p.sb("l0a_bcrow", [1, 16 * TC], BF16)
        onesrb = p.sb("l0a_or", [1, 128], BF16)
        brep = p.sb("l0a_brep", [128, 16, TC], BF16)
        crep = p.sb("l0a_crep", [128, 16, TC], BF16)
        da2 = [p.sb(f"l0a_da{i}", [128, 16, TC], BF16) for i in range(2)]
        u_t = p.sb("l0a_u", [128, 16, TC], BF16)
        h_t = p.sb("l0a_h", [128, 16, TC], F32)
        dtva = p.sb("l0a_dtva", [128, 16, TC], BF16)
        dtxa = p.sb("l0a_dtxa", [128, 16, TC], BF16)
        zsil = p.sb("l0a_zsil", [128, 16, TC], BF16)
        ysl = p.sb("l0a_y", [128, TC], F32)
        yb = p.sb("l0a_yb", [128, 16, TC], BF16)
        tmp = p.sb("l0a_tmp", [128, TC], F32)
        ps_a = p.ps("l0a_psa", [128, 4, 512], F32)
        ps_b = p.ps("l0a_psb", [128, 2, 512], F32)

        p.emit("vector", lambda e: e.tensor_copy(onesrb[:], cx.ones_row[:]),
               reads=("cx_onesrow",), writes=("l0a_or",))
        for tc in range(NTC):
            t = b * NTC + tc
            p.dma("sync", xt[:], xin_t[t], reads=("l0a_xt",), writes=("l0a_xt",))
            cx.ln_fm(xt[:], "l0a_xt", xno[:], "l0a_xno", "l0ao", Tt=TC)
            lnscr = h_t[:, 0:8, :]
            cx.ln_fm(xno[:], "l0a_xno", lnscr, "l0a_h", "l0ai", Tt=TC)
            for oc in range(8):
                p.emit("scalar", lambda e, oc=oc, lnscr=lnscr: e.activation(
                    out=xnib[:, oc, :], in_=lnscr[:, oc, :], func=AF.Identity,
                    scale=lnwb[:, oc:oc + 1], bias=lnwb[:, 8 + oc:9 + oc]),
                    reads=("l0a_h", "l0a_lnwb"), writes=("l0a_xnib",))
            for half, dst, dkey, dt_ in ((0, None, "l0a_xin_ps", None),
                                         (1, zsl, "l0a_z", None)):
                for og in range(8):
                    load_weight_bf16(p, None, "", wio[:], "l0a_wio",
                                     W["ipT"][:, half * 2048 + og * 256:
                                              half * 2048 + (og + 1) * 256],
                                     8, 256)
                    for oc4 in range(2):
                        oc = og * 2 + oc4
                        bnk = oc4 % 4
                        for k in range(8):
                            p.emit("tensor", lambda e, oc4=oc4, k=k, bnk=bnk: e.matmul(
                                ps_a[:, bnk, 0:TC], wio[:, k, oc4 * 128:(oc4 + 1) * 128],
                                xnib[:, k, :], start=(k == 0), stop=(k == 7)),
                                reads=("l0a_wio", "l0a_xnib"), writes=(("l0a_psa", bnk),))
                        if half == 1:
                            p.emit("vector", lambda e, oc=oc, bnk=bnk: e.tensor_copy(
                                zsl[:, oc, :], ps_a[:, bnk, 0:TC]),
                                reads=(("l0a_psa", bnk),), writes=("l0a_z",))
                        else:
                            p.emit("vector", lambda e, oc=oc, bnk=bnk: e.tensor_copy(
                                xc_f[:, oc, :], ps_a[:, bnk, 0:TC]),
                                reads=(("l0a_psa", bnk),), writes=("l0a_xcf0",))
                for oc in range(16):
                    if half == 1:
                        continue
                    p.emit("scalar", lambda e, oc=oc: e.activation(
                        out=tmp[:], in_=xc_f[:, oc, :], func=AF.Identity,
                        scale=dcv[:, 48 + oc:49 + oc]),
                        reads=("l0a_xcf0", "l0a_dcv"), writes=("l0a_tmp",))
                    for kk in range(3):
                        sh = 3 - kk
                        p.emit("vector", lambda e, oc=oc, kk=kk, sh=sh: e.scalar_tensor_tensor(
                            tmp[:, 0:sh], halo[:, oc, 4 - sh:4],
                            dcv[:, kk * 16 + oc:kk * 16 + oc + 1], tmp[:, 0:sh],
                            op0=ALU.mult, op1=ALU.add),
                            reads=("l0a_halo", "l0a_dcv", "l0a_tmp"),
                            writes=("l0a_tmp",))
                        p.emit("vector", lambda e, oc=oc, kk=kk, sh=sh: e.scalar_tensor_tensor(
                            tmp[:, sh:TC], xc_f[:, oc, 0:TC - sh],
                            dcv[:, kk * 16 + oc:kk * 16 + oc + 1], tmp[:, sh:TC],
                            op0=ALU.mult, op1=ALU.add),
                            reads=("l0a_xcf0", "l0a_dcv", "l0a_tmp"),
                            writes=("l0a_tmp",))
                    p.emit("vector", lambda e, oc=oc: e.tensor_copy(
                        halo[:, oc, 1:4], xc_f[:, oc, TC - 3:TC]),
                        reads=("l0a_xcf0",), writes=("l0a_halo",))
                    p.emit("scalar", lambda e, oc=oc: e.activation(
                        out=xc_f[:, oc, :], in_=tmp[:], func=AF.Silu,
                        bias=dcv[:, 64 + oc:65 + oc]),
                        reads=("l0a_tmp", "l0a_dcv"), writes=("l0a_xcf",))
                    p.emit("vector", lambda e, oc=oc: e.tensor_copy(
                        yb[:, oc, :], xc_f[:, oc, :]),
                        reads=("l0a_xcf",), writes=("l0a_yb",))
            for k in range(16):
                p.emit("tensor", lambda e, k=k: e.matmul(
                    ps_b[:, 0, 0:TC], xpT[:, k, :], yb[:, k, :],
                    start=(k == 0), stop=(k == 15)),
                    reads=("l0a_xpT", "l0a_yb"), writes=(("l0a_psb", 0),))
            p.emit("vector", lambda e: e.tensor_copy(
                dtr[:], ps_b[0:64, 0, 0:TC]),
                reads=(("l0a_psb", 0),), writes=("l0a_dtr",))
            p.emit("vector", lambda e: e.tensor_copy(
                bc_sb[64:96, 0:TC], ps_b[64:96, 0, 0:TC]),
                reads=(("l0a_psb", 0),), writes=("l0a_bcsb",))
            p.dma("sync", bdram[:], bc_sb[64:96, 0:TC],
                  reads=("l0a_bcsb", "l0a_bd"), writes=("l0a_bd",))
            for part, dstr, dkeyr in ((0, brep, "l0a_brep"), (1, crep, "l0a_crep")):
                p.dma("sync", bcrow[:],
                      bdram[part * 16:(part + 1) * 16, :]
                      .rearrange("a c -> (a c)").unsqueeze(0),
                      reads=("l0a_bd", "l0a_bcrow"), writes=("l0a_bcrow",))
                for g in range((16 * TC) // 512):
                    p.emit("tensor", lambda e, g=g: e.matmul(
                        ps_a[:, g % 2, :], onesrb[:],
                        bcrow[:, g * 512:(g + 1) * 512],
                        start=True, stop=True),
                        reads=("l0a_bcrow", "l0a_or"), writes=(("l0a_psa", g % 2),))
                    p.emit("vector", lambda e, g=g, dstr=dstr, dkeyr=dkeyr: e.tensor_copy(
                        dstr[:].rearrange("p a c -> p (a c)")[:, g * 512:(g + 1) * 512],
                        ps_a[:, g % 2, :]),
                        reads=(("l0a_psa", g % 2),), writes=(dkeyr,))
            p.emit("scalar", lambda e: e.activation(
                out=zsil[:].rearrange("p a c -> p (a c)"),
                in_=zsl[:].rearrange("p a c -> p (a c)"), func=AF.Silu),
                reads=("l0a_z",), writes=("l0a_zsil",))
            for dc in range(16):
                bk = dc % 2
                hk = "l0a_hA" if dc < 8 else "l0a_hB"
                pi = dc % 2
                da_t = da2[pi]
                dk = f"l0a_da{'A' if dc < 8 else 'B'}{pi}"
                p.emit("tensor", lambda e, dc=dc, bk=bk: e.matmul(
                    ps_b[:, bk, 0:TC], dtpT[:, dc * 128:(dc + 1) * 128], dtr[:],
                    start=True, stop=True),
                    reads=("l0a_dtpT", "l0a_dtr"), writes=(("l0a_psb", bk),))
                p.emit("vector", lambda e, dc=dc, bk=bk, hk=hk: e.scalar_tensor_tensor(
                    h_t[:, dc, :], ps_b[:, bk, 0:TC], dcv[:, 96 + dc:97 + dc],
                    h_t[:, dc, :], op0=ALU.add, op1=ALU.bypass),
                    reads=(("l0a_psb", bk), "l0a_dcv", hk), writes=(hk,))
                p.emit("scalar", lambda e, dc=dc, da_t=da_t: e.activation(
                    out=da_t[:, dc, :], in_=h_t[:, dc, :], func=AF.Abs),
                    reads=(hk,), writes=(dk,))
                p.emit("scalar", lambda e, dc=dc, da_t=da_t: e.activation(
                    out=da_t[:, dc, :], in_=da_t[:, dc, :], func=AF.Exp,
                    scale=-1.0), reads=(dk,), writes=(dk,))
                p.emit("scalar", lambda e, dc=dc, da_t=da_t: e.activation(
                    out=da_t[:, dc, :], in_=da_t[:, dc, :], func=AF.Ln,
                    bias=1.0), reads=(dk,), writes=(dk,))
                p.emit("vector", lambda e, dc=dc: e.tensor_scalar(
                    h_t[:, dc, :], h_t[:, dc, :], 0.0, None, op0=ALU.max),
                    reads=(hk,), writes=(hk,))
                p.emit("vector", lambda e, dc=dc, da_t=da_t: e.tensor_tensor(
                    dtva[:, dc, :], h_t[:, dc, :], da_t[:, dc, :], op=ALU.add),
                    reads=(hk, dk), writes=("l0a_dtva",))
            p.emit("vector", lambda e: e.tensor_tensor(
                dtxa[:], dtva[:], xc_f[:], op=ALU.mult),
                reads=("l0a_dtva", "l0a_xcf"), writes=("l0a_dtxa",))
            for dc in range(16):
                pi = dc % 2
                da_t = da2[pi]
                dkA, dkB = f"l0a_daA{pi}", f"l0a_daB{pi}"
                p.emit("gpsimd", lambda e, dc=dc, da_t=da_t: e.tensor_tensor(
                    da_t[:], dtva[:, dc, :].unsqueeze(1).broadcast_to([128, 16, TC]),
                    ae[:, dc, :].unsqueeze(2).broadcast_to([128, 16, TC]),
                    op=ALU.mult), reads=("l0a_dtva", "l0a_ae"),
                    writes=(dkA, dkB))
                p.emit("scalar", lambda e, da_t=da_t: e.activation(
                    out=da_t[:].rearrange("p a c -> p (a c)"),
                    in_=da_t[:].rearrange("p a c -> p (a c)"), func=AF.Exp,
                    scale=-1.0), reads=(dkA, dkB),
                    writes=(dkA, dkB))
                p.emit("gpsimd", lambda e, dc=dc: e.tensor_tensor(
                    u_t[:], dtxa[:, dc, :].unsqueeze(1).broadcast_to([128, 16, TC]),
                    brep[:], op=ALU.mult),
                    reads=("l0a_dtxa", "l0a_brep"), writes=("l0a_u",))
                for s in range(16):
                    hk2 = "l0a_hA" if s < 8 else "l0a_hB"
                    dk2 = f"l0a_da{'A' if s < 8 else 'B'}{pi}"
                    sk2 = "l0a_hstA" if s < 8 else "l0a_hstB"
                    p.emit("vector", lambda e, s=s, dc=dc, da_t=da_t: e.tensor_tensor_scan(
                        h_t[:, s, :], da_t[:, s, :], u_t[:, s, :],
                        hstate[:, dc, s:s + 1], op0=ALU.mult, op1=ALU.add),
                        reads=(dk2, "l0a_u", sk2), writes=(hk2,))
                p.emit("vector", lambda e, dc=dc: e.tensor_copy(
                    hstate[:, dc, 0:8].unsqueeze(2), h_t[:, 0:8, TC - 1:TC]),
                    reads=("l0a_hA",), writes=("l0a_hstA",))
                p.emit("vector", lambda e, dc=dc: e.tensor_copy(
                    hstate[:, dc, 8:16].unsqueeze(2), h_t[:, 8:16, TC - 1:TC]),
                    reads=("l0a_hB",), writes=("l0a_hstB",))
                p.emit("vector", lambda e: e.tensor_tensor(
                    h_t[:, 0:8, :], h_t[:, 0:8, :], crep[:, 0:8, :],
                    op=ALU.mult), reads=("l0a_hA", "l0a_crep"),
                    writes=("l0a_hA",))
                p.emit("gpsimd", lambda e: e.tensor_tensor(
                    h_t[:, 8:16, :], h_t[:, 8:16, :], crep[:, 8:16, :],
                    op=ALU.mult), reads=("l0a_hB", "l0a_crep"),
                    writes=("l0a_hB",))
                p.emit("vector", lambda e: e.tensor_tensor(
                    h_t[:, 0:8, :], h_t[:, 0:8, :], h_t[:, 8:16, :],
                    op=ALU.add), reads=("l0a_hA", "l0a_hB"),
                    writes=("l0a_hA",))
                p.emit("vector", lambda e: e.tensor_tensor(
                    h_t[:, 0:4, :], h_t[:, 0:4, :], h_t[:, 4:8, :],
                    op=ALU.add), reads=("l0a_hA",), writes=("l0a_hA",))
                p.emit("vector", lambda e: e.tensor_tensor(
                    h_t[:, 0:2, :], h_t[:, 0:2, :], h_t[:, 2:4, :],
                    op=ALU.add), reads=("l0a_hA",), writes=("l0a_hA",))
                p.emit("vector", lambda e: e.tensor_tensor(
                    ysl[:], h_t[:, 0, :], h_t[:, 1, :],
                    op=ALU.add), reads=("l0a_hA",), writes=("l0a_y",))
                p.emit("scalar", lambda e, dc=dc: e.activation(
                    out=tmp[:], in_=xc_f[:, dc, :], func=AF.Identity,
                    scale=dcv[:, 64 + 16 + dc:64 + 17 + dc]),
                    reads=("l0a_xcf", "l0a_dcv"), writes=("l0a_tmp",))
                p.emit("vector", lambda e: e.tensor_tensor(
                    ysl[:], ysl[:], tmp[:], op=ALU.add),
                    reads=("l0a_y", "l0a_tmp"), writes=("l0a_y",))
                p.emit("vector", lambda e, dc=dc: e.tensor_tensor(
                    yb[:, dc, :], ysl[:], zsil[:, dc, :], op=ALU.mult),
                    reads=("l0a_y", "l0a_zsil"), writes=("l0a_yb",))
            for oc in range(8):
                bnk = oc % 4
                for k in range(16):
                    p.emit("tensor", lambda e, oc=oc, k=k, bnk=bnk: e.matmul(
                        ps_a[:, bnk, 0:TC], opT[:, k, oc * 128:(oc + 1) * 128],
                        yb[:, k, :], start=(k == 0), stop=(k == 15)),
                        reads=("l0a_opT", "l0a_yb"), writes=(("l0a_psa", bnk),))
                p.emit("vector", lambda e, oc=oc, bnk=bnk: e.tensor_tensor(
                    xno[:, oc, :], xno[:, oc, :], ps_a[:, bnk, 0:TC],
                    op=ALU.add), reads=("l0a_xno", ("l0a_psa", bnk)),
                    writes=("l0a_xno",))
            cx.ln_fm(xno[:], "l0a_xno", xno[:], "l0a_xno", "l0az", Tt=TC)
            p.emit("vector", lambda e: e.tensor_tensor(
                xno[:], xno[:], xt[:], op=ALU.add),
                reads=("l0a_xno", "l0a_xt"), writes=("l0a_xno",))
            p.dma("sync", xout_t[t], xno[:], reads=("l0a_xno",), writes=("l0a_st",))
        p.pop_scope()
        p.pop_scope()
    p.pop_scope()


def build_l0_attn(consts):
    p = Prog()
    xin = p.dram_in("xin", [1024, TCORE])
    xout = p.dram_out("xout", [1024, TCORE])
    c_aps = consts_aps(p, consts)
    cx = Ctx(p, c_aps)
    W = {
        "lnwb_pc": p.dram_in("w_lnwb", [128, 16]),
        "aexp": p.dram_in("w_aexp", [128, 256]),
        "dcv_pc": p.dram_in("w_dcv", [128, 112]),
        "xpT": p.dram_in("w_xpT", [2048, 128]),
        "dtpT": p.dram_in("w_dtpT", [64, 2048]),
        "ipT": p.dram_in("w_ipT", [1024, 4096]),
        "opT": p.dram_in("w_opT", [2048, 1024]),
    }
    emit_l0_attn(p, cx, xin, xout, W, c_aps)
    nc = p.finalize()
    return nc, p


def l0_attn_in_map(p, consts, inputs):
    g = lambda n: np.asarray(inputs[n], np.float32)
    lnwb = np.concatenate([pc(g("L0_attn_ln_w")), pc(g("L0_attn_ln_b"))], 1)
    alog = g("L0_attn_A_log")
    aexp = np.zeros((128, 256), np.float32)
    for dc in range(16):
        aexp[:, dc * 16:(dc + 1) * 16] = alog[dc * 128:(dc + 1) * 128, :]
    convw = g("L0_attn_conv_w")
    dcv2 = np.zeros((128, 112), np.float32)
    for kk in range(4):
        dcv2[:, kk * 16:(kk + 1) * 16] = convw[:, kk].reshape(16, 128).T
    dcv2[:, 64:80] = g("L0_attn_conv_b").reshape(16, 128).T
    dcv2[:, 80:96] = g("L0_attn_D_param").reshape(16, 128).T
    dcv2[:, 96:112] = g("L0_attn_dt_proj_b").reshape(16, 128).T
    xp = g("L0_attn_x_proj_w")
    xpT = np.zeros((2048, 128), np.float32)
    xpT[:, 0:96] = xp.T
    return {
        "c_fvals_in": p.fconst_host_array(),
        "w_lnwb": lnwb,
        "w_aexp": aexp,
        "w_dcv": dcv2,
        "w_xpT": xpT,
        "w_dtpT": np.ascontiguousarray(g("L0_attn_dt_proj_w").T),
        "w_ipT": np.ascontiguousarray(g("L0_attn_in_proj_w").T),
        "w_opT": np.ascontiguousarray(g("L0_attn_out_proj_w").T),
        **consts_in_map(consts),
    }


def emit_l2_attn(p: Prog, cx: Ctx, xin, xout, W, C):
    NCH = 32
    ROWS = 1024
    p.push_scope()
    k_loc = p.dram_tmp("l2a_kloc", [1024, TCORE], BF16)
    v_loc = p.dram_tmp("l2a_vloc", [1024, TCORE], BF16)
    q_loc = p.dram_tmp("l2a_qloc", [1024, TCORE], BF16)
    kv_in = p.dram_tmp("l2a_kvin", [2048, TCORE], BF16)
    kv_out = p.dram_shared("l2a_kvout", [8 * 2048, TCORE], BF16)
    th_in = p.dram_tmp("l2a_thin", [6, 32])
    th_out = p.dram_shared("l2a_thout", [48, 32])
    thr_d = p.dram_tmp("l2a_thrd", [32, 1024])
    y_in = p.dram_tmp("l2a_yin", [1024, 1024], BF16)
    y_out = p.dram_shared("l2a_yout2", [1024, 1024], BF16)
    sc_in = p.dram_tmp("l2a_scin", [1, 2])
    sc_out = p.dram_shared("l2a_scout", [1, 2])
    f_in = p.dram_tmp("l2a_fin", [1024, TCORE], BF16)
    f_out = p.dram_shared("l2a_fout", [1024, TCORE], BF16)

    p.push_scope()
    wqb = p.sb("l2a_wq", [128, 8, 1024], BF16)
    wkb = p.sb("l2a_wk", [128, 8, 1024], BF16)
    wvb = p.sb("l2a_wv", [128, 8, 1024], BF16)
    wst = p.sb("l2a_wst", [128, 2, 1024], F32)
    nrmw = p.sb("l2a_nrmw", [128, 16], F32)
    agw = p.sb("l2a_agw", [128, 8, 192], BF16)
    xt = p.sb("l2a_xt", [128, 8, 512], F32)
    xnb = p.sb("l2a_xnb", [128, 8, 512], BF16)
    sil = p.sb("l2a_sil", [128, 8, 512], BF16)
    sq = p.sb("l2a_sq", [128, 8, 512], BF16)
    row = p.sb("l2a_row", [1, 4, 512], F32)
    prod = p.sb("l2a_prod", [128, 8, 512], BF16)
    tht = p.sb("l2a_tht", [1, 6, 32], F32)
    out_b = p.sb("l2a_outb", [128, 8, 512], BF16)
    ps_m = p.ps("l2a_psm", [128, 4, 512], F32)
    load_weight_bf16(p, wst, "l2a_wst", wqb[:], "l2a_wq", W["wqT"], 8, 1024)
    load_weight_bf16(p, wst, "l2a_wst", wkb[:], "l2a_wk", W["wkT"], 8, 1024)
    load_weight_bf16(p, wst, "l2a_wst", wvb[:], "l2a_wv", W["wvT"], 8, 1024)
    p.dma("sync", nrmw[:], W["nrm_pc"], writes=("l2a_nrmw",))
    load_weight_bf16(p, wst, "l2a_wst", agw[:], "l2a_agw", W["agwT"], 8, 192)
    p.emit("vector", lambda e: e.memset(tht[:].rearrange("z a b -> z (a b)"), 0.0),
           writes=("l2a_tht",))
    p.fconst(1e-6)
    p.fconst(0.01)
    xin_t = xin.rearrange("(c p) (n t) -> n p c t", p=128, t=512)
    xout_t = xout.rearrange("(c p) (n t) -> n p c t", p=128, t=512)

    def proj_norm(wb, wkey, nrm_col, dst_dram, t):
        for oc in range(8):
            bnk = oc % 4
            for k in range(8):
                p.emit("tensor", lambda e, oc=oc, k=k, bnk=bnk: e.matmul(
                    ps_m[:, bnk, :], wb[:, k, oc * 128:(oc + 1) * 128],
                    xnb[:, k, :], start=(k == 0), stop=(k == 7)),
                    reads=(wkey, "l2a_xnb"), writes=(("l2a_psm", bnk),))
            p.emit("scalar", lambda e, oc=oc, bnk=bnk: e.activation(
                out=sil[:, oc, :], in_=ps_m[:, bnk, :], func=AF.Silu),
                reads=(("l2a_psm", bnk),), writes=("l2a_sil",))
        if nrm_col is None:
            p.dma("sync", dst_dram.rearrange("(c p) (n t) -> n p c t", p=128, t=512)[t],
                  sil[:], reads=("l2a_sil",), writes=("l2a_dst",))
            return
        p.emit("scalar", lambda e: e.activation(
            out=sq[:], in_=sil[:], func=AF.Square),
            reads=("l2a_sil",), writes=("l2a_sq",))
        cx.part_sum(lambda k: sq[:, k, :], "l2a_sq",
                    cx.ps_st[0:1, 0, :], ("cx_psst", 0), 8)
        p.emit("vector", lambda e: e.tensor_scalar(
            row[:, 0, :], cx.ps_st[0:1, 0, :], 1.0 / 1024.0, None, op0=ALU.mult),
            reads=(("cx_psst", 0),), writes=("l2a_row",))
        p.emit("scalar", lambda e: e.activation(
            out=row[:, 1, :], in_=row[:, 0, :], func=AF.Sqrt, bias=1e-6),
            reads=("l2a_row",), writes=("l2a_row",))
        p.emit("vector", lambda e: e.reciprocal(
            row[:, 2, :], row[:, 1, :]), reads=("l2a_row",), writes=("l2a_row",))
        cx.replicate_row(row[:, 2, :], "l2a_row", cx.ps_st[:, 1, :], ("cx_psst", 1))
        p.emit("vector", lambda e: e.tensor_tensor(
            sil[:], sil[:],
            cx.ps_st[:, 1, :].unsqueeze(1).broadcast_to([128, 8, 512]),
            op=ALU.mult), reads=("l2a_sil", ("cx_psst", 1)), writes=("l2a_sil",))
        for oc in range(8):
            p.emit("scalar", lambda e, oc=oc, nrm_col=nrm_col: e.activation(
                out=out_b[:, oc, :], in_=sil[:, oc, :], func=AF.Identity,
                scale=nrmw[:, nrm_col + oc:nrm_col + oc + 1]),
                reads=("l2a_sil", "l2a_nrmw"), writes=("l2a_outb",))
        p.dma("sync", dst_dram.rearrange("(c p) (n t) -> n p c t", p=128, t=512)[t],
              out_b[:], reads=("l2a_outb",), writes=("l2a_dst",))

    for t in range(NT):
        p.dma("sync", xt[:], xin_t[t], reads=("l2a_xt",), writes=("l2a_xt",))
        cx.ln_fm(xt[:], "l2a_xt", xnb[:], "l2a_xnb", "l2at", sub_inplace=True)
        proj_norm(wkb, "l2a_wk", 8, k_loc, t)
        proj_norm(wvb, "l2a_wv", None, v_loc, t)
        proj_norm(wqb, "l2a_wq", 0, q_loc, t)
        for gi in range(3):
            p.emit("vector", lambda e, gi=gi: e.tensor_tensor(
                prod[:], xnb[:].rearrange("p c (n u) -> p c n u", u=64),
                agw[:, :, gi * 64:(gi + 1) * 64].unsqueeze(2)
                .broadcast_to([128, 8, 8, 64]), op=ALU.mult),
                reads=("l2a_xnb", "l2a_agw"), writes=("l2a_prod",))
            cx.part_sum(lambda k: prod[:, k, :], "l2a_prod",
                        cx.ps_st[0:1, 0, :], ("cx_psst", 0), 8)
            p.emit("vector", lambda e: e.tensor_copy(
                row[:, 0, :], cx.ps_st[0:1, 0, :]),
                reads=(("cx_psst", 0),), writes=("l2a_row",))
            p.emit("vector", lambda e, gi=gi, t=t: e.tensor_reduce(
                tht[0:1, gi * 2 + (t // 4), (t % 4) * 8:(t % 4) * 8 + 8]
                .unsqueeze(2),
                row[:, 0, :].rearrange("z (n u) -> z n u", u=64),
                axis=AX.X, op=ALU.add),
                reads=("l2a_row",), writes=("l2a_tht",))
    p.dma("sync", th_in.rearrange("(z a) b -> z a b", z=1), tht[0:1, :, :],
          reads=("l2a_tht",), writes=("l2a_thin",))
    p.dma("sync", kv_in[0:1024, :], k_loc, reads=("l2a_dst",), writes=("l2a_kvin",))
    p.dma("sync", kv_in[1024:2048, :], v_loc, reads=("l2a_kvin",), writes=("l2a_kvin",))
    p.emit("gpsimd", lambda e: e.collective_compute(
        "AllGather", ALU.bypass, replica_groups=[list(range(NCORES))],
        ins=[kv_in.opt()], outs=[kv_out.opt()]),
        reads=("l2a_kvin",), writes=("l2a_kvout",), cc=True)
    p.emit("gpsimd", lambda e: e.collective_compute(
        "AllGather", ALU.bypass, replica_groups=[list(range(NCORES))],
        ins=[th_in.opt()], outs=[th_out.opt()]),
        reads=("l2a_thin",), writes=("l2a_thout",), cc=True)
    p.pop_scope()

    becol = p.sb("l2a_becol", [128, 64], F32)
    p.push_scope()
    thsb = p.sb("l2a_thsb", [48, 32], F32)
    e64 = p.sb("l2a_e64", [16, 1024], BF16)
    thbf = p.sb("l2a_thbf", [16, 32], BF16)
    gtab = p.sb("l2a_gtab", [1, 3, 32], F32)
    brow = p.sb("l2a_brow", [1, 64], F32)
    ps_g = p.ps("l2a_psg", [128, 2, 512], F32)
    p.dma("sync", e64[:], C["e64"], writes=("l2a_e64",))
    p.dma("sync", thsb[:], th_out, reads=("l2a_thout",), writes=("l2a_thsb",))
    sel2 = p.sb("l2a_sel2", [48, 2], BF16)
    p.dma("sync", sel2[:], C["sel_ae"], writes=("l2a_sel2",))
    thbf48 = p.sb("l2a_thbf48", [48, 32], BF16)
    p.emit("vector", lambda e: e.tensor_copy(thbf48[:], thsb[:]),
           reads=("l2a_thsb",), writes=("l2a_thbf48",))
    p.emit("tensor", lambda e: e.matmul(
        ps_g[0:2, 0, 0:32], sel2[:], thbf48[:], start=True, stop=True),
        reads=("l2a_thbf48", "l2a_sel2"), writes=(("l2a_psg", 0),))
    p.emit("scalar", lambda e: e.activation(
        out=gtab[0:1, 0, :], in_=ps_g[0:1, 0, 0:32], func=AF.Sigmoid,
        scale=1.0 / 16.0), reads=(("l2a_psg", 0),), writes=("l2a_gtab",))
    p.emit("vector", lambda e: e.tensor_scalar(
        gtab[0:1, 0, :], gtab[0:1, 0, :], -1.0, 1.0, op0=ALU.mult, op1=ALU.add),
        reads=("l2a_gtab",), writes=("l2a_gtab",))
    p.emit("vector", lambda e: e.tensor_copy(brow[0:1, 0:32], gtab[0:1, 0, :]),
           reads=("l2a_gtab",), writes=("l2a_brow",))
    sel2b = p.sb("l2a_sel2b", [48, 2], BF16)
    p.dma("sync", sel2b[:], C["sel_ea"], writes=("l2a_sel2b",))
    p.emit("tensor", lambda e: e.matmul(
        ps_g[0:2, 1, 0:32], sel2b[:], thbf48[:], start=True, stop=True),
        reads=("l2a_thbf48", "l2a_sel2b"), writes=(("l2a_psg", 1),))
    p.emit("scalar", lambda e: e.activation(
        out=brow[0:1, 32:64], in_=ps_g[0:1, 1, 0:32], func=AF.Sigmoid,
        scale=1.0 / 16.0), reads=(("l2a_psg", 1),), writes=("l2a_brow",))
    cx.replicate_row(brow[0:1, 0:64], "l2a_brow", ps_g[:, 0, 0:64], ("l2a_psg", 0))
    p.emit("vector", lambda e: e.tensor_copy(becol[:], ps_g[:, 0, 0:64]),
           reads=(("l2a_psg", 0),), writes=("l2a_becol",))
    thsg = p.sb("l2a_thsg", [16, 32], BF16)
    selth = p.sb("l2a_selth", [48, 16], BF16)
    p.dma("sync", selth[:], C["sel_th"], writes=("l2a_selth",))
    p.emit("tensor", lambda e: e.matmul(
        ps_g[0:16, 0, 0:32], selth[:], thbf48[:], start=True, stop=True),
        reads=("l2a_thbf48", "l2a_selth"), writes=(("l2a_psg", 0),))
    p.emit("scalar", lambda e: e.activation(
        out=thsg[:], in_=ps_g[0:16, 0, 0:32], func=AF.Sigmoid),
        reads=(("l2a_psg", 0),), writes=("l2a_thsg",))
    for g2 in range(2):
        p.emit("tensor", lambda e, g2=g2: e.matmul(
            ps_g[0:32, g2, 0:512], thsg[:], e64[:, g2 * 512:(g2 + 1) * 512],
            start=True, stop=True),
            reads=("l2a_thsg", "l2a_e64"), writes=(("l2a_psg", g2),))
    thr_sb = p.sb("l2a_thrsb", [32, 1024], F32)
    for g2 in range(2):
        p.emit("scalar", lambda e, g2=g2: e.activation(
            out=thr_sb[:, g2 * 512:(g2 + 1) * 512], in_=ps_g[0:32, g2, 0:512],
            func=AF.Identity, scale=0.01),
            reads=(("l2a_psg", g2),), writes=("l2a_thrsb",))
    p.dma("sync", thr_d, thr_sb[:], reads=("l2a_thrsb",), writes=("l2a_thrd",))
    p.pop_scope()

    w0m = p.sb("l2a_w0m", [128, 2, 1024], F32)
    w1m = p.sb("l2a_w1m", [128, 8, 256], F32)
    s0m = p.sb("l2a_s0m", [128, 2, 1024], F32)
    s1m = p.sb("l2a_s1m", [128, 8, 256], F32)
    lnm = p.sb("l2a_lnm", [128, 8], F32)
    slnm = p.sb("l2a_slnm", [128, 8], F32)
    dlnp = p.sb("l2a_dlnp", [128, 8], F32)
    w0T = p.sb("l2a_w0T", [128, 8, 256], BF16)
    w1T = p.sb("l2a_w1T", [128, 2, 1024], BF16)
    w1N = p.sb("l2a_w1N", [128, 8, 256], BF16)
    mscr = p.sb("l2a_mscr", [128, 2, 1024], BF16)
    ps_m2 = p.ps("l2a_psm2", [128, 4, 512], F32)
    ps_t3 = p.ps("l2a_pst3", [128, 1024], BF16)
    p.push_scope()
    kc_sb = p.sb("l2a_kc", [128, 8, 1024], BF16)
    vc_sb = p.sb("l2a_vc", [128, 8, 512], BF16)
    trbuf = p.sb("l2a_trbuf", [128, 512], BF16)
    hT_sb = p.sb("l2a_hT", [128, 8, 256], BF16)
    dpT_sb = p.sb("l2a_dpT", [128, 8, 256], BF16)
    h_b = p.sb("l2a_hb", [128, 2, 1024], BF16)
    dsil = p.sb("l2a_dsil", [128, 2, 1024], BF16)
    dy_b = p.sb("l2a_dyb", [128, 8, 1024], BF16)
    y_sb = p.sb("l2a_ysb", [128, 8, 1024], BF16)
    scr = p.sb("l2a_scr", [128, 8, 1024], BF16)
    yhat = scr
    dpf = p.sb("l2a_dpf", [128, 8, 1024], BF16)
    dW0 = p.sb("l2a_dW0", [128, 2, 1024], F32)
    dW1 = p.sb("l2a_dW1", [128, 8, 256], F32)
    rrep = p.sb("l2a_rrep", [128, 1024], BF16)
    srep2 = p.sb("l2a_srep", [128, 1024], BF16)
    rw2 = p.sb("l2a_rw2", [1, 2, 1024], F32)
    ccol = p.sb("l2a_ccol", [128, 2], F32)
    ccbf = p.sb("l2a_ccbf", [128, 2], BF16)
    scs = p.sb("l2a_scs", [1, 2], F32)
    p.fconst(2.0 / (16 * 64 * 1024.0))
    p.fconst(1.0 / 1024.0)
    p.dma("sync", w0m[:], W["w0sh"].rearrange("(c p) n -> p c n", p=128),
          writes=("l2a_w0m",))
    p.dma("sync", w1m[:], W["w1shT"].rearrange("(c p) n -> p c n", p=128),
          writes=("l2a_w1m",))
    p.dma("sync", lnm[:], W["memln_pc"], writes=("l2a_lnm",))
    p.emit("vector", lambda e: e.memset(s0m[:].rearrange("p a b -> p (a b)"), 0.0),
           writes=("l2a_s0m",))
    p.emit("vector", lambda e: e.memset(s1m[:].rearrange("p a b -> p (a b)"), 0.0),
           writes=("l2a_s1m",))
    p.emit("vector", lambda e: e.memset(slnm[:], 0.0), writes=("l2a_slnm",))

    def transpose_into(src_ap, key_src, dst_ap, key_dst, pr, pc_):
        p.emit("tensor", lambda e: e.transpose(
            ps_t3[0:pc_, 0:pr], src_ap, cx.iden[0:pr, 0:pr]),
            reads=(key_src, "cx_iden"), writes=("l2a_pst3",))
        p.emit("vector", lambda e: e.tensor_copy(dst_ap, ps_t3[0:pc_, 0:pr]),
            reads=("l2a_pst3",), writes=(key_dst,))

    def refresh_mirrors():
        p.emit("vector", lambda e: e.tensor_copy(mscr[:], w0m[:]),
               reads=("l2a_w0m",), writes=("l2a_mscr",))
        for ic in range(2):
            for dkc in range(8):
                transpose_into(mscr[:, ic, dkc * 128:(dkc + 1) * 128], "l2a_mscr",
                               w0T[:, dkc, ic * 128:(ic + 1) * 128], "l2a_w0T",
                               128, 128)
        p.emit("vector", lambda e: e.tensor_copy(
            mscr[:].rearrange("p a b -> p (a b)")[:, 0:2048],
            w1m[:].rearrange("p a b -> p (a b)")),
            reads=("l2a_w1m",), writes=("l2a_mscr",))
        p.emit("vector", lambda e: e.tensor_copy(
            w1N[:].rearrange("p a b -> p (a b)"),
            mscr[:].rearrange("p a b -> p (a b)")[:, 0:2048]),
            reads=("l2a_mscr",), writes=("l2a_w1N",))
        mv = mscr[:].rearrange("p a b -> p (a b)").rearrange(
            "p (a b) -> p a b", a=8)
        for oc in range(8):
            for ic in range(2):
                transpose_into(mv[:, oc, ic * 128:(ic + 1) * 128], "l2a_mscr",
                               w1T[:, ic, oc * 128:(oc + 1) * 128], "l2a_w1T",
                               128, 128)

    kv_v = kv_out.rearrange("(r x p) (bl n) -> p x r bl n", p=128, x=16, bl=2)
    for c in range(NCH):
        refresh_mirrors()
        for r8 in range(8):
            for b2 in range(2):
                p.dma("sync",
                      kc_sb[:, :, r8 * 128 + b2 * 64:r8 * 128 + b2 * 64 + 64],
                      kv_v[:, 0:8, r8, b2, c * 64:(c + 1) * 64],
                      reads=("l2a_kc",), writes=("l2a_kc",))

        p.dma("sync", rw2[0:1, 0, 0:1024], thr_d[c:c + 1, :],
              reads=("l2a_rw2",), writes=("l2a_rw2",))
        for ic in range(2):
            for rc2 in range(2):
                bnk = ic * 2 + rc2
                for dk in range(8):
                    p.emit("tensor", lambda e, ic=ic, rc2=rc2, dk=dk, bnk=bnk: e.matmul(
                        ps_m2[:, bnk, :], w0T[:, dk, ic * 128:(ic + 1) * 128],
                        kc_sb[:, dk, rc2 * 512:(rc2 + 1) * 512],
                        start=(dk == 0), stop=(dk == 7)),
                        reads=("l2a_w0T", "l2a_kc"), writes=(("l2a_psm2", bnk),))
                p.emit("scalar", lambda e, ic=ic, rc2=rc2, bnk=bnk: e.activation(
                    out=h_b[:, ic, rc2 * 512:(rc2 + 1) * 512],
                    in_=ps_m2[:, bnk, :], func=AF.Silu),
                    reads=(("l2a_psm2", bnk),), writes=("l2a_hb",))
                p.emit("scalar", lambda e, ic=ic, rc2=rc2, bnk=bnk: e.activation(
                    out=dsil[:, ic, rc2 * 512:(rc2 + 1) * 512],
                    in_=ps_m2[:, bnk, :], func=AF.Derivative_silu),
                    reads=(("l2a_psm2", bnk),), writes=("l2a_dsil",))
        for oc in range(8):
            for rc2 in range(2):
                bnk = (oc * 2 + rc2) % 4
                for ic in range(2):
                    p.emit("tensor", lambda e, oc=oc, rc2=rc2, ic=ic, bnk=bnk: e.matmul(
                        ps_m2[:, bnk, :], w1T[:, ic, oc * 128:(oc + 1) * 128],
                        h_b[:, ic, rc2 * 512:(rc2 + 1) * 512],
                        start=(ic == 0), stop=(ic == 1)),
                        reads=("l2a_w1T", "l2a_hb"), writes=(("l2a_psm2", bnk),))
                p.emit("vector", lambda e, oc=oc, rc2=rc2, bnk=bnk: e.tensor_copy(
                    y_sb[:, oc, rc2 * 512:(rc2 + 1) * 512], ps_m2[:, bnk, :]),
                    reads=(("l2a_psm2", bnk),), writes=("l2a_ysb",))
        p.dma("sync", y_in.rearrange("(a p) n -> p a n", p=128), y_sb[:],
              reads=("l2a_ysb",), writes=("l2a_yin",))
        p.emit("gpsimd", lambda e: e.collective_compute(
            "AllReduce", ALU.add, replica_groups=[list(range(NCORES))],
            ins=[y_in.opt()], outs=[y_out.opt()]),
            reads=("l2a_yin",), writes=("l2a_yout2",), cc=True)
        p.dma("sync", y_sb[:], y_out.rearrange("(a p) n -> p a n", p=128),
              reads=("l2a_ysb", "l2a_yout2"), writes=("l2a_ysb",))
        p.emit("scalar", lambda e: e.activation(
            out=scr[:], in_=y_sb[:], func=AF.Square),
            reads=("l2a_ysb",), writes=("l2a_scr",))
        for g2 in range(2):
            cx.part_sum(lambda k, g2=g2: scr[:, k, g2 * 512:(g2 + 1) * 512],
                        "l2a_scr", cx.ps_st[0:1, 0, :], ("cx_psst", 0), 8)
            p.emit("vector", lambda e, g2=g2: e.tensor_scalar(
                rw2[:, 0, g2 * 512:(g2 + 1) * 512], cx.ps_st[0:1, 0, :],
                1.0 / 1024.0, None, op0=ALU.mult),
                reads=(("cx_psst", 0),), writes=("l2a_rw2",))
        p.emit("scalar", lambda e: e.activation(
            out=rw2[:, 1, :], in_=rw2[:, 0, :], func=AF.Sqrt, bias=1e-6),
            reads=("l2a_rw2",), writes=("l2a_rw2",))
        p.emit("vector", lambda e: e.reciprocal(
            rw2[:, 1, :], rw2[:, 1, :]), reads=("l2a_rw2",), writes=("l2a_rw2",))
        for g2 in range(2):
            cx.replicate_row(rw2[0:1, 1, g2 * 512:(g2 + 1) * 512], "l2a_rw2",
                             ps_m2[:, g2, :], ("l2a_psm2", g2))
            p.emit("vector", lambda e, g2=g2: e.tensor_copy(
                rrep[:, g2 * 512:(g2 + 1) * 512], ps_m2[:, g2, :]),
                reads=(("l2a_psm2", g2),), writes=("l2a_rrep",))
        p.emit("vector", lambda e: e.tensor_tensor(
            scr[:], y_sb[:], rrep[:].unsqueeze(1).broadcast_to([128, 8, 1024]),
            op=ALU.mult), reads=("l2a_ysb", "l2a_rrep"), writes=("l2a_scr",))
        for oc in range(8):
            p.emit("scalar", lambda e, oc=oc: e.activation(
                out=dpf[:, oc, :], in_=scr[:, oc, :], func=AF.Identity,
                scale=lnm[:, oc:oc + 1]),
                reads=("l2a_scr", "l2a_lnm"), writes=("l2a_dpf",))
        p.emit("vector", lambda e: e.tensor_tensor(
            dpf[:], dpf[:], kc_sb[:], op=ALU.add),
            reads=("l2a_dpf", "l2a_kc"), writes=("l2a_dpf",))
        for vh2 in range(2):
            for r4 in range(4):
                for b2 in range(2):
                    p.dma("sync",
                          vc_sb[:, :, r4 * 128 + b2 * 64:r4 * 128 + b2 * 64 + 64],
                          kv_v[:, 8:16, vh2 * 4 + r4, b2, c * 64:(c + 1) * 64],
                          reads=("l2a_vc",), writes=("l2a_vc",))
            p.emit("vector", lambda e, vh2=vh2: e.tensor_tensor(
                dpf[:, :, vh2 * 512:(vh2 + 1) * 512],
                dpf[:, :, vh2 * 512:(vh2 + 1) * 512], vc_sb[:],
                op=ALU.subtract),
                reads=("l2a_dpf", "l2a_vc"), writes=("l2a_dpf",))
        for g2 in range(2):
            cx.replicate_row(rw2[0:1, 0, g2 * 512:(g2 + 1) * 512], "l2a_rw2",
                             ps_m2[:, g2, :], ("l2a_psm2", g2))
            p.emit("vector", lambda e, g2=g2: e.tensor_tensor(
                dpf[:, :, g2 * 512:(g2 + 1) * 512],
                dpf[:, :, g2 * 512:(g2 + 1) * 512],
                ps_m2[:, g2, :].unsqueeze(1).broadcast_to([128, 8, 512]),
                op=ALU.mult),
                reads=("l2a_dpf", ("l2a_psm2", g2)), writes=("l2a_dpf",))
        p.emit("vector", lambda e: e.tensor_scalar(
            dpf[:], dpf[:], 2.0 / (16 * 64 * 1024.0), None, op0=ALU.mult),
            reads=("l2a_dpf",), writes=("l2a_dpf",))
        p.emit("vector", lambda e: e.tensor_tensor(
            scr[:], dpf[:], scr[:], op=ALU.mult),
            reads=("l2a_dpf", "l2a_scr"), writes=("l2a_scr",))
        p.emit("vector", lambda e: e.tensor_reduce(
            dlnp[:].unsqueeze(2), scr[:], axis=AX.X, op=ALU.add),
            reads=("l2a_scr",), writes=("l2a_dlnp",))
        for oc in range(8):
            p.emit("scalar", lambda e, oc=oc: e.activation(
                out=scr[:, oc, :], in_=dpf[:, oc, :], func=AF.Identity,
                scale=lnm[:, oc:oc + 1]),
                reads=("l2a_dpf", "l2a_lnm"), writes=("l2a_scr",))
        p.emit("vector", lambda e: e.tensor_tensor(
            scr[:], scr[:], y_sb[:], op=ALU.mult),
            reads=("l2a_scr", "l2a_ysb"), writes=("l2a_scr",))
        for g2 in range(2):
            cx.part_sum(lambda k, g2=g2: scr[:, k, g2 * 512:(g2 + 1) * 512],
                        "l2a_scr", cx.ps_st[0:1, 0, :], ("cx_psst", 0), 8)
            p.emit("vector", lambda e, g2=g2: e.tensor_scalar(
                rw2[:, 0, g2 * 512:(g2 + 1) * 512], cx.ps_st[0:1, 0, :],
                1.0 / 1024.0, None, op0=ALU.mult),
                reads=(("cx_psst", 0),), writes=("l2a_rw2",))
        for _ in range(3):
            p.emit("vector", lambda e: e.tensor_tensor(
                rw2[:, 0, :], rw2[:, 0, :], rw2[:, 1, :], op=ALU.mult),
                reads=("l2a_rw2",), writes=("l2a_rw2",))
        for g2 in range(2):
            cx.replicate_row(rw2[0:1, 0, g2 * 512:(g2 + 1) * 512], "l2a_rw2",
                             ps_m2[:, g2, :], ("l2a_psm2", g2))
            p.emit("vector", lambda e, g2=g2: e.tensor_copy(
                srep2[:, g2 * 512:(g2 + 1) * 512], ps_m2[:, g2, :]),
                reads=(("l2a_psm2", g2),), writes=("l2a_srep",))
        for oc in range(8):
            p.emit("scalar", lambda e, oc=oc: e.activation(
                out=scr[:, oc, :], in_=dpf[:, oc, :], func=AF.Identity,
                scale=lnm[:, oc:oc + 1]),
                reads=("l2a_dpf", "l2a_lnm"), writes=("l2a_scr",))
        p.emit("vector", lambda e: e.tensor_tensor(
            scr[:], scr[:], rrep[:].unsqueeze(1).broadcast_to([128, 8, 1024]),
            op=ALU.mult), reads=("l2a_scr", "l2a_rrep"), writes=("l2a_scr",))
        p.emit("vector", lambda e: e.tensor_tensor(
            dy_b[:], y_sb[:], srep2[:].unsqueeze(1).broadcast_to([128, 8, 1024]),
            op=ALU.mult), reads=("l2a_ysb", "l2a_srep"), writes=("l2a_dyb",))
        p.emit("vector", lambda e: e.tensor_tensor(
            dy_b[:], scr[:], dy_b[:], op=ALU.subtract),
            reads=("l2a_scr", "l2a_dyb"), writes=("l2a_dyb",))
        for ic in range(2):
            for rc2 in range(2):
                bnk = ic * 2 + rc2
                for oc in range(8):
                    p.emit("tensor", lambda e, ic=ic, rc2=rc2, oc=oc, bnk=bnk: e.matmul(
                        ps_m2[:, bnk, :], w1N[:, oc, ic * 128:(ic + 1) * 128],
                        dy_b[:, oc, rc2 * 512:(rc2 + 1) * 512],
                        start=(oc == 0), stop=(oc == 7)),
                        reads=("l2a_w1N", "l2a_dyb"), writes=(("l2a_psm2", bnk),))
                p.emit("vector", lambda e, ic=ic, rc2=rc2, bnk=bnk: e.tensor_tensor(
                    h_b[:, ic, rc2 * 512:(rc2 + 1) * 512] if False else
                    dsil[:, ic, rc2 * 512:(rc2 + 1) * 512],
                    dsil[:, ic, rc2 * 512:(rc2 + 1) * 512], ps_m2[:, bnk, :],
                    op=ALU.mult),
                    reads=("l2a_dsil", ("l2a_psm2", bnk)), writes=("l2a_dsil",))
        for ic in range(2):
            for rc in range(8):
                transpose_into(h_b[:, ic, rc * 128:(rc + 1) * 128], "l2a_hb",
                               hT_sb[:, rc, ic * 128:(ic + 1) * 128], "l2a_hT",
                               128, 128)
                transpose_into(dsil[:, ic, rc * 128:(rc + 1) * 128], "l2a_dsil",
                               dpT_sb[:, rc, ic * 128:(ic + 1) * 128], "l2a_dpT",
                               128, 128)
        for oc in range(8):
            bnk = oc % 2
            for rc in range(8):
                transpose_into(dy_b[:, oc, rc * 128:(rc + 1) * 128], "l2a_dyb",
                               trbuf[:, 0:128], "l2a_trbuf", 128, 128)
                p.emit("tensor", lambda e, oc=oc, rc=rc, bnk=bnk: e.matmul(
                    ps_m2[:, bnk, 0:256], trbuf[:, 0:128],
                    hT_sb[:, rc, :], start=(rc == 0), stop=(rc == 7)),
                    reads=("l2a_trbuf", "l2a_hT"), writes=(("l2a_psm2", bnk),))
            p.emit("vector", lambda e, oc=oc, bnk=bnk: e.tensor_copy(
                dW1[:, oc, :], ps_m2[:, bnk, 0:256]),
                reads=(("l2a_psm2", bnk),), writes=("l2a_dW1",))
        for d2 in range(2):
            for rc in range(8):
                for q4 in range(4):
                    transpose_into(
                        kc_sb[:, d2 * 4 + q4, rc * 128:(rc + 1) * 128], "l2a_kc",
                        trbuf[:, q4 * 128:(q4 + 1) * 128], "l2a_trbuf", 128, 128)
                for ic in range(2):
                    bnk = 2 + ic
                    p.emit("tensor", lambda e, ic=ic, d2=d2, rc=rc, bnk=bnk: e.matmul(
                        ps_m2[:, bnk, :], dpT_sb[:, rc, ic * 128:(ic + 1) * 128],
                        trbuf[:, 0:512], start=(rc == 0), stop=(rc == 7)),
                        reads=("l2a_dpT", "l2a_trbuf"), writes=(("l2a_psm2", 2 + ic),))
            for ic in range(2):
                p.emit("vector", lambda e, ic=ic, d2=d2: e.tensor_copy(
                    dW0[:, ic, d2 * 512:(d2 + 1) * 512], ps_m2[:, 2 + ic, :]),
                    reads=(("l2a_psm2", 2 + ic),), writes=("l2a_dW0",))
        p.emit("scalar", lambda e: e.activation(
            out=scr[:, 0:2, 0:1024], in_=dW0[:], func=AF.Square),
            reads=("l2a_dW0",), writes=("l2a_scr",))
        p.emit("scalar", lambda e: e.activation(
            out=scr[:, 2, 0:1024],
            in_=dW1[:].rearrange("p a b -> p (a b)")[:, 0:1024],
            func=AF.Square), reads=("l2a_dW1",), writes=("l2a_scr",))
        p.emit("scalar", lambda e: e.activation(
            out=scr[:, 3, 0:1024],
            in_=dW1[:].rearrange("p a b -> p (a b)")[:, 1024:2048],
            func=AF.Square), reads=("l2a_dW1",), writes=("l2a_scr",))
        p.emit("scalar", lambda e: e.activation(
            out=scr[:, 4, 0:8], in_=dlnp[:], func=AF.Square, scale=1.0),
            reads=("l2a_dlnp",), writes=("l2a_scr",))
        p.emit("vector", lambda e: e.tensor_scalar(
            scr[:, 4, 0:8], scr[:, 4, 0:8], 0.125, None, op0=ALU.mult),
            reads=("l2a_scr",), writes=("l2a_scr",))
        p.emit("vector", lambda e: e.memset(scr[:, 4, 8:1024], 0.0),
               writes=("l2a_scr",))
        p.emit("vector", lambda e: e.tensor_reduce(
            ccol[:, 0:1],
            scr[:, 0:5, :].rearrange("p a b -> p (a b)"), axis=AX.X,
            op=ALU.add),
            reads=("l2a_scr",), writes=("l2a_ccol",))
        p.emit("vector", lambda e: e.tensor_copy(ccbf[:, 0:1], ccol[:, 0:1]),
               reads=("l2a_ccol",), writes=("l2a_ccbf",))
        p.emit("tensor", lambda e: e.matmul(
            cx.ps_st[0:1, 0, 0:1], cx.ones_col[:], ccbf[:, 0:1],
            start=True, stop=True),
            reads=("l2a_ccbf", "cx_onescol"), writes=(("cx_psst", 0),))
        p.emit("vector", lambda e: e.tensor_copy(scs[0:1, 0:1], cx.ps_st[0:1, 0, 0:1]),
               reads=(("cx_psst", 0),), writes=("l2a_scs",))
        p.emit("vector", lambda e: e.memset(scs[0:1, 1:2], 0.0),
               writes=("l2a_scs",))
        p.dma("gpsimd", sc_in, scs[:], reads=("l2a_scs",), writes=("l2a_scin",))
        p.emit("gpsimd", lambda e: e.collective_compute(
            "AllReduce", ALU.add, replica_groups=[list(range(NCORES))],
            ins=[sc_in.opt()], outs=[sc_out.opt()]),
            reads=("l2a_scin",), writes=("l2a_scout",), cc=True)
        p.dma("gpsimd", scs[:], sc_out, reads=("l2a_scout",), writes=("l2a_scs",))
        p.emit("scalar", lambda e: e.activation(
            out=scs[0:1, 0:1], in_=scs[0:1, 0:1], func=AF.Sqrt),
            reads=("l2a_scs",), writes=("l2a_scs",))
        p.emit("vector", lambda e: e.tensor_scalar(
            scs[0:1, 0:1], scs[0:1, 0:1], 1e-6, None, op0=ALU.add),
            reads=("l2a_scs",), writes=("l2a_scs",))
        p.emit("vector", lambda e: e.reciprocal(
            scs[0:1, 0:1], scs[0:1, 0:1]), reads=("l2a_scs",), writes=("l2a_scs",))
        p.emit("vector", lambda e: e.tensor_scalar(
            scs[0:1, 0:1], scs[0:1, 0:1], 1.0, None, op0=ALU.min),
            reads=("l2a_scs",), writes=("l2a_scs",))
        p.emit("vector", lambda e: e.tensor_scalar(
            scs[0:1, 1:2], scs[0:1, 0:1], -1.0, None, op0=ALU.mult),
            reads=("l2a_scs",), writes=("l2a_scs",))
        cx.replicate_row(scs[0:1, 0:2], "l2a_scs", cx.ps_st[:, 0, 0:2],
                         ("cx_psst", 0))
        p.emit("vector", lambda e: e.tensor_copy(ccol[:, 0:2], cx.ps_st[:, 0, 0:2]),
               reads=(("cx_psst", 0),), writes=("l2a_ccol",))
        for (sm, smk, gm, gmk, wm, wmk) in (
                (s0m, "l2a_s0m", dW0, "l2a_dW0", w0m, "l2a_w0m"),
                (s1m, "l2a_s1m", dW1, "l2a_dW1", w1m, "l2a_w1m"),
                (slnm, "l2a_slnm", dlnp, "l2a_dlnp", lnm, "l2a_lnm")):
            sv = sm[:].rearrange("p a b -> p (a b)") if sm is not slnm else sm[:]
            gv = gm[:].rearrange("p a b -> p (a b)") if gm is not dlnp else gm[:]
            wv = wm[:].rearrange("p a b -> p (a b)") if wm is not lnm else wm[:]
            p.emit("vector", lambda e, sv=sv, c=c: e.tensor_scalar(
                sv, sv, becol[:, 32 + c:33 + c], None, op0=ALU.mult),
                reads=(smk, "l2a_becol"), writes=(smk,))
            p.emit("vector", lambda e, sv=sv, gv=gv: e.scalar_tensor_tensor(
                sv, gv, ccol[:, 1:2], sv, op0=ALU.mult, op1=ALU.add),
                reads=(smk, gmk, "l2a_ccol"), writes=(smk,))
            p.emit("vector", lambda e, wv=wv, c=c: e.tensor_scalar(
                wv, wv, becol[:, c:c + 1], None, op0=ALU.mult),
                reads=(wmk, "l2a_becol"), writes=(wmk,))
            p.emit("vector", lambda e, wv=wv, sv=sv: e.tensor_tensor(
                wv, wv, sv, op=ALU.add),
                reads=(wmk, smk), writes=(wmk,))

    p.pop_scope()
    p.push_scope()
    kc4 = p.sb("l2a_kc4b", [128, 8, 512], BF16)
    hb4 = p.sb("l2a_hb4b", [128, 2, 512], BF16)
    ysb4 = p.sb("l2a_ysb4", [128, 8, 512], BF16)
    scr4 = p.sb("l2a_scr4", [128, 8, 512], BF16)
    rw24 = p.sb("l2a_rw24", [1, 2, 512], F32)
    refresh_mirrors()
    qv = q_loc.rearrange("(c p) (n t) -> n p c t", p=128, t=512)
    fv_in = f_in.rearrange("(c p) (n t) -> n p c t", p=128, t=512)
    fv_out = f_out.rearrange("(c p) (n t) -> n p c t", p=128, t=512)
    for t in range(NT):
        p.dma("sync", kc4[:], qv[t], reads=("l2a_kc",),
              writes=("l2a_kc",))
        for ic in range(2):
            bnk = ic
            for dk in range(8):
                p.emit("tensor", lambda e, ic=ic, dk=dk, bnk=bnk: e.matmul(
                    ps_m2[:, bnk, :], w0T[:, dk, ic * 128:(ic + 1) * 128],
                    kc4[:, dk, 0:512], start=(dk == 0), stop=(dk == 7)),
                    reads=("l2a_w0T", "l2a_kc"), writes=(("l2a_psm2", bnk),))
            p.emit("scalar", lambda e, ic=ic, bnk=bnk: e.activation(
                out=hb4[:, ic, 0:512], in_=ps_m2[:, bnk, :], func=AF.Silu),
                reads=(("l2a_psm2", bnk),), writes=("l2a_hb",))
        for oc in range(8):
            bnk = oc % 4
            for ic in range(2):
                p.emit("tensor", lambda e, oc=oc, ic=ic, bnk=bnk: e.matmul(
                    ps_m2[:, bnk, :], w1T[:, ic, oc * 128:(oc + 1) * 128],
                    hb4[:, ic, 0:512], start=(ic == 0), stop=(ic == 1)),
                    reads=("l2a_w1T", "l2a_hb"), writes=(("l2a_psm2", bnk),))
            p.emit("vector", lambda e, oc=oc, bnk=bnk: e.tensor_copy(
                ysb4[:, oc, 0:512], ps_m2[:, bnk, :]),
                reads=(("l2a_psm2", bnk),), writes=("l2a_ysb",))
        p.dma("sync", fv_in[t], ysb4[:, :, 0:512], reads=("l2a_ysb",),
              writes=("l2a_fin",))
    p.emit("gpsimd", lambda e: e.collective_compute(
        "AllReduce", ALU.add, replica_groups=[list(range(NCORES))],
        ins=[f_in.opt()], outs=[f_out.opt()]),
        reads=("l2a_fin",), writes=("l2a_fout",), cc=True)
    for t in range(NT):
        p.dma("sync", ysb4[:, :, 0:512], fv_out[t], reads=("l2a_ysb", "l2a_fout"),
              writes=("l2a_ysb",))
        p.dma("sync", kc4[:, :, 0:512], qv[t], reads=("l2a_kc",),
              writes=("l2a_kc",))

        p.emit("scalar", lambda e: e.activation(
            out=scr4[:, :, 0:512], in_=ysb4[:, :, 0:512], func=AF.Square),
            reads=("l2a_ysb",), writes=("l2a_scr",))
        cx.part_sum(lambda k: scr4[:, k, 0:512], "l2a_scr",
                    cx.ps_st[0:1, 0, :], ("cx_psst", 0), 8)
        p.emit("vector", lambda e: e.tensor_scalar(
            rw24[:, 0, 0:512], cx.ps_st[0:1, 0, :], 1.0 / 1024.0, None,
            op0=ALU.mult), reads=(("cx_psst", 0),), writes=("l2a_rw2",))
        p.emit("scalar", lambda e: e.activation(
            out=rw24[:, 1, 0:512], in_=rw24[:, 0, 0:512], func=AF.Sqrt,
            bias=1e-6), reads=("l2a_rw2",), writes=("l2a_rw2",))
        p.emit("vector", lambda e: e.reciprocal(
            rw24[:, 1, 0:512], rw24[:, 1, 0:512]),
            reads=("l2a_rw2",), writes=("l2a_rw2",))
        cx.replicate_row(rw24[0:1, 1, 0:512], "l2a_rw2", ps_m2[:, 0, :],
                         ("l2a_psm2", 0))
        p.emit("vector", lambda e: e.tensor_tensor(
            scr4[:, :, 0:512], ysb4[:, :, 0:512],
            ps_m2[:, 0, :].unsqueeze(1).broadcast_to([128, 8, 512]),
            op=ALU.mult), reads=("l2a_ysb", ("l2a_psm2", 0)),
            writes=("l2a_scr",))
        for oc in range(8):
            p.emit("scalar", lambda e, oc=oc: e.activation(
                out=scr4[:, oc, 0:512], in_=scr4[:, oc, 0:512],
                func=AF.Identity, scale=lnm[:, oc:oc + 1]),
                reads=("l2a_scr", "l2a_lnm"), writes=("l2a_scr",))
        p.emit("vector", lambda e: e.tensor_tensor(
            scr4[:, :, 0:512], scr4[:, :, 0:512], kc4[:, :, 0:512],
            op=ALU.add), reads=("l2a_scr", "l2a_kc"), writes=("l2a_scr",))
        if t == 0:
            xtile = p.sb("l2a_xt3", [128, 8, 512], F32)
            lno = p.sb("l2a_lno", [128, 8, 512], F32)
        p.dma("sync", xtile[:], xin_t[t], reads=("l2a_xt3",), writes=("l2a_xt3",))
        p.emit("vector", lambda e: e.tensor_copy(lno[:], scr4[:, :, 0:512]),
               reads=("l2a_scr",), writes=("l2a_lno",))
        cx.ln_fm(lno[:], "l2a_lno", lno[:], "l2a_lno", "l2az")
        p.emit("vector", lambda e: e.tensor_tensor(
            lno[:], lno[:], xtile[:], op=ALU.add),
            reads=("l2a_lno", "l2a_xt3"), writes=("l2a_lno",))
        p.dma("sync", xout_t[t], lno[:], reads=("l2a_lno",), writes=("l2a_st2",))
    p.pop_scope()
    p.pop_scope()


def build_l2_attn(consts):
    p = Prog()
    xin = p.dram_in("xin", [1024, TCORE])
    xout = p.dram_out("xout", [1024, TCORE])
    c_aps = consts_aps(p, consts)
    cx = Ctx(p, c_aps)
    W = {
        "wqT": p.dram_in("w_wqT", [1024, 1024]),
        "wkT": p.dram_in("w_wkT", [1024, 1024]),
        "wvT": p.dram_in("w_wvT", [1024, 1024]),
        "nrm_pc": p.dram_in("w_nrm_pc", [128, 16]),
        "agwT": p.dram_in("w_agwT", [1024, 192]),
        "w0sh": p.dram_in("w_w0sh", [256, 1024]),
        "w1shT": p.dram_in("w_w1sh", [1024, 256]),
        "memln_pc": p.dram_in("w_memln", [128, 8]),
    }
    emit_l2_attn(p, cx, xin, xout, W, c_aps)
    nc = p.finalize()
    return nc, p


def l2_attn_in_map(p, consts, inputs, core=0):
    g = lambda n: np.asarray(inputs[n], np.float32)
    agw = np.zeros((1024, 192), np.float32)
    for gi, nm in enumerate(("L2_attn_alpha_w", "L2_attn_theta_w",
                             "L2_attn_eta_w")):
        w = g(nm).reshape(64, 1024)
        agw[:, gi * 64:(gi + 1) * 64] = w.T
    sl = slice(core * 256, (core + 1) * 256)
    return {
        "c_fvals_in": p.fconst_host_array(),
        "w_wqT": np.ascontiguousarray(g("L2_attn_wq_w").T),
        "w_wkT": np.ascontiguousarray(g("L2_attn_wk_w").T),
        "w_wvT": np.ascontiguousarray(g("L2_attn_wv_w").T),
        "w_nrm_pc": np.concatenate([pc(g("L2_attn_q_norm_w")),
                                    pc(g("L2_attn_k_norm_w"))], 1),
        "w_agwT": agw,
        "w_w0sh": np.ascontiguousarray(g("L2_attn_mem_w0")[sl, :]),
        "w_w1sh": np.ascontiguousarray(g("L2_attn_mem_w1")[:, sl]),
        "w_memln": pc(g("L2_attn_mem_ln_w")),
        **consts_in_map(consts),
    }


_CHAIN = (
    ("l0_attn", "build_l0_attn", "l0_attn_in_map"),
    ("l0_ffn", "build_l0_ffn", "l0_ffn_in_map"),
    ("l1_attn", "build_l1_attn", "l1_attn_in_map"),
    ("l1_ffn", "build_l1_ffn", "l1_ffn_in_map"),
    ("l2_attn", "build_l2_attn", "l2_attn_in_map"),
    ("l2_ffn", "build_l2_ffn", "l2_ffn_in_map"),
    ("l3_attn", "build_l3_attn", "l3_attn_in_map"),
    ("l3_ffn", "build_l3_ffn", "l3_ffn_in_map"),
)


def _wd_l0_attn(p):
    return {
        "lnwb_pc": p.dram_in("w_lnwb", [128, 16]),
        "aexp": p.dram_in("w_aexp", [128, 256]),
        "dcv_pc": p.dram_in("w_dcv", [128, 112]),
        "xpT": p.dram_in("w_xpT", [2048, 128]),
        "dtpT": p.dram_in("w_dtpT", [64, 2048]),
        "ipT": p.dram_in("w_ipT", [1024, 4096]),
        "opT": p.dram_in("w_opT", [2048, 1024]),
    }


def _wd_l0_ffn(p):
    return {
        "wqkv": p.dram_in("w_wqkv", [2, 1024, 1536]),
        "qksc": p.dram_in("w_qksc", [2, 128, 2]),
        "wo": p.dram_in("w_wo", [2, 512, 1024]),
        "lnb": p.dram_in("w_lnb", [2, 128, 16]),
        "w1pad": p.dram_in("w_w1pad", [2, 1024, 5632]),
        "b1p": p.dram_in("w_b1p", [2, 128, 44]),
        "w2pad": p.dram_in("w_w2pad", [2, 2816, 1024]),
        "b2p": p.dram_in("w_b2p", [2, 128, 8]),
    }


def _wd_l1_attn(p):
    return {
        "WqT": p.dram_in("w_WqT", [1024, 1024]),
        "WkT": p.dram_in("w_WkT", [1024, 1024]),
        "WvT": p.dram_in("w_WvT", [1024, 1024]),
        "WoT": p.dram_in("w_WoT", [1024, 1024]),
        "bqkv_pc": p.dram_in("w_bqkv_pc", [128, 24]),
        "bo_pc": p.dram_in("w_bo_pc", [128, 8]),
    }


def _wd_l1_ffn(p):
    return {
        "Wg": p.dram_in("w_Wg", [1024, 8]),
        "Ws1": p.dram_in("w_Ws1", [1024, 2048]),
        "Ws3": p.dram_in("w_Ws3", [1024, 2048]),
        "Ws2": p.dram_in("w_Ws2", [2048, 1024]),
        "We1": p.dram_in("w_We1", [8, 1024, 2048]),
        "We3": p.dram_in("w_We3", [8, 1024, 2048]),
        "We2": p.dram_in("w_We2", [8, 2048, 1024]),
    }


def _wd_l2_attn(p):
    return {
        "wqT": p.dram_in("w_wqT", [1024, 1024]),
        "wkT": p.dram_in("w_wkT", [1024, 1024]),
        "wvT": p.dram_in("w_wvT", [1024, 1024]),
        "nrm_pc": p.dram_in("w_nrm_pc", [128, 16]),
        "agwT": p.dram_in("w_agwT", [1024, 192]),
        "w0sh": p.dram_in("w_w0sh", [256, 1024]),
        "w1shT": p.dram_in("w_w1sh", [1024, 256]),
        "memln_pc": p.dram_in("w_memln", [128, 8]),
    }


def _wd_l2_ffn(p):
    return {
        "WpT": p.dram_in("w_WpT", [1024, 1024]),
        "bp_pc": p.dram_in("w_bp_pc", [128, 8]),
        "WeT": p.dram_in("w_WeT", [8, 1024, 1024]),
        "be": p.dram_in("w_be", [8, 1024]),
        "VgT": p.dram_in("w_VgT", [1024, 512]),
    }


def _wd_l3_attn(p):
    return {
        "wqT": p.dram_in("w_wqT", [1024, 1024]),
        "wkT": p.dram_in("w_wkT", [1024, 1024]),
        "wvT": p.dram_in("w_wvT", [1024, 1024]),
        "woT": p.dram_in("w_woT", [1024, 1024]),
        "bq_pc": p.dram_in("w_bq_pc", [128, 8]),
        "bkv_pc": p.dram_in("w_bkv_pc", [128, 16]),
        "bo_pc": p.dram_in("w_bo_pc", [128, 8]),
        "sw1T": p.dram_in("w_sw1T", [1024, 512]),
        "sw2T": p.dram_in("w_sw2T", [512, 3]),
        "pw1T": p.dram_in("w_pw1T", [3, 1024, 512]),
        "pw2T": p.dram_in("w_pw2T", [3, 512, 16]),
        "sb_pc": p.dram_in("w_sb_pc", [128, 32]),
    }


def _wd_l3_ffn(p):
    return {
        "WrT": p.dram_in("w_WrT", [1024, 1024]),
        "WiT": p.dram_in("w_WiT", [1024, 1024]),
        "bias_pc": p.dram_in("w_bias_pc", [128, 8]),
    }


_FUSED = (
    ("l0a", _wd_l0_attn, emit_l0_attn, "l0_attn_in_map", False),
    ("l0f", _wd_l0_ffn, emit_l0_ffn, "l0_ffn_in_map", False),
    ("l1a", _wd_l1_attn, emit_l1_attn, "l1_attn_in_map", False),
    ("l1f", _wd_l1_ffn, emit_l1_ffn, "l1_ffn_in_map", False),
    ("l2a", _wd_l2_attn, emit_l2_attn, "l2_attn_in_map", True),
    ("l2f", _wd_l2_ffn, emit_l2_ffn, "l2_ffn_in_map", False),
    ("l3a", _wd_l3_attn, emit_l3_attn, "l3_attn_in_map", False),
    ("l3f", _wd_l3_ffn, emit_l3_ffn, "l3_ffn_in_map", False),
)


def build_fused(consts):
    p = Prog()
    xin = p.dram_in("xin", [1024, TCORE])
    xout = p.dram_out("xout", [1024, TCORE])
    c_aps = consts_aps(p, consts)
    cx = Ctx(p, c_aps)
    acts = [xin] + [p.dram_tmp(f"act{i}", [1024, TCORE]) for i in range(7)] \
        + [xout]
    for i, (pref, wd, emitter, _imf, _percore) in enumerate(_FUSED):
        p._pref = pref + "_"
        W = wd(p)
        p._pref = ""
        emitter(p, cx, acts[i], acts[i + 1], W, c_aps)
    nc = p.finalize()
    return nc, p


def fused_in_maps(p, consts, inputs):
    g = globals()
    base = {"c_fvals_in": p.fconst_host_array(), **consts_in_map(consts)}
    maps = [dict(base) for _ in range(NCORES)]
    for pref, _wd, _em, imf, percore in _FUSED:
        if percore:
            for i in range(NCORES):
                for k, v in g[imf](p, consts, inputs, core=i).items():
                    if k.startswith("w_"):
                        maps[i][pref + "_" + k] = v
        else:
            m = g[imf](p, consts, inputs)
            for k, v in m.items():
                if k.startswith("w_"):
                    for i in range(NCORES):
                        maps[i][pref + "_" + k] = v
    return maps


def kernel(**inputs):
    consts = make_consts()
    x = np.asarray(inputs["L0_attn_x"], dtype=np.float32)
    shards = [fm(x[2 * i:2 * i + 2]) for i in range(NCORES)]
    nc, p = build_fused(consts)
    maps = fused_in_maps(p, consts, inputs)
    in_maps = [dict(maps[i], xin=shards[i]) for i in range(NCORES)]
    res = run_bass_kernel_spmd(nc, in_maps, core_ids=list(range(NCORES)))
    out = np.concatenate([np.ascontiguousarray(r["xout"]).T.reshape(2, 2048, 1024)
                          for r in res.results])
    return out
